# version 07
# baseline (speedup 1.0000x reference)
import numpy as np
import ml_dtypes
import concourse.bass as bass
import concourse.mybir as mybir
from concourse.bass_utils import run_bass_kernel_spmd

F32 = mybir.dt.float32
BF16 = mybir.dt.bfloat16
I32 = mybir.dt.int32
AF = mybir.ActivationFunctionType
ALU = mybir.AluOpType
AX = mybir.AxisListType.X
A = bass.AP

DM = 4096
H, P, NS, G, KC = 64, 64, 128, 8, 4
NI = 4096
GN = G * NS
CONV = NI + 2 * GN
PROJ = NI + CONV + H
EPS = 1e-5
Q = 128
NCH = 8
NTOK = Q * NCH
HALO = 4
NROW = 1152
TT = (0, 512, 1024)
TW = (512, 512, HALO)
COL_GATE, COL_X, COL_B, COL_C, COL_DT = 0, NI, 2 * NI, 2 * NI + GN, PROJ - H
PHASES = 9
DEBUG = False
LN_SERIAL = False
WSL_TEST = False
LN_NOSTATS = False
RH0, EV0, SPA, SPB = 0, 1024, 1536, 2048


def build_program(phases=9, debug=False):
    LN_SERIAL = globals().get('LN_SERIAL_G', False)
    WSL_TEST = globals().get('WSL_TEST_G', False)
    LN_NOSTATS = globals().get('LN_NOSTATS_G', False)
    nc = bass.Bass("TRN2", target_bir_lowering=False,
                   detect_race_conditions=False)

    x_own = nc.dram_tensor("x_own", [NROW, DM], F32, kind="ExternalInput").ap()
    x_pre = nc.dram_tensor("x_pre", [NROW, DM], F32, kind="ExternalInput").ap()
    w_in_t = nc.dram_tensor("w_in_t", [DM, PROJ], F32, kind="ExternalInput").ap()
    w_out_t = nc.dram_tensor("w_out_t", [DM, DM], F32, kind="ExternalInput").ap()
    ffn_k = nc.dram_tensor("ffn_k", [DM, DM], F32, kind="ExternalInput").ap()
    a_cat = nc.dram_tensor("a_cat", [DM, 544], F32, kind="ExternalInput").ap()
    b_gat = nc.dram_tensor("b_gat", [32, 16 * DM], F32, kind="ExternalInput").ap()
    b_shd = nc.dram_tensor("b_shd", [32, DM], F32, kind="ExternalInput").ap()
    bias_r = nc.dram_tensor("bias_r", [1, DM], F32, kind="ExternalInput").ap()
    nw_r = nc.dram_tensor("nw_r", [1, DM], F32, kind="ExternalInput").ap()
    dvec_r = nc.dram_tensor("dvec_r", [1, H], F32, kind="ExternalInput").ap()
    alog_r = nc.dram_tensor("alog_r", [1, H], F32, kind="ExternalInput").ap()
    dtb_r = nc.dram_tensor("dtb_r", [1, H], F32, kind="ExternalInput").ap()
    convw = nc.dram_tensor("convw", [128, 48 * KC], F32, kind="ExternalInput").ap()
    convb = nc.dram_tensor("convb", [128, 48], F32, kind="ExternalInput").ap()
    gate_r = nc.dram_tensor("gate_r", [1, 16], F32, kind="ExternalInput").ap()
    eye_i = nc.dram_tensor("eye_i", [128, 128], F32, kind="ExternalInput").ap()
    triu_i = nc.dram_tensor("triu_i", [128, 128], F32, kind="ExternalInput").ap()
    stri_i = nc.dram_tensor("stri_i", [128, 128], F32, kind="ExternalInput").ap()
    prm_f = nc.dram_tensor("prm_f", [128, 8], F32, kind="ExternalInput").ap()
    prm_i = nc.dram_tensor("prm_i", [1, 16], I32, kind="ExternalInput").ap()
    nonce_r = nc.dram_tensor("nonce_r", [1, 8], I32, kind="ExternalInput").ap()

    out = nc.dram_tensor("out", [NTOK, DM], F32, kind="ExternalOutput").ap()

    pay_T = nc.dram_tensor("pay_T", [4, 128, NI], BF16, kind="Internal",
                           addr_space="Shared").ap()
    pay_D = nc.dram_tensor("pay_D", [4, H, 1], F32, kind="Internal",
                           addr_space="Shared").ap()
    pay_F = nc.dram_tensor("pay_F", [4, 1, 4], I32, kind="Internal",
                           addr_space="Shared").ap()
    _ik = "ExternalOutput" if debug else "Internal"
    xtm_h = nc.dram_tensor("xtm_h", [NCH, 128, NI], BF16, kind=_ik).ap()
    btm_h = nc.dram_tensor("btm_h", [NCH, 128, GN], BF16, kind=_ik).ap()
    bcm_h = nc.dram_tensor("bcm_h", [G, 128, NTOK], BF16, kind=_ik).ap()
    ccm_h = nc.dram_tensor("ccm_h", [G, 128, NTOK], BF16, kind=_ik).ap()
    gate_h = nc.dram_tensor("gate_h", [NCH, 128, DM], BF16, kind=_ik).ap()
    y_h = nc.dram_tensor("y_h", [NCH, 128, NI], BF16, kind=_ik).ap()
    a2_h = nc.dram_tensor("a2_h", [128, 2048], BF16, kind="Internal").ap()
    be_h = nc.dram_tensor("be_h", [32, DM], BF16, kind="Internal").ap()
    d_y = (nc.dram_tensor("d_y", [128, NCH * NI], BF16, kind="ExternalOutput").ap()
           if debug else None)
    d_sin = (nc.dram_tensor("d_sin", [128, NI], BF16, kind="ExternalOutput").ap()
             if debug else None)

    ctxs = []

    def sb(name, shape, dtype):
        g = nc.sbuf_tensor(name, shape, dtype)
        t = g.__enter__()
        ctxs.append(g)
        return t

    def psalloc(name):
        g = nc.psum_tensor(name, [128, 512], F32)
        t = g.__enter__()
        ctxs.append(g)
        return t

    eye = sb("eye", [128, 128], F32)
    triu = sb("triu", [128, 128], F32)
    stri = sb("stri", [128, 128], F32)
    ones1 = sb("ones1", [1, 128], F32)
    ones64 = sb("ones64", [64, 128], F32)
    emsk = sb("emsk", [64, 128], F32)
    zrow = sb("zrow", [1, H], F32)
    zcol = sb("zcol", [64, 1], F32)
    dprc = sb("dprc", [64, 1], F32)
    dwnc = sb("dwnc", [64, 1], F32)
    drvc = sb("drvc", [64, 1], F32)
    rowb = sb("rowb", [1, 4 * H], F32)
    rowb2 = sb("rowb2", [1, 256], F32)
    biasl = sb("biasl", [1, 256], BF16)
    ones1b = sb("ones1b", [1, 128], BF16)
    abc = sb("abc", [128, H], F32)
    dtbbc = sb("dtbbc", [128, H], F32)
    dbc = sb("dbc", [128, H], F32)
    nwt = sb("nwt", [128, 256], BF16)
    cw = sb("cw", [128, 48 * KC], F32)
    cb = sb("cb", [128, 48], F32)
    gcol = sb("gcol", [128, 16], F32)
    prmf = sb("prmf", [128, 8], F32)
    prmi = sb("prmi", [1, 16], I32)
    flagsb = sb("flagsb", [1, 4], I32)
    noncesb = sb("noncesb", [1, 8], I32)
    drbc = sb("drbc", [128, H], F32)
    drb3 = sb("drb3", [128, H], F32)
    drv2 = sb("drv2", [64, 1], F32)
    ccol = sb("ccol", [64, 4], F32)
    emsk2 = sb("emsk2", [64, 64], F32)
    dobc = sb("dobc", [128, H], F32)

    xnar = sb("xnar", [128, 32 * NROW], BF16)
    wsl0 = sb("wsl0", [128, 8192], BF16)
    strip = sb("strip", [128, NTOK + HALO], F32)
    cacc = sb("cacc", [128, NTOK], F32)
    xbcbf = sb("xbcbf", [128, NTOK], BF16)
    tpstg = sb("tpstg", [128, NTOK], BF16)
    ccmc = sb("ccmc", [128, GN], BF16)
    bcmc = sb("bcmc", [128, GN], BF16)
    btmc = sb("btmc", [128, GN], BF16)
    dtp = sb("dtp", [128, 512], F32)
    a_tm = sb("a_tm", [128, 512], F32)
    l_tm = sb("l_tm", [128, 512], F32)
    wexp = sb("wexp", [128, H], F32)
    lamtm = sb("lamtm", [128, H], F32)
    lambc = sb("lambc", [128, H], F32)
    lsg = sb("lsg", [128, H], F32)
    qtm = sb("qtm", [128, H], F32)
    S = sb("S", [128, NI], F32)
    sbfw = sb("sbfw", [128, 512], BF16)
    tpre = sb("tpre", [128, NI], BF16)
    sin = sb("sin", [128, NI], BF16)
    xtm = [sb("xtm0", [128, NI], BF16), sb("xtm1", [128, NI], BF16)]
    ubuf = sb("ubuf", [128, NI], BF16)
    y2b = sb("y2b", [128, DM], F32)
    ssb = sb("ssb", [128, 16], F32)
    gld = [sb("gld0", [128, 512], BF16), sb("gld1", [128, 512], BF16)]
    ynst = sb("ynst", [128, DM], BF16)
    lnst = sb("lnst", [128, 8], F32)

    ystg = gld
    wsl = [wsl0, wsl0 if WSL_TEST else S.bitcast(BF16)]
    ptl = [y2b[:, 3072:3328], y2b[:, 3328:3584]]
    ost = [y2b[:, 3584:3840], y2b[:, 3840:4096]]
    gmsb = ynst
    lpw = ynst
    mpw = ynst
    udw = ynst
    xdw = ynst

    xin = S
    dtraw = l_tm

    psb = [psalloc(f"psb{i}") for i in range(8)]

    class Sem:
        __slots__ = ("h", "v")

        def __init__(self, h):
            self.h = h
            self.v = 0

    def mksem(name):
        g = nc.semaphore(name)
        h = g.__enter__()
        ctxs.append(g)
        return Sem(h)

    s_x = mksem("s_x")
    s_w = mksem("s_w")
    s_tp = mksem("s_tp")
    s_bi = mksem("s_bi")
    s_bo = mksem("s_bo")
    s_gl = mksem("s_gl")
    s_so = mksem("s_so")
    s_pay = mksem("s_pay")
    s_rcv = mksem("s_rcv")
    s_dv = mksem("s_dv")
    s_ac = mksem("s_ac")
    s_gp = mksem("s_gp")
    s_tb = mksem("s_tb")
    s_ab = mksem("s_ab")
    s_ph = mksem("s_ph")
    s_pe = [mksem(f"s_pe{i}") for i in range(8)]
    s_fr = [mksem(f"s_fr{i}") for i in range(8)]
    DMAS = [s_x, s_w, s_tp, s_bi, s_bo, s_gl, s_so, s_pay]
    ALLSEMS = DMAS + [s_rcv, s_dv, s_ac, s_gp, s_tb, s_ab, s_ph] + s_pe + s_fr

    blkg = nc.Block()
    block = blkg.__enter__()
    ctxs.append(blkg)

    def emit(me):
        for _s in ALLSEMS:
            _s.v = 0
        E = {"sync": nc.sync, "gpsimd": nc.gpsimd, "vector": nc.vector,
             "scalar": nc.scalar, "tensor": nc.tensor}
        eng = E[me]
        regs = {}
        if me == "gpsimd":
            for rn in ["s1T", "s1D", "s1F", "s2T", "s2D", "s2F", "rT", "rD",
                       "rF", "non", "r2T", "r2D", "nd", "cnt", "flg"]:
                g = eng.register("r_" + rn)
                regs[rn] = g.__enter__()
                ctxs.append(g)

        def w8(en, sem, v=None):
            if me == en:
                eng.wait_ge(sem.h, sem.v if v is None else v)

        def dma(en, outap, inap, sem):
            sem.v += 16
            if me == en:
                eng.dma_start(out=outap, in_=inap).then_inc(sem.h, 16)
            return sem.v

        def dmat(en, outap, inap, sem):
            sem.v += 16
            if me == en:
                eng.dma_start_transpose(outap, inap).then_inc(sem.h, 16)
            return sem.v

        def mm(outap, lhsT, rhs, start, stop, sem=None):
            if sem is not None:
                sem.v += 1
            if me == "tensor":
                i = eng.matmul(outap, lhsT, rhs, start=start, stop=stop)
                if sem is not None:
                    i.then_inc(sem.h, 1)

        def tr(outap, inap, sem=None, ident=None):
            if sem is not None:
                sem.v += 1
            if me == "tensor":
                i = eng.transpose(outap, inap, eye[:] if ident is None else ident)
                if sem is not None:
                    i.then_inc(sem.h, 1)

        def vec(fn, sem=None):
            if sem is not None:
                sem.v += 1
            if me == "vector":
                i = fn(eng)
                if sem is not None:
                    i.then_inc(sem.h, 1)

        def sca(fn, sem=None):
            if sem is not None:
                sem.v += 1
            if me == "scalar":
                i = fn(eng)
                if sem is not None:
                    i.then_inc(sem.h, 1)

        def gp(fn, sem=None, inc=1):
            if sem is not None:
                sem.v += inc
            if me == "gpsimd":
                i = fn(eng)
                if sem is not None:
                    i.then_inc(sem.h, inc)

        def barrier():
            s_tb.v += 1
            if me == "tensor":
                eng.nop().then_inc(s_tb.h, 1)
            s_ab.v += 1
            if me == "scalar":
                eng.nop().then_inc(s_ab.h, 1)
            s_gp.v += 1
            if me == "gpsimd":
                eng.nop().then_inc(s_gp.h, 1)
            if me == "vector":
                for s in DMAS + [s_tb, s_ab, s_gp]:
                    eng.wait_ge(s.h, s.v)
            s_ph.v += 1
            if me == "vector":
                eng.nop().then_inc(s_ph.h, 1)
            if me != "vector":
                eng.wait_ge(s_ph.h, s_ph.v)

        def l_cm_col(j):
            return strip[0:64, j:j + 1]

        def pe_take(bank):
            w8("tensor", s_fr[bank])

        def use(en, bank):
            w8(en, s_pe[bank])

        def rel(bank):
            return s_fr[bank]

        dma("sync", eye[:], eye_i, s_x)
        dma("sync", triu[:], triu_i, s_x)
        dma("sync", stri[:], stri_i, s_x)
        dma("sync", cw[:], convw, s_x)
        dma("sync", cb[:], convb, s_x)
        dma("sync", prmf[:], prm_f, s_x)
        dma("sync", prmi[:], prm_i, s_x)
        dma("sync", noncesb[:], nonce_r, s_x)
        dma("sync", gcol[:, 0:16], A(gate_r.tensor, 0, [[0, 128], [1, 16]]), s_x)
        dma("sync", rowb[:, 0:H], alog_r, s_x)
        dma("sync", rowb[:, H:2 * H], dtb_r, s_x)
        dma("sync", rowb[:, 2 * H:3 * H], dvec_r, s_x)
        w8("vector", s_x)
        w8("scalar", s_x)
        w8("gpsimd", s_x)
        w8("tensor", s_x)
        vec(lambda v: v.memset(ones1[:], 1.0))
        vec(lambda v: v.memset(ones1b[:], 1.0))
        vec(lambda v: v.memset(ones64[:], 1.0))
        vec(lambda v: v.memset(zcol[:], 0.0))
        vec(lambda v: v.memset(zrow[:], 0.0), s_dv)
        sca(lambda s: s.activation(rowb[:, 3 * H:4 * H], rowb[:, 0:H], AF.Exp), s_ac)
        w8("vector", s_ac)
        vec(lambda v: v.tensor_scalar_mul(rowb[:, 3 * H:4 * H], rowb[:, 3 * H:4 * H],
                                          -1.0), s_dv)
        w8("tensor", s_dv)
        pe_take(0)
        mm(psb[0][:, 0:H], ones1[:], rowb[:, 3 * H:4 * H], True, True)
        mm(psb[0][:, H:2 * H], ones1[:], rowb[:, H:2 * H], True, True)
        mm(psb[0][:, 2 * H:3 * H], ones1[:], rowb[:, 2 * H:3 * H], True, True, s_pe[0])
        use("vector", 0)
        vec(lambda v: v.tensor_copy(abc[:], psb[0][:, 0:H]))
        vec(lambda v: v.tensor_copy(dtbbc[:], psb[0][:, H:2 * H]))
        vec(lambda v: v.tensor_copy(dbc[:], psb[0][:, 2 * H:3 * H]), rel(0))
        if me == "gpsimd":
            names = ["s1T", "s1D", "s1F", "s2T", "s2D", "s2F", "rT", "rD", "rF",
                     "non", "r2T", "r2D"]
            for j, rn in enumerate(names):
                eng.reg_load(regs[rn], prmi[0:1, j:j + 1])
        barrier()

        def ln_phase(x_dram):
            xbufs = [xin, y2b]
            xrel = [s_ac.v, s_ac.v]
            ubuf_rel = s_dv.v
            for i in range(9):
                b = i % 2
                xb = xbufs[b]
                c0 = b * 4
                w8("sync", s_ac, xrel[b])
                dma("sync", xb[:], x_dram[i * 128:(i + 1) * 128, :], s_x)
                w8("vector", s_x)
                w8("vector", s_ac, xrel[b])
                for j in range(8):
                    vec(lambda v, j=j, b=b, xb=xb: v.bn_stats(
                        strip[:, b * 48 + j * 6:b * 48 + (j + 1) * 6],
                        xb[:, j * 512:(j + 1) * 512]), s_dv if j == 7 else None)
                w8("vector", s_dv)
                vec(lambda v, c0=c0, b=b: v.bn_aggr(
                    lnst[:, c0:c0 + 2], strip[:, b * 48:b * 48 + 48]), s_dv)
                w8("vector", s_dv)
                vec(lambda v, c0=c0: v.tensor_scalar_add(
                    lnst[:, c0 + 1:c0 + 2], lnst[:, c0 + 1:c0 + 2], EPS), s_dv)
                w8("vector", s_dv)
                vec(lambda v, c0=c0: v.reciprocal(
                    lnst[:, c0 + 2:c0 + 3], lnst[:, c0 + 1:c0 + 2]), s_dv)
                w8("scalar", s_dv)
                sca(lambda s, c0=c0: s.activation(
                    lnst[:, c0 + 2:c0 + 3], lnst[:, c0 + 2:c0 + 3], AF.Sqrt), s_ac)
                w8("vector", s_ac)
                vec(lambda v, c0=c0: v.tensor_scalar(
                    lnst[:, c0 + 3:c0 + 4], lnst[:, c0:c0 + 1],
                    lnst[:, c0 + 2:c0 + 3], -1.0, ALU.mult, ALU.mult), s_dv)
                w8("scalar", s_dv)
                w8("scalar", s_tp, s_tp.v)
                if LN_NOSTATS:
                    sca(lambda s, xb=xb: s.activation(ynst[:], xb[:], AF.Identity),
                        s_ac)
                else:
                    sca(lambda s, xb=xb, c0=c0: s.activation(
                        ynst[:], xb[:], AF.Identity,
                        bias=lnst[:, c0 + 3:c0 + 4], scale=lnst[:, c0 + 2:c0 + 3]),
                        s_ac)
                xrel[b] = s_ac.v
                w8("sync", s_ac)
                w8("sync", s_dv, ubuf_rel)
                dmat("sync", A(ubuf, 0, [[NI, 128], [128, 32], [1, 128]]),
                     ynst[:], s_tp)
                w8("vector", s_tp)
                vec(lambda v, i=i: v.tensor_copy(
                    A(xnar, i * 128, [[32 * NROW, 128], [NROW, 32], [1, 128]]),
                    ubuf[:]), s_dv)
                ubuf_rel = s_dv.v

        def conv_silu(cslot):
            vec(lambda v, cslot=cslot: v.tensor_scalar(
                cacc[:], strip[:, 1:1 + NTOK], cw[:, cslot * KC:cslot * KC + 1],
                None, ALU.mult))
            for k in range(1, KC):
                vec(lambda v, k=k, cslot=cslot: v.scalar_tensor_tensor(
                    cacc[:], strip[:, 1 + k:1 + k + NTOK],
                    cw[:, cslot * KC + k:cslot * KC + k + 1], cacc[:],
                    ALU.mult, ALU.add))
            vec(lambda v: v.engine_nop(), s_dv)
            w8("scalar", s_dv)
            sca(lambda s, cslot=cslot: s.activation(
                xbcbf[:], cacc[:], AF.Silu, bias=cb[:, cslot:cslot + 1]), s_ac)

        def cm_group(plan):
            slab_rel = [None, None]
            cm_group.slab_rel = slab_rel
            strip_rel = 0
            xbc_guard = (s_dv, 0)
            for idx, (kind, cc) in enumerate(plan):
                b = idx % 2
                col = {"x": COL_X, "bo": COL_B, "bp": COL_B, "c": COL_C}[kind] + cc * 128
                if slab_rel[b] is not None:
                    w8("gpsimd", s_pe[slab_rel[b][0]], slab_rel[b][1])
                dma("gpsimd", wsl[b][:, 0:4096],
                    A(w_in_t.tensor, col, [[PROJ, 128], [128 * PROJ, 32], [1, 128]]),
                    s_w)
                for ti in range(3):
                    bank = (idx * 3 + ti) % 2
                    pe_take(bank)
                    w8("tensor", s_w)
                    for ic in range(32):
                        mm(psb[bank][:, 0:TW[ti]], wsl[b][:, ic * 128:(ic + 1) * 128],
                           A(xnar, ic * NROW + TT[ti], [[32 * NROW, 128], [1, TW[ti]]]),
                           ic == 0, ic == 31, s_pe[bank] if ic == 31 else None)
                    if ti == 2:
                        slab_rel[b] = (bank, s_pe[bank].v)
                    use("scalar", bank)
                    w8("scalar", s_dv, strip_rel)
                    sca(lambda s, bank=bank, ti=ti: s.copy(
                        strip[:, TT[ti]:TT[ti] + TW[ti]], psb[bank][:, 0:TW[ti]]),
                        rel(bank))
                w8("vector", s_fr[0], s_fr[0].v)
                w8("vector", s_fr[1], s_fr[1].v)
                w8("scalar", xbc_guard[0], xbc_guard[1])
                conv_silu({"x": cc, "bo": 32 + cc, "bp": 32 + cc, "c": 40 + cc}[kind])
                strip_rel = s_dv.v
                if kind in ("x", "bp"):
                    w8("sync", s_ac)
                    dmat("sync", A(tpstg, 0, [[NTOK, 128], [128, 8], [1, 128]]),
                         xbcbf[:], s_tp)
                    xbc_guard = (s_tp, s_tp.v)
                    w8("sync", s_tp)
                    if kind == "x":
                        dma("sync", A(xtm_h.tensor, cc * 128,
                                      [[NI, 128], [128 * NI, 8], [1, 128]]), tpstg[:],
                            s_bo)
                    else:
                        dma("sync", A(btm_h.tensor, cc * 128,
                                      [[GN, 128], [128 * GN, 8], [1, 128]]), tpstg[:],
                            s_bo)
                else:
                    w8("sync", s_ac)
                    hb = bcm_h if kind == "bo" else ccm_h
                    dma("sync", hb[cc], xbcbf[:], s_bo)
                    xbc_guard = (s_bo, s_bo.v)

        def tm_group(own):
            tiles = ([("g", COL_GATE + i * 256, 256) for i in range(16)] if own
                     else [])
            tiles.append(("d", COL_DT, 64))
            slab_rel = list(getattr(cm_group, "slab_rel", [None, None]))
            gld_rel = [s_bo.v, s_bo.v]
            for tj, (kind, col, wid) in enumerate(tiles):
                b = tj % 2
                if slab_rel[b] is not None:
                    w8("gpsimd", s_pe[slab_rel[b][0]], slab_rel[b][1])
                dma("gpsimd", wsl[b][:, 0:32 * wid],
                    A(w_in_t.tensor, col, [[PROJ, 128], [128 * PROJ, 32], [1, wid]]),
                    s_w)
                if kind == "g":
                    bank5 = 5
                    pe_take(bank5)
                    w8("tensor", s_dv, s_dv.v)
                    coff = col - COL_GATE
                    w8("sync", s_pe[bank5], s_pe[bank5].v)
                    dma("sync", rowb2[:], A(nw_r.tensor, coff, [[256, 1], [1, 256]]),
                        s_x)
                    w8("tensor", s_x)
                    mm(psb[bank5][:, 0:256], ones1[:], rowb2[:], True, True,
                       s_pe[bank5])
                    use("vector", bank5)
                    vec(lambda v, bank5=bank5: v.tensor_copy(
                        nwt[:], psb[bank5][:, 0:256]), rel(bank5))
                    vec(lambda v: v.engine_nop(), s_dv)
                for mi in range(8):
                    bank = (tj * 8 + mi) % 2
                    pe_take(bank)
                    w8("tensor", s_w)
                    for ic in range(32):
                        mm(psb[bank][:, 0:wid],
                           A(xnar, ic * NROW + HALO + mi * 128,
                             [[32 * NROW, 128], [1, 128]]),
                           wsl[b][:, ic * wid:(ic + 1) * wid],
                           ic == 0, ic == 31, s_pe[bank] if ic == 31 else None)
                    if mi == 7:
                        slab_rel[b] = (bank, s_pe[bank].v)
                    if kind == "g":
                        gb = (tj * 8 + mi) % 2
                        use("scalar", bank)
                        sca(lambda s, bank=bank: s.activation(
                            ynst[:, 0:256], psb[bank][:, 0:256], AF.Silu), s_ac)
                        w8("vector", s_ac)
                        w8("vector", s_bo, gld_rel[gb])
                        w8("vector", s_dv, s_dv.v)
                        vec(lambda v, gb=gb: v.tensor_tensor(
                            gld[gb][:, 0:256], ynst[:, 0:256], nwt[:], ALU.mult),
                            rel(bank))
                        vec(lambda v: v.engine_nop(), s_dv)
                        w8("scalar", s_fr[bank], s_fr[bank].v)
                        coff = col - COL_GATE
                        dma("scalar",
                            A(gate_h.tensor, mi * 128 * DM + coff,
                              [[DM, 128], [1, 256]]), gld[gb][:, 0:256], s_bo)
                        gld_rel[gb] = s_bo.v
                    else:
                        use("vector", bank)
                        vec(lambda v, bank=bank, mi=mi: v.tensor_copy(
                            dtraw[:, mi * 64:(mi + 1) * 64], psb[bank][:, 0:64]),
                            rel(bank))

        def dt_chain():
            vec(lambda v: v.tensor_tensor(
                dtp[:], dtraw[:], A(dtbbc, 0, [[H, 128], [0, 8], [1, H]]), ALU.add))
            vec(lambda v: v.tensor_scalar_mul(y2b[:, SPA:SPA + 512], dtp[:], -1.0))
            vec(lambda v: v.tensor_tensor(y2b[:, SPA:SPA + 512], y2b[:, SPA:SPA + 512],
                                          dtp[:], ALU.min), s_dv)
            w8("scalar", s_dv)
            sca(lambda s: s.activation(y2b[:, SPB:SPB + 512], y2b[:, SPA:SPA + 512],
                                       AF.Exp), s_ac)
            sca(lambda s: s.activation(y2b[:, SPB:SPB + 512], y2b[:, SPB:SPB + 512],
                                       AF.Ln, bias=1.0), s_ac)
            w8("vector", s_ac)
            vec(lambda v: v.tensor_relu(y2b[:, SPA:SPA + 512], dtp[:]))
            vec(lambda v: v.tensor_tensor(dtp[:], y2b[:, SPA:SPA + 512],
                                          y2b[:, SPB:SPB + 512], ALU.add))
            vec(lambda v: v.tensor_tensor(
                a_tm[:], dtp[:], A(abc, 0, [[H, 128], [0, 8], [1, H]]), ALU.mult), s_dv)
            w8("tensor", s_dv)
            for k in range(8):
                bank = 4 + k % 2
                pe_take(bank)
                tr(psb[bank][0:64, 0:128], a_tm[:, k * 64:(k + 1) * 64], s_pe[bank])
                use("vector", bank)
                vec(lambda v, k=k, bank=bank: v.tensor_copy(
                    cacc[0:64, k * 128:(k + 1) * 128], psb[bank][0:64, 0:128]),
                    rel(bank))
            vec(lambda v: v.tensor_tensor_scan(
                strip[0:64, 0:NTOK], cacc[0:64, 0:NTOK],
                A(zcol, 0, [[1, 64], [0, NTOK]]), 0.0, ALU.add, ALU.add), s_dv)
            w8("tensor", s_dv)
            for k in range(8):
                bank = 4 + k % 2
                pe_take(bank)
                tr(psb[bank][:, 0:64], strip[0:64, k * 128:(k + 1) * 128],
                   s_pe[bank], ident=eye[0:64, 0:64])
                use("vector", bank)
                vec(lambda v, k=k, bank=bank: v.tensor_copy(
                    l_tm[:, k * 64:(k + 1) * 64], psb[bank][:, 0:64]), rel(bank))
            vec(lambda v: v.engine_nop(), s_dv)

        def chunk_decays(k):
            bank = 5
            w8("vector", s_pe[bank], s_pe[bank].v)
            vec(lambda v, k=k: v.tensor_scalar(
                emsk[:, 0:64], eye[0:64, 0:64],
                l_cm_col(k * 128 + 127), None, ALU.mult))
            prevcol = zcol[:, 0:1] if k == 0 else l_cm_col(k * 128 - 1)
            vec(lambda v, prevcol=prevcol: v.tensor_scalar(
                emsk[:, 64:128], eye[0:64, 0:64], prevcol, None, ALU.mult), s_dv)
            pe_take(bank)
            w8("tensor", s_dv)
            mm(psb[bank][:, 0:H], ones64[:], emsk[:, 0:64], True, True)
            mm(psb[bank][:, H:2 * H], ones64[:], emsk[:, 64:128], True, True,
               s_pe[bank])
            use("vector", bank)
            vec(lambda v, bank=bank, k=k: v.tensor_tensor(
                y2b[:, SPA:SPA + H], psb[bank][:, 0:H], l_tm[:, k * 64:(k + 1) * 64],
                ALU.subtract))
            vec(lambda v, bank=bank, k=k: v.tensor_tensor(
                y2b[:, SPA + H:SPA + 2 * H], l_tm[:, k * 64:(k + 1) * 64],
                psb[bank][:, H:2 * H], ALU.subtract))
            vec(lambda v: v.tensor_tensor(
                y2b[:, SPA + 2 * H:SPA + 3 * H], y2b[:, SPA:SPA + H],
                y2b[:, SPA + H:SPA + 2 * H], ALU.add), rel(bank))
            vec(lambda v: v.engine_nop(), s_dv)
            w8("scalar", s_dv)
            sca(lambda s: s.activation(wexp[:], y2b[:, SPA:SPA + H], AF.Exp))
            sca(lambda s: s.activation(lamtm[:], y2b[:, SPA + H:SPA + 2 * H], AF.Exp))
            sca(lambda s: s.activation(lambc[:], y2b[:, SPA + 2 * H:SPA + 3 * H],
                                       AF.Exp), s_ac)
            w8("vector", s_ac)
            vec(lambda v, k=k: v.tensor_tensor(
                qtm[:], dtp[:, k * 64:(k + 1) * 64], wexp[:], ALU.mult), s_dv)

        def scan_phase(own):
            vec(lambda v: v.memset(S[:], 0.0), s_dv)
            dt_chain()
            ystg_rel = [s_bo.v, s_bo.v]
            xtm_rel = [s_dv.v, s_dv.v]
            lpw_rel = s_dv.v
            btm_rel = s_pe[4].v
            bc_rel = s_pe[3].v
            dma("sync", xtm[0][:], xtm_h[0], s_bi)
            xtm_ld = [s_bi.v, 0]
            for k in range(NCH):
                bx = k % 2
                if k + 1 < NCH:
                    w8("sync", s_dv, xtm_rel[(k + 1) % 2])
                    dma("sync", xtm[(k + 1) % 2][:], xtm_h[k + 1], s_bi)
                    xtm_ld[(k + 1) % 2] = s_bi.v
                chunk_decays(k)
                w8("sync", s_pe[4], btm_rel)
                if own:
                    w8("sync", s_pe[3], bc_rel)
                    dma("sync", bcmc[:],
                        A(bcm_h.tensor, k * 128,
                          [[NTOK, 128], [128 * NTOK, 8], [1, 128]]), s_bi)
                    dma("sync", ccmc[:],
                        A(ccm_h.tensor, k * 128,
                          [[NTOK, 128], [128 * NTOK, 8], [1, 128]]), s_bi)
                    w8("sync", s_bi)
                    for g8 in range(8):
                        dmat("sync", A(btmc, g8 * 128, [[GN, 128], [1, 128]]),
                             bcmc[:, g8 * 128:(g8 + 1) * 128], s_tp)
                    btm_ld = (s_tp, s_tp.v)
                    ccm_ld = s_bi.v
                else:
                    dma("sync", btmc[:], btm_h[k], s_bi)
                    btm_ld = (s_bi, s_bi.v)
                w8("vector", s_bi, xtm_ld[bx])
                if own:
                    vec(lambda v, k=k, bx=bx: v.tensor_tensor(
                        ubuf[:], xtm[bx][:],
                        A(dtp, k * 64, [[512, 128], [1, 64], [0, 64]]), ALU.mult),
                        s_dv)
                    w8("tensor", s_dv)
                    w8("tensor", s_bi, ccm_ld)
                    for g8 in range(8):
                        bank = 2 + g8 // 4
                        if g8 % 4 == 0:
                            pe_take(bank)
                        mm(psb[bank][:, (g8 % 4) * 128:(g8 % 4) * 128 + 128],
                           bcmc[:, g8 * 128:(g8 + 1) * 128],
                           ccmc[:, g8 * 128:(g8 + 1) * 128],
                           True, True, s_pe[bank] if g8 % 4 == 3 else None)
                    use("vector", 2)
                    vec(lambda v: v.tensor_tensor(
                        ynst[:, 0:512], psb[2][:],
                        A(triu, 0, [[128, 128], [0, 4], [1, 128]]), ALU.mult), rel(2))
                    use("vector", 3)
                    vec(lambda v: v.tensor_tensor(
                        ynst[:, 512:1024], psb[3][:],
                        A(triu, 0, [[128, 128], [0, 4], [1, 128]]), ALU.mult), rel(3))
                    bc_rel = s_pe[3].v
                for wv in range(8):
                    h0 = wv * 8
                    if own:
                        w8("vector", s_pe[3], s_pe[3].v)
                        vec(lambda v, wv=wv: v.tensor_copy(
                            sbfw[:], S[:, wv * 512:(wv + 1) * 512]), s_dv)
                        w8("vector", s_pe[7], s_pe[7].v)
                        vec(lambda v, k=k, h0=h0: v.tensor_tensor(
                            y2b[:, RH0:RH0 + 1024],
                            A(a_tm, k * 64 + h0, [[512, 128], [1, 8], [0, 128]]),
                            A(triu, 0, [[128, 128], [0, 8], [1, 128]]), ALU.mult),
                            s_dv)
                        w8("tensor", s_dv)
                        pe_take(6)
                        mm(psb[6][:], stri[:], y2b[:, RH0:RH0 + 512], True, True,
                           s_pe[6])
                        pe_take(7)
                        mm(psb[7][:], stri[:], y2b[:, RH0 + 512:RH0 + 1024], True,
                           True, s_pe[7])
                        use("scalar", 6)
                        w8("scalar", s_dv, lpw_rel)
                        sca(lambda s: s.activation(ynst[:, 1024:1536], psb[6][:], AF.Exp),
                            rel(6))
                        use("scalar", 7)
                        sca(lambda s: s.activation(ynst[:, 1536:2048], psb[7][:],
                                                   AF.Exp), rel(7))
                        sca(lambda s: s.nop(), s_ac)
                        w8("vector", s_ac)
                        w8("vector", s_pe[2], s_pe[2].v)
                        vec(lambda v, wv=wv: v.tensor_tensor(
                            ynst[:, 2048:3072], ynst[:, 1024:2048],
                            A(ynst, wv * 128, [[DM, 128], [0, 8], [1, 128]]),
                            ALU.mult), s_dv)
                        lpw_rel = s_dv.v
                        w8("tensor", s_dv)
                        pe_take(2)
                        pe_take(3)
                        w8("tensor", btm_ld[0], btm_ld[1])
                        for hh in range(8):
                            mm(psb[2][:, hh * 64:(hh + 1) * 64],
                               ynst[:, 2048 + hh * 128:2048 + (hh + 1) * 128],
                               ubuf[:, (h0 + hh) * 64:(h0 + hh + 1) * 64],
                               True, True, s_pe[2] if hh == 7 else None)
                        for hh in range(8):
                            mm(psb[3][:, hh * 64:(hh + 1) * 64],
                               ccmc[:, wv * 128:(wv + 1) * 128],
                               sbfw[:, hh * 64:(hh + 1) * 64],
                               True, True, s_pe[3] if hh == 7 else None)
                    w8("vector", s_pe[4], s_pe[4].v)
                    vec(lambda v, wv=wv, bx=bx, h0=h0: v.tensor_tensor(
                        ynst[:, 3072:3584], xtm[bx][:, wv * 512:(wv + 1) * 512],
                        A(qtm, h0, [[H, 128], [1, 8], [0, 64]]), ALU.mult), s_dv)
                    w8("tensor", s_dv)
                    if not own:
                        w8("tensor", btm_ld[0], btm_ld[1])
                    pe_take(4)
                    for hh in range(8):
                        mm(psb[4][:, hh * 64:(hh + 1) * 64],
                           btmc[:, wv * 128:(wv + 1) * 128],
                           ynst[:, 3072 + hh * 64:3072 + (hh + 1) * 64],
                           True, True, s_pe[4] if hh == 7 else None)
                    if wv == 7:
                        btm_rel = s_pe[4].v
                    if own:
                        vec(lambda v, wv=wv, bx=bx, h0=h0: v.tensor_tensor(
                            ynst[:, 3584:4096], xtm[bx][:, wv * 512:(wv + 1) * 512],
                            A(dbc, h0, [[H, 128], [1, 8], [0, 64]]), ALU.mult))
                        use("vector", 3)
                        vec(lambda v, h0=h0: v.tensor_tensor(
                            y2b[:, EV0:EV0 + 512], psb[3][:],
                            A(lamtm, h0, [[H, 128], [1, 8], [0, 64]]), ALU.mult),
                            rel(3))
                        use("vector", 2)
                        vec(lambda v: v.scalar_tensor_tensor(
                            y2b[:, EV0:EV0 + 512], psb[2][:], 1.0,
                            y2b[:, EV0:EV0 + 512], ALU.mult, ALU.add), rel(2))
                        yb = (k * 8 + wv) % 2
                        w8("vector", s_bo, ystg_rel[yb])
                        vec(lambda v, yb=yb: v.tensor_tensor(
                            ystg[yb][:], y2b[:, EV0:EV0 + 512],
                            ynst[:, 3584:4096], ALU.add), s_dv)
                        w8("sync", s_dv)
                        dma("sync", A(y_h.tensor, k * 128 * NI + wv * 512,
                                      [[NI, 128], [1, 512]]), ystg[yb][:], s_bo)
                        ystg_rel[yb] = s_bo.v
                    use("vector", 4)
                    vec(lambda v, wv=wv, h0=h0: v.tensor_tensor(
                        S[:, wv * 512:(wv + 1) * 512], S[:, wv * 512:(wv + 1) * 512],
                        A(lambc, h0, [[H, 128], [1, 8], [0, 64]]), ALU.mult))
                    vec(lambda v, wv=wv: v.tensor_tensor(
                        S[:, wv * 512:(wv + 1) * 512], S[:, wv * 512:(wv + 1) * 512],
                        psb[4][:], ALU.add), rel(4))
                    vec(lambda v: v.engine_nop(), s_dv)
                    if wv == 7:
                        xtm_rel[bx] = s_dv.v

        ln_phase(x_pre)
        barrier()
        plan_pre = [("x", cc) for cc in range(32)] + [("bp", cc) for cc in range(8)]
        cm_group(plan_pre)
        tm_group(own=False)
        ac_rel = s_dv.v
        for icg in range(8):
            w8("sync", s_dv, ac_rel)
            dma("sync", y2b[:, 0:2176],
                A(a_cat.tensor, icg * 4 * 128 * 544,
                  [[544, 128], [128 * 544, 4], [1, 544]]), s_x)
            w8("vector", s_x)
            for n in range(16):
                for j in range(4):
                    ic = icg * 4 + j
                    if n == 0:
                        vec(lambda v, ic=ic, j=j: v.tensor_scalar(
                            ubuf[:, ic * 64:ic * 64 + 32],
                            y2b[:, j * 544:j * 544 + 32],
                            gcol[:, 0:1], None, ALU.mult))
                    else:
                        vec(lambda v, ic=ic, j=j, n=n: v.scalar_tensor_tensor(
                            ubuf[:, ic * 64:ic * 64 + 32],
                            y2b[:, j * 544 + n * 32:j * 544 + n * 32 + 32],
                            gcol[:, n:n + 1], ubuf[:, ic * 64:ic * 64 + 32],
                            ALU.mult, ALU.add))
            for j in range(4):
                ic = icg * 4 + j
                vec(lambda v, ic=ic, j=j: v.tensor_copy(
                    ubuf[:, ic * 64 + 32:ic * 64 + 64],
                    y2b[:, j * 544 + 512:j * 544 + 544]),
                    s_dv if j == 3 else None)
            ac_rel = s_dv.v
        bg_rel = [s_dv.v, s_dv.v]
        for n in range(16):
            w8("sync", s_dv, bg_rel[0])
            dma("sync", y2b[0:32, 0:DM], b_gat[:, n * DM:(n + 1) * DM], s_x)
            w8("vector", s_x)
            if n == 0:
                vec(lambda v: v.tensor_scalar(xtm[0][0:32, 0:DM], y2b[0:32, 0:DM],
                                              gcol[0:32, 0:1], None, ALU.mult), s_dv)
            else:
                vec(lambda v, n=n: v.scalar_tensor_tensor(
                    xtm[0][0:32, 0:DM], y2b[0:32, 0:DM],
                    gcol[0:32, n:n + 1], xtm[0][0:32, 0:DM], ALU.mult, ALU.add),
                    s_dv)
            bg_rel[0] = s_dv.v
        vec(lambda v: v.engine_nop(), s_dv)
        w8("sync", s_dv)
        dma("sync", a2_h, ubuf[:, 0:2048], s_bo)
        dma("sync", be_h, xtm[0][0:32, 0:DM], s_bo)
        barrier()
        scan_phase(own=False)
        vec(lambda v: v.tensor_copy(tpre[:], S[:]), s_dv)
        w8("scalar", s_dv)
        sca(lambda s: s.activation(dprc[:], l_cm_col(1023), AF.Exp), s_ac)
        w8("gpsimd", s_dv)
        w8("gpsimd", s_ac)
        gp(lambda g: g.dma_start(
            out=A(pay_T.tensor, regs["s1T"] if me == "gpsimd" else 0,
                  [[NI, 128], [1, NI]]), in_=tpre[:]), s_pay, 16)
        gp(lambda g: g.dma_start(
            out=A(pay_D.tensor, regs["s1D"] if me == "gpsimd" else 0,
                  [[1, 64], [1, 1]]), in_=dprc[:]), s_pay, 16)
        w8("gpsimd", s_pay)
        gp(lambda g: g.dma_start(
            out=A(pay_F.tensor, regs["s1F"] if me == "gpsimd" else 0,
                  [[4, 1], [1, 4]]), in_=noncesb[:, 0:4]), s_pay, 16)
        barrier()
        if phases <= 0:
            w8("scalar", s_pay)
            return
        ln_phase(x_own)
        barrier()

        plan_own = ([("x", cc) for cc in range(32)] + [("bo", cc) for cc in range(8)]
                    + [("c", cc) for cc in range(8)])
        cm_group(plan_own)
        tm_group(own=True)
        barrier()
        scan_phase(own=True)
        w8("scalar", s_dv, s_dv.v)
        sca(lambda s: s.activation(dwnc[:], l_cm_col(1023), AF.Exp), s_ac)
        w8("vector", s_pe[2], s_pe[2].v)
        w8("vector", s_pe[3], s_pe[3].v)
        w8("vector", s_pe[4], s_pe[4].v)
        vec(lambda v: v.tensor_copy(ubuf[:], S[:]), s_dv)
        w8("gpsimd", s_dv)
        w8("gpsimd", s_ac)
        gp(lambda g: g.dma_start(
            out=A(pay_T.tensor, regs["s2T"] if me == "gpsimd" else 0,
                  [[NI, 128], [1, NI]]), in_=ubuf[:]), s_pay, 16)
        gp(lambda g: g.dma_start(
            out=A(pay_D.tensor, regs["s2D"] if me == "gpsimd" else 0,
                  [[1, 64], [1, 1]]), in_=dwnc[:]), s_pay, 16)
        w8("gpsimd", s_pay)
        gp(lambda g: g.dma_start(
            out=A(pay_F.tensor, regs["s2F"] if me == "gpsimd" else 0,
                  [[4, 1], [1, 4]]), in_=noncesb[:, 4:8]), s_pay, 16)
        w8("sync", s_pay, s_pay.v)
        w8("sync", s_dv, s_dv.v)
        dma("sync", ubuf[:, 0:2048], a2_h, s_bi)
        dma("sync", xtm[0][0:32, 0:DM], be_h, s_bi)
        w8("gpsimd", s_dv, s_dv.v)
        dma("gpsimd", xtm[1][0:32, 0:DM], b_shd, s_w)
        w8("tensor", s_bi)
        w8("tensor", s_dv)
        w8("tensor", s_w)
        for tt2 in range(2):
            for half in range(2):
                bank = 2 + half
                pe_take(bank)
                for ic in range(32):
                    mm(psb[bank][0:32, 0:512],
                       ubuf[:, ic * 64 + half * 32:ic * 64 + half * 32 + 32],
                       A(xnar, ic * NROW + HALO + tt2 * 512,
                         [[32 * NROW, 128], [1, 512]]),
                       ic == 0, ic == 31, s_pe[bank] if ic == 31 else None)
                use("vector", bank)
                doff = 2048 + half * 1024 + tt2 * 512
                vec(lambda v, doff=doff, bank=bank: v.tensor_copy(
                    ubuf[0:32, doff:doff + 512], psb[bank][0:32, 0:512]), rel(bank))
        vec(lambda v: v.engine_nop(), s_dv)
        w8("tensor", s_dv)
        slab_rel = [None, None]
        ost_rel = [s_so.v, s_so.v]
        ptl_rel = [s_dv.v, s_dv.v]
        bias_rel = None
        for ot in range(16):
            b = ot % 2
            if slab_rel[b] is not None:
                w8("gpsimd", s_pe[slab_rel[b][0]], slab_rel[b][1])
            dma("gpsimd", wsl[b][:, 0:8192],
                A(ffn_k.tensor, ot * 256, [[DM, 128], [128 * DM, 32], [1, 256]]), s_w)
            if bias_rel is not None:
                w8("gpsimd", s_pe[bias_rel[0]], bias_rel[1])
            dma("gpsimd", biasl[:],
                A(bias_r.tensor, ot * 256, [[256, 1], [1, 256]]), s_w)
            for mi in range(8):
                bank = (ot * 8 + mi) % 2
                pe_take(bank)
                w8("tensor", s_w)
                for ic in range(32):
                    mm(psb[bank][:, 0:256],
                       A(xnar, ic * NROW + HALO + mi * 128,
                         [[32 * NROW, 128], [1, 128]]),
                       wsl[b][:, ic * 256:(ic + 1) * 256], ic == 0, False)
                mm(psb[bank][:, 0:256], ubuf[0:32, 2048 + mi * 128:2048 + mi * 128 + 128],
                   xtm[0][0:32, ot * 256:(ot + 1) * 256], False, False)
                mm(psb[bank][:, 0:256], ubuf[0:32, 3072 + mi * 128:3072 + mi * 128 + 128],
                   xtm[1][0:32, ot * 256:(ot + 1) * 256], False, False)
                mm(psb[bank][:, 0:256], ones1b[:], biasl[:], False, True,
                   s_pe[bank])
                if mi == 7:
                    slab_rel[b] = (bank, s_pe[bank].v)
                    bias_rel = (bank, s_pe[bank].v)
                pb = (ot * 8 + mi) % 2
                w8("scalar", s_dv, ptl_rel[pb])
                dma("scalar", ptl[pb],
                    A(x_own.tensor, (HALO + mi * 128) * DM + ot * 256,
                      [[DM, 128], [1, 256]]), s_gl)
                ob = (ot * 8 + mi) % 2
                use("vector", bank)
                w8("vector", s_gl)
                w8("vector", s_so, ost_rel[ob])
                vec(lambda v, bank=bank, pb=pb, ob=ob: v.scalar_tensor_tensor(
                    ost[ob], psb[bank][:, 0:256], 1.0, ptl[pb],
                    ALU.mult, ALU.add), rel(bank))
                vec(lambda v: v.engine_nop(), s_dv)
                ptl_rel[pb] = s_dv.v
                w8("scalar", s_fr[bank], s_fr[bank].v)
                dma("scalar", A(out.tensor, mi * 128 * DM + ot * 256,
                                [[DM, 128], [1, 256]]), ost[ob], s_so)
                ost_rel[ob] = s_so.v
        barrier()
        if phases <= 1:
            if debug:
                dma("sync", d_y, A(xnar, 0, [[32 * NROW, 128], [1, NCH * NI]]), s_bo)
                w8("sync", s_bo)
            w8("scalar", s_so)
            w8("vector", s_so)
            return

        if debug:
            w8("sync", s_bo, s_bo.v)
            dma("sync", d_y, A(y_h.tensor, 0, [[NI, 128], [128 * NI, 8], [1, NI]]),
                s_bo)
        barrier()
        if phases <= 2:
            w8("scalar", s_bo)
            w8("vector", s_bo)
            return

        if me == "gpsimd":
            eng.reg_mov(regs["nd"], 1)
            eng.reg_mov(regs["cnt"], s_rcv.v)
            wg = eng.While(regs["nd"])
            wg.__enter__()
            eng.reg_add(regs["cnt"], regs["cnt"], 16)
            eng.dma_start(out=flagsb[:],
                          in_=A(pay_F.tensor, regs["rF"], [[4, 1], [1, 4]])
                          ).then_inc(s_rcv.h, 16)
            eng.wait_ge(s_rcv.h, regs["cnt"])
            eng.reg_load(regs["flg"], flagsb[0:1, 0:1])
            eng.reg_sub(regs["nd"], regs["flg"], regs["non"])
            wg.__exit__(None, None, None)
        w8("gpsimd", s_dv, s_dv.v)
        w8("gpsimd", s_pe[0], s_pe[0].v)
        w8("gpsimd", s_pe[1], s_pe[1].v)
        gp(lambda g: g.dma_start(
            out=sin[:], in_=A(pay_T.tensor, regs["rT"] if me == "gpsimd" else 0,
                              [[NI, 128], [1, NI]])), s_pay, 16)
        gp(lambda g: g.dma_start(
            out=drvc[:], in_=A(pay_D.tensor, regs["rD"] if me == "gpsimd" else 0,
                               [[1, 64], [1, 1]])), s_pay, 16)
        gp(lambda g: g.dma_start(
            out=ubuf[:], in_=A(pay_T.tensor, regs["r2T"] if me == "gpsimd" else 0,
                               [[NI, 128], [1, NI]])), s_pay, 16)
        gp(lambda g: g.dma_start(
            out=drv2[:], in_=A(pay_D.tensor, regs["r2D"] if me == "gpsimd" else 0,
                               [[1, 64], [1, 1]])), s_pay, 16)
        w8("vector", s_pay)
        vec(lambda v: v.tensor_tensor(ccol[:, 0:1], drv2[:], dprc[:], ALU.mult), s_dv)
        w8("vector", s_dv)
        vec(lambda v: v.tensor_scalar(ccol[:, 0:1], ccol[:, 0:1], prmf[0:64, 1:2],
                                      prmf[0:64, 0:1], ALU.mult, ALU.add), s_dv)
        w8("vector", s_dv)
        vec(lambda v: v.tensor_scalar(ccol[:, 1:2], drvc[:], prmf[0:64, 2:3], None,
                                      ALU.mult), s_dv)
        w8("vector", s_dv)
        vec(lambda v: v.scalar_tensor_tensor(ccol[:, 1:2], drv2[:],
                                             prmf[0:64, 3:4], ccol[:, 1:2],
                                             ALU.mult, ALU.add), s_dv)
        w8("vector", s_dv)
        vec(lambda v: v.tensor_copy(ccol[:, 2:3], prmf[0:64, 4:5]), s_dv)
        w8("vector", s_dv)
        w8("vector", s_pe[5], s_pe[5].v)
        for j in range(3):
            vec(lambda v, j=j: v.tensor_scalar(
                emsk[:, j * 64:(j + 1) * 64] if j < 2 else emsk2[:, 0:64],
                eye[0:64, 0:64], ccol[:, j:j + 1], None, ALU.mult), s_dv)
        pe_take(5)
        w8("tensor", s_dv)
        mm(psb[5][:, 0:H], ones64[:], emsk[:, 0:64], True, True)
        mm(psb[5][:, H:2 * H], ones64[:], emsk[:, 64:128], True, True)
        mm(psb[5][:, 2 * H:3 * H], ones64[:], emsk2[:, 0:64], True, True, s_pe[5])
        use("vector", 5)
        vec(lambda v: v.tensor_copy(drbc[:], psb[5][:, 0:H]))
        vec(lambda v: v.tensor_copy(dobc[:], psb[5][:, H:2 * H]))
        vec(lambda v: v.tensor_copy(drb3[:], psb[5][:, 2 * H:3 * H]), rel(5))
        vec(lambda v: v.tensor_tensor(
            y2b[:], sin[:], A(drbc, 0, [[H, 128], [1, 64], [0, 64]]), ALU.mult))
        vec(lambda v: v.tensor_tensor(
            sin[:], tpre[:], A(dobc, 0, [[H, 128], [1, 64], [0, 64]]), ALU.mult))
        vec(lambda v: v.tensor_tensor(y2b[:], y2b[:], sin[:], ALU.add))
        vec(lambda v: v.tensor_tensor(
            sin[:], ubuf[:], A(drb3, 0, [[H, 128], [1, 64], [0, 64]]), ALU.mult))
        vec(lambda v: v.tensor_tensor(y2b[:], y2b[:], sin[:], ALU.add))
        vec(lambda v: v.tensor_copy(sin[:], y2b[:]), s_dv)
        if debug:
            w8("sync", s_dv, s_dv.v)
            dma("sync", d_sin, sin[:], s_bo)
        barrier()
        if phases <= 3:
            w8("scalar", s_bo)
            w8("vector", s_bo)
            return

        gch_rel = [s_dv.v, s_dv.v]
        dma("sync", xtm[0][:], gate_h[0], s_gl)
        gch_ld = [s_gl.v, 0]
        yld = [0] * NCH
        dma("sync", A(xnar, 0, [[32 * NROW, 128], [1, NI]]), y_h[0], s_bi)
        yld[0] = s_bi.v
        for k in range(NCH):
            gb = k % 2
            if k + 1 < NCH:
                w8("sync", s_dv, gch_rel[(k + 1) % 2])
                dma("sync", xtm[(k + 1) % 2][:], gate_h[k + 1], s_gl)
                gch_ld[(k + 1) % 2] = s_gl.v
                dma("sync", A(xnar, (k + 1) * NI, [[32 * NROW, 128], [1, NI]]),
                    y_h[k + 1], s_bi)
                yld[k + 1] = s_bi.v
            w8("vector", s_bi, yld[k])
            w8("tensor", s_bi, yld[k])
            w8("sync", s_pe[0], s_pe[0].v)
            w8("sync", s_pe[1], s_pe[1].v)
            dma("sync", ccmc[:],
                A(ccm_h.tensor, k * 128, [[NTOK, 128], [128 * NTOK, 8], [1, 128]]),
                s_bi)
            w8("scalar", s_dv, s_dv.v)
            sca(lambda s, k=k: s.activation(lsg[:], l_tm[:, k * 64:(k + 1) * 64],
                                            AF.Exp), s_ac)
            w8("tensor", s_ac)
            w8("tensor", s_bi)
            for wv in range(8):
                bank = wv % 2
                pe_take(bank)
                for hh in range(8):
                    mm(psb[bank][:, hh * 64:(hh + 1) * 64],
                       ccmc[:, wv * 128:(wv + 1) * 128],
                       sin[:, (wv * 8 + hh) * 64:(wv * 8 + hh + 1) * 64],
                       True, True, s_pe[bank] if hh == 7 else None)
                use("vector", bank)
                vec(lambda v, wv=wv, bank=bank: v.tensor_tensor(
                    y2b[:, EV0:EV0 + 512], psb[bank][:],
                    A(lsg, wv * 8, [[H, 128], [1, 8], [0, 64]]), ALU.mult), rel(bank))
                vec(lambda v, k=k, wv=wv: v.tensor_tensor(
                    A(xnar, k * NI + wv * 512, [[32 * NROW, 128], [1, 512]]),
                    A(xnar, k * NI + wv * 512, [[32 * NROW, 128], [1, 512]]),
                    y2b[:, EV0:EV0 + 512], ALU.add), s_dv)
            w8("scalar", s_dv)
            sca(lambda s, k=k: s.activation(
                y2b[:], A(xnar, k * NI, [[32 * NROW, 128], [1, NI]]), AF.Square),
                s_ac)
            w8("vector", s_ac)
            for g8 in range(8):
                vec(lambda v, g8=g8: v.reduce_sum(
                    ssb[:, g8:g8 + 1], y2b[:, g8 * 512:(g8 + 1) * 512], AX),
                    s_dv if g8 == 7 else None)
            w8("vector", s_dv)
            vec(lambda v: v.tensor_scalar(ssb[:, 0:8], ssb[:, 0:8], 1.0 / 512.0, EPS,
                                          ALU.mult, ALU.add), s_dv)
            w8("vector", s_dv)
            vec(lambda v: v.reciprocal(ssb[:, 8:16], ssb[:, 0:8]), s_dv)
            w8("scalar", s_dv)
            sca(lambda s: s.activation(ssb[:, 8:16], ssb[:, 8:16], AF.Sqrt), s_ac)
            w8("vector", s_ac)
            w8("vector", s_gl, gch_ld[gb])
            w8("vector", s_tp, s_tp.v)
            vec(lambda v, k=k: v.tensor_tensor(
                ynst[:], A(xnar, k * NI, [[32 * NROW, 128], [1, NI]]),
                A(ssb, 8, [[16, 128], [1, 8], [0, 512]]), ALU.mult))
            vec(lambda v, gb=gb: v.tensor_tensor(ynst[:], ynst[:], xtm[gb][:],
                                                 ALU.mult), s_dv)
            gch_rel[gb] = s_dv.v
            w8("sync", s_dv)
            dmat("sync", A(xnar, k * NI, [[32 * NROW, 128], [128, 32], [1, 128]]),
                 ynst[:], s_tp)
        barrier()

        slab_rel = [None, None]
        ost_rel = [s_so.v, s_so.v]
        ptl_rel = [s_dv.v, s_dv.v]
        for ot in range(16):
            b = ot % 2
            if slab_rel[b] is not None:
                w8("gpsimd", s_pe[slab_rel[b][0]], slab_rel[b][1])
            dma("gpsimd", wsl[b][:, 0:8192],
                A(w_out_t.tensor, ot * 256, [[DM, 128], [128 * DM, 32], [1, 256]]),
                s_w)
            for mi in range(8):
                bank = (ot * 8 + mi) % 2
                pe_take(bank)
                w8("tensor", s_w)
                for ic in range(32):
                    mm(psb[bank][:, 0:256],
                       A(xnar, mi * NI + ic * 128, [[32 * NROW, 128], [1, 128]]),
                       wsl[b][:, ic * 256:(ic + 1) * 256],
                       ic == 0, ic == 31, s_pe[bank] if ic == 31 else None)
                if mi == 7:
                    slab_rel[b] = (bank, s_pe[bank].v)
                pb = (ot * 8 + mi) % 2
                w8("scalar", s_dv, ptl_rel[pb])
                dma("scalar", ptl[pb],
                    A(out.tensor, mi * 128 * DM + ot * 256, [[DM, 128], [1, 256]]),
                    s_gl)
                ob = (ot * 8 + mi) % 2
                use("vector", bank)
                w8("vector", s_gl)
                w8("vector", s_so, ost_rel[ob])
                vec(lambda v, bank=bank, pb=pb, ob=ob: v.scalar_tensor_tensor(
                    ost[ob], psb[bank][:, 0:256], 1.0, ptl[pb],
                    ALU.mult, ALU.add), rel(bank))
                vec(lambda v: v.engine_nop(), s_dv)
                ptl_rel[pb] = s_dv.v
                w8("scalar", s_fr[bank], s_fr[bank].v)
                dma("scalar", A(out.tensor, mi * 128 * DM + ot * 256,
                                [[DM, 128], [1, 256]]), ost[ob], s_so)
                ost_rel[ob] = s_so.v
        w8("scalar", s_so)
        w8("vector", s_so)

    for en in ["sync", "gpsimd", "vector", "scalar", "tensor"]:
        getattr(block, en)(lambda e, en=en: emit(en))

    for g in reversed(ctxs):
        g.__exit__(None, None, None)

    return nc, {}


def kernel(**inputs):
    x = np.asarray(inputs["L0_attn_hidden_states"], dtype=np.float32)
    w_in_t = np.ascontiguousarray(np.asarray(inputs["L0_attn_in_proj_w"],
                                             dtype=np.float32).T)
    conv_w = np.asarray(inputs["L0_attn_conv_weight"], dtype=np.float32)
    convw = np.ascontiguousarray(
        conv_w.reshape(48, 128, KC).transpose(1, 0, 2).reshape(128, 48 * KC))
    convb = np.ascontiguousarray(
        np.asarray(inputs["L0_attn_conv_bias"], dtype=np.float32).reshape(48, 128).T)
    dtb = np.asarray(inputs["L0_attn_dt_bias"], dtype=np.float32).reshape(1, H)
    alog = np.asarray(inputs["L0_attn_A_log"], dtype=np.float32).reshape(1, H)
    dvec = np.asarray(inputs["L0_attn_D"], dtype=np.float32).reshape(1, H)
    nw = np.asarray(inputs["L0_attn_norm_weight"], dtype=np.float32).reshape(1, DM)
    w_out_t = np.ascontiguousarray(np.asarray(inputs["L0_attn_out_proj_w"],
                                              dtype=np.float32).T)
    gate = np.asarray(inputs["L0_ffn_gate"], dtype=np.float32)
    ffn_k = np.ascontiguousarray(np.asarray(inputs["L0_ffn_kernel"],
                                            dtype=np.float32))
    bias = np.asarray(inputs["L0_ffn_bias"], dtype=np.float32).reshape(1, DM)
    lora_A = np.asarray(inputs["L0_ffn_lora_A"], dtype=np.float32)
    lora_B = np.asarray(inputs["L0_ffn_lora_B"], dtype=np.float32)
    lora_As = np.asarray(inputs["L0_ffn_lora_A_shared"], dtype=np.float32)
    lora_Bs = np.asarray(inputs["L0_ffn_lora_B_shared"], dtype=np.float32)

    a_cat = np.ascontiguousarray(np.concatenate(
        [lora_A.transpose(2, 0, 1).reshape(DM, 512), lora_As.T], axis=1))
    b_gat = np.ascontiguousarray(lora_B.transpose(2, 0, 1).reshape(32, 16 * DM))
    b_shd = np.ascontiguousarray(lora_Bs.T)

    eye = np.eye(128, dtype=np.float32)
    triu01 = np.triu(np.ones((128, 128), dtype=np.float32))
    strict = np.tril(np.ones((128, 128), dtype=np.float32), -1)

    n1, n2 = [int(v) for v in np.random.randint(1, 2 ** 30, size=2)]
    nonce = np.array([[n1] * 4 + [n2] * 4], dtype=np.int32)

    globals()['LN_SERIAL_G'] = LN_SERIAL
    globals()['WSL_TEST_G'] = WSL_TEST
    globals()['LN_NOSTATS_G'] = LN_NOSTATS
    nc, _ = build_program(phases=PHASES, debug=DEBUG)

    in_maps = []
    for c in range(8):
        bb, r = c // 4, c % 4
        q = c % 2
        own0 = r * 1024
        xo = np.zeros((NROW, DM), dtype=np.float32)
        h0 = max(own0 - HALO, 0)
        xo[HALO - (own0 - h0):HALO] = x[bb, h0:own0]
        xo[HALO:HALO + 1024] = x[bb, own0:own0 + 1024]
        xp = np.zeros((NROW, DM), dtype=np.float32)
        if r >= 2:
            pre0 = (r - 2) * 1024
            ph0 = max(pre0 - HALO, 0)
            xp[HALO - (pre0 - ph0):HALO] = x[bb, ph0:pre0]
            xp[HALO:HALO + 1024] = x[bb, pre0:pre0 + 1024]
        prmf = np.zeros((128, 8), dtype=np.float32)
        prmf[:, 0] = 1.0 if r == 2 else 0.0
        prmf[:, 1] = 0.0 if r == 0 else 1.0
        s1, s2 = q * 2 + 0, q * 2 + 1
        rkind = 0 if r in (0, 2) else 1
        rslot = (1 - q) * 2 + rkind
        prmi = np.zeros((1, 16), dtype=np.int32)
        prmi[0, 0:3] = [s1 * 128 * NI, s1 * H, s1 * 4]
        prmi[0, 3:6] = [s2 * 128 * NI, s2 * H, s2 * 4]
        prmi[0, 6:9] = [rslot * 128 * NI, rslot * H, rslot * 4]
        prmi[0, 9] = n1 if rkind == 0 else n2
        in_maps.append({
            "x_own": xo, "x_pre": xp, "w_in_t": w_in_t, "w_out_t": w_out_t,
            "ffn_k": ffn_k, "a_cat": a_cat, "b_gat": b_gat, "b_shd": b_shd,
            "bias_r": bias, "nw_r": nw, "dvec_r": dvec, "alog_r": alog,
            "dtb_r": dtb, "convw": convw, "convb": convb,
            "gate_r": np.ascontiguousarray(gate[4 * bb + r].reshape(1, 16)),
            "eye_i": eye, "triu_i": triu01, "stri_i": strict,
            "prm_f": prmf, "prm_i": prmi, "nonce_r": nonce,
        })

    res = run_bass_kernel_spmd(nc, in_maps, core_ids=list(range(8)))
    outf = np.empty((2, 4096, DM), dtype=np.float32)
    for c in range(8):
        bb, r = c // 4, c % 4
        outf[bb, r * 1024:(r + 1) * 1024] = res.results[c]["out"]
    return outf


# version 08
# speedup vs baseline: 1.7026x; 1.7026x over previous
import numpy as np
import ml_dtypes
import concourse.bass as bass
import concourse.mybir as mybir
from concourse.bass_utils import run_bass_kernel_spmd

F32 = mybir.dt.float32
BF16 = mybir.dt.bfloat16
I32 = mybir.dt.int32
AF = mybir.ActivationFunctionType
ALU = mybir.AluOpType
AX = mybir.AxisListType.X
A = bass.AP

DM = 4096
H, P, NS, G, KC = 64, 64, 128, 8, 4
NI = 4096
GN = G * NS
CONV = NI + 2 * GN
PROJ = NI + CONV + H
EPS = 1e-5
Q = 128
NCH = 8
NTOK = Q * NCH
HALO = 4
NROW = 1152
TT = (0, 512, 1024)
TW = (512, 512, HALO)
COL_GATE, COL_X, COL_B, COL_C, COL_DT = 0, NI, 2 * NI, 2 * NI + GN, PROJ - H
PHASES = 9
DEBUG = False
LN_SERIAL = False
WSL_TEST = False
LN_NOSTATS = False
RH0, EV0, SPA, SPB = 0, 1024, 1536, 2048


def build_program(phases=9, debug=False):
    LN_SERIAL = globals().get('LN_SERIAL_G', False)
    WSL_TEST = globals().get('WSL_TEST_G', False)
    LN_NOSTATS = globals().get('LN_NOSTATS_G', False)
    nc = bass.Bass("TRN2", target_bir_lowering=False,
                   detect_race_conditions=False)

    x_own = nc.dram_tensor("x_own", [NROW, DM], F32, kind="ExternalInput").ap()
    x_pre = nc.dram_tensor("x_pre", [NROW, DM], F32, kind="ExternalInput").ap()
    w_cm = nc.dram_tensor("w_cm", [48 * 128, 4096], F32, kind="ExternalInput").ap()
    w_tm = nc.dram_tensor("w_tm", [16 * 128, 8192], F32, kind="ExternalInput").ap()
    w_dt = nc.dram_tensor("w_dt", [128, 2048], F32, kind="ExternalInput").ap()
    w_op = nc.dram_tensor("w_op", [16 * 128, 8192], F32, kind="ExternalInput").ap()
    ffn_kp = nc.dram_tensor("ffn_kp", [16 * 128, 8192], F32,
                            kind="ExternalInput").ap()
    a_cat = nc.dram_tensor("a_cat", [DM, 544], F32, kind="ExternalInput").ap()
    b_gat = nc.dram_tensor("b_gat", [32, 16 * DM], F32, kind="ExternalInput").ap()
    b_shd = nc.dram_tensor("b_shd", [32, DM], F32, kind="ExternalInput").ap()
    bias_r = nc.dram_tensor("bias_r", [1, DM], F32, kind="ExternalInput").ap()
    nw_r = nc.dram_tensor("nw_r", [1, DM], F32, kind="ExternalInput").ap()
    dvec_r = nc.dram_tensor("dvec_r", [1, H], F32, kind="ExternalInput").ap()
    alog_r = nc.dram_tensor("alog_r", [1, H], F32, kind="ExternalInput").ap()
    dtb_r = nc.dram_tensor("dtb_r", [1, H], F32, kind="ExternalInput").ap()
    convw = nc.dram_tensor("convw", [128, 48 * KC], F32, kind="ExternalInput").ap()
    convb = nc.dram_tensor("convb", [128, 48], F32, kind="ExternalInput").ap()
    gate_r = nc.dram_tensor("gate_r", [1, 16], F32, kind="ExternalInput").ap()
    eye_i = nc.dram_tensor("eye_i", [128, 128], F32, kind="ExternalInput").ap()
    triu_i = nc.dram_tensor("triu_i", [128, 128], F32, kind="ExternalInput").ap()
    stri_i = nc.dram_tensor("stri_i", [128, 128], F32, kind="ExternalInput").ap()
    prm_f = nc.dram_tensor("prm_f", [128, 8], F32, kind="ExternalInput").ap()
    prm_i = nc.dram_tensor("prm_i", [1, 16], I32, kind="ExternalInput").ap()
    nonce_r = nc.dram_tensor("nonce_r", [1, 8], I32, kind="ExternalInput").ap()

    out = nc.dram_tensor("out", [NTOK, DM], F32, kind="ExternalOutput").ap()

    pay_T = nc.dram_tensor("pay_T", [4, 128, NI], BF16, kind="Internal",
                           addr_space="Shared").ap()
    pay_D = nc.dram_tensor("pay_D", [4, H, 1], F32, kind="Internal",
                           addr_space="Shared").ap()
    pay_F = nc.dram_tensor("pay_F", [4, 1, 4], I32, kind="Internal",
                           addr_space="Shared").ap()
    _ik = "ExternalOutput" if debug else "Internal"
    xtm_h = nc.dram_tensor("xtm_h", [NCH, 128, NI], BF16, kind=_ik).ap()
    btm_h = nc.dram_tensor("btm_h", [NCH, 128, GN], BF16, kind=_ik).ap()
    bcm_h = nc.dram_tensor("bcm_h", [G, 128, NTOK], BF16, kind=_ik).ap()
    ccm_h = nc.dram_tensor("ccm_h", [G, 128, NTOK], BF16, kind=_ik).ap()
    gate_h = nc.dram_tensor("gate_h", [NCH, 128, DM], BF16, kind=_ik).ap()
    y_h = nc.dram_tensor("y_h", [NCH, 128, NI], BF16, kind=_ik).ap()
    a2_h = nc.dram_tensor("a2_h", [128, 2048], BF16, kind="Internal").ap()
    be_h = nc.dram_tensor("be_h", [32, DM], BF16, kind="Internal").ap()
    d_y = (nc.dram_tensor("d_y", [128, NCH * NI], BF16, kind="ExternalOutput").ap()
           if debug else None)
    d_sin = (nc.dram_tensor("d_sin", [128, NI], BF16, kind="ExternalOutput").ap()
             if debug else None)

    ctxs = []

    def sb(name, shape, dtype):
        g = nc.sbuf_tensor(name, shape, dtype)
        t = g.__enter__()
        ctxs.append(g)
        return t

    def psalloc(name):
        g = nc.psum_tensor(name, [128, 512], F32)
        t = g.__enter__()
        ctxs.append(g)
        return t

    eye = sb("eye", [128, 128], F32)
    triu = sb("triu", [128, 128], F32)
    stri = sb("stri", [128, 128], F32)
    ones1 = sb("ones1", [1, 128], F32)
    ones64 = sb("ones64", [64, 128], F32)
    emsk = sb("emsk", [64, 128], F32)
    zrow = sb("zrow", [1, H], F32)
    zcol = sb("zcol", [64, 1], F32)
    dprc = sb("dprc", [64, 1], F32)
    dwnc = sb("dwnc", [64, 1], F32)
    drvc = sb("drvc", [64, 1], F32)
    rowb = sb("rowb", [1, 4 * H], F32)
    rowb2 = sb("rowb2", [1, 256], F32)
    biasl = sb("biasl", [1, 256], BF16)
    ones1b = sb("ones1b", [1, 128], BF16)
    abc = sb("abc", [128, H], F32)
    dtbbc = sb("dtbbc", [128, H], F32)
    dbc = sb("dbc", [128, H], F32)
    nwt = sb("nwt", [128, 256], BF16)
    cw = sb("cw", [128, 48 * KC], F32)
    cb = sb("cb", [128, 48], F32)
    gcol = sb("gcol", [128, 16], F32)
    prmf = sb("prmf", [128, 8], F32)
    prmi = sb("prmi", [1, 16], I32)
    flagsb = sb("flagsb", [1, 4], I32)
    noncesb = sb("noncesb", [1, 8], I32)
    drbc = sb("drbc", [128, H], F32)
    drb3 = sb("drb3", [128, H], F32)
    drv2 = sb("drv2", [64, 1], F32)
    ccol = sb("ccol", [64, 4], F32)
    emsk2 = sb("emsk2", [64, 64], F32)
    dobc = sb("dobc", [128, H], F32)

    xnar = sb("xnar", [128, 32 * NROW], BF16)
    wsl0 = sb("wsl0", [128, 8192], BF16)
    strip = sb("strip", [128, NTOK + HALO], F32)
    cacc = sb("cacc", [128, NTOK], F32)
    xbcbf = sb("xbcbf", [128, NTOK], BF16)
    tpstg = sb("tpstg", [128, NTOK], BF16)
    ccmc = sb("ccmc", [128, GN], BF16)
    bcmc = sb("bcmc", [128, GN], BF16)
    btmc = sb("btmc", [128, GN], BF16)
    dtp = sb("dtp", [128, 512], F32)
    a_tm = sb("a_tm", [128, 512], F32)
    l_tm = sb("l_tm", [128, 512], F32)
    wexp = sb("wexp", [128, H], F32)
    lamtm = sb("lamtm", [128, H], F32)
    lambc = sb("lambc", [128, H], F32)
    lsg = sb("lsg", [128, H], F32)
    qtm = sb("qtm", [128, H], F32)
    S = sb("S", [128, NI], F32)
    sbfw = sb("sbfw", [128, 512], BF16)
    tpre = sb("tpre", [128, NI], BF16)
    sin = sb("sin", [128, NI], BF16)
    xtm = [sb("xtm0", [128, NI], BF16), sb("xtm1", [128, NI], BF16)]
    ubuf = sb("ubuf", [128, NI], BF16)
    y2b = sb("y2b", [128, DM], F32)
    ssb = sb("ssb", [128, 16], F32)
    gld = [sb("gld0", [128, 512], BF16), sb("gld1", [128, 512], BF16)]
    ynst = sb("ynst", [128, DM], BF16)
    lnst = sb("lnst", [128, 8], F32)

    ystg = gld
    wsl = [wsl0, wsl0 if WSL_TEST else S.bitcast(BF16)]
    ptl = [y2b[:, 3072:3328], y2b[:, 3328:3584]]
    ost = [y2b[:, 3584:3840], y2b[:, 3840:4096]]
    gmsb = ynst
    lpw = ynst
    mpw = ynst
    udw = ynst
    xdw = ynst

    xin = S
    dtraw = l_tm

    psb = [psalloc(f"psb{i}") for i in range(8)]

    class Sem:
        __slots__ = ("h", "v")

        def __init__(self, h):
            self.h = h
            self.v = 0

    def mksem(name):
        g = nc.semaphore(name)
        h = g.__enter__()
        ctxs.append(g)
        return Sem(h)

    s_x = mksem("s_x")
    s_w = mksem("s_w")
    s_tp = mksem("s_tp")
    s_bi = mksem("s_bi")
    s_bo = mksem("s_bo")
    s_gl = mksem("s_gl")
    s_so = mksem("s_so")
    s_pay = mksem("s_pay")
    s_rcv = mksem("s_rcv")
    s_dv = mksem("s_dv")
    s_ac = mksem("s_ac")
    s_gp = mksem("s_gp")
    s_tb = mksem("s_tb")
    s_ab = mksem("s_ab")
    s_ph = mksem("s_ph")
    s_pe = [mksem(f"s_pe{i}") for i in range(8)]
    s_fr = [mksem(f"s_fr{i}") for i in range(8)]
    DMAS = [s_x, s_w, s_tp, s_bi, s_bo, s_gl, s_so, s_pay]
    ALLSEMS = DMAS + [s_rcv, s_dv, s_ac, s_gp, s_tb, s_ab, s_ph] + s_pe + s_fr

    blkg = nc.Block()
    block = blkg.__enter__()
    ctxs.append(blkg)

    def emit(me):
        for _s in ALLSEMS:
            _s.v = 0
        E = {"sync": nc.sync, "gpsimd": nc.gpsimd, "vector": nc.vector,
             "scalar": nc.scalar, "tensor": nc.tensor}
        eng = E[me]
        regs = {}
        if me == "gpsimd":
            for rn in ["s1T", "s1D", "s1F", "s2T", "s2D", "s2F", "rT", "rD",
                       "rF", "non", "r2T", "r2D", "nd", "cnt", "flg"]:
                g = eng.register("r_" + rn)
                regs[rn] = g.__enter__()
                ctxs.append(g)

        def w8(en, sem, v=None):
            if me == en:
                eng.wait_ge(sem.h, sem.v if v is None else v)

        def dma(en, outap, inap, sem):
            sem.v += 16
            if me == en:
                eng.dma_start(out=outap, in_=inap).then_inc(sem.h, 16)
            return sem.v

        def dmat(en, outap, inap, sem):
            sem.v += 16
            if me == en:
                eng.dma_start_transpose(outap, inap).then_inc(sem.h, 16)
            return sem.v

        def mm(outap, lhsT, rhs, start, stop, sem=None):
            if sem is not None:
                sem.v += 1
            if me == "tensor":
                i = eng.matmul(outap, lhsT, rhs, start=start, stop=stop)
                if sem is not None:
                    i.then_inc(sem.h, 1)

        def tr(outap, inap, sem=None, ident=None):
            if sem is not None:
                sem.v += 1
            if me == "tensor":
                i = eng.transpose(outap, inap, eye[:] if ident is None else ident)
                if sem is not None:
                    i.then_inc(sem.h, 1)

        def vec(fn, sem=None):
            if sem is not None:
                sem.v += 1
            if me == "vector":
                i = fn(eng)
                if sem is not None:
                    i.then_inc(sem.h, 1)

        def sca(fn, sem=None):
            if sem is not None:
                sem.v += 1
            if me == "scalar":
                i = fn(eng)
                if sem is not None:
                    i.then_inc(sem.h, 1)

        def gp(fn, sem=None, inc=1):
            if sem is not None:
                sem.v += inc
            if me == "gpsimd":
                i = fn(eng)
                if sem is not None:
                    i.then_inc(sem.h, inc)

        def barrier():
            s_tb.v += 1
            if me == "tensor":
                eng.nop().then_inc(s_tb.h, 1)
            s_ab.v += 1
            if me == "scalar":
                eng.nop().then_inc(s_ab.h, 1)
            s_gp.v += 1
            if me == "gpsimd":
                eng.nop().then_inc(s_gp.h, 1)
            if me == "vector":
                for s in DMAS + [s_tb, s_ab, s_gp]:
                    eng.wait_ge(s.h, s.v)
            s_ph.v += 1
            if me == "vector":
                eng.nop().then_inc(s_ph.h, 1)
            if me != "vector":
                eng.wait_ge(s_ph.h, s_ph.v)

        def l_cm_col(j):
            return strip[0:64, j:j + 1]

        def pe_take(bank):
            w8("tensor", s_fr[bank])

        def use(en, bank):
            w8(en, s_pe[bank])

        def rel(bank):
            return s_fr[bank]

        dma("sync", eye[:], eye_i, s_x)
        dma("sync", triu[:], triu_i, s_x)
        dma("sync", stri[:], stri_i, s_x)
        dma("sync", cw[:], convw, s_x)
        dma("sync", cb[:], convb, s_x)
        dma("sync", prmf[:], prm_f, s_x)
        dma("sync", prmi[:], prm_i, s_x)
        dma("sync", noncesb[:], nonce_r, s_x)
        dma("sync", gcol[:, 0:16], A(gate_r.tensor, 0, [[0, 128], [1, 16]]), s_x)
        dma("sync", rowb[:, 0:H], alog_r, s_x)
        dma("sync", rowb[:, H:2 * H], dtb_r, s_x)
        dma("sync", rowb[:, 2 * H:3 * H], dvec_r, s_x)
        w8("vector", s_x)
        w8("scalar", s_x)
        w8("gpsimd", s_x)
        w8("tensor", s_x)
        vec(lambda v: v.memset(ones1[:], 1.0))
        vec(lambda v: v.memset(ones1b[:], 1.0))
        vec(lambda v: v.memset(ones64[:], 1.0))
        vec(lambda v: v.memset(zcol[:], 0.0))
        vec(lambda v: v.memset(zrow[:], 0.0), s_dv)
        sca(lambda s: s.activation(rowb[:, 3 * H:4 * H], rowb[:, 0:H], AF.Exp), s_ac)
        w8("vector", s_ac)
        vec(lambda v: v.tensor_scalar_mul(rowb[:, 3 * H:4 * H], rowb[:, 3 * H:4 * H],
                                          -1.0), s_dv)
        w8("tensor", s_dv)
        pe_take(0)
        mm(psb[0][:, 0:H], ones1[:], rowb[:, 3 * H:4 * H], True, True)
        mm(psb[0][:, H:2 * H], ones1[:], rowb[:, H:2 * H], True, True)
        mm(psb[0][:, 2 * H:3 * H], ones1[:], rowb[:, 2 * H:3 * H], True, True, s_pe[0])
        use("vector", 0)
        vec(lambda v: v.tensor_copy(abc[:], psb[0][:, 0:H]))
        vec(lambda v: v.tensor_copy(dtbbc[:], psb[0][:, H:2 * H]))
        vec(lambda v: v.tensor_copy(dbc[:], psb[0][:, 2 * H:3 * H]), rel(0))
        if me == "gpsimd":
            names = ["s1T", "s1D", "s1F", "s2T", "s2D", "s2F", "rT", "rD", "rF",
                     "non", "r2T", "r2D"]
            for j, rn in enumerate(names):
                eng.reg_load(regs[rn], prmi[0:1, j:j + 1])
        barrier()

        def ln_phase(x_dram):
            xbufs = [xin, y2b]
            xrel = [s_ac.v, s_ac.v]
            ubuf_rel = s_dv.v
            for i in range(9):
                b = i % 2
                xb = xbufs[b]
                c0 = b * 4
                w8("sync", s_ac, xrel[b])
                dma("sync", xb[:], x_dram[i * 128:(i + 1) * 128, :], s_x)
                w8("vector", s_x)
                w8("vector", s_ac, xrel[b])
                for j in range(8):
                    vec(lambda v, j=j, b=b, xb=xb: v.bn_stats(
                        strip[:, b * 48 + j * 6:b * 48 + (j + 1) * 6],
                        xb[:, j * 512:(j + 1) * 512]), s_dv if j == 7 else None)
                w8("vector", s_dv)
                vec(lambda v, c0=c0, b=b: v.bn_aggr(
                    lnst[:, c0:c0 + 2], strip[:, b * 48:b * 48 + 48]), s_dv)
                w8("vector", s_dv)
                vec(lambda v, c0=c0: v.tensor_scalar_add(
                    lnst[:, c0 + 1:c0 + 2], lnst[:, c0 + 1:c0 + 2], EPS), s_dv)
                w8("vector", s_dv)
                vec(lambda v, c0=c0: v.reciprocal(
                    lnst[:, c0 + 2:c0 + 3], lnst[:, c0 + 1:c0 + 2]), s_dv)
                w8("scalar", s_dv)
                sca(lambda s, c0=c0: s.activation(
                    lnst[:, c0 + 2:c0 + 3], lnst[:, c0 + 2:c0 + 3], AF.Sqrt), s_ac)
                w8("vector", s_ac)
                vec(lambda v, c0=c0: v.tensor_scalar(
                    lnst[:, c0 + 3:c0 + 4], lnst[:, c0:c0 + 1],
                    lnst[:, c0 + 2:c0 + 3], -1.0, ALU.mult, ALU.mult), s_dv)
                w8("scalar", s_dv)
                w8("scalar", s_tp, s_tp.v)
                if LN_NOSTATS:
                    sca(lambda s, xb=xb: s.activation(ynst[:], xb[:], AF.Identity),
                        s_ac)
                else:
                    sca(lambda s, xb=xb, c0=c0: s.activation(
                        ynst[:], xb[:], AF.Identity,
                        bias=lnst[:, c0 + 3:c0 + 4], scale=lnst[:, c0 + 2:c0 + 3]),
                        s_ac)
                xrel[b] = s_ac.v
                w8("sync", s_ac)
                w8("sync", s_dv, ubuf_rel)
                dmat("sync", A(ubuf, 0, [[NI, 128], [128, 32], [1, 128]]),
                     ynst[:], s_tp)
                w8("vector", s_tp)
                vec(lambda v, i=i: v.tensor_copy(
                    A(xnar, i * 128, [[32 * NROW, 128], [NROW, 32], [1, 128]]),
                    ubuf[:]), s_dv)
                ubuf_rel = s_dv.v

        def conv_silu(cslot):
            vec(lambda v, cslot=cslot: v.tensor_scalar(
                cacc[:], strip[:, 1:1 + NTOK], cw[:, cslot * KC:cslot * KC + 1],
                None, ALU.mult))
            for k in range(1, KC):
                vec(lambda v, k=k, cslot=cslot: v.scalar_tensor_tensor(
                    cacc[:], strip[:, 1 + k:1 + k + NTOK],
                    cw[:, cslot * KC + k:cslot * KC + k + 1], cacc[:],
                    ALU.mult, ALU.add))
            vec(lambda v: v.engine_nop(), s_dv)
            w8("scalar", s_dv)
            sca(lambda s, cslot=cslot: s.activation(
                xbcbf[:], cacc[:], AF.Silu, bias=cb[:, cslot:cslot + 1]), s_ac)

        def cm_group(plan):
            slab_rel = [None, None]
            cm_group.slab_rel = slab_rel
            strip_rel = 0
            xbc_guard = (s_dv, 0)
            for idx, (kind, cc) in enumerate(plan):
                b = idx % 2
                j = {"x": 0, "bo": 32, "bp": 32, "c": 40}[kind] + cc
                if slab_rel[b] is not None:
                    w8("gpsimd", s_pe[slab_rel[b][0]], slab_rel[b][1])
                dma("gpsimd", wsl[b][:, 0:4096],
                    A(w_cm.tensor, j * 128 * 4096, [[4096, 128], [1, 4096]]),
                    s_w)
                for ti in range(3):
                    bank = (idx * 3 + ti) % 2
                    pe_take(bank)
                    w8("tensor", s_w)
                    for ic in range(32):
                        mm(psb[bank][:, 0:TW[ti]], wsl[b][:, ic * 128:(ic + 1) * 128],
                           A(xnar, ic * NROW + TT[ti], [[32 * NROW, 128], [1, TW[ti]]]),
                           ic == 0, ic == 31, s_pe[bank] if ic == 31 else None)
                    if ti == 2:
                        slab_rel[b] = (bank, s_pe[bank].v)
                    use("scalar", bank)
                    w8("scalar", s_dv, strip_rel)
                    sca(lambda s, bank=bank, ti=ti: s.copy(
                        strip[:, TT[ti]:TT[ti] + TW[ti]], psb[bank][:, 0:TW[ti]]),
                        rel(bank))
                w8("vector", s_fr[0], s_fr[0].v)
                w8("vector", s_fr[1], s_fr[1].v)
                w8("scalar", xbc_guard[0], xbc_guard[1])
                conv_silu({"x": cc, "bo": 32 + cc, "bp": 32 + cc, "c": 40 + cc}[kind])
                strip_rel = s_dv.v
                if kind in ("x", "bp"):
                    w8("sync", s_ac)
                    dmat("sync", A(tpstg, 0, [[NTOK, 128], [128, 8], [1, 128]]),
                         xbcbf[:], s_tp)
                    xbc_guard = (s_tp, s_tp.v)
                    w8("sync", s_tp)
                    if kind == "x":
                        dma("sync", A(xtm_h.tensor, cc * 128,
                                      [[NI, 128], [128 * NI, 8], [1, 128]]), tpstg[:],
                            s_bo)
                    else:
                        dma("sync", A(btm_h.tensor, cc * 128,
                                      [[GN, 128], [128 * GN, 8], [1, 128]]), tpstg[:],
                            s_bo)
                else:
                    w8("sync", s_ac)
                    hb = bcm_h if kind == "bo" else ccm_h
                    dma("sync", hb[cc], xbcbf[:], s_bo)
                    xbc_guard = (s_bo, s_bo.v)

        def tm_group(own):
            tiles = ([("g", COL_GATE + i * 256, 256) for i in range(16)] if own
                     else [])
            tiles.append(("d", COL_DT, 64))
            slab_rel = list(getattr(cm_group, "slab_rel", [None, None]))
            gld_rel = [s_bo.v, s_bo.v]
            for tj, (kind, col, wid) in enumerate(tiles):
                b = tj % 2
                if slab_rel[b] is not None:
                    w8("gpsimd", s_pe[slab_rel[b][0]], slab_rel[b][1])
                wsrc = (A(w_tm.tensor, (col // 256) * 128 * 8192,
                          [[8192, 128], [1, 8192]]) if kind == "g"
                        else A(w_dt.tensor, 0, [[2048, 128], [1, 2048]]))
                dma("gpsimd", wsl[b][:, 0:32 * wid], wsrc, s_w)
                if kind == "g":
                    bank5 = 5
                    pe_take(bank5)
                    w8("tensor", s_dv, s_dv.v)
                    coff = col - COL_GATE
                    w8("sync", s_pe[bank5], s_pe[bank5].v)
                    dma("sync", rowb2[:], A(nw_r.tensor, coff, [[256, 1], [1, 256]]),
                        s_x)
                    w8("tensor", s_x)
                    mm(psb[bank5][:, 0:256], ones1[:], rowb2[:], True, True,
                       s_pe[bank5])
                    use("vector", bank5)
                    vec(lambda v, bank5=bank5: v.tensor_copy(
                        nwt[:], psb[bank5][:, 0:256]), rel(bank5))
                    vec(lambda v: v.engine_nop(), s_dv)
                for mi in range(8):
                    bank = (tj * 8 + mi) % 2
                    pe_take(bank)
                    w8("tensor", s_w)
                    for ic in range(32):
                        mm(psb[bank][:, 0:wid],
                           A(xnar, ic * NROW + HALO + mi * 128,
                             [[32 * NROW, 128], [1, 128]]),
                           wsl[b][:, ic * wid:(ic + 1) * wid],
                           ic == 0, ic == 31, s_pe[bank] if ic == 31 else None)
                    if mi == 7:
                        slab_rel[b] = (bank, s_pe[bank].v)
                    if kind == "g":
                        gb = (tj * 8 + mi) % 2
                        use("scalar", bank)
                        sca(lambda s, bank=bank: s.activation(
                            ynst[:, 0:256], psb[bank][:, 0:256], AF.Silu), s_ac)
                        w8("vector", s_ac)
                        w8("vector", s_bo, gld_rel[gb])
                        w8("vector", s_dv, s_dv.v)
                        vec(lambda v, gb=gb: v.tensor_tensor(
                            gld[gb][:, 0:256], ynst[:, 0:256], nwt[:], ALU.mult),
                            rel(bank))
                        vec(lambda v: v.engine_nop(), s_dv)
                        w8("scalar", s_fr[bank], s_fr[bank].v)
                        coff = col - COL_GATE
                        dma("scalar",
                            A(gate_h.tensor, mi * 128 * DM + coff,
                              [[DM, 128], [1, 256]]), gld[gb][:, 0:256], s_bo)
                        gld_rel[gb] = s_bo.v
                    else:
                        use("vector", bank)
                        vec(lambda v, bank=bank, mi=mi: v.tensor_copy(
                            dtraw[:, mi * 64:(mi + 1) * 64], psb[bank][:, 0:64]),
                            rel(bank))

        def dt_chain():
            vec(lambda v: v.tensor_tensor(
                dtp[:], dtraw[:], A(dtbbc, 0, [[H, 128], [0, 8], [1, H]]), ALU.add))
            vec(lambda v: v.tensor_scalar_mul(y2b[:, SPA:SPA + 512], dtp[:], -1.0))
            vec(lambda v: v.tensor_tensor(y2b[:, SPA:SPA + 512], y2b[:, SPA:SPA + 512],
                                          dtp[:], ALU.min), s_dv)
            w8("scalar", s_dv)
            sca(lambda s: s.activation(y2b[:, SPB:SPB + 512], y2b[:, SPA:SPA + 512],
                                       AF.Exp), s_ac)
            sca(lambda s: s.activation(y2b[:, SPB:SPB + 512], y2b[:, SPB:SPB + 512],
                                       AF.Ln, bias=1.0), s_ac)
            w8("vector", s_ac)
            vec(lambda v: v.tensor_relu(y2b[:, SPA:SPA + 512], dtp[:]))
            vec(lambda v: v.tensor_tensor(dtp[:], y2b[:, SPA:SPA + 512],
                                          y2b[:, SPB:SPB + 512], ALU.add))
            vec(lambda v: v.tensor_tensor(
                a_tm[:], dtp[:], A(abc, 0, [[H, 128], [0, 8], [1, H]]), ALU.mult), s_dv)
            w8("tensor", s_dv)
            for k in range(8):
                bank = 4 + k % 2
                pe_take(bank)
                tr(psb[bank][0:64, 0:128], a_tm[:, k * 64:(k + 1) * 64], s_pe[bank])
                use("vector", bank)
                vec(lambda v, k=k, bank=bank: v.tensor_copy(
                    cacc[0:64, k * 128:(k + 1) * 128], psb[bank][0:64, 0:128]),
                    rel(bank))
            vec(lambda v: v.tensor_tensor_scan(
                strip[0:64, 0:NTOK], cacc[0:64, 0:NTOK],
                A(zcol, 0, [[1, 64], [0, NTOK]]), 0.0, ALU.add, ALU.add), s_dv)
            w8("tensor", s_dv)
            for k in range(8):
                bank = 4 + k % 2
                pe_take(bank)
                tr(psb[bank][:, 0:64], strip[0:64, k * 128:(k + 1) * 128],
                   s_pe[bank], ident=eye[0:64, 0:64])
                use("vector", bank)
                vec(lambda v, k=k, bank=bank: v.tensor_copy(
                    l_tm[:, k * 64:(k + 1) * 64], psb[bank][:, 0:64]), rel(bank))
            vec(lambda v: v.engine_nop(), s_dv)

        def chunk_decays(k):
            bank = 5
            w8("vector", s_pe[bank], s_pe[bank].v)
            vec(lambda v, k=k: v.tensor_scalar(
                emsk[:, 0:64], eye[0:64, 0:64],
                l_cm_col(k * 128 + 127), None, ALU.mult))
            prevcol = zcol[:, 0:1] if k == 0 else l_cm_col(k * 128 - 1)
            vec(lambda v, prevcol=prevcol: v.tensor_scalar(
                emsk[:, 64:128], eye[0:64, 0:64], prevcol, None, ALU.mult), s_dv)
            pe_take(bank)
            w8("tensor", s_dv)
            mm(psb[bank][:, 0:H], ones64[:], emsk[:, 0:64], True, True)
            mm(psb[bank][:, H:2 * H], ones64[:], emsk[:, 64:128], True, True,
               s_pe[bank])
            use("vector", bank)
            vec(lambda v, bank=bank, k=k: v.tensor_tensor(
                y2b[:, SPA:SPA + H], psb[bank][:, 0:H], l_tm[:, k * 64:(k + 1) * 64],
                ALU.subtract))
            vec(lambda v, bank=bank, k=k: v.tensor_tensor(
                y2b[:, SPA + H:SPA + 2 * H], l_tm[:, k * 64:(k + 1) * 64],
                psb[bank][:, H:2 * H], ALU.subtract))
            vec(lambda v: v.tensor_tensor(
                y2b[:, SPA + 2 * H:SPA + 3 * H], y2b[:, SPA:SPA + H],
                y2b[:, SPA + H:SPA + 2 * H], ALU.add), rel(bank))
            vec(lambda v: v.engine_nop(), s_dv)
            w8("scalar", s_dv)
            sca(lambda s: s.activation(wexp[:], y2b[:, SPA:SPA + H], AF.Exp))
            sca(lambda s: s.activation(lamtm[:], y2b[:, SPA + H:SPA + 2 * H], AF.Exp))
            sca(lambda s: s.activation(lambc[:], y2b[:, SPA + 2 * H:SPA + 3 * H],
                                       AF.Exp), s_ac)
            w8("vector", s_ac)
            vec(lambda v, k=k: v.tensor_tensor(
                qtm[:], dtp[:, k * 64:(k + 1) * 64], wexp[:], ALU.mult), s_dv)

        def scan_phase(own):
            vec(lambda v: v.memset(S[:], 0.0), s_dv)
            dt_chain()
            ystg_rel = [s_bo.v, s_bo.v]
            xtm_rel = [s_dv.v, s_dv.v]
            lpw_rel = s_dv.v
            btm_rel = s_pe[4].v
            bc_rel = s_pe[3].v
            dma("sync", xtm[0][:], xtm_h[0], s_bi)
            xtm_ld = [s_bi.v, 0]
            for k in range(NCH):
                bx = k % 2
                if k + 1 < NCH:
                    w8("sync", s_dv, xtm_rel[(k + 1) % 2])
                    dma("sync", xtm[(k + 1) % 2][:], xtm_h[k + 1], s_bi)
                    xtm_ld[(k + 1) % 2] = s_bi.v
                chunk_decays(k)
                w8("sync", s_pe[4], btm_rel)
                if own:
                    w8("sync", s_pe[3], bc_rel)
                    dma("sync", bcmc[:],
                        A(bcm_h.tensor, k * 128,
                          [[NTOK, 128], [128 * NTOK, 8], [1, 128]]), s_bi)
                    dma("sync", ccmc[:],
                        A(ccm_h.tensor, k * 128,
                          [[NTOK, 128], [128 * NTOK, 8], [1, 128]]), s_bi)
                    w8("sync", s_bi)
                    for g8 in range(8):
                        dmat("sync", A(btmc, g8 * 128, [[GN, 128], [1, 128]]),
                             bcmc[:, g8 * 128:(g8 + 1) * 128], s_tp)
                    btm_ld = (s_tp, s_tp.v)
                    ccm_ld = s_bi.v
                else:
                    dma("sync", btmc[:], btm_h[k], s_bi)
                    btm_ld = (s_bi, s_bi.v)
                w8("vector", s_bi, xtm_ld[bx])
                if own:
                    vec(lambda v, k=k, bx=bx: v.tensor_tensor(
                        ubuf[:], xtm[bx][:],
                        A(dtp, k * 64, [[512, 128], [1, 64], [0, 64]]), ALU.mult),
                        s_dv)
                    w8("tensor", s_dv)
                    w8("tensor", s_bi, ccm_ld)
                    for g8 in range(8):
                        bank = 2 + g8 // 4
                        if g8 % 4 == 0:
                            pe_take(bank)
                        mm(psb[bank][:, (g8 % 4) * 128:(g8 % 4) * 128 + 128],
                           bcmc[:, g8 * 128:(g8 + 1) * 128],
                           ccmc[:, g8 * 128:(g8 + 1) * 128],
                           True, True, s_pe[bank] if g8 % 4 == 3 else None)
                    use("vector", 2)
                    vec(lambda v: v.tensor_tensor(
                        ynst[:, 0:512], psb[2][:],
                        A(triu, 0, [[128, 128], [0, 4], [1, 128]]), ALU.mult), rel(2))
                    use("vector", 3)
                    vec(lambda v: v.tensor_tensor(
                        ynst[:, 512:1024], psb[3][:],
                        A(triu, 0, [[128, 128], [0, 4], [1, 128]]), ALU.mult), rel(3))
                    bc_rel = s_pe[3].v
                for wv in range(8):
                    h0 = wv * 8
                    if own:
                        w8("vector", s_pe[3], s_pe[3].v)
                        vec(lambda v, wv=wv: v.tensor_copy(
                            sbfw[:], S[:, wv * 512:(wv + 1) * 512]), s_dv)
                        w8("vector", s_pe[7], s_pe[7].v)
                        vec(lambda v, k=k, h0=h0: v.tensor_tensor(
                            y2b[:, RH0:RH0 + 1024],
                            A(a_tm, k * 64 + h0, [[512, 128], [1, 8], [0, 128]]),
                            A(triu, 0, [[128, 128], [0, 8], [1, 128]]), ALU.mult),
                            s_dv)
                        w8("tensor", s_dv)
                        pe_take(6)
                        mm(psb[6][:], stri[:], y2b[:, RH0:RH0 + 512], True, True,
                           s_pe[6])
                        pe_take(7)
                        mm(psb[7][:], stri[:], y2b[:, RH0 + 512:RH0 + 1024], True,
                           True, s_pe[7])
                        use("scalar", 6)
                        w8("scalar", s_dv, lpw_rel)
                        sca(lambda s: s.activation(ynst[:, 1024:1536], psb[6][:], AF.Exp),
                            rel(6))
                        use("scalar", 7)
                        sca(lambda s: s.activation(ynst[:, 1536:2048], psb[7][:],
                                                   AF.Exp), rel(7))
                        sca(lambda s: s.nop(), s_ac)
                        w8("vector", s_ac)
                        w8("vector", s_pe[2], s_pe[2].v)
                        vec(lambda v, wv=wv: v.tensor_tensor(
                            ynst[:, 2048:3072], ynst[:, 1024:2048],
                            A(ynst, wv * 128, [[DM, 128], [0, 8], [1, 128]]),
                            ALU.mult), s_dv)
                        lpw_rel = s_dv.v
                        w8("tensor", s_dv)
                        pe_take(2)
                        pe_take(3)
                        w8("tensor", btm_ld[0], btm_ld[1])
                        for hh in range(8):
                            mm(psb[2][:, hh * 64:(hh + 1) * 64],
                               ynst[:, 2048 + hh * 128:2048 + (hh + 1) * 128],
                               ubuf[:, (h0 + hh) * 64:(h0 + hh + 1) * 64],
                               True, True, s_pe[2] if hh == 7 else None)
                        for hh in range(8):
                            mm(psb[3][:, hh * 64:(hh + 1) * 64],
                               ccmc[:, wv * 128:(wv + 1) * 128],
                               sbfw[:, hh * 64:(hh + 1) * 64],
                               True, True, s_pe[3] if hh == 7 else None)
                    w8("vector", s_pe[4], s_pe[4].v)
                    vec(lambda v, wv=wv, bx=bx, h0=h0: v.tensor_tensor(
                        ynst[:, 3072:3584], xtm[bx][:, wv * 512:(wv + 1) * 512],
                        A(qtm, h0, [[H, 128], [1, 8], [0, 64]]), ALU.mult), s_dv)
                    w8("tensor", s_dv)
                    if not own:
                        w8("tensor", btm_ld[0], btm_ld[1])
                    pe_take(4)
                    for hh in range(8):
                        mm(psb[4][:, hh * 64:(hh + 1) * 64],
                           btmc[:, wv * 128:(wv + 1) * 128],
                           ynst[:, 3072 + hh * 64:3072 + (hh + 1) * 64],
                           True, True, s_pe[4] if hh == 7 else None)
                    if wv == 7:
                        btm_rel = s_pe[4].v
                    if own:
                        vec(lambda v, wv=wv, bx=bx, h0=h0: v.tensor_tensor(
                            ynst[:, 3584:4096], xtm[bx][:, wv * 512:(wv + 1) * 512],
                            A(dbc, h0, [[H, 128], [1, 8], [0, 64]]), ALU.mult))
                        use("vector", 3)
                        vec(lambda v, h0=h0: v.tensor_tensor(
                            y2b[:, EV0:EV0 + 512], psb[3][:],
                            A(lamtm, h0, [[H, 128], [1, 8], [0, 64]]), ALU.mult),
                            rel(3))
                        use("vector", 2)
                        vec(lambda v: v.scalar_tensor_tensor(
                            y2b[:, EV0:EV0 + 512], psb[2][:], 1.0,
                            y2b[:, EV0:EV0 + 512], ALU.mult, ALU.add), rel(2))
                        yb = (k * 8 + wv) % 2
                        w8("vector", s_bo, ystg_rel[yb])
                        vec(lambda v, yb=yb: v.tensor_tensor(
                            ystg[yb][:], y2b[:, EV0:EV0 + 512],
                            ynst[:, 3584:4096], ALU.add), s_dv)
                        w8("sync", s_dv)
                        dma("sync", A(y_h.tensor, k * 128 * NI + wv * 512,
                                      [[NI, 128], [1, 512]]), ystg[yb][:], s_bo)
                        ystg_rel[yb] = s_bo.v
                    use("vector", 4)
                    vec(lambda v, wv=wv, h0=h0: v.tensor_tensor(
                        S[:, wv * 512:(wv + 1) * 512], S[:, wv * 512:(wv + 1) * 512],
                        A(lambc, h0, [[H, 128], [1, 8], [0, 64]]), ALU.mult))
                    vec(lambda v, wv=wv: v.tensor_tensor(
                        S[:, wv * 512:(wv + 1) * 512], S[:, wv * 512:(wv + 1) * 512],
                        psb[4][:], ALU.add), rel(4))
                    vec(lambda v: v.engine_nop(), s_dv)
                    if wv == 7:
                        xtm_rel[bx] = s_dv.v

        ln_phase(x_pre)
        barrier()
        plan_pre = [("x", cc) for cc in range(32)] + [("bp", cc) for cc in range(8)]
        cm_group(plan_pre)
        tm_group(own=False)
        ac_rel = s_dv.v
        for icg in range(8):
            w8("sync", s_dv, ac_rel)
            dma("sync", y2b[:, 0:2176],
                A(a_cat.tensor, icg * 4 * 128 * 544,
                  [[544, 128], [128 * 544, 4], [1, 544]]), s_x)
            w8("vector", s_x)
            for n in range(16):
                for j in range(4):
                    ic = icg * 4 + j
                    if n == 0:
                        vec(lambda v, ic=ic, j=j: v.tensor_scalar(
                            ubuf[:, ic * 64:ic * 64 + 32],
                            y2b[:, j * 544:j * 544 + 32],
                            gcol[:, 0:1], None, ALU.mult))
                    else:
                        vec(lambda v, ic=ic, j=j, n=n: v.scalar_tensor_tensor(
                            ubuf[:, ic * 64:ic * 64 + 32],
                            y2b[:, j * 544 + n * 32:j * 544 + n * 32 + 32],
                            gcol[:, n:n + 1], ubuf[:, ic * 64:ic * 64 + 32],
                            ALU.mult, ALU.add))
            for j in range(4):
                ic = icg * 4 + j
                vec(lambda v, ic=ic, j=j: v.tensor_copy(
                    ubuf[:, ic * 64 + 32:ic * 64 + 64],
                    y2b[:, j * 544 + 512:j * 544 + 544]),
                    s_dv if j == 3 else None)
            ac_rel = s_dv.v
        bg_rel = [s_dv.v, s_dv.v]
        for n in range(16):
            w8("sync", s_dv, bg_rel[0])
            dma("sync", y2b[0:32, 0:DM], b_gat[:, n * DM:(n + 1) * DM], s_x)
            w8("vector", s_x)
            if n == 0:
                vec(lambda v: v.tensor_scalar(xtm[0][0:32, 0:DM], y2b[0:32, 0:DM],
                                              gcol[0:32, 0:1], None, ALU.mult), s_dv)
            else:
                vec(lambda v, n=n: v.scalar_tensor_tensor(
                    xtm[0][0:32, 0:DM], y2b[0:32, 0:DM],
                    gcol[0:32, n:n + 1], xtm[0][0:32, 0:DM], ALU.mult, ALU.add),
                    s_dv)
            bg_rel[0] = s_dv.v
        vec(lambda v: v.engine_nop(), s_dv)
        w8("sync", s_dv)
        dma("sync", a2_h, ubuf[:, 0:2048], s_bo)
        dma("sync", be_h, xtm[0][0:32, 0:DM], s_bo)
        barrier()
        scan_phase(own=False)
        vec(lambda v: v.tensor_copy(tpre[:], S[:]), s_dv)
        w8("scalar", s_dv)
        sca(lambda s: s.activation(dprc[:], l_cm_col(1023), AF.Exp), s_ac)
        w8("gpsimd", s_dv)
        w8("gpsimd", s_ac)
        gp(lambda g: g.dma_start(
            out=A(pay_T.tensor, regs["s1T"] if me == "gpsimd" else 0,
                  [[NI, 128], [1, NI]]), in_=tpre[:]), s_pay, 16)
        gp(lambda g: g.dma_start(
            out=A(pay_D.tensor, regs["s1D"] if me == "gpsimd" else 0,
                  [[1, 64], [1, 1]]), in_=dprc[:]), s_pay, 16)
        w8("gpsimd", s_pay)
        gp(lambda g: g.dma_start(
            out=A(pay_F.tensor, regs["s1F"] if me == "gpsimd" else 0,
                  [[4, 1], [1, 4]]), in_=noncesb[:, 0:4]), s_pay, 16)
        barrier()
        if phases <= 0:
            w8("scalar", s_pay)
            return
        ln_phase(x_own)
        barrier()

        plan_own = ([("x", cc) for cc in range(32)] + [("bo", cc) for cc in range(8)]
                    + [("c", cc) for cc in range(8)])
        cm_group(plan_own)
        tm_group(own=True)
        barrier()
        scan_phase(own=True)
        w8("scalar", s_dv, s_dv.v)
        sca(lambda s: s.activation(dwnc[:], l_cm_col(1023), AF.Exp), s_ac)
        w8("vector", s_pe[2], s_pe[2].v)
        w8("vector", s_pe[3], s_pe[3].v)
        w8("vector", s_pe[4], s_pe[4].v)
        vec(lambda v: v.tensor_copy(ubuf[:], S[:]), s_dv)
        w8("gpsimd", s_dv)
        w8("gpsimd", s_ac)
        gp(lambda g: g.dma_start(
            out=A(pay_T.tensor, regs["s2T"] if me == "gpsimd" else 0,
                  [[NI, 128], [1, NI]]), in_=ubuf[:]), s_pay, 16)
        gp(lambda g: g.dma_start(
            out=A(pay_D.tensor, regs["s2D"] if me == "gpsimd" else 0,
                  [[1, 64], [1, 1]]), in_=dwnc[:]), s_pay, 16)
        w8("gpsimd", s_pay)
        gp(lambda g: g.dma_start(
            out=A(pay_F.tensor, regs["s2F"] if me == "gpsimd" else 0,
                  [[4, 1], [1, 4]]), in_=noncesb[:, 4:8]), s_pay, 16)
        w8("sync", s_pay, s_pay.v)
        w8("sync", s_dv, s_dv.v)
        dma("sync", ubuf[:, 0:2048], a2_h, s_bi)
        dma("sync", xtm[0][0:32, 0:DM], be_h, s_bi)
        w8("gpsimd", s_dv, s_dv.v)
        dma("gpsimd", xtm[1][0:32, 0:DM], b_shd, s_w)
        w8("tensor", s_bi)
        w8("tensor", s_dv)
        w8("tensor", s_w)
        for tt2 in range(2):
            for half in range(2):
                bank = 2 + half
                pe_take(bank)
                for ic in range(32):
                    mm(psb[bank][0:32, 0:512],
                       ubuf[:, ic * 64 + half * 32:ic * 64 + half * 32 + 32],
                       A(xnar, ic * NROW + HALO + tt2 * 512,
                         [[32 * NROW, 128], [1, 512]]),
                       ic == 0, ic == 31, s_pe[bank] if ic == 31 else None)
                use("vector", bank)
                doff = 2048 + half * 1024 + tt2 * 512
                vec(lambda v, doff=doff, bank=bank: v.tensor_copy(
                    ubuf[0:32, doff:doff + 512], psb[bank][0:32, 0:512]), rel(bank))
        vec(lambda v: v.engine_nop(), s_dv)
        w8("tensor", s_dv)
        slab_rel = [None, None]
        ost_rel = [s_so.v, s_so.v]
        ptl_rel = [s_dv.v, s_dv.v]
        bias_rel = None
        for ot in range(16):
            b = ot % 2
            if slab_rel[b] is not None:
                w8("gpsimd", s_pe[slab_rel[b][0]], slab_rel[b][1])
            dma("gpsimd", wsl[b][:, 0:8192],
                A(ffn_kp.tensor, ot * 128 * 8192, [[8192, 128], [1, 8192]]), s_w)
            if bias_rel is not None:
                w8("gpsimd", s_pe[bias_rel[0]], bias_rel[1])
            dma("gpsimd", biasl[:],
                A(bias_r.tensor, ot * 256, [[256, 1], [1, 256]]), s_w)
            for mi in range(8):
                bank = (ot * 8 + mi) % 2
                pe_take(bank)
                w8("tensor", s_w)
                for ic in range(32):
                    mm(psb[bank][:, 0:256],
                       A(xnar, ic * NROW + HALO + mi * 128,
                         [[32 * NROW, 128], [1, 128]]),
                       wsl[b][:, ic * 256:(ic + 1) * 256], ic == 0, False)
                mm(psb[bank][:, 0:256], ubuf[0:32, 2048 + mi * 128:2048 + mi * 128 + 128],
                   xtm[0][0:32, ot * 256:(ot + 1) * 256], False, False)
                mm(psb[bank][:, 0:256], ubuf[0:32, 3072 + mi * 128:3072 + mi * 128 + 128],
                   xtm[1][0:32, ot * 256:(ot + 1) * 256], False, False)
                mm(psb[bank][:, 0:256], ones1b[:], biasl[:], False, True,
                   s_pe[bank])
                if mi == 7:
                    slab_rel[b] = (bank, s_pe[bank].v)
                    bias_rel = (bank, s_pe[bank].v)
                pb = (ot * 8 + mi) % 2
                w8("scalar", s_dv, ptl_rel[pb])
                dma("scalar", ptl[pb],
                    A(x_own.tensor, (HALO + mi * 128) * DM + ot * 256,
                      [[DM, 128], [1, 256]]), s_gl)
                ob = (ot * 8 + mi) % 2
                use("vector", bank)
                w8("vector", s_gl)
                w8("vector", s_so, ost_rel[ob])
                vec(lambda v, bank=bank, pb=pb, ob=ob: v.scalar_tensor_tensor(
                    ost[ob], psb[bank][:, 0:256], 1.0, ptl[pb],
                    ALU.mult, ALU.add), rel(bank))
                vec(lambda v: v.engine_nop(), s_dv)
                ptl_rel[pb] = s_dv.v
                w8("scalar", s_fr[bank], s_fr[bank].v)
                dma("scalar", A(out.tensor, mi * 128 * DM + ot * 256,
                                [[DM, 128], [1, 256]]), ost[ob], s_so)
                ost_rel[ob] = s_so.v
        barrier()
        if phases <= 1:
            if debug:
                dma("sync", d_y, A(xnar, 0, [[32 * NROW, 128], [1, NCH * NI]]), s_bo)
                w8("sync", s_bo)
            w8("scalar", s_so)
            w8("vector", s_so)
            return

        if debug:
            w8("sync", s_bo, s_bo.v)
            dma("sync", d_y, A(y_h.tensor, 0, [[NI, 128], [128 * NI, 8], [1, NI]]),
                s_bo)
        barrier()
        if phases <= 2:
            w8("scalar", s_bo)
            w8("vector", s_bo)
            return

        if me == "gpsimd":
            eng.reg_mov(regs["nd"], 1)
            eng.reg_mov(regs["cnt"], s_rcv.v)
            wg = eng.While(regs["nd"])
            wg.__enter__()
            eng.reg_add(regs["cnt"], regs["cnt"], 16)
            eng.dma_start(out=flagsb[:],
                          in_=A(pay_F.tensor, regs["rF"], [[4, 1], [1, 4]])
                          ).then_inc(s_rcv.h, 16)
            eng.wait_ge(s_rcv.h, regs["cnt"])
            eng.reg_load(regs["flg"], flagsb[0:1, 0:1])
            eng.reg_sub(regs["nd"], regs["flg"], regs["non"])
            wg.__exit__(None, None, None)
        w8("gpsimd", s_dv, s_dv.v)
        w8("gpsimd", s_pe[0], s_pe[0].v)
        w8("gpsimd", s_pe[1], s_pe[1].v)
        gp(lambda g: g.dma_start(
            out=sin[:], in_=A(pay_T.tensor, regs["rT"] if me == "gpsimd" else 0,
                              [[NI, 128], [1, NI]])), s_pay, 16)
        gp(lambda g: g.dma_start(
            out=drvc[:], in_=A(pay_D.tensor, regs["rD"] if me == "gpsimd" else 0,
                               [[1, 64], [1, 1]])), s_pay, 16)
        gp(lambda g: g.dma_start(
            out=ubuf[:], in_=A(pay_T.tensor, regs["r2T"] if me == "gpsimd" else 0,
                               [[NI, 128], [1, NI]])), s_pay, 16)
        gp(lambda g: g.dma_start(
            out=drv2[:], in_=A(pay_D.tensor, regs["r2D"] if me == "gpsimd" else 0,
                               [[1, 64], [1, 1]])), s_pay, 16)
        w8("vector", s_pay)
        vec(lambda v: v.tensor_tensor(ccol[:, 0:1], drv2[:], dprc[:], ALU.mult), s_dv)
        w8("vector", s_dv)
        vec(lambda v: v.tensor_scalar(ccol[:, 0:1], ccol[:, 0:1], prmf[0:64, 1:2],
                                      prmf[0:64, 0:1], ALU.mult, ALU.add), s_dv)
        w8("vector", s_dv)
        vec(lambda v: v.tensor_scalar(ccol[:, 1:2], drvc[:], prmf[0:64, 2:3], None,
                                      ALU.mult), s_dv)
        w8("vector", s_dv)
        vec(lambda v: v.scalar_tensor_tensor(ccol[:, 1:2], drv2[:],
                                             prmf[0:64, 3:4], ccol[:, 1:2],
                                             ALU.mult, ALU.add), s_dv)
        w8("vector", s_dv)
        vec(lambda v: v.tensor_copy(ccol[:, 2:3], prmf[0:64, 4:5]), s_dv)
        w8("vector", s_dv)
        w8("vector", s_pe[5], s_pe[5].v)
        for j in range(3):
            vec(lambda v, j=j: v.tensor_scalar(
                emsk[:, j * 64:(j + 1) * 64] if j < 2 else emsk2[:, 0:64],
                eye[0:64, 0:64], ccol[:, j:j + 1], None, ALU.mult), s_dv)
        pe_take(5)
        w8("tensor", s_dv)
        mm(psb[5][:, 0:H], ones64[:], emsk[:, 0:64], True, True)
        mm(psb[5][:, H:2 * H], ones64[:], emsk[:, 64:128], True, True)
        mm(psb[5][:, 2 * H:3 * H], ones64[:], emsk2[:, 0:64], True, True, s_pe[5])
        use("vector", 5)
        vec(lambda v: v.tensor_copy(drbc[:], psb[5][:, 0:H]))
        vec(lambda v: v.tensor_copy(dobc[:], psb[5][:, H:2 * H]))
        vec(lambda v: v.tensor_copy(drb3[:], psb[5][:, 2 * H:3 * H]), rel(5))
        vec(lambda v: v.tensor_tensor(
            y2b[:], sin[:], A(drbc, 0, [[H, 128], [1, 64], [0, 64]]), ALU.mult))
        vec(lambda v: v.tensor_tensor(
            sin[:], tpre[:], A(dobc, 0, [[H, 128], [1, 64], [0, 64]]), ALU.mult))
        vec(lambda v: v.tensor_tensor(y2b[:], y2b[:], sin[:], ALU.add))
        vec(lambda v: v.tensor_tensor(
            sin[:], ubuf[:], A(drb3, 0, [[H, 128], [1, 64], [0, 64]]), ALU.mult))
        vec(lambda v: v.tensor_tensor(y2b[:], y2b[:], sin[:], ALU.add))
        vec(lambda v: v.tensor_copy(sin[:], y2b[:]), s_dv)
        if debug:
            w8("sync", s_dv, s_dv.v)
            dma("sync", d_sin, sin[:], s_bo)
        barrier()
        if phases <= 3:
            w8("scalar", s_bo)
            w8("vector", s_bo)
            return

        gch_rel = [s_dv.v, s_dv.v]
        dma("sync", xtm[0][:], gate_h[0], s_gl)
        gch_ld = [s_gl.v, 0]
        yld = [0] * NCH
        dma("sync", A(xnar, 0, [[32 * NROW, 128], [1, NI]]), y_h[0], s_bi)
        yld[0] = s_bi.v
        for k in range(NCH):
            gb = k % 2
            if k + 1 < NCH:
                w8("sync", s_dv, gch_rel[(k + 1) % 2])
                dma("sync", xtm[(k + 1) % 2][:], gate_h[k + 1], s_gl)
                gch_ld[(k + 1) % 2] = s_gl.v
                dma("sync", A(xnar, (k + 1) * NI, [[32 * NROW, 128], [1, NI]]),
                    y_h[k + 1], s_bi)
                yld[k + 1] = s_bi.v
            w8("vector", s_bi, yld[k])
            w8("tensor", s_bi, yld[k])
            w8("sync", s_pe[0], s_pe[0].v)
            w8("sync", s_pe[1], s_pe[1].v)
            dma("sync", ccmc[:],
                A(ccm_h.tensor, k * 128, [[NTOK, 128], [128 * NTOK, 8], [1, 128]]),
                s_bi)
            w8("scalar", s_dv, s_dv.v)
            sca(lambda s, k=k: s.activation(lsg[:], l_tm[:, k * 64:(k + 1) * 64],
                                            AF.Exp), s_ac)
            w8("tensor", s_ac)
            w8("tensor", s_bi)
            for wv in range(8):
                bank = wv % 2
                pe_take(bank)
                for hh in range(8):
                    mm(psb[bank][:, hh * 64:(hh + 1) * 64],
                       ccmc[:, wv * 128:(wv + 1) * 128],
                       sin[:, (wv * 8 + hh) * 64:(wv * 8 + hh + 1) * 64],
                       True, True, s_pe[bank] if hh == 7 else None)
                use("vector", bank)
                vec(lambda v, wv=wv, bank=bank: v.tensor_tensor(
                    y2b[:, EV0:EV0 + 512], psb[bank][:],
                    A(lsg, wv * 8, [[H, 128], [1, 8], [0, 64]]), ALU.mult), rel(bank))
                vec(lambda v, k=k, wv=wv: v.tensor_tensor(
                    A(xnar, k * NI + wv * 512, [[32 * NROW, 128], [1, 512]]),
                    A(xnar, k * NI + wv * 512, [[32 * NROW, 128], [1, 512]]),
                    y2b[:, EV0:EV0 + 512], ALU.add), s_dv)
            w8("scalar", s_dv)
            sca(lambda s, k=k: s.activation(
                y2b[:], A(xnar, k * NI, [[32 * NROW, 128], [1, NI]]), AF.Square),
                s_ac)
            w8("vector", s_ac)
            for g8 in range(8):
                vec(lambda v, g8=g8: v.reduce_sum(
                    ssb[:, g8:g8 + 1], y2b[:, g8 * 512:(g8 + 1) * 512], AX),
                    s_dv if g8 == 7 else None)
            w8("vector", s_dv)
            vec(lambda v: v.tensor_scalar(ssb[:, 0:8], ssb[:, 0:8], 1.0 / 512.0, EPS,
                                          ALU.mult, ALU.add), s_dv)
            w8("vector", s_dv)
            vec(lambda v: v.reciprocal(ssb[:, 8:16], ssb[:, 0:8]), s_dv)
            w8("scalar", s_dv)
            sca(lambda s: s.activation(ssb[:, 8:16], ssb[:, 8:16], AF.Sqrt), s_ac)
            w8("vector", s_ac)
            w8("vector", s_gl, gch_ld[gb])
            w8("vector", s_tp, s_tp.v)
            vec(lambda v, k=k: v.tensor_tensor(
                ynst[:], A(xnar, k * NI, [[32 * NROW, 128], [1, NI]]),
                A(ssb, 8, [[16, 128], [1, 8], [0, 512]]), ALU.mult))
            vec(lambda v, gb=gb: v.tensor_tensor(ynst[:], ynst[:], xtm[gb][:],
                                                 ALU.mult), s_dv)
            gch_rel[gb] = s_dv.v
            w8("sync", s_dv)
            dmat("sync", A(xnar, k * NI, [[32 * NROW, 128], [128, 32], [1, 128]]),
                 ynst[:], s_tp)
        barrier()

        slab_rel = [None, None]
        ost_rel = [s_so.v, s_so.v]
        ptl_rel = [s_dv.v, s_dv.v]
        for ot in range(16):
            b = ot % 2
            if slab_rel[b] is not None:
                w8("gpsimd", s_pe[slab_rel[b][0]], slab_rel[b][1])
            dma("gpsimd", wsl[b][:, 0:8192],
                A(w_op.tensor, ot * 128 * 8192, [[8192, 128], [1, 8192]]), s_w)
            for mi in range(8):
                bank = (ot * 8 + mi) % 2
                pe_take(bank)
                w8("tensor", s_w)
                for ic in range(32):
                    mm(psb[bank][:, 0:256],
                       A(xnar, mi * NI + ic * 128, [[32 * NROW, 128], [1, 128]]),
                       wsl[b][:, ic * 256:(ic + 1) * 256],
                       ic == 0, ic == 31, s_pe[bank] if ic == 31 else None)
                if mi == 7:
                    slab_rel[b] = (bank, s_pe[bank].v)
                pb = (ot * 8 + mi) % 2
                w8("scalar", s_dv, ptl_rel[pb])
                dma("scalar", ptl[pb],
                    A(out.tensor, mi * 128 * DM + ot * 256, [[DM, 128], [1, 256]]),
                    s_gl)
                ob = (ot * 8 + mi) % 2
                use("vector", bank)
                w8("vector", s_gl)
                w8("vector", s_so, ost_rel[ob])
                vec(lambda v, bank=bank, pb=pb, ob=ob: v.scalar_tensor_tensor(
                    ost[ob], psb[bank][:, 0:256], 1.0, ptl[pb],
                    ALU.mult, ALU.add), rel(bank))
                vec(lambda v: v.engine_nop(), s_dv)
                ptl_rel[pb] = s_dv.v
                w8("scalar", s_fr[bank], s_fr[bank].v)
                dma("scalar", A(out.tensor, mi * 128 * DM + ot * 256,
                                [[DM, 128], [1, 256]]), ost[ob], s_so)
                ost_rel[ob] = s_so.v
        w8("scalar", s_so)
        w8("vector", s_so)

    for en in ["sync", "gpsimd", "vector", "scalar", "tensor"]:
        getattr(block, en)(lambda e, en=en: emit(en))

    for g in reversed(ctxs):
        g.__exit__(None, None, None)

    return nc, {}


def kernel(**inputs):
    x = np.asarray(inputs["L0_attn_hidden_states"], dtype=np.float32)
    w_in_t = np.ascontiguousarray(np.asarray(inputs["L0_attn_in_proj_w"],
                                             dtype=np.float32).T)
    w_cm = np.ascontiguousarray(
        w_in_t[:, 4096:10240].reshape(32, 128, 48, 128)
        .transpose(2, 1, 0, 3).reshape(48 * 128, 4096))
    w_tm = np.ascontiguousarray(
        w_in_t[:, 0:4096].reshape(32, 128, 16, 256)
        .transpose(2, 1, 0, 3).reshape(16 * 128, 8192))
    w_dt = np.ascontiguousarray(
        w_in_t[:, 10240:10304].reshape(32, 128, 64)
        .transpose(1, 0, 2).reshape(128, 2048))
    conv_w = np.asarray(inputs["L0_attn_conv_weight"], dtype=np.float32)
    convw = np.ascontiguousarray(
        conv_w.reshape(48, 128, KC).transpose(1, 0, 2).reshape(128, 48 * KC))
    convb = np.ascontiguousarray(
        np.asarray(inputs["L0_attn_conv_bias"], dtype=np.float32).reshape(48, 128).T)
    dtb = np.asarray(inputs["L0_attn_dt_bias"], dtype=np.float32).reshape(1, H)
    alog = np.asarray(inputs["L0_attn_A_log"], dtype=np.float32).reshape(1, H)
    dvec = np.asarray(inputs["L0_attn_D"], dtype=np.float32).reshape(1, H)
    nw = np.asarray(inputs["L0_attn_norm_weight"], dtype=np.float32).reshape(1, DM)
    w_out_t = np.ascontiguousarray(np.asarray(inputs["L0_attn_out_proj_w"],
                                              dtype=np.float32).T)
    w_op = np.ascontiguousarray(
        w_out_t.reshape(32, 128, 16, 256).transpose(2, 1, 0, 3)
        .reshape(16 * 128, 8192))
    gate = np.asarray(inputs["L0_ffn_gate"], dtype=np.float32)
    ffn_k = np.ascontiguousarray(np.asarray(inputs["L0_ffn_kernel"],
                                            dtype=np.float32))
    ffn_kp = np.ascontiguousarray(
        ffn_k.reshape(32, 128, 16, 256).transpose(2, 1, 0, 3)
        .reshape(16 * 128, 8192))
    bias = np.asarray(inputs["L0_ffn_bias"], dtype=np.float32).reshape(1, DM)
    lora_A = np.asarray(inputs["L0_ffn_lora_A"], dtype=np.float32)
    lora_B = np.asarray(inputs["L0_ffn_lora_B"], dtype=np.float32)
    lora_As = np.asarray(inputs["L0_ffn_lora_A_shared"], dtype=np.float32)
    lora_Bs = np.asarray(inputs["L0_ffn_lora_B_shared"], dtype=np.float32)

    a_cat = np.ascontiguousarray(np.concatenate(
        [lora_A.transpose(2, 0, 1).reshape(DM, 512), lora_As.T], axis=1))
    b_gat = np.ascontiguousarray(lora_B.transpose(2, 0, 1).reshape(32, 16 * DM))
    b_shd = np.ascontiguousarray(lora_Bs.T)

    eye = np.eye(128, dtype=np.float32)
    triu01 = np.triu(np.ones((128, 128), dtype=np.float32))
    strict = np.tril(np.ones((128, 128), dtype=np.float32), -1)

    n1, n2 = [int(v) for v in np.random.randint(1, 2 ** 30, size=2)]
    nonce = np.array([[n1] * 4 + [n2] * 4], dtype=np.int32)

    globals()['LN_SERIAL_G'] = LN_SERIAL
    globals()['WSL_TEST_G'] = WSL_TEST
    globals()['LN_NOSTATS_G'] = LN_NOSTATS
    nc, _ = build_program(phases=PHASES, debug=DEBUG)

    in_maps = []
    for c in range(8):
        bb, r = c // 4, c % 4
        q = c % 2
        own0 = r * 1024
        xo = np.zeros((NROW, DM), dtype=np.float32)
        h0 = max(own0 - HALO, 0)
        xo[HALO - (own0 - h0):HALO] = x[bb, h0:own0]
        xo[HALO:HALO + 1024] = x[bb, own0:own0 + 1024]
        xp = np.zeros((NROW, DM), dtype=np.float32)
        if r >= 2:
            pre0 = (r - 2) * 1024
            ph0 = max(pre0 - HALO, 0)
            xp[HALO - (pre0 - ph0):HALO] = x[bb, ph0:pre0]
            xp[HALO:HALO + 1024] = x[bb, pre0:pre0 + 1024]
        prmf = np.zeros((128, 8), dtype=np.float32)
        prmf[:, 0] = 1.0 if r == 2 else 0.0
        prmf[:, 1] = 0.0 if r == 0 else 1.0
        s1, s2 = q * 2 + 0, q * 2 + 1
        rkind = 0 if r in (0, 2) else 1
        rslot = (1 - q) * 2 + rkind
        prmi = np.zeros((1, 16), dtype=np.int32)
        prmi[0, 0:3] = [s1 * 128 * NI, s1 * H, s1 * 4]
        prmi[0, 3:6] = [s2 * 128 * NI, s2 * H, s2 * 4]
        prmi[0, 6:9] = [rslot * 128 * NI, rslot * H, rslot * 4]
        prmi[0, 9] = n1 if rkind == 0 else n2
        in_maps.append({
            "x_own": xo, "x_pre": xp, "w_cm": w_cm, "w_tm": w_tm, "w_dt": w_dt,
            "w_op": w_op, "ffn_kp": ffn_kp,
            "a_cat": a_cat, "b_gat": b_gat, "b_shd": b_shd,
            "bias_r": bias, "nw_r": nw, "dvec_r": dvec, "alog_r": alog,
            "dtb_r": dtb, "convw": convw, "convb": convb,
            "gate_r": np.ascontiguousarray(gate[4 * bb + r].reshape(1, 16)),
            "eye_i": eye, "triu_i": triu01, "stri_i": strict,
            "prm_f": prmf, "prm_i": prmi, "nonce_r": nonce,
        })

    res = run_bass_kernel_spmd(nc, in_maps, core_ids=list(range(8)))
    outf = np.empty((2, 4096, DM), dtype=np.float32)
    for c in range(8):
        bb, r = c // 4, c % 4
        outf[bb, r * 1024:(r + 1) * 1024] = res.results[c]["out"]
    return outf


# version 09
# speedup vs baseline: 1.7156x; 1.0077x over previous
import numpy as np
import ml_dtypes
import concourse.bass as bass
import concourse.mybir as mybir
from concourse.bass_utils import run_bass_kernel_spmd

F32 = mybir.dt.float32
BF16 = mybir.dt.bfloat16
I32 = mybir.dt.int32
AF = mybir.ActivationFunctionType
ALU = mybir.AluOpType
AX = mybir.AxisListType.X
A = bass.AP

DM = 4096
H, P, NS, G, KC = 64, 64, 128, 8, 4
NI = 4096
GN = G * NS
CONV = NI + 2 * GN
PROJ = NI + CONV + H
EPS = 1e-5
Q = 128
NCH = 8
NTOK = Q * NCH
HALO = 4
NROW = 1152
TT = (0, 512, 1024)
TW = (512, 512, HALO)
COL_GATE, COL_X, COL_B, COL_C, COL_DT = 0, NI, 2 * NI, 2 * NI + GN, PROJ - H
PHASES = 9
DEBUG = False
LN_SERIAL = False
WSL_TEST = False
LN_NOSTATS = False
RH0, EV0, SPA, SPB = 0, 1024, 1536, 2048


def build_program(phases=9, debug=False):
    LN_SERIAL = globals().get('LN_SERIAL_G', False)
    WSL_TEST = globals().get('WSL_TEST_G', False)
    LN_NOSTATS = globals().get('LN_NOSTATS_G', False)
    nc = bass.Bass("TRN2", target_bir_lowering=False,
                   detect_race_conditions=False)

    x_own = nc.dram_tensor("x_own", [NROW, DM], F32, kind="ExternalInput").ap()
    x_pre = nc.dram_tensor("x_pre", [NROW, DM], F32, kind="ExternalInput").ap()
    w_cm = nc.dram_tensor("w_cm", [48 * 128, 4096], F32, kind="ExternalInput").ap()
    w_tm = nc.dram_tensor("w_tm", [16 * 128, 8192], F32, kind="ExternalInput").ap()
    w_dt = nc.dram_tensor("w_dt", [128, 2048], F32, kind="ExternalInput").ap()
    w_op = nc.dram_tensor("w_op", [16 * 128, 8192], F32, kind="ExternalInput").ap()
    ffn_kp = nc.dram_tensor("ffn_kp", [16 * 128, 8192], F32,
                            kind="ExternalInput").ap()
    a_cat = nc.dram_tensor("a_cat", [DM, 544], F32, kind="ExternalInput").ap()
    b_gat = nc.dram_tensor("b_gat", [32, 16 * DM], F32, kind="ExternalInput").ap()
    b_shd = nc.dram_tensor("b_shd", [32, DM], F32, kind="ExternalInput").ap()
    bias_r = nc.dram_tensor("bias_r", [1, DM], F32, kind="ExternalInput").ap()
    nw_r = nc.dram_tensor("nw_r", [1, DM], F32, kind="ExternalInput").ap()
    dvec_r = nc.dram_tensor("dvec_r", [1, H], F32, kind="ExternalInput").ap()
    alog_r = nc.dram_tensor("alog_r", [1, H], F32, kind="ExternalInput").ap()
    dtb_r = nc.dram_tensor("dtb_r", [1, H], F32, kind="ExternalInput").ap()
    convw = nc.dram_tensor("convw", [128, 48 * KC], F32, kind="ExternalInput").ap()
    convb = nc.dram_tensor("convb", [128, 48], F32, kind="ExternalInput").ap()
    gate_r = nc.dram_tensor("gate_r", [1, 16], F32, kind="ExternalInput").ap()
    eye_i = nc.dram_tensor("eye_i", [128, 128], F32, kind="ExternalInput").ap()
    triu_i = nc.dram_tensor("triu_i", [128, 128], F32, kind="ExternalInput").ap()
    stri_i = nc.dram_tensor("stri_i", [128, 128], F32, kind="ExternalInput").ap()
    prm_f = nc.dram_tensor("prm_f", [128, 8], F32, kind="ExternalInput").ap()
    prm_i = nc.dram_tensor("prm_i", [1, 16], I32, kind="ExternalInput").ap()
    nonce_r = nc.dram_tensor("nonce_r", [1, 8], I32, kind="ExternalInput").ap()

    out = nc.dram_tensor("out", [NTOK, DM], F32, kind="ExternalOutput").ap()

    pay_T = nc.dram_tensor("pay_T", [4, 128, NI], BF16, kind="Internal",
                           addr_space="Shared").ap()
    pay_D = nc.dram_tensor("pay_D", [4, H, 1], F32, kind="Internal",
                           addr_space="Shared").ap()
    pay_F = nc.dram_tensor("pay_F", [4, 1, 4], I32, kind="Internal",
                           addr_space="Shared").ap()
    _ik = "ExternalOutput" if debug else "Internal"
    xtm_h = nc.dram_tensor("xtm_h", [NCH, 128, NI], BF16, kind=_ik).ap()
    btm_h = nc.dram_tensor("btm_h", [NCH, 128, GN], BF16, kind=_ik).ap()
    bcm_h = nc.dram_tensor("bcm_h", [G, 128, NTOK], BF16, kind=_ik).ap()
    ccm_h = nc.dram_tensor("ccm_h", [G, 128, NTOK], BF16, kind=_ik).ap()
    gate_h = nc.dram_tensor("gate_h", [NCH, 128, DM], BF16, kind=_ik).ap()
    y_h = nc.dram_tensor("y_h", [NCH, 128, NI], BF16, kind=_ik).ap()
    a2_h = nc.dram_tensor("a2_h", [128, 2048], BF16, kind="Internal").ap()
    be_h = nc.dram_tensor("be_h", [32, DM], BF16, kind="Internal").ap()
    d_y = (nc.dram_tensor("d_y", [128, NCH * NI], BF16, kind="ExternalOutput").ap()
           if debug else None)
    d_sin = (nc.dram_tensor("d_sin", [128, NI], BF16, kind="ExternalOutput").ap()
             if debug else None)

    ctxs = []

    def sb(name, shape, dtype):
        g = nc.sbuf_tensor(name, shape, dtype)
        t = g.__enter__()
        ctxs.append(g)
        return t

    def psalloc(name):
        g = nc.psum_tensor(name, [128, 512], F32)
        t = g.__enter__()
        ctxs.append(g)
        return t

    eye = sb("eye", [128, 128], F32)
    triu = sb("triu", [128, 128], F32)
    stri = sb("stri", [128, 128], F32)
    ones1 = sb("ones1", [1, 128], F32)
    ones64 = sb("ones64", [64, 128], F32)
    emsk = sb("emsk", [64, 128], F32)
    zrow = sb("zrow", [1, H], F32)
    zcol = sb("zcol", [64, 1], F32)
    dprc = sb("dprc", [64, 1], F32)
    dwnc = sb("dwnc", [64, 1], F32)
    drvc = sb("drvc", [64, 1], F32)
    rowb = sb("rowb", [1, 4 * H], F32)
    rowb2 = sb("rowb2", [1, 256], F32)
    biasl = sb("biasl", [1, 256], BF16)
    ones1b = sb("ones1b", [1, 128], BF16)
    abc = sb("abc", [128, H], F32)
    dtbbc = sb("dtbbc", [128, H], F32)
    dbc = sb("dbc", [128, H], F32)
    nwt = sb("nwt", [128, 256], BF16)
    cw = sb("cw", [128, 48 * KC], F32)
    cb = sb("cb", [128, 48], F32)
    gcol = sb("gcol", [128, 16], F32)
    prmf = sb("prmf", [128, 8], F32)
    prmi = sb("prmi", [1, 16], I32)
    flagsb = sb("flagsb", [1, 4], I32)
    noncesb = sb("noncesb", [1, 8], I32)
    drbc = sb("drbc", [128, H], F32)
    drb3 = sb("drb3", [128, H], F32)
    drv2 = sb("drv2", [64, 1], F32)
    ccol = sb("ccol", [64, 4], F32)
    emsk2 = sb("emsk2", [64, 64], F32)
    dobc = sb("dobc", [128, H], F32)

    xnar = sb("xnar", [128, 32 * NROW], BF16)
    wsl0 = sb("wsl0", [128, 8192], BF16)
    strip = sb("strip", [128, NTOK + HALO], F32)
    cacc = sb("cacc", [128, NTOK], F32)
    xbcbf = sb("xbcbf", [128, NTOK], BF16)
    tpstg = sb("tpstg", [128, NTOK], BF16)
    ccmc = sb("ccmc", [128, GN], BF16)
    bcmc = sb("bcmc", [128, GN], BF16)
    btmc = sb("btmc", [128, GN], BF16)
    dtp = sb("dtp", [128, 512], F32)
    a_tm = sb("a_tm", [128, 512], F32)
    l_tm = sb("l_tm", [128, 512], F32)
    wexp = sb("wexp", [128, H], F32)
    lamtm = sb("lamtm", [128, H], F32)
    lambc = sb("lambc", [128, H], F32)
    lsg = sb("lsg", [128, H], F32)
    qtm = sb("qtm", [128, H], F32)
    S = sb("S", [128, NI], F32)
    sbfw = sb("sbfw", [128, 512], BF16)
    tpre = sb("tpre", [128, NI], BF16)
    sin = sb("sin", [128, NI], BF16)
    xtm = [sb("xtm0", [128, NI], BF16), sb("xtm1", [128, NI], BF16)]
    ubuf = sb("ubuf", [128, NI], BF16)
    y2b = sb("y2b", [128, DM], F32)
    ssb = sb("ssb", [128, 16], F32)
    gld = [sb("gld0", [128, 512], BF16), sb("gld1", [128, 512], BF16)]
    ynst = sb("ynst", [128, DM], BF16)
    lnst = sb("lnst", [128, 8], F32)

    ystg = gld
    wsl = [wsl0, wsl0 if WSL_TEST else S.bitcast(BF16)]
    ptl = [y2b[:, 3072:3328], y2b[:, 3328:3584]]
    ost = [y2b[:, 3584:3840], y2b[:, 3840:4096]]
    gmsb = ynst
    lpw = ynst
    mpw = ynst
    udw = ynst
    xdw = ynst

    xin = S
    dtraw = l_tm

    psb = [psalloc(f"psb{i}") for i in range(8)]

    class Sem:
        __slots__ = ("h", "v")

        def __init__(self, h):
            self.h = h
            self.v = 0

    def mksem(name):
        g = nc.semaphore(name)
        h = g.__enter__()
        ctxs.append(g)
        return Sem(h)

    s_x = mksem("s_x")
    s_w = mksem("s_w")
    s_tp = mksem("s_tp")
    s_bi = mksem("s_bi")
    s_bo = mksem("s_bo")
    s_gl = mksem("s_gl")
    s_so = mksem("s_so")
    s_pay = mksem("s_pay")
    s_rcv = mksem("s_rcv")
    s_dv = mksem("s_dv")
    s_ac = mksem("s_ac")
    s_gp = mksem("s_gp")
    s_tb = mksem("s_tb")
    s_ab = mksem("s_ab")
    s_ph = mksem("s_ph")
    s_pe = [mksem(f"s_pe{i}") for i in range(8)]
    s_fr = [mksem(f"s_fr{i}") for i in range(8)]
    DMAS = [s_x, s_w, s_tp, s_bi, s_bo, s_gl, s_so, s_pay]
    ALLSEMS = DMAS + [s_rcv, s_dv, s_ac, s_gp, s_tb, s_ab, s_ph] + s_pe + s_fr

    blkg = nc.Block()
    block = blkg.__enter__()
    ctxs.append(blkg)

    def emit(me):
        for _s in ALLSEMS:
            _s.v = 0
        E = {"sync": nc.sync, "gpsimd": nc.gpsimd, "vector": nc.vector,
             "scalar": nc.scalar, "tensor": nc.tensor}
        eng = E[me]
        regs = {}
        if me == "gpsimd":
            for rn in ["s1T", "s1D", "s1F", "s2T", "s2D", "s2F", "rT", "rD",
                       "rF", "non", "r2T", "r2D", "nd", "cnt", "flg"]:
                g = eng.register("r_" + rn)
                regs[rn] = g.__enter__()
                ctxs.append(g)

        def w8(en, sem, v=None):
            if me == en:
                eng.wait_ge(sem.h, sem.v if v is None else v)

        def dma(en, outap, inap, sem):
            sem.v += 16
            if me == en:
                eng.dma_start(out=outap, in_=inap).then_inc(sem.h, 16)
            return sem.v

        def dmat(en, outap, inap, sem):
            sem.v += 16
            if me == en:
                eng.dma_start_transpose(outap, inap).then_inc(sem.h, 16)
            return sem.v

        def mm(outap, lhsT, rhs, start, stop, sem=None):
            if sem is not None:
                sem.v += 1
            if me == "tensor":
                i = eng.matmul(outap, lhsT, rhs, start=start, stop=stop)
                if sem is not None:
                    i.then_inc(sem.h, 1)

        def tr(outap, inap, sem=None, ident=None):
            if sem is not None:
                sem.v += 1
            if me == "tensor":
                i = eng.transpose(outap, inap, eye[:] if ident is None else ident)
                if sem is not None:
                    i.then_inc(sem.h, 1)

        def vec(fn, sem=None):
            if sem is not None:
                sem.v += 1
            if me == "vector":
                i = fn(eng)
                if sem is not None:
                    i.then_inc(sem.h, 1)

        def sca(fn, sem=None):
            if sem is not None:
                sem.v += 1
            if me == "scalar":
                i = fn(eng)
                if sem is not None:
                    i.then_inc(sem.h, 1)

        def gp(fn, sem=None, inc=1):
            if sem is not None:
                sem.v += inc
            if me == "gpsimd":
                i = fn(eng)
                if sem is not None:
                    i.then_inc(sem.h, inc)

        def barrier():
            s_tb.v += 1
            if me == "tensor":
                eng.nop().then_inc(s_tb.h, 1)
            s_ab.v += 1
            if me == "scalar":
                eng.nop().then_inc(s_ab.h, 1)
            s_gp.v += 1
            if me == "gpsimd":
                eng.nop().then_inc(s_gp.h, 1)
            if me == "vector":
                for s in DMAS + [s_tb, s_ab, s_gp]:
                    eng.wait_ge(s.h, s.v)
            s_ph.v += 1
            if me == "vector":
                eng.nop().then_inc(s_ph.h, 1)
            if me != "vector":
                eng.wait_ge(s_ph.h, s_ph.v)

        def l_cm_col(j):
            return strip[0:64, j:j + 1]

        def pe_take(bank):
            w8("tensor", s_fr[bank])

        def use(en, bank):
            w8(en, s_pe[bank])

        def rel(bank):
            return s_fr[bank]

        dma("sync", eye[:], eye_i, s_x)
        dma("sync", triu[:], triu_i, s_x)
        dma("sync", stri[:], stri_i, s_x)
        dma("sync", cw[:], convw, s_x)
        dma("sync", cb[:], convb, s_x)
        dma("sync", prmf[:], prm_f, s_x)
        dma("sync", prmi[:], prm_i, s_x)
        dma("sync", noncesb[:], nonce_r, s_x)
        dma("sync", gcol[:, 0:16], A(gate_r.tensor, 0, [[0, 128], [1, 16]]), s_x)
        dma("sync", rowb[:, 0:H], alog_r, s_x)
        dma("sync", rowb[:, H:2 * H], dtb_r, s_x)
        dma("sync", rowb[:, 2 * H:3 * H], dvec_r, s_x)
        w8("vector", s_x)
        w8("scalar", s_x)
        w8("gpsimd", s_x)
        w8("tensor", s_x)
        vec(lambda v: v.memset(ones1[:], 1.0))
        vec(lambda v: v.memset(ones1b[:], 1.0))
        vec(lambda v: v.memset(ones64[:], 1.0))
        vec(lambda v: v.memset(zcol[:], 0.0))
        vec(lambda v: v.memset(zrow[:], 0.0), s_dv)
        sca(lambda s: s.activation(rowb[:, 3 * H:4 * H], rowb[:, 0:H], AF.Exp), s_ac)
        w8("vector", s_ac)
        vec(lambda v: v.tensor_scalar_mul(rowb[:, 3 * H:4 * H], rowb[:, 3 * H:4 * H],
                                          -1.0), s_dv)
        w8("tensor", s_dv)
        pe_take(0)
        mm(psb[0][:, 0:H], ones1[:], rowb[:, 3 * H:4 * H], True, True)
        mm(psb[0][:, H:2 * H], ones1[:], rowb[:, H:2 * H], True, True)
        mm(psb[0][:, 2 * H:3 * H], ones1[:], rowb[:, 2 * H:3 * H], True, True, s_pe[0])
        use("vector", 0)
        vec(lambda v: v.tensor_copy(abc[:], psb[0][:, 0:H]))
        vec(lambda v: v.tensor_copy(dtbbc[:], psb[0][:, H:2 * H]))
        vec(lambda v: v.tensor_copy(dbc[:], psb[0][:, 2 * H:3 * H]), rel(0))
        if me == "gpsimd":
            names = ["s1T", "s1D", "s1F", "s2T", "s2D", "s2F", "rT", "rD", "rF",
                     "non", "r2T", "r2D"]
            for j, rn in enumerate(names):
                eng.reg_load(regs[rn], prmi[0:1, j:j + 1])
        barrier()

        def ln_phase(x_dram):
            xbufs = [xin, y2b]
            ybufs = [ynst[:], wsl0[:, 0:4096]]
            ubufs = [ubuf[:], wsl0[:, 4096:8192]]
            udst = [A(ubuf, 0, [[NI, 128], [128, 32], [1, 128]]),
                    A(wsl0, 4096, [[8192, 128], [128, 32], [1, 128]])]
            xrel = [s_ac.v, s_ac.v]
            yrel = [s_tp.v, s_tp.v]
            urel = [s_dv.v, s_dv.v]
            pend = None
            for i in range(9):
                b = i % 2
                xb, yb = xbufs[b], ybufs[b]
                c0 = b * 4
                w8("gpsimd", s_ac, xrel[b])
                dma("gpsimd", xb[:], x_dram[i * 128:(i + 1) * 128, :], s_w)
                w8("vector", s_w)
                w8("vector", s_ac, xrel[b])
                for j in range(8):
                    vec(lambda v, j=j, b=b, xb=xb: v.bn_stats(
                        strip[:, b * 48 + j * 6:b * 48 + (j + 1) * 6],
                        xb[:, j * 512:(j + 1) * 512]), s_dv if j == 7 else None)
                w8("vector", s_dv)
                vec(lambda v, c0=c0, b=b: v.bn_aggr(
                    lnst[:, c0:c0 + 2], strip[:, b * 48:b * 48 + 48]), s_dv)
                w8("vector", s_dv)
                vec(lambda v, c0=c0: v.tensor_scalar_add(
                    lnst[:, c0 + 1:c0 + 2], lnst[:, c0 + 1:c0 + 2], EPS), s_dv)
                w8("scalar", s_dv)
                sca(lambda s, c0=c0: s.activation(
                    lnst[:, c0 + 2:c0 + 3], lnst[:, c0 + 1:c0 + 2], AF.Sqrt),
                    s_ac)
                w8("vector", s_ac)
                vec(lambda v, c0=c0: v.reciprocal(
                    lnst[:, c0 + 2:c0 + 3], lnst[:, c0 + 2:c0 + 3]), s_dv)
                w8("vector", s_dv)
                vec(lambda v, c0=c0: v.tensor_scalar(
                    lnst[:, c0 + 3:c0 + 4], lnst[:, c0:c0 + 1],
                    lnst[:, c0 + 2:c0 + 3], -1.0, ALU.mult, ALU.mult), s_dv)
                bias_done = s_dv.v
                if pend is not None:
                    pi, pb, ptp = pend
                    w8("vector", s_tp, ptp)
                    vec(lambda v, pi=pi, pb=pb: v.tensor_copy(
                        A(xnar, pi * 128, [[32 * NROW, 128], [NROW, 32],
                                           [1, 128]]), ubufs[pb]), s_dv)
                    urel[pb] = s_dv.v
                w8("scalar", s_dv, bias_done)
                w8("scalar", s_tp, yrel[b])
                sca(lambda s, xb=xb, yb=yb, c0=c0: s.activation(
                    yb, xb[:], AF.Identity,
                    bias=lnst[:, c0 + 3:c0 + 4], scale=lnst[:, c0 + 2:c0 + 3]),
                    s_ac)
                xrel[b] = s_ac.v
                w8("sync", s_ac)
                w8("sync", s_dv, urel[b])
                dmat("sync", udst[b], yb, s_tp)
                yrel[b] = s_tp.v
                pend = (i, b, s_tp.v)
            pi, pb, ptp = pend
            w8("vector", s_tp, ptp)
            vec(lambda v, pi=pi, pb=pb: v.tensor_copy(
                A(xnar, pi * 128, [[32 * NROW, 128], [NROW, 32], [1, 128]]),
                ubufs[pb]), s_dv)

        def conv_silu(cslot):
            vec(lambda v, cslot=cslot: v.tensor_scalar(
                cacc[:], strip[:, 1:1 + NTOK], cw[:, cslot * KC:cslot * KC + 1],
                None, ALU.mult))
            for k in range(1, KC):
                vec(lambda v, k=k, cslot=cslot: v.scalar_tensor_tensor(
                    cacc[:], strip[:, 1 + k:1 + k + NTOK],
                    cw[:, cslot * KC + k:cslot * KC + k + 1], cacc[:],
                    ALU.mult, ALU.add))
            vec(lambda v: v.engine_nop(), s_dv)
            w8("scalar", s_dv)
            sca(lambda s, cslot=cslot: s.activation(
                xbcbf[:], cacc[:], AF.Silu, bias=cb[:, cslot:cslot + 1]), s_ac)

        def cm_group(plan):
            slab_rel = [None, None]
            cm_group.slab_rel = slab_rel
            strip_rel = 0
            xbc_guard = (s_dv, 0)
            for idx, (kind, cc) in enumerate(plan):
                b = idx % 2
                j = {"x": 0, "bo": 32, "bp": 32, "c": 40}[kind] + cc
                if slab_rel[b] is not None:
                    w8("gpsimd", s_pe[slab_rel[b][0]], slab_rel[b][1])
                dma("gpsimd", wsl[b][:, 0:4096],
                    A(w_cm.tensor, j * 128 * 4096, [[4096, 128], [1, 4096]]),
                    s_w)
                for ti in range(3):
                    bank = (idx * 3 + ti) % 2
                    pe_take(bank)
                    w8("tensor", s_w)
                    for ic in range(32):
                        mm(psb[bank][:, 0:TW[ti]], wsl[b][:, ic * 128:(ic + 1) * 128],
                           A(xnar, ic * NROW + TT[ti], [[32 * NROW, 128], [1, TW[ti]]]),
                           ic == 0, ic == 31, s_pe[bank] if ic == 31 else None)
                    if ti == 2:
                        slab_rel[b] = (bank, s_pe[bank].v)
                    use("scalar", bank)
                    w8("scalar", s_dv, strip_rel)
                    sca(lambda s, bank=bank, ti=ti: s.copy(
                        strip[:, TT[ti]:TT[ti] + TW[ti]], psb[bank][:, 0:TW[ti]]),
                        rel(bank))
                w8("vector", s_fr[0], s_fr[0].v)
                w8("vector", s_fr[1], s_fr[1].v)
                w8("scalar", xbc_guard[0], xbc_guard[1])
                conv_silu({"x": cc, "bo": 32 + cc, "bp": 32 + cc, "c": 40 + cc}[kind])
                strip_rel = s_dv.v
                if kind in ("x", "bp"):
                    w8("sync", s_ac)
                    dmat("sync", A(tpstg, 0, [[NTOK, 128], [128, 8], [1, 128]]),
                         xbcbf[:], s_tp)
                    xbc_guard = (s_tp, s_tp.v)
                    w8("sync", s_tp)
                    if kind == "x":
                        dma("sync", A(xtm_h.tensor, cc * 128,
                                      [[NI, 128], [128 * NI, 8], [1, 128]]), tpstg[:],
                            s_bo)
                    else:
                        dma("sync", A(btm_h.tensor, cc * 128,
                                      [[GN, 128], [128 * GN, 8], [1, 128]]), tpstg[:],
                            s_bo)
                else:
                    w8("sync", s_ac)
                    hb = bcm_h if kind == "bo" else ccm_h
                    dma("sync", hb[cc], xbcbf[:], s_bo)
                    xbc_guard = (s_bo, s_bo.v)

        def tm_group(own):
            tiles = ([("g", COL_GATE + i * 256, 256) for i in range(16)] if own
                     else [])
            tiles.append(("d", COL_DT, 64))
            slab_rel = list(getattr(cm_group, "slab_rel", [None, None]))
            gld_rel = [s_bo.v, s_bo.v]
            for tj, (kind, col, wid) in enumerate(tiles):
                b = tj % 2
                if slab_rel[b] is not None:
                    w8("gpsimd", s_pe[slab_rel[b][0]], slab_rel[b][1])
                wsrc = (A(w_tm.tensor, (col // 256) * 128 * 8192,
                          [[8192, 128], [1, 8192]]) if kind == "g"
                        else A(w_dt.tensor, 0, [[2048, 128], [1, 2048]]))
                dma("gpsimd", wsl[b][:, 0:32 * wid], wsrc, s_w)
                if kind == "g":
                    bank5 = 5
                    pe_take(bank5)
                    w8("tensor", s_dv, s_dv.v)
                    coff = col - COL_GATE
                    w8("sync", s_pe[bank5], s_pe[bank5].v)
                    dma("sync", rowb2[:], A(nw_r.tensor, coff, [[256, 1], [1, 256]]),
                        s_x)
                    w8("tensor", s_x)
                    mm(psb[bank5][:, 0:256], ones1[:], rowb2[:], True, True,
                       s_pe[bank5])
                    use("vector", bank5)
                    vec(lambda v, bank5=bank5: v.tensor_copy(
                        nwt[:], psb[bank5][:, 0:256]), rel(bank5))
                    vec(lambda v: v.engine_nop(), s_dv)
                for mi in range(8):
                    bank = (tj * 8 + mi) % 2
                    pe_take(bank)
                    w8("tensor", s_w)
                    for ic in range(32):
                        mm(psb[bank][:, 0:wid],
                           A(xnar, ic * NROW + HALO + mi * 128,
                             [[32 * NROW, 128], [1, 128]]),
                           wsl[b][:, ic * wid:(ic + 1) * wid],
                           ic == 0, ic == 31, s_pe[bank] if ic == 31 else None)
                    if mi == 7:
                        slab_rel[b] = (bank, s_pe[bank].v)
                    if kind == "g":
                        gb = (tj * 8 + mi) % 2
                        use("scalar", bank)
                        sca(lambda s, bank=bank: s.activation(
                            ynst[:, 0:256], psb[bank][:, 0:256], AF.Silu), s_ac)
                        w8("vector", s_ac)
                        w8("vector", s_bo, gld_rel[gb])
                        w8("vector", s_dv, s_dv.v)
                        vec(lambda v, gb=gb: v.tensor_tensor(
                            gld[gb][:, 0:256], ynst[:, 0:256], nwt[:], ALU.mult),
                            rel(bank))
                        vec(lambda v: v.engine_nop(), s_dv)
                        w8("scalar", s_fr[bank], s_fr[bank].v)
                        coff = col - COL_GATE
                        dma("scalar",
                            A(gate_h.tensor, mi * 128 * DM + coff,
                              [[DM, 128], [1, 256]]), gld[gb][:, 0:256], s_bo)
                        gld_rel[gb] = s_bo.v
                    else:
                        use("vector", bank)
                        vec(lambda v, bank=bank, mi=mi: v.tensor_copy(
                            dtraw[:, mi * 64:(mi + 1) * 64], psb[bank][:, 0:64]),
                            rel(bank))

        def dt_chain():
            vec(lambda v: v.tensor_tensor(
                dtp[:], dtraw[:], A(dtbbc, 0, [[H, 128], [0, 8], [1, H]]), ALU.add))
            vec(lambda v: v.tensor_scalar_mul(y2b[:, SPA:SPA + 512], dtp[:], -1.0))
            vec(lambda v: v.tensor_tensor(y2b[:, SPA:SPA + 512], y2b[:, SPA:SPA + 512],
                                          dtp[:], ALU.min), s_dv)
            w8("scalar", s_dv)
            sca(lambda s: s.activation(y2b[:, SPB:SPB + 512], y2b[:, SPA:SPA + 512],
                                       AF.Exp), s_ac)
            sca(lambda s: s.activation(y2b[:, SPB:SPB + 512], y2b[:, SPB:SPB + 512],
                                       AF.Ln, bias=1.0), s_ac)
            w8("vector", s_ac)
            vec(lambda v: v.tensor_relu(y2b[:, SPA:SPA + 512], dtp[:]))
            vec(lambda v: v.tensor_tensor(dtp[:], y2b[:, SPA:SPA + 512],
                                          y2b[:, SPB:SPB + 512], ALU.add))
            vec(lambda v: v.tensor_tensor(
                a_tm[:], dtp[:], A(abc, 0, [[H, 128], [0, 8], [1, H]]), ALU.mult), s_dv)
            w8("tensor", s_dv)
            for k in range(8):
                bank = 4 + k % 2
                pe_take(bank)
                tr(psb[bank][0:64, 0:128], a_tm[:, k * 64:(k + 1) * 64], s_pe[bank])
                use("vector", bank)
                vec(lambda v, k=k, bank=bank: v.tensor_copy(
                    cacc[0:64, k * 128:(k + 1) * 128], psb[bank][0:64, 0:128]),
                    rel(bank))
            vec(lambda v: v.tensor_tensor_scan(
                strip[0:64, 0:NTOK], cacc[0:64, 0:NTOK],
                A(zcol, 0, [[1, 64], [0, NTOK]]), 0.0, ALU.add, ALU.add), s_dv)
            w8("tensor", s_dv)
            for k in range(8):
                bank = 4 + k % 2
                pe_take(bank)
                tr(psb[bank][:, 0:64], strip[0:64, k * 128:(k + 1) * 128],
                   s_pe[bank], ident=eye[0:64, 0:64])
                use("vector", bank)
                vec(lambda v, k=k, bank=bank: v.tensor_copy(
                    l_tm[:, k * 64:(k + 1) * 64], psb[bank][:, 0:64]), rel(bank))
            vec(lambda v: v.engine_nop(), s_dv)

        def chunk_decays(k):
            bank = 5
            w8("vector", s_pe[bank], s_pe[bank].v)
            vec(lambda v, k=k: v.tensor_scalar(
                emsk[:, 0:64], eye[0:64, 0:64],
                l_cm_col(k * 128 + 127), None, ALU.mult))
            prevcol = zcol[:, 0:1] if k == 0 else l_cm_col(k * 128 - 1)
            vec(lambda v, prevcol=prevcol: v.tensor_scalar(
                emsk[:, 64:128], eye[0:64, 0:64], prevcol, None, ALU.mult), s_dv)
            pe_take(bank)
            w8("tensor", s_dv)
            mm(psb[bank][:, 0:H], ones64[:], emsk[:, 0:64], True, True)
            mm(psb[bank][:, H:2 * H], ones64[:], emsk[:, 64:128], True, True,
               s_pe[bank])
            use("vector", bank)
            vec(lambda v, bank=bank, k=k: v.tensor_tensor(
                y2b[:, SPA:SPA + H], psb[bank][:, 0:H], l_tm[:, k * 64:(k + 1) * 64],
                ALU.subtract))
            vec(lambda v, bank=bank, k=k: v.tensor_tensor(
                y2b[:, SPA + H:SPA + 2 * H], l_tm[:, k * 64:(k + 1) * 64],
                psb[bank][:, H:2 * H], ALU.subtract))
            vec(lambda v: v.tensor_tensor(
                y2b[:, SPA + 2 * H:SPA + 3 * H], y2b[:, SPA:SPA + H],
                y2b[:, SPA + H:SPA + 2 * H], ALU.add), rel(bank))
            vec(lambda v: v.engine_nop(), s_dv)
            w8("scalar", s_dv)
            sca(lambda s: s.activation(wexp[:], y2b[:, SPA:SPA + H], AF.Exp))
            sca(lambda s: s.activation(lamtm[:], y2b[:, SPA + H:SPA + 2 * H], AF.Exp))
            sca(lambda s: s.activation(lambc[:], y2b[:, SPA + 2 * H:SPA + 3 * H],
                                       AF.Exp), s_ac)
            w8("vector", s_ac)
            vec(lambda v, k=k: v.tensor_tensor(
                qtm[:], dtp[:, k * 64:(k + 1) * 64], wexp[:], ALU.mult), s_dv)

        def scan_phase(own):
            vec(lambda v: v.memset(S[:], 0.0), s_dv)
            dt_chain()
            ystg_rel = [s_bo.v, s_bo.v]
            xtm_rel = [s_dv.v, s_dv.v]
            lpw_rel = s_dv.v
            btm_rel = s_pe[4].v
            bc_rel = s_pe[3].v
            dma("sync", xtm[0][:], xtm_h[0], s_bi)
            xtm_ld = [s_bi.v, 0]
            for k in range(NCH):
                bx = k % 2
                if k + 1 < NCH:
                    w8("sync", s_dv, xtm_rel[(k + 1) % 2])
                    dma("sync", xtm[(k + 1) % 2][:], xtm_h[k + 1], s_bi)
                    xtm_ld[(k + 1) % 2] = s_bi.v
                chunk_decays(k)
                w8("sync", s_pe[4], btm_rel)
                if own:
                    w8("sync", s_pe[3], bc_rel)
                    dma("sync", bcmc[:],
                        A(bcm_h.tensor, k * 128,
                          [[NTOK, 128], [128 * NTOK, 8], [1, 128]]), s_bi)
                    dma("sync", ccmc[:],
                        A(ccm_h.tensor, k * 128,
                          [[NTOK, 128], [128 * NTOK, 8], [1, 128]]), s_bi)
                    w8("sync", s_bi)
                    for g8 in range(8):
                        dmat("sync", A(btmc, g8 * 128, [[GN, 128], [1, 128]]),
                             bcmc[:, g8 * 128:(g8 + 1) * 128], s_tp)
                    btm_ld = (s_tp, s_tp.v)
                    ccm_ld = s_bi.v
                else:
                    dma("sync", btmc[:], btm_h[k], s_bi)
                    btm_ld = (s_bi, s_bi.v)
                w8("vector", s_bi, xtm_ld[bx])
                if own:
                    vec(lambda v, k=k, bx=bx: v.tensor_tensor(
                        ubuf[:], xtm[bx][:],
                        A(dtp, k * 64, [[512, 128], [1, 64], [0, 64]]), ALU.mult),
                        s_dv)
                    w8("tensor", s_dv)
                    w8("tensor", s_bi, ccm_ld)
                    for g8 in range(8):
                        bank = 2 + g8 // 4
                        if g8 % 4 == 0:
                            pe_take(bank)
                        mm(psb[bank][:, (g8 % 4) * 128:(g8 % 4) * 128 + 128],
                           bcmc[:, g8 * 128:(g8 + 1) * 128],
                           ccmc[:, g8 * 128:(g8 + 1) * 128],
                           True, True, s_pe[bank] if g8 % 4 == 3 else None)
                    use("vector", 2)
                    vec(lambda v: v.tensor_tensor(
                        ynst[:, 0:512], psb[2][:],
                        A(triu, 0, [[128, 128], [0, 4], [1, 128]]), ALU.mult), rel(2))
                    use("vector", 3)
                    vec(lambda v: v.tensor_tensor(
                        ynst[:, 512:1024], psb[3][:],
                        A(triu, 0, [[128, 128], [0, 4], [1, 128]]), ALU.mult), rel(3))
                    bc_rel = s_pe[3].v
                for wv in range(8):
                    h0 = wv * 8
                    if own:
                        w8("vector", s_pe[3], s_pe[3].v)
                        vec(lambda v, wv=wv: v.tensor_copy(
                            sbfw[:], S[:, wv * 512:(wv + 1) * 512]), s_dv)
                        w8("vector", s_pe[7], s_pe[7].v)
                        vec(lambda v, k=k, h0=h0: v.tensor_tensor(
                            y2b[:, RH0:RH0 + 1024],
                            A(a_tm, k * 64 + h0, [[512, 128], [1, 8], [0, 128]]),
                            A(triu, 0, [[128, 128], [0, 8], [1, 128]]), ALU.mult),
                            s_dv)
                        w8("tensor", s_dv)
                        pe_take(6)
                        mm(psb[6][:], stri[:], y2b[:, RH0:RH0 + 512], True, True,
                           s_pe[6])
                        pe_take(7)
                        mm(psb[7][:], stri[:], y2b[:, RH0 + 512:RH0 + 1024], True,
                           True, s_pe[7])
                        use("scalar", 6)
                        w8("scalar", s_dv, lpw_rel)
                        sca(lambda s: s.activation(ynst[:, 1024:1536], psb[6][:], AF.Exp),
                            rel(6))
                        use("scalar", 7)
                        sca(lambda s: s.activation(ynst[:, 1536:2048], psb[7][:],
                                                   AF.Exp), rel(7))
                        sca(lambda s: s.nop(), s_ac)
                        w8("vector", s_ac)
                        w8("vector", s_pe[2], s_pe[2].v)
                        vec(lambda v, wv=wv: v.tensor_tensor(
                            ynst[:, 2048:3072], ynst[:, 1024:2048],
                            A(ynst, wv * 128, [[DM, 128], [0, 8], [1, 128]]),
                            ALU.mult), s_dv)
                        lpw_rel = s_dv.v
                        w8("tensor", s_dv)
                        pe_take(2)
                        pe_take(3)
                        w8("tensor", btm_ld[0], btm_ld[1])
                        for hh in range(8):
                            mm(psb[2][:, hh * 64:(hh + 1) * 64],
                               ynst[:, 2048 + hh * 128:2048 + (hh + 1) * 128],
                               ubuf[:, (h0 + hh) * 64:(h0 + hh + 1) * 64],
                               True, True, s_pe[2] if hh == 7 else None)
                        for hh in range(8):
                            mm(psb[3][:, hh * 64:(hh + 1) * 64],
                               ccmc[:, wv * 128:(wv + 1) * 128],
                               sbfw[:, hh * 64:(hh + 1) * 64],
                               True, True, s_pe[3] if hh == 7 else None)
                    w8("vector", s_pe[4], s_pe[4].v)
                    vec(lambda v, wv=wv, bx=bx, h0=h0: v.tensor_tensor(
                        ynst[:, 3072:3584], xtm[bx][:, wv * 512:(wv + 1) * 512],
                        A(qtm, h0, [[H, 128], [1, 8], [0, 64]]), ALU.mult), s_dv)
                    w8("tensor", s_dv)
                    if not own:
                        w8("tensor", btm_ld[0], btm_ld[1])
                    pe_take(4)
                    for hh in range(8):
                        mm(psb[4][:, hh * 64:(hh + 1) * 64],
                           btmc[:, wv * 128:(wv + 1) * 128],
                           ynst[:, 3072 + hh * 64:3072 + (hh + 1) * 64],
                           True, True, s_pe[4] if hh == 7 else None)
                    if wv == 7:
                        btm_rel = s_pe[4].v
                    if own:
                        vec(lambda v, wv=wv, bx=bx, h0=h0: v.tensor_tensor(
                            ynst[:, 3584:4096], xtm[bx][:, wv * 512:(wv + 1) * 512],
                            A(dbc, h0, [[H, 128], [1, 8], [0, 64]]), ALU.mult))
                        use("vector", 3)
                        vec(lambda v, h0=h0: v.tensor_tensor(
                            y2b[:, EV0:EV0 + 512], psb[3][:],
                            A(lamtm, h0, [[H, 128], [1, 8], [0, 64]]), ALU.mult),
                            rel(3))
                        use("vector", 2)
                        vec(lambda v: v.scalar_tensor_tensor(
                            y2b[:, EV0:EV0 + 512], psb[2][:], 1.0,
                            y2b[:, EV0:EV0 + 512], ALU.mult, ALU.add), rel(2))
                        yb = (k * 8 + wv) % 2
                        w8("vector", s_bo, ystg_rel[yb])
                        vec(lambda v, yb=yb: v.tensor_tensor(
                            ystg[yb][:], y2b[:, EV0:EV0 + 512],
                            ynst[:, 3584:4096], ALU.add), s_dv)
                        w8("sync", s_dv)
                        dma("sync", A(y_h.tensor, k * 128 * NI + wv * 512,
                                      [[NI, 128], [1, 512]]), ystg[yb][:], s_bo)
                        ystg_rel[yb] = s_bo.v
                    use("vector", 4)
                    vec(lambda v, wv=wv, h0=h0: v.tensor_tensor(
                        S[:, wv * 512:(wv + 1) * 512], S[:, wv * 512:(wv + 1) * 512],
                        A(lambc, h0, [[H, 128], [1, 8], [0, 64]]), ALU.mult))
                    vec(lambda v, wv=wv: v.tensor_tensor(
                        S[:, wv * 512:(wv + 1) * 512], S[:, wv * 512:(wv + 1) * 512],
                        psb[4][:], ALU.add), rel(4))
                    vec(lambda v: v.engine_nop(), s_dv)
                    if wv == 7:
                        xtm_rel[bx] = s_dv.v

        ln_phase(x_pre)
        barrier()
        plan_pre = [("x", cc) for cc in range(32)] + [("bp", cc) for cc in range(8)]
        cm_group(plan_pre)
        tm_group(own=False)
        ac_rel = s_dv.v
        for icg in range(8):
            w8("sync", s_dv, ac_rel)
            dma("sync", y2b[:, 0:2176],
                A(a_cat.tensor, icg * 4 * 128 * 544,
                  [[544, 128], [128 * 544, 4], [1, 544]]), s_x)
            w8("vector", s_x)
            for n in range(16):
                for j in range(4):
                    ic = icg * 4 + j
                    if n == 0:
                        vec(lambda v, ic=ic, j=j: v.tensor_scalar(
                            ubuf[:, ic * 64:ic * 64 + 32],
                            y2b[:, j * 544:j * 544 + 32],
                            gcol[:, 0:1], None, ALU.mult))
                    else:
                        vec(lambda v, ic=ic, j=j, n=n: v.scalar_tensor_tensor(
                            ubuf[:, ic * 64:ic * 64 + 32],
                            y2b[:, j * 544 + n * 32:j * 544 + n * 32 + 32],
                            gcol[:, n:n + 1], ubuf[:, ic * 64:ic * 64 + 32],
                            ALU.mult, ALU.add))
            for j in range(4):
                ic = icg * 4 + j
                vec(lambda v, ic=ic, j=j: v.tensor_copy(
                    ubuf[:, ic * 64 + 32:ic * 64 + 64],
                    y2b[:, j * 544 + 512:j * 544 + 544]),
                    s_dv if j == 3 else None)
            ac_rel = s_dv.v
        bg_rel = [s_dv.v, s_dv.v]
        for n in range(16):
            w8("sync", s_dv, bg_rel[0])
            dma("sync", y2b[0:32, 0:DM], b_gat[:, n * DM:(n + 1) * DM], s_x)
            w8("vector", s_x)
            if n == 0:
                vec(lambda v: v.tensor_scalar(xtm[0][0:32, 0:DM], y2b[0:32, 0:DM],
                                              gcol[0:32, 0:1], None, ALU.mult), s_dv)
            else:
                vec(lambda v, n=n: v.scalar_tensor_tensor(
                    xtm[0][0:32, 0:DM], y2b[0:32, 0:DM],
                    gcol[0:32, n:n + 1], xtm[0][0:32, 0:DM], ALU.mult, ALU.add),
                    s_dv)
            bg_rel[0] = s_dv.v
        vec(lambda v: v.engine_nop(), s_dv)
        w8("sync", s_dv)
        dma("sync", a2_h, ubuf[:, 0:2048], s_bo)
        dma("sync", be_h, xtm[0][0:32, 0:DM], s_bo)
        barrier()
        scan_phase(own=False)
        vec(lambda v: v.tensor_copy(tpre[:], S[:]), s_dv)
        w8("scalar", s_dv)
        sca(lambda s: s.activation(dprc[:], l_cm_col(1023), AF.Exp), s_ac)
        w8("gpsimd", s_dv)
        w8("gpsimd", s_ac)
        gp(lambda g: g.dma_start(
            out=A(pay_T.tensor, regs["s1T"] if me == "gpsimd" else 0,
                  [[NI, 128], [1, NI]]), in_=tpre[:]), s_pay, 16)
        gp(lambda g: g.dma_start(
            out=A(pay_D.tensor, regs["s1D"] if me == "gpsimd" else 0,
                  [[1, 64], [1, 1]]), in_=dprc[:]), s_pay, 16)
        w8("gpsimd", s_pay)
        gp(lambda g: g.dma_start(
            out=A(pay_F.tensor, regs["s1F"] if me == "gpsimd" else 0,
                  [[4, 1], [1, 4]]), in_=noncesb[:, 0:4]), s_pay, 16)
        barrier()
        if phases <= 0:
            w8("scalar", s_pay)
            return
        ln_phase(x_own)
        barrier()

        plan_own = ([("x", cc) for cc in range(32)] + [("bo", cc) for cc in range(8)]
                    + [("c", cc) for cc in range(8)])
        cm_group(plan_own)
        tm_group(own=True)
        barrier()
        scan_phase(own=True)
        w8("scalar", s_dv, s_dv.v)
        sca(lambda s: s.activation(dwnc[:], l_cm_col(1023), AF.Exp), s_ac)
        w8("vector", s_pe[2], s_pe[2].v)
        w8("vector", s_pe[3], s_pe[3].v)
        w8("vector", s_pe[4], s_pe[4].v)
        vec(lambda v: v.tensor_copy(ubuf[:], S[:]), s_dv)
        w8("gpsimd", s_dv)
        w8("gpsimd", s_ac)
        gp(lambda g: g.dma_start(
            out=A(pay_T.tensor, regs["s2T"] if me == "gpsimd" else 0,
                  [[NI, 128], [1, NI]]), in_=ubuf[:]), s_pay, 16)
        gp(lambda g: g.dma_start(
            out=A(pay_D.tensor, regs["s2D"] if me == "gpsimd" else 0,
                  [[1, 64], [1, 1]]), in_=dwnc[:]), s_pay, 16)
        w8("gpsimd", s_pay)
        gp(lambda g: g.dma_start(
            out=A(pay_F.tensor, regs["s2F"] if me == "gpsimd" else 0,
                  [[4, 1], [1, 4]]), in_=noncesb[:, 4:8]), s_pay, 16)
        w8("sync", s_pay, s_pay.v)
        w8("sync", s_dv, s_dv.v)
        dma("sync", ubuf[:, 0:2048], a2_h, s_bi)
        dma("sync", xtm[0][0:32, 0:DM], be_h, s_bi)
        w8("gpsimd", s_dv, s_dv.v)
        dma("gpsimd", xtm[1][0:32, 0:DM], b_shd, s_w)
        w8("tensor", s_bi)
        w8("tensor", s_dv)
        w8("tensor", s_w)
        for tt2 in range(2):
            for half in range(2):
                bank = 2 + half
                pe_take(bank)
                for ic in range(32):
                    mm(psb[bank][0:32, 0:512],
                       ubuf[:, ic * 64 + half * 32:ic * 64 + half * 32 + 32],
                       A(xnar, ic * NROW + HALO + tt2 * 512,
                         [[32 * NROW, 128], [1, 512]]),
                       ic == 0, ic == 31, s_pe[bank] if ic == 31 else None)
                use("vector", bank)
                doff = 2048 + half * 1024 + tt2 * 512
                vec(lambda v, doff=doff, bank=bank: v.tensor_copy(
                    ubuf[0:32, doff:doff + 512], psb[bank][0:32, 0:512]), rel(bank))
        vec(lambda v: v.engine_nop(), s_dv)
        w8("tensor", s_dv)
        slab_rel = [None, None]
        ost_rel = [s_so.v, s_so.v]
        ptl_rel = [s_dv.v, s_dv.v]
        bias_rel = None
        for ot in range(16):
            b = ot % 2
            if slab_rel[b] is not None:
                w8("gpsimd", s_pe[slab_rel[b][0]], slab_rel[b][1])
            dma("gpsimd", wsl[b][:, 0:8192],
                A(ffn_kp.tensor, ot * 128 * 8192, [[8192, 128], [1, 8192]]), s_w)
            if bias_rel is not None:
                w8("gpsimd", s_pe[bias_rel[0]], bias_rel[1])
            dma("gpsimd", biasl[:],
                A(bias_r.tensor, ot * 256, [[256, 1], [1, 256]]), s_w)
            for mi in range(8):
                bank = (ot * 8 + mi) % 2
                pe_take(bank)
                w8("tensor", s_w)
                for ic in range(32):
                    mm(psb[bank][:, 0:256],
                       A(xnar, ic * NROW + HALO + mi * 128,
                         [[32 * NROW, 128], [1, 128]]),
                       wsl[b][:, ic * 256:(ic + 1) * 256], ic == 0, False)
                mm(psb[bank][:, 0:256], ubuf[0:32, 2048 + mi * 128:2048 + mi * 128 + 128],
                   xtm[0][0:32, ot * 256:(ot + 1) * 256], False, False)
                mm(psb[bank][:, 0:256], ubuf[0:32, 3072 + mi * 128:3072 + mi * 128 + 128],
                   xtm[1][0:32, ot * 256:(ot + 1) * 256], False, False)
                mm(psb[bank][:, 0:256], ones1b[:], biasl[:], False, True,
                   s_pe[bank])
                if mi == 7:
                    slab_rel[b] = (bank, s_pe[bank].v)
                    bias_rel = (bank, s_pe[bank].v)
                pb = (ot * 8 + mi) % 2
                w8("scalar", s_dv, ptl_rel[pb])
                dma("scalar", ptl[pb],
                    A(x_own.tensor, (HALO + mi * 128) * DM + ot * 256,
                      [[DM, 128], [1, 256]]), s_gl)
                ob = (ot * 8 + mi) % 2
                use("vector", bank)
                w8("vector", s_gl)
                w8("vector", s_so, ost_rel[ob])
                vec(lambda v, bank=bank, pb=pb, ob=ob: v.scalar_tensor_tensor(
                    ost[ob], psb[bank][:, 0:256], 1.0, ptl[pb],
                    ALU.mult, ALU.add), rel(bank))
                vec(lambda v: v.engine_nop(), s_dv)
                ptl_rel[pb] = s_dv.v
                w8("scalar", s_fr[bank], s_fr[bank].v)
                dma("scalar", A(out.tensor, mi * 128 * DM + ot * 256,
                                [[DM, 128], [1, 256]]), ost[ob], s_so)
                ost_rel[ob] = s_so.v
        barrier()
        if phases <= 1:
            if debug:
                dma("sync", d_y, A(xnar, 0, [[32 * NROW, 128], [1, NCH * NI]]), s_bo)
                w8("sync", s_bo)
            w8("scalar", s_so)
            w8("vector", s_so)
            return

        if debug:
            w8("sync", s_bo, s_bo.v)
            dma("sync", d_y, A(y_h.tensor, 0, [[NI, 128], [128 * NI, 8], [1, NI]]),
                s_bo)
        barrier()
        if phases <= 2:
            w8("scalar", s_bo)
            w8("vector", s_bo)
            return

        if me == "gpsimd":
            eng.reg_mov(regs["nd"], 1)
            eng.reg_mov(regs["cnt"], s_rcv.v)
            wg = eng.While(regs["nd"])
            wg.__enter__()
            eng.reg_add(regs["cnt"], regs["cnt"], 16)
            eng.dma_start(out=flagsb[:],
                          in_=A(pay_F.tensor, regs["rF"], [[4, 1], [1, 4]])
                          ).then_inc(s_rcv.h, 16)
            eng.wait_ge(s_rcv.h, regs["cnt"])
            eng.reg_load(regs["flg"], flagsb[0:1, 0:1])
            eng.reg_sub(regs["nd"], regs["flg"], regs["non"])
            wg.__exit__(None, None, None)
        w8("gpsimd", s_dv, s_dv.v)
        w8("gpsimd", s_pe[0], s_pe[0].v)
        w8("gpsimd", s_pe[1], s_pe[1].v)
        gp(lambda g: g.dma_start(
            out=sin[:], in_=A(pay_T.tensor, regs["rT"] if me == "gpsimd" else 0,
                              [[NI, 128], [1, NI]])), s_pay, 16)
        gp(lambda g: g.dma_start(
            out=drvc[:], in_=A(pay_D.tensor, regs["rD"] if me == "gpsimd" else 0,
                               [[1, 64], [1, 1]])), s_pay, 16)
        gp(lambda g: g.dma_start(
            out=ubuf[:], in_=A(pay_T.tensor, regs["r2T"] if me == "gpsimd" else 0,
                               [[NI, 128], [1, NI]])), s_pay, 16)
        gp(lambda g: g.dma_start(
            out=drv2[:], in_=A(pay_D.tensor, regs["r2D"] if me == "gpsimd" else 0,
                               [[1, 64], [1, 1]])), s_pay, 16)
        w8("vector", s_pay)
        vec(lambda v: v.tensor_tensor(ccol[:, 0:1], drv2[:], dprc[:], ALU.mult), s_dv)
        w8("vector", s_dv)
        vec(lambda v: v.tensor_scalar(ccol[:, 0:1], ccol[:, 0:1], prmf[0:64, 1:2],
                                      prmf[0:64, 0:1], ALU.mult, ALU.add), s_dv)
        w8("vector", s_dv)
        vec(lambda v: v.tensor_scalar(ccol[:, 1:2], drvc[:], prmf[0:64, 2:3], None,
                                      ALU.mult), s_dv)
        w8("vector", s_dv)
        vec(lambda v: v.scalar_tensor_tensor(ccol[:, 1:2], drv2[:],
                                             prmf[0:64, 3:4], ccol[:, 1:2],
                                             ALU.mult, ALU.add), s_dv)
        w8("vector", s_dv)
        vec(lambda v: v.tensor_copy(ccol[:, 2:3], prmf[0:64, 4:5]), s_dv)
        w8("vector", s_dv)
        w8("vector", s_pe[5], s_pe[5].v)
        for j in range(3):
            vec(lambda v, j=j: v.tensor_scalar(
                emsk[:, j * 64:(j + 1) * 64] if j < 2 else emsk2[:, 0:64],
                eye[0:64, 0:64], ccol[:, j:j + 1], None, ALU.mult), s_dv)
        pe_take(5)
        w8("tensor", s_dv)
        mm(psb[5][:, 0:H], ones64[:], emsk[:, 0:64], True, True)
        mm(psb[5][:, H:2 * H], ones64[:], emsk[:, 64:128], True, True)
        mm(psb[5][:, 2 * H:3 * H], ones64[:], emsk2[:, 0:64], True, True, s_pe[5])
        use("vector", 5)
        vec(lambda v: v.tensor_copy(drbc[:], psb[5][:, 0:H]))
        vec(lambda v: v.tensor_copy(dobc[:], psb[5][:, H:2 * H]))
        vec(lambda v: v.tensor_copy(drb3[:], psb[5][:, 2 * H:3 * H]), rel(5))
        vec(lambda v: v.tensor_tensor(
            y2b[:], sin[:], A(drbc, 0, [[H, 128], [1, 64], [0, 64]]), ALU.mult))
        vec(lambda v: v.tensor_tensor(
            sin[:], tpre[:], A(dobc, 0, [[H, 128], [1, 64], [0, 64]]), ALU.mult))
        vec(lambda v: v.tensor_tensor(y2b[:], y2b[:], sin[:], ALU.add))
        vec(lambda v: v.tensor_tensor(
            sin[:], ubuf[:], A(drb3, 0, [[H, 128], [1, 64], [0, 64]]), ALU.mult))
        vec(lambda v: v.tensor_tensor(y2b[:], y2b[:], sin[:], ALU.add))
        vec(lambda v: v.tensor_copy(sin[:], y2b[:]), s_dv)
        if debug:
            w8("sync", s_dv, s_dv.v)
            dma("sync", d_sin, sin[:], s_bo)
        barrier()
        if phases <= 3:
            w8("scalar", s_bo)
            w8("vector", s_bo)
            return

        gch_rel = [s_dv.v, s_dv.v]
        dma("sync", xtm[0][:], gate_h[0], s_gl)
        gch_ld = [s_gl.v, 0]
        yld = [0] * NCH
        dma("sync", A(xnar, 0, [[32 * NROW, 128], [1, NI]]), y_h[0], s_bi)
        yld[0] = s_bi.v
        for k in range(NCH):
            gb = k % 2
            if k + 1 < NCH:
                w8("sync", s_dv, gch_rel[(k + 1) % 2])
                dma("sync", xtm[(k + 1) % 2][:], gate_h[k + 1], s_gl)
                gch_ld[(k + 1) % 2] = s_gl.v
                dma("sync", A(xnar, (k + 1) * NI, [[32 * NROW, 128], [1, NI]]),
                    y_h[k + 1], s_bi)
                yld[k + 1] = s_bi.v
            w8("vector", s_bi, yld[k])
            w8("tensor", s_bi, yld[k])
            w8("sync", s_pe[0], s_pe[0].v)
            w8("sync", s_pe[1], s_pe[1].v)
            dma("sync", ccmc[:],
                A(ccm_h.tensor, k * 128, [[NTOK, 128], [128 * NTOK, 8], [1, 128]]),
                s_bi)
            w8("scalar", s_dv, s_dv.v)
            sca(lambda s, k=k: s.activation(lsg[:], l_tm[:, k * 64:(k + 1) * 64],
                                            AF.Exp), s_ac)
            w8("tensor", s_ac)
            w8("tensor", s_bi)
            for wv in range(8):
                bank = wv % 2
                pe_take(bank)
                for hh in range(8):
                    mm(psb[bank][:, hh * 64:(hh + 1) * 64],
                       ccmc[:, wv * 128:(wv + 1) * 128],
                       sin[:, (wv * 8 + hh) * 64:(wv * 8 + hh + 1) * 64],
                       True, True, s_pe[bank] if hh == 7 else None)
                use("vector", bank)
                vec(lambda v, wv=wv, bank=bank: v.tensor_tensor(
                    y2b[:, EV0:EV0 + 512], psb[bank][:],
                    A(lsg, wv * 8, [[H, 128], [1, 8], [0, 64]]), ALU.mult), rel(bank))
                vec(lambda v, k=k, wv=wv: v.tensor_tensor(
                    A(xnar, k * NI + wv * 512, [[32 * NROW, 128], [1, 512]]),
                    A(xnar, k * NI + wv * 512, [[32 * NROW, 128], [1, 512]]),
                    y2b[:, EV0:EV0 + 512], ALU.add), s_dv)
            w8("scalar", s_dv)
            sca(lambda s, k=k: s.activation(
                y2b[:], A(xnar, k * NI, [[32 * NROW, 128], [1, NI]]), AF.Square),
                s_ac)
            w8("vector", s_ac)
            for g8 in range(8):
                vec(lambda v, g8=g8: v.reduce_sum(
                    ssb[:, g8:g8 + 1], y2b[:, g8 * 512:(g8 + 1) * 512], AX),
                    s_dv if g8 == 7 else None)
            w8("vector", s_dv)
            vec(lambda v: v.tensor_scalar(ssb[:, 0:8], ssb[:, 0:8], 1.0 / 512.0, EPS,
                                          ALU.mult, ALU.add), s_dv)
            w8("vector", s_dv)
            vec(lambda v: v.reciprocal(ssb[:, 8:16], ssb[:, 0:8]), s_dv)
            w8("scalar", s_dv)
            sca(lambda s: s.activation(ssb[:, 8:16], ssb[:, 8:16], AF.Sqrt), s_ac)
            w8("vector", s_ac)
            w8("vector", s_gl, gch_ld[gb])
            w8("vector", s_tp, s_tp.v)
            vec(lambda v, k=k: v.tensor_tensor(
                ynst[:], A(xnar, k * NI, [[32 * NROW, 128], [1, NI]]),
                A(ssb, 8, [[16, 128], [1, 8], [0, 512]]), ALU.mult))
            vec(lambda v, gb=gb: v.tensor_tensor(ynst[:], ynst[:], xtm[gb][:],
                                                 ALU.mult), s_dv)
            gch_rel[gb] = s_dv.v
            w8("sync", s_dv)
            dmat("sync", A(xnar, k * NI, [[32 * NROW, 128], [128, 32], [1, 128]]),
                 ynst[:], s_tp)
        barrier()

        slab_rel = [None, None]
        ost_rel = [s_so.v, s_so.v]
        ptl_rel = [s_dv.v, s_dv.v]
        for ot in range(16):
            b = ot % 2
            if slab_rel[b] is not None:
                w8("gpsimd", s_pe[slab_rel[b][0]], slab_rel[b][1])
            dma("gpsimd", wsl[b][:, 0:8192],
                A(w_op.tensor, ot * 128 * 8192, [[8192, 128], [1, 8192]]), s_w)
            for mi in range(8):
                bank = (ot * 8 + mi) % 2
                pe_take(bank)
                w8("tensor", s_w)
                for ic in range(32):
                    mm(psb[bank][:, 0:256],
                       A(xnar, mi * NI + ic * 128, [[32 * NROW, 128], [1, 128]]),
                       wsl[b][:, ic * 256:(ic + 1) * 256],
                       ic == 0, ic == 31, s_pe[bank] if ic == 31 else None)
                if mi == 7:
                    slab_rel[b] = (bank, s_pe[bank].v)
                pb = (ot * 8 + mi) % 2
                w8("scalar", s_dv, ptl_rel[pb])
                dma("scalar", ptl[pb],
                    A(out.tensor, mi * 128 * DM + ot * 256, [[DM, 128], [1, 256]]),
                    s_gl)
                ob = (ot * 8 + mi) % 2
                use("vector", bank)
                w8("vector", s_gl)
                w8("vector", s_so, ost_rel[ob])
                vec(lambda v, bank=bank, pb=pb, ob=ob: v.scalar_tensor_tensor(
                    ost[ob], psb[bank][:, 0:256], 1.0, ptl[pb],
                    ALU.mult, ALU.add), rel(bank))
                vec(lambda v: v.engine_nop(), s_dv)
                ptl_rel[pb] = s_dv.v
                w8("scalar", s_fr[bank], s_fr[bank].v)
                dma("scalar", A(out.tensor, mi * 128 * DM + ot * 256,
                                [[DM, 128], [1, 256]]), ost[ob], s_so)
                ost_rel[ob] = s_so.v
        w8("scalar", s_so)
        w8("vector", s_so)

    for en in ["sync", "gpsimd", "vector", "scalar", "tensor"]:
        getattr(block, en)(lambda e, en=en: emit(en))

    for g in reversed(ctxs):
        g.__exit__(None, None, None)

    return nc, {}


def kernel(**inputs):
    x = np.asarray(inputs["L0_attn_hidden_states"], dtype=np.float32)
    w_in_t = np.ascontiguousarray(np.asarray(inputs["L0_attn_in_proj_w"],
                                             dtype=np.float32).T)
    w_cm = np.ascontiguousarray(
        w_in_t[:, 4096:10240].reshape(32, 128, 48, 128)
        .transpose(2, 1, 0, 3).reshape(48 * 128, 4096))
    w_tm = np.ascontiguousarray(
        w_in_t[:, 0:4096].reshape(32, 128, 16, 256)
        .transpose(2, 1, 0, 3).reshape(16 * 128, 8192))
    w_dt = np.ascontiguousarray(
        w_in_t[:, 10240:10304].reshape(32, 128, 64)
        .transpose(1, 0, 2).reshape(128, 2048))
    conv_w = np.asarray(inputs["L0_attn_conv_weight"], dtype=np.float32)
    convw = np.ascontiguousarray(
        conv_w.reshape(48, 128, KC).transpose(1, 0, 2).reshape(128, 48 * KC))
    convb = np.ascontiguousarray(
        np.asarray(inputs["L0_attn_conv_bias"], dtype=np.float32).reshape(48, 128).T)
    dtb = np.asarray(inputs["L0_attn_dt_bias"], dtype=np.float32).reshape(1, H)
    alog = np.asarray(inputs["L0_attn_A_log"], dtype=np.float32).reshape(1, H)
    dvec = np.asarray(inputs["L0_attn_D"], dtype=np.float32).reshape(1, H)
    nw = np.asarray(inputs["L0_attn_norm_weight"], dtype=np.float32).reshape(1, DM)
    w_out_t = np.ascontiguousarray(np.asarray(inputs["L0_attn_out_proj_w"],
                                              dtype=np.float32).T)
    w_op = np.ascontiguousarray(
        w_out_t.reshape(32, 128, 16, 256).transpose(2, 1, 0, 3)
        .reshape(16 * 128, 8192))
    gate = np.asarray(inputs["L0_ffn_gate"], dtype=np.float32)
    ffn_k = np.ascontiguousarray(np.asarray(inputs["L0_ffn_kernel"],
                                            dtype=np.float32))
    ffn_kp = np.ascontiguousarray(
        ffn_k.reshape(32, 128, 16, 256).transpose(2, 1, 0, 3)
        .reshape(16 * 128, 8192))
    bias = np.asarray(inputs["L0_ffn_bias"], dtype=np.float32).reshape(1, DM)
    lora_A = np.asarray(inputs["L0_ffn_lora_A"], dtype=np.float32)
    lora_B = np.asarray(inputs["L0_ffn_lora_B"], dtype=np.float32)
    lora_As = np.asarray(inputs["L0_ffn_lora_A_shared"], dtype=np.float32)
    lora_Bs = np.asarray(inputs["L0_ffn_lora_B_shared"], dtype=np.float32)

    a_cat = np.ascontiguousarray(np.concatenate(
        [lora_A.transpose(2, 0, 1).reshape(DM, 512), lora_As.T], axis=1))
    b_gat = np.ascontiguousarray(lora_B.transpose(2, 0, 1).reshape(32, 16 * DM))
    b_shd = np.ascontiguousarray(lora_Bs.T)

    eye = np.eye(128, dtype=np.float32)
    triu01 = np.triu(np.ones((128, 128), dtype=np.float32))
    strict = np.tril(np.ones((128, 128), dtype=np.float32), -1)

    n1, n2 = [int(v) for v in np.random.randint(1, 2 ** 30, size=2)]
    nonce = np.array([[n1] * 4 + [n2] * 4], dtype=np.int32)

    globals()['LN_SERIAL_G'] = LN_SERIAL
    globals()['WSL_TEST_G'] = WSL_TEST
    globals()['LN_NOSTATS_G'] = LN_NOSTATS
    nc, _ = build_program(phases=PHASES, debug=DEBUG)

    in_maps = []
    for c in range(8):
        bb, r = c // 4, c % 4
        q = c % 2
        own0 = r * 1024
        xo = np.zeros((NROW, DM), dtype=np.float32)
        h0 = max(own0 - HALO, 0)
        xo[HALO - (own0 - h0):HALO] = x[bb, h0:own0]
        xo[HALO:HALO + 1024] = x[bb, own0:own0 + 1024]
        xp = np.zeros((NROW, DM), dtype=np.float32)
        if r >= 2:
            pre0 = (r - 2) * 1024
            ph0 = max(pre0 - HALO, 0)
            xp[HALO - (pre0 - ph0):HALO] = x[bb, ph0:pre0]
            xp[HALO:HALO + 1024] = x[bb, pre0:pre0 + 1024]
        prmf = np.zeros((128, 8), dtype=np.float32)
        prmf[:, 0] = 1.0 if r == 2 else 0.0
        prmf[:, 1] = 0.0 if r == 0 else 1.0
        s1, s2 = q * 2 + 0, q * 2 + 1
        rkind = 0 if r in (0, 2) else 1
        rslot = (1 - q) * 2 + rkind
        prmi = np.zeros((1, 16), dtype=np.int32)
        prmi[0, 0:3] = [s1 * 128 * NI, s1 * H, s1 * 4]
        prmi[0, 3:6] = [s2 * 128 * NI, s2 * H, s2 * 4]
        prmi[0, 6:9] = [rslot * 128 * NI, rslot * H, rslot * 4]
        prmi[0, 9] = n1 if rkind == 0 else n2
        in_maps.append({
            "x_own": xo, "x_pre": xp, "w_cm": w_cm, "w_tm": w_tm, "w_dt": w_dt,
            "w_op": w_op, "ffn_kp": ffn_kp,
            "a_cat": a_cat, "b_gat": b_gat, "b_shd": b_shd,
            "bias_r": bias, "nw_r": nw, "dvec_r": dvec, "alog_r": alog,
            "dtb_r": dtb, "convw": convw, "convb": convb,
            "gate_r": np.ascontiguousarray(gate[4 * bb + r].reshape(1, 16)),
            "eye_i": eye, "triu_i": triu01, "stri_i": strict,
            "prm_f": prmf, "prm_i": prmi, "nonce_r": nonce,
        })

    res = run_bass_kernel_spmd(nc, in_maps, core_ids=list(range(8)))
    outf = np.empty((2, 4096, DM), dtype=np.float32)
    for c in range(8):
        bb, r = c // 4, c % 4
        outf[bb, r * 1024:(r + 1) * 1024] = res.results[c]["out"]
    return outf


# version 10
# speedup vs baseline: 1.7560x; 1.0235x over previous
import numpy as np
import ml_dtypes
import concourse.bass as bass
import concourse.mybir as mybir
from concourse.bass_utils import run_bass_kernel_spmd

F32 = mybir.dt.float32
BF16 = mybir.dt.bfloat16
I32 = mybir.dt.int32
AF = mybir.ActivationFunctionType
ALU = mybir.AluOpType
AX = mybir.AxisListType.X
A = bass.AP

DM = 4096
H, P, NS, G, KC = 64, 64, 128, 8, 4
NI = 4096
GN = G * NS
CONV = NI + 2 * GN
PROJ = NI + CONV + H
EPS = 1e-5
Q = 128
NCH = 8
NTOK = Q * NCH
HALO = 4
NROW = 1152
TT = (0, 512, 1024)
TW = (512, 512, HALO)
COL_GATE, COL_X, COL_B, COL_C, COL_DT = 0, NI, 2 * NI, 2 * NI + GN, PROJ - H
PHASES = 9
DEBUG = False
LN_SERIAL = False
WSL_TEST = False
LN_NOSTATS = False
RH0, EV0, SPA, SPB = 0, 1024, 1536, 2048


def build_program(phases=9, debug=False):
    LN_SERIAL = globals().get('LN_SERIAL_G', False)
    WSL_TEST = globals().get('WSL_TEST_G', False)
    LN_NOSTATS = globals().get('LN_NOSTATS_G', False)
    nc = bass.Bass("TRN2", target_bir_lowering=False,
                   detect_race_conditions=False)

    x_own = nc.dram_tensor("x_own", [NROW, DM], F32, kind="ExternalInput").ap()
    x_pre = nc.dram_tensor("x_pre", [NROW, DM], F32, kind="ExternalInput").ap()
    w_cm = nc.dram_tensor("w_cm", [48 * 128, 4096], F32, kind="ExternalInput").ap()
    w_tm = nc.dram_tensor("w_tm", [16 * 128, 8192], F32, kind="ExternalInput").ap()
    w_dt = nc.dram_tensor("w_dt", [128, 2048], F32, kind="ExternalInput").ap()
    w_op = nc.dram_tensor("w_op", [16 * 128, 8192], F32, kind="ExternalInput").ap()
    ffn_kp = nc.dram_tensor("ffn_kp", [16 * 128, 8192], F32,
                            kind="ExternalInput").ap()
    a_cat = nc.dram_tensor("a_cat", [DM, 544], F32, kind="ExternalInput").ap()
    b_gat = nc.dram_tensor("b_gat", [32, 16 * DM], F32, kind="ExternalInput").ap()
    b_shd = nc.dram_tensor("b_shd", [32, DM], F32, kind="ExternalInput").ap()
    bias_r = nc.dram_tensor("bias_r", [1, DM], F32, kind="ExternalInput").ap()
    nw_r = nc.dram_tensor("nw_r", [1, DM], F32, kind="ExternalInput").ap()
    dvec_r = nc.dram_tensor("dvec_r", [1, H], F32, kind="ExternalInput").ap()
    alog_r = nc.dram_tensor("alog_r", [1, H], F32, kind="ExternalInput").ap()
    dtb_r = nc.dram_tensor("dtb_r", [1, H], F32, kind="ExternalInput").ap()
    convw = nc.dram_tensor("convw", [128, 48 * KC], F32, kind="ExternalInput").ap()
    convb = nc.dram_tensor("convb", [128, 48], F32, kind="ExternalInput").ap()
    gate_r = nc.dram_tensor("gate_r", [1, 16], F32, kind="ExternalInput").ap()
    eye_i = nc.dram_tensor("eye_i", [128, 128], F32, kind="ExternalInput").ap()
    triu_i = nc.dram_tensor("triu_i", [128, 128], F32, kind="ExternalInput").ap()
    stri_i = nc.dram_tensor("stri_i", [128, 128], F32, kind="ExternalInput").ap()
    prm_f = nc.dram_tensor("prm_f", [128, 8], F32, kind="ExternalInput").ap()
    prm_i = nc.dram_tensor("prm_i", [1, 16], I32, kind="ExternalInput").ap()
    nonce_r = nc.dram_tensor("nonce_r", [1, 8], I32, kind="ExternalInput").ap()

    out = nc.dram_tensor("out", [NTOK, DM], F32, kind="ExternalOutput").ap()

    pay_T = nc.dram_tensor("pay_T", [4, 128, NI], BF16, kind="Internal",
                           addr_space="Shared").ap()
    pay_D = nc.dram_tensor("pay_D", [4, H, 1], F32, kind="Internal",
                           addr_space="Shared").ap()
    pay_F = nc.dram_tensor("pay_F", [4, 1, 4], I32, kind="Internal",
                           addr_space="Shared").ap()
    _ik = "ExternalOutput" if debug else "Internal"
    xtm_h = nc.dram_tensor("xtm_h", [NCH, 128, NI], BF16, kind=_ik).ap()
    btm_h = nc.dram_tensor("btm_h", [NCH, 128, GN], BF16, kind=_ik).ap()
    bcm_h = nc.dram_tensor("bcm_h", [G, 128, NTOK], BF16, kind=_ik).ap()
    ccm_h = nc.dram_tensor("ccm_h", [G, 128, NTOK], BF16, kind=_ik).ap()
    gate_h = nc.dram_tensor("gate_h", [NCH, 128, DM], BF16, kind=_ik).ap()
    y_h = nc.dram_tensor("y_h", [NCH, 128, NI], BF16, kind=_ik).ap()
    a2_h = nc.dram_tensor("a2_h", [128, 2048], BF16, kind="Internal").ap()
    be_h = nc.dram_tensor("be_h", [32, DM], BF16, kind="Internal").ap()
    d_y = (nc.dram_tensor("d_y", [128, NCH * NI], BF16, kind="ExternalOutput").ap()
           if debug else None)
    d_sin = (nc.dram_tensor("d_sin", [128, NI], BF16, kind="ExternalOutput").ap()
             if debug else None)

    ctxs = []

    def sb(name, shape, dtype):
        g = nc.sbuf_tensor(name, shape, dtype)
        t = g.__enter__()
        ctxs.append(g)
        return t

    def psalloc(name):
        g = nc.psum_tensor(name, [128, 512], F32)
        t = g.__enter__()
        ctxs.append(g)
        return t

    eye = sb("eye", [128, 128], F32)
    triu = sb("triu", [128, 128], F32)
    stri = sb("stri", [128, 128], F32)
    ones1 = sb("ones1", [1, 128], F32)
    ones64 = sb("ones64", [64, 128], F32)
    emsk = sb("emsk", [64, 128], F32)
    zrow = sb("zrow", [1, H], F32)
    zcol = sb("zcol", [64, 1], F32)
    dprc = sb("dprc", [64, 1], F32)
    dwnc = sb("dwnc", [64, 1], F32)
    drvc = sb("drvc", [64, 1], F32)
    rowb = sb("rowb", [1, 4 * H], F32)
    rowb2 = sb("rowb2", [1, 256], F32)
    biasl = sb("biasl", [1, 256], BF16)
    ones1b = sb("ones1b", [1, 128], BF16)
    abc = sb("abc", [128, H], F32)
    dtbbc = sb("dtbbc", [128, H], F32)
    dbc = sb("dbc", [128, H], F32)
    nwt = sb("nwt", [128, 256], BF16)
    cw = sb("cw", [128, 48 * KC], F32)
    cb = sb("cb", [128, 48], F32)
    gcol = sb("gcol", [128, 16], F32)
    prmf = sb("prmf", [128, 8], F32)
    prmi = sb("prmi", [1, 16], I32)
    flagsb = sb("flagsb", [1, 4], I32)
    noncesb = sb("noncesb", [1, 8], I32)
    drbc = sb("drbc", [128, H], F32)
    drb3 = sb("drb3", [128, H], F32)
    drv2 = sb("drv2", [64, 1], F32)
    ccol = sb("ccol", [64, 4], F32)
    emsk2 = sb("emsk2", [64, 64], F32)
    dobc = sb("dobc", [128, H], F32)

    xnar = sb("xnar", [128, 32 * NROW], BF16)
    wsl0 = sb("wsl0", [128, 8192], BF16)
    strip = sb("strip", [128, NTOK + HALO], F32)
    cacc = sb("cacc", [128, NTOK], F32)
    xbcbf = sb("xbcbf", [128, NTOK], BF16)
    tpstg = sb("tpstg", [128, NTOK], BF16)
    ccmc = sb("ccmc", [128, GN], BF16)
    bcmc = sb("bcmc", [128, GN], BF16)
    btmc = sb("btmc", [128, GN], BF16)
    dtp = sb("dtp", [128, 512], F32)
    a_tm = sb("a_tm", [128, 512], F32)
    l_tm = sb("l_tm", [128, 512], F32)
    wexp = sb("wexp", [128, H], F32)
    lamtm = sb("lamtm", [128, H], F32)
    lambc = sb("lambc", [128, H], F32)
    lsg = sb("lsg", [128, H], F32)
    qtm = sb("qtm", [128, H], F32)
    S = sb("S", [128, NI], F32)
    sbfw = sb("sbfw", [128, 512], BF16)
    tpre = sb("tpre", [128, NI], BF16)
    sin = sb("sin", [128, NI], BF16)
    xtm = [sb("xtm0", [128, NI], BF16), sb("xtm1", [128, NI], BF16)]
    ubuf = sb("ubuf", [128, NI], BF16)
    y2b = sb("y2b", [128, DM], F32)
    ssb = sb("ssb", [128, 16], F32)
    gld = [sb("gld0", [128, 512], BF16), sb("gld1", [128, 512], BF16)]
    ynst = sb("ynst", [128, DM], BF16)
    lnst = sb("lnst", [128, 8], F32)

    ystg = gld
    wsl = [wsl0, wsl0 if WSL_TEST else S.bitcast(BF16)]
    ptl = [y2b[:, 3072:3328], y2b[:, 3328:3584]]
    ost = [y2b[:, 3584:3840], y2b[:, 3840:4096]]
    gmsb = ynst
    lpw = ynst
    mpw = ynst
    udw = ynst
    xdw = ynst

    xin = S
    dtraw = l_tm

    psb = [psalloc(f"psb{i}") for i in range(8)]

    class Sem:
        __slots__ = ("h", "v")

        def __init__(self, h):
            self.h = h
            self.v = 0

    def mksem(name):
        g = nc.semaphore(name)
        h = g.__enter__()
        ctxs.append(g)
        return Sem(h)

    s_x = mksem("s_x")
    s_w = mksem("s_w")
    s_tp = mksem("s_tp")
    s_bi = mksem("s_bi")
    s_bo = mksem("s_bo")
    s_gl = mksem("s_gl")
    s_so = mksem("s_so")
    s_pay = mksem("s_pay")
    s_rcv = mksem("s_rcv")
    s_dv = mksem("s_dv")
    s_ac = mksem("s_ac")
    s_gp = mksem("s_gp")
    s_tb = mksem("s_tb")
    s_ab = mksem("s_ab")
    s_ph = mksem("s_ph")
    s_pe = [mksem(f"s_pe{i}") for i in range(8)]
    s_fr = [mksem(f"s_fr{i}") for i in range(8)]
    DMAS = [s_x, s_w, s_tp, s_bi, s_bo, s_gl, s_so, s_pay]
    ALLSEMS = DMAS + [s_rcv, s_dv, s_ac, s_gp, s_tb, s_ab, s_ph] + s_pe + s_fr

    blkg = nc.Block()
    block = blkg.__enter__()
    ctxs.append(blkg)

    def emit(me):
        for _s in ALLSEMS:
            _s.v = 0
        E = {"sync": nc.sync, "gpsimd": nc.gpsimd, "vector": nc.vector,
             "scalar": nc.scalar, "tensor": nc.tensor}
        eng = E[me]
        regs = {}
        if me == "gpsimd":
            for rn in ["s1T", "s1D", "s1F", "s2T", "s2D", "s2F", "rT", "rD",
                       "rF", "non", "r2T", "r2D", "nd", "cnt", "flg"]:
                g = eng.register("r_" + rn)
                regs[rn] = g.__enter__()
                ctxs.append(g)

        def w8(en, sem, v=None):
            if me == en:
                eng.wait_ge(sem.h, sem.v if v is None else v)

        def dma(en, outap, inap, sem):
            sem.v += 16
            if me == en:
                eng.dma_start(out=outap, in_=inap).then_inc(sem.h, 16)
            return sem.v

        def dmat(en, outap, inap, sem):
            sem.v += 16
            if me == en:
                eng.dma_start_transpose(outap, inap).then_inc(sem.h, 16)
            return sem.v

        def mm(outap, lhsT, rhs, start, stop, sem=None):
            if sem is not None:
                sem.v += 1
            if me == "tensor":
                i = eng.matmul(outap, lhsT, rhs, start=start, stop=stop)
                if sem is not None:
                    i.then_inc(sem.h, 1)

        def tr(outap, inap, sem=None, ident=None):
            if sem is not None:
                sem.v += 1
            if me == "tensor":
                i = eng.transpose(outap, inap, eye[:] if ident is None else ident)
                if sem is not None:
                    i.then_inc(sem.h, 1)

        def vec(fn, sem=None):
            if sem is not None:
                sem.v += 1
            if me == "vector":
                i = fn(eng)
                if sem is not None:
                    i.then_inc(sem.h, 1)

        def sca(fn, sem=None):
            if sem is not None:
                sem.v += 1
            if me == "scalar":
                i = fn(eng)
                if sem is not None:
                    i.then_inc(sem.h, 1)

        def gp(fn, sem=None, inc=1):
            if sem is not None:
                sem.v += inc
            if me == "gpsimd":
                i = fn(eng)
                if sem is not None:
                    i.then_inc(sem.h, inc)

        def barrier():
            s_tb.v += 1
            if me == "tensor":
                eng.nop().then_inc(s_tb.h, 1)
            s_ab.v += 1
            if me == "scalar":
                eng.nop().then_inc(s_ab.h, 1)
            s_gp.v += 1
            if me == "gpsimd":
                eng.nop().then_inc(s_gp.h, 1)
            if me == "vector":
                for s in DMAS + [s_tb, s_ab, s_gp]:
                    eng.wait_ge(s.h, s.v)
            s_ph.v += 1
            if me == "vector":
                eng.nop().then_inc(s_ph.h, 1)
            if me != "vector":
                eng.wait_ge(s_ph.h, s_ph.v)

        def l_cm_col(j):
            return strip[0:64, j:j + 1]

        def pe_take(bank):
            w8("tensor", s_fr[bank])

        def use(en, bank):
            w8(en, s_pe[bank])

        def rel(bank):
            return s_fr[bank]

        dma("sync", eye[:], eye_i, s_x)
        dma("sync", triu[:], triu_i, s_x)
        dma("sync", stri[:], stri_i, s_x)
        dma("sync", cw[:], convw, s_x)
        dma("sync", cb[:], convb, s_x)
        dma("sync", prmf[:], prm_f, s_x)
        dma("sync", prmi[:], prm_i, s_x)
        dma("sync", noncesb[:], nonce_r, s_x)
        dma("sync", gcol[:, 0:16], A(gate_r.tensor, 0, [[0, 128], [1, 16]]), s_x)
        dma("sync", rowb[:, 0:H], alog_r, s_x)
        dma("sync", rowb[:, H:2 * H], dtb_r, s_x)
        dma("sync", rowb[:, 2 * H:3 * H], dvec_r, s_x)
        w8("vector", s_x)
        w8("scalar", s_x)
        w8("gpsimd", s_x)
        w8("tensor", s_x)
        vec(lambda v: v.memset(ones1[:], 1.0))
        vec(lambda v: v.memset(ones1b[:], 1.0))
        vec(lambda v: v.memset(ones64[:], 1.0))
        vec(lambda v: v.memset(zcol[:], 0.0))
        vec(lambda v: v.memset(zrow[:], 0.0), s_dv)
        sca(lambda s: s.activation(rowb[:, 3 * H:4 * H], rowb[:, 0:H], AF.Exp), s_ac)
        w8("vector", s_ac)
        vec(lambda v: v.tensor_scalar_mul(rowb[:, 3 * H:4 * H], rowb[:, 3 * H:4 * H],
                                          -1.0), s_dv)
        w8("tensor", s_dv)
        pe_take(0)
        mm(psb[0][:, 0:H], ones1[:], rowb[:, 3 * H:4 * H], True, True)
        mm(psb[0][:, H:2 * H], ones1[:], rowb[:, H:2 * H], True, True)
        mm(psb[0][:, 2 * H:3 * H], ones1[:], rowb[:, 2 * H:3 * H], True, True, s_pe[0])
        use("vector", 0)
        vec(lambda v: v.tensor_copy(abc[:], psb[0][:, 0:H]))
        vec(lambda v: v.tensor_copy(dtbbc[:], psb[0][:, H:2 * H]))
        vec(lambda v: v.tensor_copy(dbc[:], psb[0][:, 2 * H:3 * H]), rel(0))
        if me == "gpsimd":
            names = ["s1T", "s1D", "s1F", "s2T", "s2D", "s2F", "rT", "rD", "rF",
                     "non", "r2T", "r2D"]
            for j, rn in enumerate(names):
                eng.reg_load(regs[rn], prmi[0:1, j:j + 1])
        barrier()

        def ln_phase(x_dram):
            xbufs = [xin, y2b]
            ybufs = [ynst[:], wsl0[:, 0:4096]]
            ubufs = [ubuf[:], wsl0[:, 4096:8192]]
            udst = [A(ubuf, 0, [[NI, 128], [128, 32], [1, 128]]),
                    A(wsl0, 4096, [[8192, 128], [128, 32], [1, 128]])]
            xrel = [s_ac.v, s_ac.v]
            yrel = [s_tp.v, s_tp.v]
            urel = [s_dv.v, s_dv.v]
            pend = None
            for i in range(9):
                b = i % 2
                xb, yb = xbufs[b], ybufs[b]
                c0 = b * 4
                w8("gpsimd", s_ac, xrel[b])
                dma("gpsimd", xb[:], x_dram[i * 128:(i + 1) * 128, :], s_w)
                w8("vector", s_w)
                w8("vector", s_ac, xrel[b])
                for j in range(8):
                    vec(lambda v, j=j, b=b, xb=xb: v.bn_stats(
                        strip[:, b * 48 + j * 6:b * 48 + (j + 1) * 6],
                        xb[:, j * 512:(j + 1) * 512]), s_dv if j == 7 else None)
                w8("vector", s_dv)
                vec(lambda v, c0=c0, b=b: v.bn_aggr(
                    lnst[:, c0:c0 + 2], strip[:, b * 48:b * 48 + 48]), s_dv)
                w8("vector", s_dv)
                vec(lambda v, c0=c0: v.tensor_scalar_add(
                    lnst[:, c0 + 1:c0 + 2], lnst[:, c0 + 1:c0 + 2], EPS), s_dv)
                w8("scalar", s_dv)
                sca(lambda s, c0=c0: s.activation(
                    lnst[:, c0 + 2:c0 + 3], lnst[:, c0 + 1:c0 + 2], AF.Sqrt),
                    s_ac)
                w8("vector", s_ac)
                vec(lambda v, c0=c0: v.reciprocal(
                    lnst[:, c0 + 2:c0 + 3], lnst[:, c0 + 2:c0 + 3]), s_dv)
                w8("vector", s_dv)
                vec(lambda v, c0=c0: v.tensor_scalar(
                    lnst[:, c0 + 3:c0 + 4], lnst[:, c0:c0 + 1],
                    lnst[:, c0 + 2:c0 + 3], -1.0, ALU.mult, ALU.mult), s_dv)
                bias_done = s_dv.v
                if pend is not None:
                    pi, pb, ptp = pend
                    w8("vector", s_tp, ptp)
                    vec(lambda v, pi=pi, pb=pb: v.tensor_copy(
                        A(xnar, pi * 128, [[32 * NROW, 128], [NROW, 32],
                                           [1, 128]]), ubufs[pb]), s_dv)
                    urel[pb] = s_dv.v
                w8("scalar", s_dv, bias_done)
                w8("scalar", s_tp, yrel[b])
                sca(lambda s, xb=xb, yb=yb, c0=c0: s.activation(
                    yb, xb[:], AF.Identity,
                    bias=lnst[:, c0 + 3:c0 + 4], scale=lnst[:, c0 + 2:c0 + 3]),
                    s_ac)
                xrel[b] = s_ac.v
                w8("sync", s_ac)
                w8("sync", s_dv, urel[b])
                dmat("sync", udst[b], yb, s_tp)
                yrel[b] = s_tp.v
                pend = (i, b, s_tp.v)
            pi, pb, ptp = pend
            w8("vector", s_tp, ptp)
            vec(lambda v, pi=pi, pb=pb: v.tensor_copy(
                A(xnar, pi * 128, [[32 * NROW, 128], [NROW, 32], [1, 128]]),
                ubufs[pb]), s_dv)

        def conv_silu(cslot):
            vec(lambda v, cslot=cslot: v.tensor_scalar(
                cacc[:], strip[:, 1:1 + NTOK], cw[:, cslot * KC:cslot * KC + 1],
                None, ALU.mult))
            for k in range(1, KC):
                vec(lambda v, k=k, cslot=cslot: v.scalar_tensor_tensor(
                    cacc[:], strip[:, 1 + k:1 + k + NTOK],
                    cw[:, cslot * KC + k:cslot * KC + k + 1], cacc[:],
                    ALU.mult, ALU.add))
            vec(lambda v: v.engine_nop(), s_dv)
            w8("scalar", s_dv)
            sca(lambda s, cslot=cslot: s.activation(
                xbcbf[:], cacc[:], AF.Silu, bias=cb[:, cslot:cslot + 1]), s_ac)

        def cm_group(plan):
            slab_rel = [None, None]
            cm_group.slab_rel = slab_rel
            strip_rel = 0
            xbc_guard = (s_dv, 0)
            for idx, (kind, cc) in enumerate(plan):
                b = idx % 2
                j = {"x": 0, "bo": 32, "bp": 32, "c": 40}[kind] + cc
                if slab_rel[b] is not None:
                    w8("gpsimd", s_pe[slab_rel[b][0]], slab_rel[b][1])
                dma("gpsimd", wsl[b][:, 0:4096],
                    A(w_cm.tensor, j * 128 * 4096, [[4096, 128], [1, 4096]]),
                    s_w)
                for ti in range(3):
                    bank = (idx * 3 + ti) % 2
                    pe_take(bank)
                    w8("tensor", s_w)
                    for ic in range(32):
                        mm(psb[bank][:, 0:TW[ti]], wsl[b][:, ic * 128:(ic + 1) * 128],
                           A(xnar, ic * NROW + TT[ti], [[32 * NROW, 128], [1, TW[ti]]]),
                           ic == 0, ic == 31, s_pe[bank] if ic == 31 else None)
                    if ti == 2:
                        slab_rel[b] = (bank, s_pe[bank].v)
                    use("scalar", bank)
                    w8("scalar", s_dv, strip_rel)
                    sca(lambda s, bank=bank, ti=ti: s.copy(
                        strip[:, TT[ti]:TT[ti] + TW[ti]], psb[bank][:, 0:TW[ti]]),
                        rel(bank))
                w8("vector", s_fr[0], s_fr[0].v)
                w8("vector", s_fr[1], s_fr[1].v)
                w8("scalar", xbc_guard[0], xbc_guard[1])
                conv_silu({"x": cc, "bo": 32 + cc, "bp": 32 + cc, "c": 40 + cc}[kind])
                strip_rel = s_dv.v
                if kind in ("x", "bp"):
                    w8("sync", s_ac)
                    dmat("sync", A(tpstg, 0, [[NTOK, 128], [128, 8], [1, 128]]),
                         xbcbf[:], s_tp)
                    xbc_guard = (s_tp, s_tp.v)
                    w8("sync", s_tp)
                    if kind == "x":
                        dma("sync", A(xtm_h.tensor, cc * 128,
                                      [[NI, 128], [128 * NI, 8], [1, 128]]), tpstg[:],
                            s_bo)
                    else:
                        dma("sync", A(btm_h.tensor, cc * 128,
                                      [[GN, 128], [128 * GN, 8], [1, 128]]), tpstg[:],
                            s_bo)
                else:
                    w8("sync", s_ac)
                    hb = bcm_h if kind == "bo" else ccm_h
                    dma("sync", hb[cc], xbcbf[:], s_bo)
                    xbc_guard = (s_bo, s_bo.v)

        def tm_group(own):
            tiles = ([("g", COL_GATE + i * 256, 256) for i in range(16)] if own
                     else [])
            tiles.append(("d", COL_DT, 64))
            slab_rel = list(getattr(cm_group, "slab_rel", [None, None]))
            gld_rel = [s_bo.v, s_bo.v]
            for tj, (kind, col, wid) in enumerate(tiles):
                b = tj % 2
                if slab_rel[b] is not None:
                    w8("gpsimd", s_pe[slab_rel[b][0]], slab_rel[b][1])
                wsrc = (A(w_tm.tensor, (col // 256) * 128 * 8192,
                          [[8192, 128], [1, 8192]]) if kind == "g"
                        else A(w_dt.tensor, 0, [[2048, 128], [1, 2048]]))
                dma("gpsimd", wsl[b][:, 0:32 * wid], wsrc, s_w)
                if kind == "g":
                    bank5 = 5
                    pe_take(bank5)
                    w8("tensor", s_dv, s_dv.v)
                    coff = col - COL_GATE
                    w8("sync", s_pe[bank5], s_pe[bank5].v)
                    dma("sync", rowb2[:], A(nw_r.tensor, coff, [[256, 1], [1, 256]]),
                        s_x)
                    w8("tensor", s_x)
                    mm(psb[bank5][:, 0:256], ones1[:], rowb2[:], True, True,
                       s_pe[bank5])
                    use("vector", bank5)
                    vec(lambda v, bank5=bank5: v.tensor_copy(
                        nwt[:], psb[bank5][:, 0:256]), rel(bank5))
                    vec(lambda v: v.engine_nop(), s_dv)
                for mi in range(8):
                    bank = (tj * 8 + mi) % 2
                    pe_take(bank)
                    w8("tensor", s_w)
                    for ic in range(32):
                        mm(psb[bank][:, 0:wid],
                           A(xnar, ic * NROW + HALO + mi * 128,
                             [[32 * NROW, 128], [1, 128]]),
                           wsl[b][:, ic * wid:(ic + 1) * wid],
                           ic == 0, ic == 31, s_pe[bank] if ic == 31 else None)
                    if mi == 7:
                        slab_rel[b] = (bank, s_pe[bank].v)
                    if kind == "g":
                        gb = (tj * 8 + mi) % 2
                        use("scalar", bank)
                        sca(lambda s, bank=bank: s.activation(
                            ynst[:, 0:256], psb[bank][:, 0:256], AF.Silu), s_ac)
                        w8("vector", s_ac)
                        w8("vector", s_bo, gld_rel[gb])
                        w8("vector", s_dv, s_dv.v)
                        vec(lambda v, gb=gb: v.tensor_tensor(
                            gld[gb][:, 0:256], ynst[:, 0:256], nwt[:], ALU.mult),
                            rel(bank))
                        vec(lambda v: v.engine_nop(), s_dv)
                        w8("scalar", s_fr[bank], s_fr[bank].v)
                        coff = col - COL_GATE
                        dma("scalar",
                            A(gate_h.tensor, mi * 128 * DM + coff,
                              [[DM, 128], [1, 256]]), gld[gb][:, 0:256], s_bo)
                        gld_rel[gb] = s_bo.v
                    else:
                        use("vector", bank)
                        vec(lambda v, bank=bank, mi=mi: v.tensor_copy(
                            dtraw[:, mi * 64:(mi + 1) * 64], psb[bank][:, 0:64]),
                            rel(bank))

        def dt_chain():
            vec(lambda v: v.tensor_tensor(
                dtp[:], dtraw[:], A(dtbbc, 0, [[H, 128], [0, 8], [1, H]]), ALU.add))
            vec(lambda v: v.tensor_scalar_mul(y2b[:, SPA:SPA + 512], dtp[:], -1.0))
            vec(lambda v: v.tensor_tensor(y2b[:, SPA:SPA + 512], y2b[:, SPA:SPA + 512],
                                          dtp[:], ALU.min), s_dv)
            w8("scalar", s_dv)
            sca(lambda s: s.activation(y2b[:, SPB:SPB + 512], y2b[:, SPA:SPA + 512],
                                       AF.Exp), s_ac)
            sca(lambda s: s.activation(y2b[:, SPB:SPB + 512], y2b[:, SPB:SPB + 512],
                                       AF.Ln, bias=1.0), s_ac)
            w8("vector", s_ac)
            vec(lambda v: v.tensor_relu(y2b[:, SPA:SPA + 512], dtp[:]))
            vec(lambda v: v.tensor_tensor(dtp[:], y2b[:, SPA:SPA + 512],
                                          y2b[:, SPB:SPB + 512], ALU.add))
            vec(lambda v: v.tensor_tensor(
                a_tm[:], dtp[:], A(abc, 0, [[H, 128], [0, 8], [1, H]]), ALU.mult), s_dv)
            w8("tensor", s_dv)
            for k in range(8):
                bank = 4 + k % 2
                pe_take(bank)
                tr(psb[bank][0:64, 0:128], a_tm[:, k * 64:(k + 1) * 64], s_pe[bank])
                use("vector", bank)
                vec(lambda v, k=k, bank=bank: v.tensor_copy(
                    cacc[0:64, k * 128:(k + 1) * 128], psb[bank][0:64, 0:128]),
                    rel(bank))
            vec(lambda v: v.tensor_tensor_scan(
                strip[0:64, 0:NTOK], cacc[0:64, 0:NTOK],
                A(zcol, 0, [[1, 64], [0, NTOK]]), 0.0, ALU.add, ALU.add), s_dv)
            w8("tensor", s_dv)
            for k in range(8):
                bank = 4 + k % 2
                pe_take(bank)
                tr(psb[bank][:, 0:64], strip[0:64, k * 128:(k + 1) * 128],
                   s_pe[bank], ident=eye[0:64, 0:64])
                use("vector", bank)
                vec(lambda v, k=k, bank=bank: v.tensor_copy(
                    l_tm[:, k * 64:(k + 1) * 64], psb[bank][:, 0:64]), rel(bank))
            vec(lambda v: v.engine_nop(), s_dv)

        def chunk_decays(k):
            bank = 5
            w8("vector", s_pe[bank], s_pe[bank].v)
            vec(lambda v, k=k: v.tensor_scalar(
                emsk[:, 0:64], eye[0:64, 0:64],
                l_cm_col(k * 128 + 127), None, ALU.mult))
            prevcol = zcol[:, 0:1] if k == 0 else l_cm_col(k * 128 - 1)
            vec(lambda v, prevcol=prevcol: v.tensor_scalar(
                emsk[:, 64:128], eye[0:64, 0:64], prevcol, None, ALU.mult), s_dv)
            pe_take(bank)
            w8("tensor", s_dv)
            mm(psb[bank][:, 0:H], ones64[:], emsk[:, 0:64], True, True)
            mm(psb[bank][:, H:2 * H], ones64[:], emsk[:, 64:128], True, True,
               s_pe[bank])
            use("vector", bank)
            vec(lambda v, bank=bank, k=k: v.tensor_tensor(
                y2b[:, SPA:SPA + H], psb[bank][:, 0:H], l_tm[:, k * 64:(k + 1) * 64],
                ALU.subtract))
            vec(lambda v, bank=bank, k=k: v.tensor_tensor(
                y2b[:, SPA + H:SPA + 2 * H], l_tm[:, k * 64:(k + 1) * 64],
                psb[bank][:, H:2 * H], ALU.subtract))
            vec(lambda v: v.tensor_tensor(
                y2b[:, SPA + 2 * H:SPA + 3 * H], y2b[:, SPA:SPA + H],
                y2b[:, SPA + H:SPA + 2 * H], ALU.add), rel(bank))
            vec(lambda v: v.engine_nop(), s_dv)
            w8("scalar", s_dv)
            sca(lambda s: s.activation(wexp[:], y2b[:, SPA:SPA + H], AF.Exp))
            sca(lambda s: s.activation(lamtm[:], y2b[:, SPA + H:SPA + 2 * H], AF.Exp))
            sca(lambda s: s.activation(lambc[:], y2b[:, SPA + 2 * H:SPA + 3 * H],
                                       AF.Exp), s_ac)
            w8("vector", s_ac)
            vec(lambda v, k=k: v.tensor_tensor(
                qtm[:], dtp[:, k * 64:(k + 1) * 64], wexp[:], ALU.mult), s_dv)

        def scan_phase(own):
            vec(lambda v: v.memset(S[:], 0.0), s_dv)
            dt_chain()
            ystg_rel = [s_bo.v, s_bo.v]
            xtm_rel = [s_dv.v, s_dv.v]
            lpw_rel = s_dv.v
            btm_rel = s_pe[4].v
            bc_rel = s_pe[3].v
            dma("sync", xtm[0][:], xtm_h[0], s_bi)
            xtm_ld = [s_bi.v, 0]
            for k in range(NCH):
                bx = k % 2
                if k + 1 < NCH:
                    w8("sync", s_dv, xtm_rel[(k + 1) % 2])
                    dma("sync", xtm[(k + 1) % 2][:], xtm_h[k + 1], s_bi)
                    xtm_ld[(k + 1) % 2] = s_bi.v
                chunk_decays(k)
                w8("sync", s_pe[4], btm_rel)
                if own:
                    w8("sync", s_pe[3], bc_rel)
                    dma("sync", bcmc[:],
                        A(bcm_h.tensor, k * 128,
                          [[NTOK, 128], [128 * NTOK, 8], [1, 128]]), s_bi)
                    dma("sync", ccmc[:],
                        A(ccm_h.tensor, k * 128,
                          [[NTOK, 128], [128 * NTOK, 8], [1, 128]]), s_bi)
                    w8("sync", s_bi)
                    for g8 in range(8):
                        dmat("sync", A(btmc, g8 * 128, [[GN, 128], [1, 128]]),
                             bcmc[:, g8 * 128:(g8 + 1) * 128], s_tp)
                    btm_ld = (s_tp, s_tp.v)
                    ccm_ld = s_bi.v
                else:
                    dma("sync", btmc[:], btm_h[k], s_bi)
                    btm_ld = (s_bi, s_bi.v)
                w8("vector", s_bi, xtm_ld[bx])
                if own:
                    vec(lambda v, k=k, bx=bx: v.tensor_tensor(
                        ubuf[:], xtm[bx][:],
                        A(dtp, k * 64, [[512, 128], [1, 64], [0, 64]]), ALU.mult),
                        s_dv)
                    w8("tensor", s_dv)
                    w8("tensor", s_bi, ccm_ld)
                    for g8 in range(8):
                        bank = 2 + g8 // 4
                        if g8 % 4 == 0:
                            pe_take(bank)
                        mm(psb[bank][:, (g8 % 4) * 128:(g8 % 4) * 128 + 128],
                           bcmc[:, g8 * 128:(g8 + 1) * 128],
                           ccmc[:, g8 * 128:(g8 + 1) * 128],
                           True, True, s_pe[bank] if g8 % 4 == 3 else None)
                    use("vector", 2)
                    vec(lambda v: v.tensor_tensor(
                        ynst[:, 0:512], psb[2][:],
                        A(triu, 0, [[128, 128], [0, 4], [1, 128]]), ALU.mult), rel(2))
                    use("vector", 3)
                    vec(lambda v: v.tensor_tensor(
                        ynst[:, 512:1024], psb[3][:],
                        A(triu, 0, [[128, 128], [0, 4], [1, 128]]), ALU.mult), rel(3))
                    bc_rel = s_pe[3].v
                for wv in range(8):
                    h0 = wv * 8
                    if own:
                        w8("vector", s_pe[3], s_pe[3].v)
                        vec(lambda v, wv=wv: v.tensor_copy(
                            sbfw[:], S[:, wv * 512:(wv + 1) * 512]), s_dv)
                        w8("vector", s_pe[7], s_pe[7].v)
                        vec(lambda v, k=k, h0=h0: v.tensor_tensor(
                            y2b[:, RH0:RH0 + 1024],
                            A(a_tm, k * 64 + h0, [[512, 128], [1, 8], [0, 128]]),
                            A(triu, 0, [[128, 128], [0, 8], [1, 128]]), ALU.mult),
                            s_dv)
                        w8("tensor", s_dv)
                        pe_take(6)
                        mm(psb[6][:], stri[:], y2b[:, RH0:RH0 + 512], True, True,
                           s_pe[6])
                        pe_take(7)
                        mm(psb[7][:], stri[:], y2b[:, RH0 + 512:RH0 + 1024], True,
                           True, s_pe[7])
                        use("scalar", 6)
                        w8("scalar", s_dv, lpw_rel)
                        sca(lambda s: s.activation(ynst[:, 1024:1536], psb[6][:], AF.Exp),
                            rel(6))
                        use("scalar", 7)
                        sca(lambda s: s.activation(ynst[:, 1536:2048], psb[7][:],
                                                   AF.Exp), rel(7))
                        sca(lambda s: s.nop(), s_ac)
                        w8("vector", s_ac)
                        w8("vector", s_pe[2], s_pe[2].v)
                        vec(lambda v, wv=wv: v.tensor_tensor(
                            ynst[:, 2048:3072], ynst[:, 1024:2048],
                            A(ynst, wv * 128, [[DM, 128], [0, 8], [1, 128]]),
                            ALU.mult), s_dv)
                        lpw_rel = s_dv.v
                        w8("tensor", s_dv)
                        pe_take(2)
                        pe_take(3)
                        w8("tensor", btm_ld[0], btm_ld[1])
                        for hh in range(8):
                            mm(psb[2][:, hh * 64:(hh + 1) * 64],
                               ynst[:, 2048 + hh * 128:2048 + (hh + 1) * 128],
                               ubuf[:, (h0 + hh) * 64:(h0 + hh + 1) * 64],
                               True, True, s_pe[2] if hh == 7 else None)
                        for hh in range(8):
                            mm(psb[3][:, hh * 64:(hh + 1) * 64],
                               ccmc[:, wv * 128:(wv + 1) * 128],
                               sbfw[:, hh * 64:(hh + 1) * 64],
                               True, True, s_pe[3] if hh == 7 else None)
                    w8("vector", s_pe[4], s_pe[4].v)
                    vec(lambda v, wv=wv, bx=bx, h0=h0: v.tensor_tensor(
                        ynst[:, 3072:3584], xtm[bx][:, wv * 512:(wv + 1) * 512],
                        A(qtm, h0, [[H, 128], [1, 8], [0, 64]]), ALU.mult), s_dv)
                    w8("tensor", s_dv)
                    if not own:
                        w8("tensor", btm_ld[0], btm_ld[1])
                    pe_take(4)
                    for hh in range(8):
                        mm(psb[4][:, hh * 64:(hh + 1) * 64],
                           btmc[:, wv * 128:(wv + 1) * 128],
                           ynst[:, 3072 + hh * 64:3072 + (hh + 1) * 64],
                           True, True, s_pe[4] if hh == 7 else None)
                    if wv == 7:
                        btm_rel = s_pe[4].v
                    if own:
                        vec(lambda v, wv=wv, bx=bx, h0=h0: v.tensor_tensor(
                            ynst[:, 3584:4096], xtm[bx][:, wv * 512:(wv + 1) * 512],
                            A(dbc, h0, [[H, 128], [1, 8], [0, 64]]), ALU.mult))
                        use("vector", 3)
                        vec(lambda v, h0=h0: v.tensor_tensor(
                            y2b[:, EV0:EV0 + 512], psb[3][:],
                            A(lamtm, h0, [[H, 128], [1, 8], [0, 64]]), ALU.mult),
                            rel(3))
                        use("vector", 2)
                        vec(lambda v: v.scalar_tensor_tensor(
                            y2b[:, EV0:EV0 + 512], psb[2][:], 1.0,
                            y2b[:, EV0:EV0 + 512], ALU.mult, ALU.add), rel(2))
                        yb = (k * 8 + wv) % 2
                        w8("vector", s_bo, ystg_rel[yb])
                        vec(lambda v, yb=yb: v.tensor_tensor(
                            ystg[yb][:], y2b[:, EV0:EV0 + 512],
                            ynst[:, 3584:4096], ALU.add), s_dv)
                        w8("sync", s_dv)
                        dma("sync", A(y_h.tensor, k * 128 * NI + wv * 512,
                                      [[NI, 128], [1, 512]]), ystg[yb][:], s_bo)
                        ystg_rel[yb] = s_bo.v
                    use("vector", 4)
                    vec(lambda v, wv=wv, h0=h0: v.tensor_tensor(
                        S[:, wv * 512:(wv + 1) * 512], S[:, wv * 512:(wv + 1) * 512],
                        A(lambc, h0, [[H, 128], [1, 8], [0, 64]]), ALU.mult))
                    vec(lambda v, wv=wv: v.tensor_tensor(
                        S[:, wv * 512:(wv + 1) * 512], S[:, wv * 512:(wv + 1) * 512],
                        psb[4][:], ALU.add), rel(4))
                    vec(lambda v: v.engine_nop(), s_dv)
                    if wv == 7:
                        xtm_rel[bx] = s_dv.v

        ln_phase(x_pre)
        barrier()
        plan_pre = [("x", cc) for cc in range(32)] + [("bp", cc) for cc in range(8)]
        cm_group(plan_pre)
        tm_group(own=False)
        ac_rel = s_dv.v
        for icg in range(8):
            w8("sync", s_dv, ac_rel)
            dma("sync", y2b[:, 0:2176],
                A(a_cat.tensor, icg * 4 * 128 * 544,
                  [[544, 128], [128 * 544, 4], [1, 544]]), s_x)
            w8("vector", s_x)
            for n in range(16):
                for j in range(4):
                    ic = icg * 4 + j
                    if n == 0:
                        vec(lambda v, ic=ic, j=j: v.tensor_scalar(
                            ubuf[:, ic * 64:ic * 64 + 32],
                            y2b[:, j * 544:j * 544 + 32],
                            gcol[:, 0:1], None, ALU.mult))
                    else:
                        vec(lambda v, ic=ic, j=j, n=n: v.scalar_tensor_tensor(
                            ubuf[:, ic * 64:ic * 64 + 32],
                            y2b[:, j * 544 + n * 32:j * 544 + n * 32 + 32],
                            gcol[:, n:n + 1], ubuf[:, ic * 64:ic * 64 + 32],
                            ALU.mult, ALU.add))
            for j in range(4):
                ic = icg * 4 + j
                vec(lambda v, ic=ic, j=j: v.tensor_copy(
                    ubuf[:, ic * 64 + 32:ic * 64 + 64],
                    y2b[:, j * 544 + 512:j * 544 + 544]),
                    s_dv if j == 3 else None)
            ac_rel = s_dv.v
        bg_rel = [s_dv.v, s_dv.v]
        for n in range(16):
            w8("sync", s_dv, bg_rel[0])
            dma("sync", y2b[0:32, 0:DM], b_gat[:, n * DM:(n + 1) * DM], s_x)
            w8("vector", s_x)
            if n == 0:
                vec(lambda v: v.tensor_scalar(xtm[0][0:32, 0:DM], y2b[0:32, 0:DM],
                                              gcol[0:32, 0:1], None, ALU.mult), s_dv)
            else:
                vec(lambda v, n=n: v.scalar_tensor_tensor(
                    xtm[0][0:32, 0:DM], y2b[0:32, 0:DM],
                    gcol[0:32, n:n + 1], xtm[0][0:32, 0:DM], ALU.mult, ALU.add),
                    s_dv)
            bg_rel[0] = s_dv.v
        vec(lambda v: v.engine_nop(), s_dv)
        w8("sync", s_dv)
        dma("sync", a2_h, ubuf[:, 0:2048], s_bo)
        dma("sync", be_h, xtm[0][0:32, 0:DM], s_bo)
        barrier()
        scan_phase(own=False)
        vec(lambda v: v.tensor_copy(tpre[:], S[:]), s_dv)
        w8("scalar", s_dv)
        sca(lambda s: s.activation(dprc[:], l_cm_col(1023), AF.Exp), s_ac)
        w8("gpsimd", s_dv)
        w8("gpsimd", s_ac)
        gp(lambda g: g.dma_start(
            out=A(pay_T.tensor, regs["s1T"] if me == "gpsimd" else 0,
                  [[NI, 128], [1, NI]]), in_=tpre[:]), s_pay, 16)
        gp(lambda g: g.dma_start(
            out=A(pay_D.tensor, regs["s1D"] if me == "gpsimd" else 0,
                  [[1, 64], [1, 1]]), in_=dprc[:]), s_pay, 16)
        w8("gpsimd", s_pay)
        gp(lambda g: g.dma_start(
            out=A(pay_F.tensor, regs["s1F"] if me == "gpsimd" else 0,
                  [[4, 1], [1, 4]]), in_=noncesb[:, 0:4]), s_pay, 16)
        barrier()
        if phases <= 0:
            w8("scalar", s_pay)
            return
        ln_phase(x_own)
        barrier()

        plan_own = ([("x", cc) for cc in range(32)] + [("bo", cc) for cc in range(8)]
                    + [("c", cc) for cc in range(8)])
        cm_group(plan_own)
        tm_group(own=True)
        barrier()
        scan_phase(own=True)
        w8("scalar", s_dv, s_dv.v)
        sca(lambda s: s.activation(dwnc[:], l_cm_col(1023), AF.Exp), s_ac)
        w8("vector", s_pe[2], s_pe[2].v)
        w8("vector", s_pe[3], s_pe[3].v)
        w8("vector", s_pe[4], s_pe[4].v)
        vec(lambda v: v.tensor_copy(ubuf[:], S[:]), s_dv)
        w8("gpsimd", s_dv)
        w8("gpsimd", s_ac)
        gp(lambda g: g.dma_start(
            out=A(pay_T.tensor, regs["s2T"] if me == "gpsimd" else 0,
                  [[NI, 128], [1, NI]]), in_=ubuf[:]), s_pay, 16)
        gp(lambda g: g.dma_start(
            out=A(pay_D.tensor, regs["s2D"] if me == "gpsimd" else 0,
                  [[1, 64], [1, 1]]), in_=dwnc[:]), s_pay, 16)
        w8("gpsimd", s_pay)
        gp(lambda g: g.dma_start(
            out=A(pay_F.tensor, regs["s2F"] if me == "gpsimd" else 0,
                  [[4, 1], [1, 4]]), in_=noncesb[:, 4:8]), s_pay, 16)
        w8("sync", s_pay, s_pay.v)
        w8("sync", s_dv, s_dv.v)
        dma("sync", ubuf[:, 0:2048], a2_h, s_bi)
        dma("sync", xtm[0][0:32, 0:DM], be_h, s_bi)
        w8("gpsimd", s_dv, s_dv.v)
        dma("gpsimd", xtm[1][0:32, 0:DM], b_shd, s_w)
        w8("tensor", s_bi)
        w8("tensor", s_dv)
        w8("tensor", s_w)
        for tt2 in range(2):
            for half in range(2):
                bank = 2 + half
                pe_take(bank)
                for ic in range(32):
                    mm(psb[bank][0:32, 0:512],
                       ubuf[:, ic * 64 + half * 32:ic * 64 + half * 32 + 32],
                       A(xnar, ic * NROW + HALO + tt2 * 512,
                         [[32 * NROW, 128], [1, 512]]),
                       ic == 0, ic == 31, s_pe[bank] if ic == 31 else None)
                use("vector", bank)
                doff = 2048 + half * 1024 + tt2 * 512
                vec(lambda v, doff=doff, bank=bank: v.tensor_copy(
                    ubuf[0:32, doff:doff + 512], psb[bank][0:32, 0:512]), rel(bank))
        vec(lambda v: v.engine_nop(), s_dv)
        w8("tensor", s_dv)
        slab_rel = [None, None]
        ost_rel = [s_so.v, s_so.v]
        ptl_rel = [s_dv.v, s_dv.v]
        bias_rel = None
        for ot in range(16):
            b = ot % 2
            if slab_rel[b] is not None:
                w8("gpsimd", s_pe[slab_rel[b][0]], slab_rel[b][1])
            dma("gpsimd", wsl[b][:, 0:8192],
                A(ffn_kp.tensor, ot * 128 * 8192, [[8192, 128], [1, 8192]]), s_w)
            if bias_rel is not None:
                w8("gpsimd", s_pe[bias_rel[0]], bias_rel[1])
            dma("gpsimd", biasl[:],
                A(bias_r.tensor, ot * 256, [[256, 1], [1, 256]]), s_w)
            for mi in range(8):
                bank = (ot * 8 + mi) % 2
                pe_take(bank)
                w8("tensor", s_w)
                for ic in range(32):
                    mm(psb[bank][:, 0:256],
                       A(xnar, ic * NROW + HALO + mi * 128,
                         [[32 * NROW, 128], [1, 128]]),
                       wsl[b][:, ic * 256:(ic + 1) * 256], ic == 0, False)
                mm(psb[bank][:, 0:256], ubuf[0:32, 2048 + mi * 128:2048 + mi * 128 + 128],
                   xtm[0][0:32, ot * 256:(ot + 1) * 256], False, False)
                mm(psb[bank][:, 0:256], ubuf[0:32, 3072 + mi * 128:3072 + mi * 128 + 128],
                   xtm[1][0:32, ot * 256:(ot + 1) * 256], False, False)
                mm(psb[bank][:, 0:256], ones1b[:], biasl[:], False, True,
                   s_pe[bank])
                if mi == 7:
                    slab_rel[b] = (bank, s_pe[bank].v)
                    bias_rel = (bank, s_pe[bank].v)
                pb = (ot * 8 + mi) % 2
                w8("scalar", s_dv, ptl_rel[pb])
                dma("scalar", ptl[pb],
                    A(x_own.tensor, (HALO + mi * 128) * DM + ot * 256,
                      [[DM, 128], [1, 256]]), s_gl)
                ob = (ot * 8 + mi) % 2
                use("vector", bank)
                w8("vector", s_gl)
                w8("vector", s_so, ost_rel[ob])
                vec(lambda v, bank=bank, pb=pb, ob=ob: v.scalar_tensor_tensor(
                    ost[ob], psb[bank][:, 0:256], 1.0, ptl[pb],
                    ALU.mult, ALU.add), rel(bank))
                vec(lambda v: v.engine_nop(), s_dv)
                ptl_rel[pb] = s_dv.v
                w8("scalar", s_fr[bank], s_fr[bank].v)
                dma("scalar", A(out.tensor, mi * 128 * DM + ot * 256,
                                [[DM, 128], [1, 256]]), ost[ob], s_so)
                ost_rel[ob] = s_so.v
        barrier()
        if phases <= 1:
            if debug:
                dma("sync", d_y, A(xnar, 0, [[32 * NROW, 128], [1, NCH * NI]]), s_bo)
                w8("sync", s_bo)
            w8("scalar", s_so)
            w8("vector", s_so)
            return

        if debug:
            w8("sync", s_bo, s_bo.v)
            dma("sync", d_y, A(y_h.tensor, 0, [[NI, 128], [128 * NI, 8], [1, NI]]),
                s_bo)
        barrier()
        if phases <= 2:
            w8("scalar", s_bo)
            w8("vector", s_bo)
            return

        if me == "gpsimd":
            eng.reg_mov(regs["nd"], 1)
            eng.reg_mov(regs["cnt"], s_rcv.v)
            wg = eng.While(regs["nd"])
            wg.__enter__()
            eng.reg_add(regs["cnt"], regs["cnt"], 16)
            eng.dma_start(out=flagsb[:],
                          in_=A(pay_F.tensor, regs["rF"], [[4, 1], [1, 4]])
                          ).then_inc(s_rcv.h, 16)
            eng.wait_ge(s_rcv.h, regs["cnt"])
            eng.reg_load(regs["flg"], flagsb[0:1, 0:1])
            eng.reg_sub(regs["nd"], regs["flg"], regs["non"])
            wg.__exit__(None, None, None)
        w8("gpsimd", s_dv, s_dv.v)
        w8("gpsimd", s_pe[0], s_pe[0].v)
        w8("gpsimd", s_pe[1], s_pe[1].v)
        gp(lambda g: g.dma_start(
            out=sin[:], in_=A(pay_T.tensor, regs["rT"] if me == "gpsimd" else 0,
                              [[NI, 128], [1, NI]])), s_pay, 16)
        gp(lambda g: g.dma_start(
            out=drvc[:], in_=A(pay_D.tensor, regs["rD"] if me == "gpsimd" else 0,
                               [[1, 64], [1, 1]])), s_pay, 16)
        gp(lambda g: g.dma_start(
            out=ubuf[:], in_=A(pay_T.tensor, regs["r2T"] if me == "gpsimd" else 0,
                               [[NI, 128], [1, NI]])), s_pay, 16)
        gp(lambda g: g.dma_start(
            out=drv2[:], in_=A(pay_D.tensor, regs["r2D"] if me == "gpsimd" else 0,
                               [[1, 64], [1, 1]])), s_pay, 16)
        w8("vector", s_pay)
        vec(lambda v: v.tensor_tensor(ccol[:, 0:1], drv2[:], dprc[:], ALU.mult), s_dv)
        w8("vector", s_dv)
        vec(lambda v: v.tensor_scalar(ccol[:, 0:1], ccol[:, 0:1], prmf[0:64, 1:2],
                                      prmf[0:64, 0:1], ALU.mult, ALU.add), s_dv)
        w8("vector", s_dv)
        vec(lambda v: v.tensor_scalar(ccol[:, 1:2], drvc[:], prmf[0:64, 2:3], None,
                                      ALU.mult), s_dv)
        w8("vector", s_dv)
        vec(lambda v: v.scalar_tensor_tensor(ccol[:, 1:2], drv2[:],
                                             prmf[0:64, 3:4], ccol[:, 1:2],
                                             ALU.mult, ALU.add), s_dv)
        w8("vector", s_dv)
        vec(lambda v: v.tensor_copy(ccol[:, 2:3], prmf[0:64, 4:5]), s_dv)
        w8("vector", s_dv)
        w8("vector", s_pe[5], s_pe[5].v)
        for j in range(3):
            vec(lambda v, j=j: v.tensor_scalar(
                emsk[:, j * 64:(j + 1) * 64] if j < 2 else emsk2[:, 0:64],
                eye[0:64, 0:64], ccol[:, j:j + 1], None, ALU.mult), s_dv)
        pe_take(5)
        w8("tensor", s_dv)
        mm(psb[5][:, 0:H], ones64[:], emsk[:, 0:64], True, True)
        mm(psb[5][:, H:2 * H], ones64[:], emsk[:, 64:128], True, True)
        mm(psb[5][:, 2 * H:3 * H], ones64[:], emsk2[:, 0:64], True, True, s_pe[5])
        use("vector", 5)
        vec(lambda v: v.tensor_copy(drbc[:], psb[5][:, 0:H]))
        vec(lambda v: v.tensor_copy(dobc[:], psb[5][:, H:2 * H]))
        vec(lambda v: v.tensor_copy(drb3[:], psb[5][:, 2 * H:3 * H]), rel(5))
        vec(lambda v: v.tensor_tensor(
            y2b[:], sin[:], A(drbc, 0, [[H, 128], [1, 64], [0, 64]]), ALU.mult))
        vec(lambda v: v.tensor_tensor(
            sin[:], tpre[:], A(dobc, 0, [[H, 128], [1, 64], [0, 64]]), ALU.mult))
        vec(lambda v: v.tensor_tensor(y2b[:], y2b[:], sin[:], ALU.add))
        vec(lambda v: v.tensor_tensor(
            sin[:], ubuf[:], A(drb3, 0, [[H, 128], [1, 64], [0, 64]]), ALU.mult))
        vec(lambda v: v.tensor_tensor(y2b[:], y2b[:], sin[:], ALU.add))
        vec(lambda v: v.tensor_copy(sin[:], y2b[:]), s_dv)
        if debug:
            w8("sync", s_dv, s_dv.v)
            dma("sync", d_sin, sin[:], s_bo)
        barrier()
        if phases <= 3:
            w8("scalar", s_bo)
            w8("vector", s_bo)
            return

        dma("gpsimd", wsl[0][:, 0:8192],
            A(w_op.tensor, 0, [[8192, 128], [1, 8192]]), s_w)
        dma("gpsimd", wsl[1][:, 0:8192],
            A(w_op.tensor, 128 * 8192, [[8192, 128], [1, 8192]]), s_w)
        opw_rdy = s_w.v
        ost_rel = [s_so.v, s_so.v]
        ptl_rel = [s_dv.v, s_dv.v]
        pend_ev = []
        ynrdy = [0] * NCH

        def op_mms(ot, mi, bank, ynv):
            pe_take(bank)
            w8("tensor", s_w, opw_rdy)
            w8("tensor", s_tp, ynv)
            for ic in range(32):
                mm(psb[bank][:, 0:256],
                   A(xnar, mi * NI + ic * 128, [[32 * NROW, 128], [1, 128]]),
                   wsl[ot % 2][:, ic * 256:(ic + 1) * 256],
                   ic == 0, ic == 31, s_pe[bank] if ic == 31 else None)

        def op_evac(ot, mi, bank):
            pb = (ot * 8 + mi) % 2
            w8("scalar", s_dv, ptl_rel[pb])
            dma("scalar", ptl[pb],
                A(out.tensor, mi * 128 * DM + ot * 256, [[DM, 128], [1, 256]]),
                s_gl)
            use("vector", bank)
            w8("vector", s_gl)
            w8("vector", s_so, ost_rel[pb])
            vec(lambda v, bank=bank, pb=pb: v.scalar_tensor_tensor(
                ost[pb], psb[bank][:, 0:256], 1.0, ptl[pb],
                ALU.mult, ALU.add), rel(bank))
            vec(lambda v: v.engine_nop(), s_dv)
            ptl_rel[pb] = s_dv.v
            w8("scalar", s_fr[bank], s_fr[bank].v)
            dma("scalar", A(out.tensor, mi * 128 * DM + ot * 256,
                            [[DM, 128], [1, 256]]), ost[pb], s_so)
            ost_rel[pb] = s_so.v

        gch_rel = [s_dv.v, s_dv.v]
        dma("sync", xtm[0][:], gate_h[0], s_gl)
        gch_ld = [s_gl.v, 0]
        yld = [0] * NCH
        dma("sync", A(xnar, 0, [[32 * NROW, 128], [1, NI]]), y_h[0], s_bi)
        yld[0] = s_bi.v
        for k in range(NCH):
            gb = k % 2
            if k + 1 < NCH:
                w8("sync", s_dv, gch_rel[(k + 1) % 2])
                dma("sync", xtm[(k + 1) % 2][:], gate_h[k + 1], s_gl)
                gch_ld[(k + 1) % 2] = s_gl.v
                dma("sync", A(xnar, (k + 1) * NI, [[32 * NROW, 128], [1, NI]]),
                    y_h[k + 1], s_bi)
                yld[k + 1] = s_bi.v
            w8("vector", s_bi, yld[k])
            w8("tensor", s_bi, yld[k])
            w8("sync", s_pe[0], s_pe[0].v)
            w8("sync", s_pe[1], s_pe[1].v)
            dma("sync", ccmc[:],
                A(ccm_h.tensor, k * 128, [[NTOK, 128], [128 * NTOK, 8], [1, 128]]),
                s_bi)
            w8("scalar", s_dv, s_dv.v)
            sca(lambda s, k=k: s.activation(lsg[:], l_tm[:, k * 64:(k + 1) * 64],
                                            AF.Exp), s_ac)
            for ev in pend_ev:
                ev()
            pend_ev.clear()
            w8("tensor", s_ac)
            w8("tensor", s_bi)
            for wv in range(8):
                bank = wv % 2
                pe_take(bank)
                for hh in range(8):
                    mm(psb[bank][:, hh * 64:(hh + 1) * 64],
                       ccmc[:, wv * 128:(wv + 1) * 128],
                       sin[:, (wv * 8 + hh) * 64:(wv * 8 + hh + 1) * 64],
                       True, True, s_pe[bank] if hh == 7 else None)
                use("vector", bank)
                vec(lambda v, wv=wv, bank=bank: v.tensor_tensor(
                    y2b[:, EV0:EV0 + 512], psb[bank][:],
                    A(lsg, wv * 8, [[H, 128], [1, 8], [0, 64]]), ALU.mult), rel(bank))
                vec(lambda v, k=k, wv=wv: v.tensor_tensor(
                    A(xnar, k * NI + wv * 512, [[32 * NROW, 128], [1, 512]]),
                    A(xnar, k * NI + wv * 512, [[32 * NROW, 128], [1, 512]]),
                    y2b[:, EV0:EV0 + 512], ALU.add), s_dv)
            w8("scalar", s_dv)
            for half in range(2):
                sca(lambda s, k=k, half=half: s.activation(
                    y2b[:, 0:2048],
                    A(xnar, k * NI + half * 2048, [[32 * NROW, 128], [1, 2048]]),
                    AF.Square), s_ac)
                w8("vector", s_ac)
                for g8 in range(4):
                    vec(lambda v, g8=g8, half=half: v.reduce_sum(
                        ssb[:, half * 4 + g8:half * 4 + g8 + 1],
                        y2b[:, g8 * 512:(g8 + 1) * 512], AX),
                        s_dv if g8 == 3 else None)
                if half == 0:
                    w8("scalar", s_dv)
            w8("vector", s_dv)
            vec(lambda v: v.tensor_scalar(ssb[:, 0:8], ssb[:, 0:8], 1.0 / 512.0, EPS,
                                          ALU.mult, ALU.add), s_dv)
            w8("vector", s_dv)
            vec(lambda v: v.reciprocal(ssb[:, 8:16], ssb[:, 0:8]), s_dv)
            w8("scalar", s_dv)
            sca(lambda s: s.activation(ssb[:, 8:16], ssb[:, 8:16], AF.Sqrt), s_ac)
            w8("vector", s_ac)
            w8("vector", s_gl, gch_ld[gb])
            w8("vector", s_tp, s_tp.v)
            vec(lambda v, k=k: v.tensor_tensor(
                ynst[:], A(xnar, k * NI, [[32 * NROW, 128], [1, NI]]),
                A(ssb, 8, [[16, 128], [1, 8], [0, 512]]), ALU.mult))
            vec(lambda v, gb=gb: v.tensor_tensor(ynst[:], ynst[:], xtm[gb][:],
                                                 ALU.mult), s_dv)
            gch_rel[gb] = s_dv.v
            w8("sync", s_dv)
            dmat("sync", A(xnar, k * NI, [[32 * NROW, 128], [128, 32], [1, 128]]),
                 ynst[:], s_tp)
            ynrdy[k] = s_tp.v
            op_mms(0, k, 6, ynrdy[k])
            pend_ev.append(lambda k=k: op_evac(0, k, 6))
            if k >= 1:
                op_mms(1, k - 1, 7, ynrdy[k - 1])
                pend_ev.append(lambda k=k: op_evac(1, k - 1, 7))
        for ev in pend_ev:
            ev()
        pend_ev.clear()
        op_mms(1, 7, 7, ynrdy[7])
        op_evac(1, 7, 7)
        barrier()

        slab_rel = [None, None]
        ost_rel = [s_so.v, s_so.v]
        ptl_rel = [s_dv.v, s_dv.v]
        for ot in range(2, 16):
            b = ot % 2
            if slab_rel[b] is not None:
                w8("gpsimd", s_pe[slab_rel[b][0]], slab_rel[b][1])
            dma("gpsimd", wsl[b][:, 0:8192],
                A(w_op.tensor, ot * 128 * 8192, [[8192, 128], [1, 8192]]), s_w)
            for mi in range(8):
                bank = (ot * 8 + mi) % 2
                pe_take(bank)
                w8("tensor", s_w)
                for ic in range(32):
                    mm(psb[bank][:, 0:256],
                       A(xnar, mi * NI + ic * 128, [[32 * NROW, 128], [1, 128]]),
                       wsl[b][:, ic * 256:(ic + 1) * 256],
                       ic == 0, ic == 31, s_pe[bank] if ic == 31 else None)
                if mi == 7:
                    slab_rel[b] = (bank, s_pe[bank].v)
                pb = (ot * 8 + mi) % 2
                w8("scalar", s_dv, ptl_rel[pb])
                dma("scalar", ptl[pb],
                    A(out.tensor, mi * 128 * DM + ot * 256, [[DM, 128], [1, 256]]),
                    s_gl)
                ob = (ot * 8 + mi) % 2
                use("vector", bank)
                w8("vector", s_gl)
                w8("vector", s_so, ost_rel[ob])
                vec(lambda v, bank=bank, pb=pb, ob=ob: v.scalar_tensor_tensor(
                    ost[ob], psb[bank][:, 0:256], 1.0, ptl[pb],
                    ALU.mult, ALU.add), rel(bank))
                vec(lambda v: v.engine_nop(), s_dv)
                ptl_rel[pb] = s_dv.v
                w8("scalar", s_fr[bank], s_fr[bank].v)
                dma("scalar", A(out.tensor, mi * 128 * DM + ot * 256,
                                [[DM, 128], [1, 256]]), ost[ob], s_so)
                ost_rel[ob] = s_so.v
        w8("scalar", s_so)
        w8("vector", s_so)

    for en in ["sync", "gpsimd", "vector", "scalar", "tensor"]:
        getattr(block, en)(lambda e, en=en: emit(en))

    for g in reversed(ctxs):
        g.__exit__(None, None, None)

    return nc, {}


def kernel(**inputs):
    x = np.asarray(inputs["L0_attn_hidden_states"], dtype=np.float32)
    w_in_t = np.ascontiguousarray(np.asarray(inputs["L0_attn_in_proj_w"],
                                             dtype=np.float32).T)
    w_cm = np.ascontiguousarray(
        w_in_t[:, 4096:10240].reshape(32, 128, 48, 128)
        .transpose(2, 1, 0, 3).reshape(48 * 128, 4096))
    w_tm = np.ascontiguousarray(
        w_in_t[:, 0:4096].reshape(32, 128, 16, 256)
        .transpose(2, 1, 0, 3).reshape(16 * 128, 8192))
    w_dt = np.ascontiguousarray(
        w_in_t[:, 10240:10304].reshape(32, 128, 64)
        .transpose(1, 0, 2).reshape(128, 2048))
    conv_w = np.asarray(inputs["L0_attn_conv_weight"], dtype=np.float32)
    convw = np.ascontiguousarray(
        conv_w.reshape(48, 128, KC).transpose(1, 0, 2).reshape(128, 48 * KC))
    convb = np.ascontiguousarray(
        np.asarray(inputs["L0_attn_conv_bias"], dtype=np.float32).reshape(48, 128).T)
    dtb = np.asarray(inputs["L0_attn_dt_bias"], dtype=np.float32).reshape(1, H)
    alog = np.asarray(inputs["L0_attn_A_log"], dtype=np.float32).reshape(1, H)
    dvec = np.asarray(inputs["L0_attn_D"], dtype=np.float32).reshape(1, H)
    nw = np.asarray(inputs["L0_attn_norm_weight"], dtype=np.float32).reshape(1, DM)
    w_out_t = np.ascontiguousarray(np.asarray(inputs["L0_attn_out_proj_w"],
                                              dtype=np.float32).T)
    w_op = np.ascontiguousarray(
        w_out_t.reshape(32, 128, 16, 256).transpose(2, 1, 0, 3)
        .reshape(16 * 128, 8192))
    gate = np.asarray(inputs["L0_ffn_gate"], dtype=np.float32)
    ffn_k = np.ascontiguousarray(np.asarray(inputs["L0_ffn_kernel"],
                                            dtype=np.float32))
    ffn_kp = np.ascontiguousarray(
        ffn_k.reshape(32, 128, 16, 256).transpose(2, 1, 0, 3)
        .reshape(16 * 128, 8192))
    bias = np.asarray(inputs["L0_ffn_bias"], dtype=np.float32).reshape(1, DM)
    lora_A = np.asarray(inputs["L0_ffn_lora_A"], dtype=np.float32)
    lora_B = np.asarray(inputs["L0_ffn_lora_B"], dtype=np.float32)
    lora_As = np.asarray(inputs["L0_ffn_lora_A_shared"], dtype=np.float32)
    lora_Bs = np.asarray(inputs["L0_ffn_lora_B_shared"], dtype=np.float32)

    a_cat = np.ascontiguousarray(np.concatenate(
        [lora_A.transpose(2, 0, 1).reshape(DM, 512), lora_As.T], axis=1))
    b_gat = np.ascontiguousarray(lora_B.transpose(2, 0, 1).reshape(32, 16 * DM))
    b_shd = np.ascontiguousarray(lora_Bs.T)

    eye = np.eye(128, dtype=np.float32)
    triu01 = np.triu(np.ones((128, 128), dtype=np.float32))
    strict = np.tril(np.ones((128, 128), dtype=np.float32), -1)

    n1, n2 = [int(v) for v in np.random.randint(1, 2 ** 30, size=2)]
    nonce = np.array([[n1] * 4 + [n2] * 4], dtype=np.int32)

    globals()['LN_SERIAL_G'] = LN_SERIAL
    globals()['WSL_TEST_G'] = WSL_TEST
    globals()['LN_NOSTATS_G'] = LN_NOSTATS
    nc, _ = build_program(phases=PHASES, debug=DEBUG)

    in_maps = []
    for c in range(8):
        bb, r = c // 4, c % 4
        q = c % 2
        own0 = r * 1024
        xo = np.zeros((NROW, DM), dtype=np.float32)
        h0 = max(own0 - HALO, 0)
        xo[HALO - (own0 - h0):HALO] = x[bb, h0:own0]
        xo[HALO:HALO + 1024] = x[bb, own0:own0 + 1024]
        xp = np.zeros((NROW, DM), dtype=np.float32)
        if r >= 2:
            pre0 = (r - 2) * 1024
            ph0 = max(pre0 - HALO, 0)
            xp[HALO - (pre0 - ph0):HALO] = x[bb, ph0:pre0]
            xp[HALO:HALO + 1024] = x[bb, pre0:pre0 + 1024]
        prmf = np.zeros((128, 8), dtype=np.float32)
        prmf[:, 0] = 1.0 if r == 2 else 0.0
        prmf[:, 1] = 0.0 if r == 0 else 1.0
        s1, s2 = q * 2 + 0, q * 2 + 1
        rkind = 0 if r in (0, 2) else 1
        rslot = (1 - q) * 2 + rkind
        prmi = np.zeros((1, 16), dtype=np.int32)
        prmi[0, 0:3] = [s1 * 128 * NI, s1 * H, s1 * 4]
        prmi[0, 3:6] = [s2 * 128 * NI, s2 * H, s2 * 4]
        prmi[0, 6:9] = [rslot * 128 * NI, rslot * H, rslot * 4]
        prmi[0, 9] = n1 if rkind == 0 else n2
        in_maps.append({
            "x_own": xo, "x_pre": xp, "w_cm": w_cm, "w_tm": w_tm, "w_dt": w_dt,
            "w_op": w_op, "ffn_kp": ffn_kp,
            "a_cat": a_cat, "b_gat": b_gat, "b_shd": b_shd,
            "bias_r": bias, "nw_r": nw, "dvec_r": dvec, "alog_r": alog,
            "dtb_r": dtb, "convw": convw, "convb": convb,
            "gate_r": np.ascontiguousarray(gate[4 * bb + r].reshape(1, 16)),
            "eye_i": eye, "triu_i": triu01, "stri_i": strict,
            "prm_f": prmf, "prm_i": prmi, "nonce_r": nonce,
        })

    res = run_bass_kernel_spmd(nc, in_maps, core_ids=list(range(8)))
    outf = np.empty((2, 4096, DM), dtype=np.float32)
    for c in range(8):
        bb, r = c // 4, c % 4
        outf[bb, r * 1024:(r + 1) * 1024] = res.results[c]["out"]
    return outf


# version 13 (final)
# speedup vs baseline: 1.7628x; 1.0039x over previous
import numpy as np
import ml_dtypes
import concourse.bass as bass
import concourse.mybir as mybir
from concourse.bass_utils import run_bass_kernel_spmd

F32 = mybir.dt.float32
BF16 = mybir.dt.bfloat16
I32 = mybir.dt.int32
AF = mybir.ActivationFunctionType
ALU = mybir.AluOpType
AX = mybir.AxisListType.X
A = bass.AP

DM = 4096
H, P, NS, G, KC = 64, 64, 128, 8, 4
NI = 4096
GN = G * NS
CONV = NI + 2 * GN
PROJ = NI + CONV + H
EPS = 1e-5
Q = 128
NCH = 8
NTOK = Q * NCH
HALO = 4
NROW = 1152
TT = (0, 512, 1024)
TW = (512, 512, HALO)
COL_GATE, COL_X, COL_B, COL_C, COL_DT = 0, NI, 2 * NI, 2 * NI + GN, PROJ - H
PHASES = 9
DEBUG = False
LN_SERIAL = False
WSL_TEST = False
LN_NOSTATS = False
RH0, EV0, SPA, SPB = 0, 1024, 1536, 2048


def build_program(phases=9, debug=False):
    LN_SERIAL = globals().get('LN_SERIAL_G', False)
    WSL_TEST = globals().get('WSL_TEST_G', False)
    LN_NOSTATS = globals().get('LN_NOSTATS_G', False)
    nc = bass.Bass("TRN2", target_bir_lowering=False,
                   detect_race_conditions=False)

    x_own = nc.dram_tensor("x_own", [NROW, DM], F32, kind="ExternalInput").ap()
    x_pre = nc.dram_tensor("x_pre", [NROW, DM], F32, kind="ExternalInput").ap()
    w_cm = nc.dram_tensor("w_cm", [48 * 128, 4096], F32, kind="ExternalInput").ap()
    w_tm = nc.dram_tensor("w_tm", [16 * 128, 8192], F32, kind="ExternalInput").ap()
    w_dt = nc.dram_tensor("w_dt", [128, 2048], F32, kind="ExternalInput").ap()
    w_op = nc.dram_tensor("w_op", [16 * 128, 8192], F32, kind="ExternalInput").ap()
    ffn_kp = nc.dram_tensor("ffn_kp", [16 * 128, 8192], F32,
                            kind="ExternalInput").ap()
    a_cat = nc.dram_tensor("a_cat", [DM, 544], F32, kind="ExternalInput").ap()
    b_gat = nc.dram_tensor("b_gat", [32, 16 * DM], F32, kind="ExternalInput").ap()
    b_shd = nc.dram_tensor("b_shd", [32, DM], F32, kind="ExternalInput").ap()
    bias_r = nc.dram_tensor("bias_r", [1, DM], F32, kind="ExternalInput").ap()
    nw_r = nc.dram_tensor("nw_r", [1, DM], F32, kind="ExternalInput").ap()
    dvec_r = nc.dram_tensor("dvec_r", [1, H], F32, kind="ExternalInput").ap()
    alog_r = nc.dram_tensor("alog_r", [1, H], F32, kind="ExternalInput").ap()
    dtb_r = nc.dram_tensor("dtb_r", [1, H], F32, kind="ExternalInput").ap()
    convw = nc.dram_tensor("convw", [128, 48 * KC], F32, kind="ExternalInput").ap()
    convb = nc.dram_tensor("convb", [128, 48], F32, kind="ExternalInput").ap()
    gate_r = nc.dram_tensor("gate_r", [1, 16], F32, kind="ExternalInput").ap()
    eye_i = nc.dram_tensor("eye_i", [128, 128], F32, kind="ExternalInput").ap()
    triu_i = nc.dram_tensor("triu_i", [128, 128], F32, kind="ExternalInput").ap()
    stri_i = nc.dram_tensor("stri_i", [128, 128], F32, kind="ExternalInput").ap()
    prm_f = nc.dram_tensor("prm_f", [128, 8], F32, kind="ExternalInput").ap()
    prm_i = nc.dram_tensor("prm_i", [1, 16], I32, kind="ExternalInput").ap()
    nonce_r = nc.dram_tensor("nonce_r", [1, 8], I32, kind="ExternalInput").ap()

    out = nc.dram_tensor("out", [NTOK, DM], F32, kind="ExternalOutput").ap()

    pay_T = nc.dram_tensor("pay_T", [4, 128, NI], BF16, kind="Internal",
                           addr_space="Shared").ap()
    pay_D = nc.dram_tensor("pay_D", [4, H, 1], F32, kind="Internal",
                           addr_space="Shared").ap()
    pay_F = nc.dram_tensor("pay_F", [4, 1, 4], I32, kind="Internal",
                           addr_space="Shared").ap()
    _ik = "ExternalOutput" if debug else "Internal"
    xtm_h = nc.dram_tensor("xtm_h", [NCH, 128, NI], BF16, kind=_ik).ap()
    btm_h = nc.dram_tensor("btm_h", [NCH, 128, GN], BF16, kind=_ik).ap()
    bcm_h = nc.dram_tensor("bcm_h", [G, 128, NTOK], BF16, kind=_ik).ap()
    ccm_h = nc.dram_tensor("ccm_h", [G, 128, NTOK], BF16, kind=_ik).ap()
    gate_h = nc.dram_tensor("gate_h", [NCH, 128, DM], BF16, kind=_ik).ap()
    y_h = nc.dram_tensor("y_h", [NCH, 128, NI], BF16, kind=_ik).ap()
    a2_h = nc.dram_tensor("a2_h", [128, 2048], BF16, kind="Internal").ap()
    be_h = nc.dram_tensor("be_h", [32, DM], BF16, kind="Internal").ap()
    d_y = (nc.dram_tensor("d_y", [128, NCH * NI], BF16, kind="ExternalOutput").ap()
           if debug else None)
    d_sin = (nc.dram_tensor("d_sin", [128, NI], BF16, kind="ExternalOutput").ap()
             if debug else None)

    ctxs = []

    def sb(name, shape, dtype):
        g = nc.sbuf_tensor(name, shape, dtype)
        t = g.__enter__()
        ctxs.append(g)
        return t

    def psalloc(name):
        g = nc.psum_tensor(name, [128, 512], F32)
        t = g.__enter__()
        ctxs.append(g)
        return t

    eye = sb("eye", [128, 128], F32)
    triu = sb("triu", [128, 128], F32)
    stri = sb("stri", [128, 128], F32)
    ones1 = sb("ones1", [1, 128], F32)
    ones64 = sb("ones64", [64, 128], F32)
    emsk = sb("emsk", [64, 128], F32)
    zrow = sb("zrow", [1, H], F32)
    zcol = sb("zcol", [64, 1], F32)
    dprc = sb("dprc", [64, 1], F32)
    dwnc = sb("dwnc", [64, 1], F32)
    drvc = sb("drvc", [64, 1], F32)
    rowb = sb("rowb", [1, 4 * H], F32)
    rowb2 = sb("rowb2", [1, 256], F32)
    biasl = sb("biasl", [1, 256], BF16)
    ones1b = sb("ones1b", [1, 128], BF16)
    abc = sb("abc", [128, H], F32)
    dtbbc = sb("dtbbc", [128, H], F32)
    dbc = sb("dbc", [128, H], F32)
    nwt = sb("nwt", [128, 256], BF16)
    cw = sb("cw", [128, 48 * KC], F32)
    cb = sb("cb", [128, 48], F32)
    gcol = sb("gcol", [128, 16], F32)
    prmf = sb("prmf", [128, 8], F32)
    prmi = sb("prmi", [1, 16], I32)
    flagsb = sb("flagsb", [1, 4], I32)
    noncesb = sb("noncesb", [1, 8], I32)
    drbc = sb("drbc", [128, H], F32)
    drb3 = sb("drb3", [128, H], F32)
    drv2 = sb("drv2", [64, 1], F32)
    ccol = sb("ccol", [64, 4], F32)
    emsk2 = sb("emsk2", [64, 64], F32)
    dobc = sb("dobc", [128, H], F32)

    xnar = sb("xnar", [128, 32 * NROW], BF16)
    wsl0 = sb("wsl0", [128, 8192], BF16)
    strip = sb("strip", [128, NTOK + HALO], F32)
    cacc = sb("cacc", [128, NTOK], F32)
    xbcbf = sb("xbcbf", [128, NTOK], BF16)
    tpstg = sb("tpstg", [128, NTOK], BF16)
    ccmc = sb("ccmc", [128, GN], BF16)
    bcmc = sb("bcmc", [128, GN], BF16)
    btmc = sb("btmc", [128, GN], BF16)
    dtp = sb("dtp", [128, 512], F32)
    a_tm = sb("a_tm", [128, 512], F32)
    l_tm = sb("l_tm", [128, 512], F32)
    wexp = sb("wexp", [128, H], F32)
    lamtm = sb("lamtm", [128, H], F32)
    lambc = sb("lambc", [128, H], F32)
    lsg = sb("lsg", [128, H], F32)
    qtm = sb("qtm", [128, H], F32)
    S = sb("S", [128, NI], F32)
    sbfw = sb("sbfw", [128, 512], BF16)
    tpre = sb("tpre", [128, NI], BF16)
    sin = sb("sin", [128, NI], BF16)
    xtm = [sb("xtm0", [128, NI], BF16), sb("xtm1", [128, NI], BF16)]
    ubuf = sb("ubuf", [128, NI], BF16)
    y2b = sb("y2b", [128, DM], F32)
    ssb = sb("ssb", [128, 16], F32)
    gld = [sb("gld0", [128, 512], BF16), sb("gld1", [128, 512], BF16)]
    ynst = sb("ynst", [128, DM], BF16)
    lnst = sb("lnst", [128, 8], F32)

    ystg = gld
    wsl = [wsl0, wsl0 if WSL_TEST else S.bitcast(BF16)]
    ptl = [y2b[:, 3072:3328], y2b[:, 3328:3584]]
    ost = [y2b[:, 3584:3840], y2b[:, 3840:4096]]
    gmsb = ynst
    lpw = ynst
    mpw = ynst
    udw = ynst
    xdw = ynst

    xin = S
    dtraw = l_tm

    psb = [psalloc(f"psb{i}") for i in range(8)]

    class Sem:
        __slots__ = ("h", "v")

        def __init__(self, h):
            self.h = h
            self.v = 0

    def mksem(name):
        g = nc.semaphore(name)
        h = g.__enter__()
        ctxs.append(g)
        return Sem(h)

    s_x = mksem("s_x")
    s_w = mksem("s_w")
    s_tp = mksem("s_tp")
    s_bi = mksem("s_bi")
    s_bo = mksem("s_bo")
    s_gl = mksem("s_gl")
    s_so = mksem("s_so")
    s_pay = mksem("s_pay")
    s_rcv = mksem("s_rcv")
    s_dv = mksem("s_dv")
    s_ac = mksem("s_ac")
    s_gp = mksem("s_gp")
    s_tb = mksem("s_tb")
    s_ab = mksem("s_ab")
    s_ph = mksem("s_ph")
    s_pe = [mksem(f"s_pe{i}") for i in range(8)]
    s_fr = [mksem(f"s_fr{i}") for i in range(8)]
    DMAS = [s_x, s_w, s_tp, s_bi, s_bo, s_gl, s_so, s_pay]
    ALLSEMS = DMAS + [s_rcv, s_dv, s_ac, s_gp, s_tb, s_ab, s_ph] + s_pe + s_fr

    blkg = nc.Block()
    block = blkg.__enter__()
    ctxs.append(blkg)

    def emit(me):
        for _s in ALLSEMS:
            _s.v = 0
        E = {"sync": nc.sync, "gpsimd": nc.gpsimd, "vector": nc.vector,
             "scalar": nc.scalar, "tensor": nc.tensor}
        eng = E[me]
        regs = {}
        if me == "gpsimd":
            for rn in ["s1T", "s1D", "s1F", "s2T", "s2D", "s2F", "rT", "rD",
                       "rF", "non", "r2T", "r2D", "nd", "cnt", "flg"]:
                g = eng.register("r_" + rn)
                regs[rn] = g.__enter__()
                ctxs.append(g)

        def w8(en, sem, v=None):
            if me == en:
                eng.wait_ge(sem.h, sem.v if v is None else v)

        def dma(en, outap, inap, sem):
            sem.v += 16
            if me == en:
                eng.dma_start(out=outap, in_=inap).then_inc(sem.h, 16)
            return sem.v

        def dmat(en, outap, inap, sem):
            sem.v += 16
            if me == en:
                eng.dma_start_transpose(outap, inap).then_inc(sem.h, 16)
            return sem.v

        def mm(outap, lhsT, rhs, start, stop, sem=None):
            if sem is not None:
                sem.v += 1
            if me == "tensor":
                i = eng.matmul(outap, lhsT, rhs, start=start, stop=stop)
                if sem is not None:
                    i.then_inc(sem.h, 1)

        def tr(outap, inap, sem=None, ident=None):
            if sem is not None:
                sem.v += 1
            if me == "tensor":
                i = eng.transpose(outap, inap, eye[:] if ident is None else ident)
                if sem is not None:
                    i.then_inc(sem.h, 1)

        def vec(fn, sem=None):
            if sem is not None:
                sem.v += 1
            if me == "vector":
                i = fn(eng)
                if sem is not None:
                    i.then_inc(sem.h, 1)

        def sca(fn, sem=None):
            if sem is not None:
                sem.v += 1
            if me == "scalar":
                i = fn(eng)
                if sem is not None:
                    i.then_inc(sem.h, 1)

        def gp(fn, sem=None, inc=1):
            if sem is not None:
                sem.v += inc
            if me == "gpsimd":
                i = fn(eng)
                if sem is not None:
                    i.then_inc(sem.h, inc)

        def barrier():
            s_tb.v += 1
            if me == "tensor":
                eng.nop().then_inc(s_tb.h, 1)
            s_ab.v += 1
            if me == "scalar":
                eng.nop().then_inc(s_ab.h, 1)
            s_gp.v += 1
            if me == "gpsimd":
                eng.nop().then_inc(s_gp.h, 1)
            if me == "vector":
                for s in DMAS + [s_tb, s_ab, s_gp]:
                    eng.wait_ge(s.h, s.v)
            s_ph.v += 1
            if me == "vector":
                eng.nop().then_inc(s_ph.h, 1)
            if me != "vector":
                eng.wait_ge(s_ph.h, s_ph.v)

        def l_cm_col(j):
            return strip[0:64, j:j + 1]

        def pe_take(bank):
            w8("tensor", s_fr[bank])

        def use(en, bank):
            w8(en, s_pe[bank])

        def rel(bank):
            return s_fr[bank]

        dma("sync", eye[:], eye_i, s_x)
        dma("sync", triu[:], triu_i, s_x)
        dma("sync", stri[:], stri_i, s_x)
        dma("sync", cw[:], convw, s_x)
        dma("sync", cb[:], convb, s_x)
        dma("sync", prmf[:], prm_f, s_x)
        dma("sync", prmi[:], prm_i, s_x)
        dma("sync", noncesb[:], nonce_r, s_x)
        dma("sync", gcol[:, 0:16], A(gate_r.tensor, 0, [[0, 128], [1, 16]]), s_x)
        dma("sync", rowb[:, 0:H], alog_r, s_x)
        dma("sync", rowb[:, H:2 * H], dtb_r, s_x)
        dma("sync", rowb[:, 2 * H:3 * H], dvec_r, s_x)
        w8("vector", s_x)
        w8("scalar", s_x)
        w8("gpsimd", s_x)
        w8("tensor", s_x)
        vec(lambda v: v.memset(ones1[:], 1.0))
        vec(lambda v: v.memset(ones1b[:], 1.0))
        vec(lambda v: v.memset(ones64[:], 1.0))
        vec(lambda v: v.memset(zcol[:], 0.0))
        vec(lambda v: v.memset(zrow[:], 0.0), s_dv)
        sca(lambda s: s.activation(rowb[:, 3 * H:4 * H], rowb[:, 0:H], AF.Exp), s_ac)
        w8("vector", s_ac)
        vec(lambda v: v.tensor_scalar_mul(rowb[:, 3 * H:4 * H], rowb[:, 3 * H:4 * H],
                                          -1.0), s_dv)
        w8("tensor", s_dv)
        pe_take(0)
        mm(psb[0][:, 0:H], ones1[:], rowb[:, 3 * H:4 * H], True, True)
        mm(psb[0][:, H:2 * H], ones1[:], rowb[:, H:2 * H], True, True)
        mm(psb[0][:, 2 * H:3 * H], ones1[:], rowb[:, 2 * H:3 * H], True, True, s_pe[0])
        use("vector", 0)
        vec(lambda v: v.tensor_copy(abc[:], psb[0][:, 0:H]))
        vec(lambda v: v.tensor_copy(dtbbc[:], psb[0][:, H:2 * H]))
        vec(lambda v: v.tensor_copy(dbc[:], psb[0][:, 2 * H:3 * H]), rel(0))
        if me == "gpsimd":
            names = ["s1T", "s1D", "s1F", "s2T", "s2D", "s2F", "rT", "rD", "rF",
                     "non", "r2T", "r2D"]
            for j, rn in enumerate(names):
                eng.reg_load(regs[rn], prmi[0:1, j:j + 1])
        barrier()

        def ln_phase(x_dram):
            xbufs = [xin, y2b]
            ybufs = [ynst[:], wsl0[:, 0:4096]]
            ubufs = [ubuf[:], wsl0[:, 4096:8192]]
            udst = [A(ubuf, 0, [[NI, 128], [128, 32], [1, 128]]),
                    A(wsl0, 4096, [[8192, 128], [128, 32], [1, 128]])]
            xrel = [s_ac.v, s_ac.v]
            yrel = [s_tp.v, s_tp.v]
            urel = [s_dv.v, s_dv.v]
            pend = None
            for i in range(9):
                b = i % 2
                xb, yb = xbufs[b], ybufs[b]
                c0 = b * 4
                w8("gpsimd", s_ac, xrel[b])
                dma("gpsimd", xb[:], x_dram[i * 128:(i + 1) * 128, :], s_w)
                w8("vector", s_w)
                w8("vector", s_ac, xrel[b])
                for j in range(8):
                    vec(lambda v, j=j, b=b, xb=xb: v.bn_stats(
                        strip[:, b * 48 + j * 6:b * 48 + (j + 1) * 6],
                        xb[:, j * 512:(j + 1) * 512]), s_dv if j == 7 else None)
                w8("vector", s_dv)
                vec(lambda v, c0=c0, b=b: v.bn_aggr(
                    lnst[:, c0:c0 + 2], strip[:, b * 48:b * 48 + 48]), s_dv)
                w8("vector", s_dv)
                vec(lambda v, c0=c0: v.tensor_scalar_add(
                    lnst[:, c0 + 1:c0 + 2], lnst[:, c0 + 1:c0 + 2], EPS), s_dv)
                w8("scalar", s_dv)
                sca(lambda s, c0=c0: s.activation(
                    lnst[:, c0 + 2:c0 + 3], lnst[:, c0 + 1:c0 + 2], AF.Sqrt),
                    s_ac)
                w8("vector", s_ac)
                vec(lambda v, c0=c0: v.reciprocal(
                    lnst[:, c0 + 2:c0 + 3], lnst[:, c0 + 2:c0 + 3]), s_dv)
                w8("vector", s_dv)
                vec(lambda v, c0=c0: v.tensor_scalar(
                    lnst[:, c0 + 3:c0 + 4], lnst[:, c0:c0 + 1],
                    lnst[:, c0 + 2:c0 + 3], -1.0, ALU.mult, ALU.mult), s_dv)
                bias_done = s_dv.v
                if pend is not None:
                    pi, pb, ptp = pend
                    w8("vector", s_tp, ptp)
                    vec(lambda v, pi=pi, pb=pb: v.tensor_copy(
                        A(xnar, pi * 128, [[32 * NROW, 128], [NROW, 32],
                                           [1, 128]]), ubufs[pb]), s_dv)
                    urel[pb] = s_dv.v
                w8("scalar", s_dv, bias_done)
                w8("scalar", s_tp, yrel[b])
                sca(lambda s, xb=xb, yb=yb, c0=c0: s.activation(
                    yb, xb[:], AF.Identity,
                    bias=lnst[:, c0 + 3:c0 + 4], scale=lnst[:, c0 + 2:c0 + 3]),
                    s_ac)
                xrel[b] = s_ac.v
                w8("sync", s_ac)
                w8("sync", s_dv, urel[b])
                dmat("sync", udst[b], yb, s_tp)
                yrel[b] = s_tp.v
                pend = (i, b, s_tp.v)
            pi, pb, ptp = pend
            w8("vector", s_tp, ptp)
            vec(lambda v, pi=pi, pb=pb: v.tensor_copy(
                A(xnar, pi * 128, [[32 * NROW, 128], [NROW, 32], [1, 128]]),
                ubufs[pb]), s_dv)

        def conv_silu(cslot):
            vec(lambda v, cslot=cslot: v.tensor_scalar(
                cacc[:], strip[:, 1:1 + NTOK], cw[:, cslot * KC:cslot * KC + 1],
                None, ALU.mult))
            for k in range(1, KC):
                vec(lambda v, k=k, cslot=cslot: v.scalar_tensor_tensor(
                    cacc[:], strip[:, 1 + k:1 + k + NTOK],
                    cw[:, cslot * KC + k:cslot * KC + k + 1], cacc[:],
                    ALU.mult, ALU.add))
            vec(lambda v: v.engine_nop(), s_dv)
            w8("scalar", s_dv)
            sca(lambda s, cslot=cslot: s.activation(
                xbcbf[:], cacc[:], AF.Silu, bias=cb[:, cslot:cslot + 1]), s_ac)

        def cm_group(plan):
            slab_rel = [None, None]
            cm_group.slab_rel = slab_rel
            strip_rel = 0
            xbc_guard = (s_dv, 0)
            for idx, (kind, cc) in enumerate(plan):
                b = idx % 2
                j = {"x": 0, "bo": 32, "bp": 32, "c": 40}[kind] + cc
                if slab_rel[b] is not None:
                    w8("gpsimd", s_pe[slab_rel[b][0]], slab_rel[b][1])
                dma("gpsimd", wsl[b][:, 0:4096],
                    A(w_cm.tensor, j * 128 * 4096, [[4096, 128], [1, 4096]]),
                    s_w)
                for ti in range(3):
                    bank = (idx * 3 + ti) % 2
                    pe_take(bank)
                    w8("tensor", s_w)
                    for ic in range(32):
                        mm(psb[bank][:, 0:TW[ti]], wsl[b][:, ic * 128:(ic + 1) * 128],
                           A(xnar, ic * NROW + TT[ti], [[32 * NROW, 128], [1, TW[ti]]]),
                           ic == 0, ic == 31, s_pe[bank] if ic == 31 else None)
                    if ti == 2:
                        slab_rel[b] = (bank, s_pe[bank].v)
                    use("scalar", bank)
                    w8("scalar", s_dv, strip_rel)
                    sca(lambda s, bank=bank, ti=ti: s.copy(
                        strip[:, TT[ti]:TT[ti] + TW[ti]], psb[bank][:, 0:TW[ti]]),
                        rel(bank))
                w8("vector", s_fr[0], s_fr[0].v)
                w8("vector", s_fr[1], s_fr[1].v)
                w8("scalar", xbc_guard[0], xbc_guard[1])
                conv_silu({"x": cc, "bo": 32 + cc, "bp": 32 + cc, "c": 40 + cc}[kind])
                strip_rel = s_dv.v
                if kind in ("x", "bp"):
                    w8("sync", s_ac)
                    dmat("sync", A(tpstg, 0, [[NTOK, 128], [128, 8], [1, 128]]),
                         xbcbf[:], s_tp)
                    xbc_guard = (s_tp, s_tp.v)
                    w8("sync", s_tp)
                    if kind == "x":
                        dma("sync", A(xtm_h.tensor, cc * 128,
                                      [[NI, 128], [128 * NI, 8], [1, 128]]), tpstg[:],
                            s_bo)
                    else:
                        dma("sync", A(btm_h.tensor, cc * 128,
                                      [[GN, 128], [128 * GN, 8], [1, 128]]), tpstg[:],
                            s_bo)
                else:
                    w8("sync", s_ac)
                    hb = bcm_h if kind == "bo" else ccm_h
                    dma("sync", hb[cc], xbcbf[:], s_bo)
                    xbc_guard = (s_bo, s_bo.v)

        def tm_group(own):
            tiles = ([("g", COL_GATE + i * 256, 256) for i in range(16)] if own
                     else [])
            tiles.append(("d", COL_DT, 64))
            slab_rel = list(getattr(cm_group, "slab_rel", [None, None]))
            gld_rel = [s_bo.v, s_bo.v]
            for tj, (kind, col, wid) in enumerate(tiles):
                b = tj % 2
                if slab_rel[b] is not None:
                    w8("gpsimd", s_pe[slab_rel[b][0]], slab_rel[b][1])
                wsrc = (A(w_tm.tensor, (col // 256) * 128 * 8192,
                          [[8192, 128], [1, 8192]]) if kind == "g"
                        else A(w_dt.tensor, 0, [[2048, 128], [1, 2048]]))
                dma("gpsimd", wsl[b][:, 0:32 * wid], wsrc, s_w)
                if kind == "g":
                    bank5 = 5
                    pe_take(bank5)
                    w8("tensor", s_dv, s_dv.v)
                    coff = col - COL_GATE
                    w8("sync", s_pe[bank5], s_pe[bank5].v)
                    dma("sync", rowb2[:], A(nw_r.tensor, coff, [[256, 1], [1, 256]]),
                        s_x)
                    w8("tensor", s_x)
                    mm(psb[bank5][:, 0:256], ones1[:], rowb2[:], True, True,
                       s_pe[bank5])
                    use("vector", bank5)
                    vec(lambda v, bank5=bank5: v.tensor_copy(
                        nwt[:], psb[bank5][:, 0:256]), rel(bank5))
                    vec(lambda v: v.engine_nop(), s_dv)
                for mi in range(8):
                    bank = (tj * 8 + mi) % 2
                    pe_take(bank)
                    w8("tensor", s_w)
                    for ic in range(32):
                        mm(psb[bank][:, 0:wid],
                           A(xnar, ic * NROW + HALO + mi * 128,
                             [[32 * NROW, 128], [1, 128]]),
                           wsl[b][:, ic * wid:(ic + 1) * wid],
                           ic == 0, ic == 31, s_pe[bank] if ic == 31 else None)
                    if mi == 7:
                        slab_rel[b] = (bank, s_pe[bank].v)
                    if kind == "g":
                        gb = (tj * 8 + mi) % 2
                        use("scalar", bank)
                        sca(lambda s, bank=bank: s.activation(
                            ynst[:, 0:256], psb[bank][:, 0:256], AF.Silu), s_ac)
                        w8("vector", s_ac)
                        w8("vector", s_bo, gld_rel[gb])
                        w8("vector", s_dv, s_dv.v)
                        vec(lambda v, gb=gb: v.tensor_tensor(
                            gld[gb][:, 0:256], ynst[:, 0:256], nwt[:], ALU.mult),
                            rel(bank))
                        vec(lambda v: v.engine_nop(), s_dv)
                        w8("scalar", s_fr[bank], s_fr[bank].v)
                        coff = col - COL_GATE
                        dma("scalar",
                            A(gate_h.tensor, mi * 128 * DM + coff,
                              [[DM, 128], [1, 256]]), gld[gb][:, 0:256], s_bo)
                        gld_rel[gb] = s_bo.v
                    else:
                        use("vector", bank)
                        vec(lambda v, bank=bank, mi=mi: v.tensor_copy(
                            dtraw[:, mi * 64:(mi + 1) * 64], psb[bank][:, 0:64]),
                            rel(bank))

        def dt_chain():
            vec(lambda v: v.tensor_tensor(
                dtp[:], dtraw[:], A(dtbbc, 0, [[H, 128], [0, 8], [1, H]]), ALU.add))
            vec(lambda v: v.tensor_scalar_mul(y2b[:, SPA:SPA + 512], dtp[:], -1.0))
            vec(lambda v: v.tensor_tensor(y2b[:, SPA:SPA + 512], y2b[:, SPA:SPA + 512],
                                          dtp[:], ALU.min), s_dv)
            w8("scalar", s_dv)
            sca(lambda s: s.activation(y2b[:, SPB:SPB + 512], y2b[:, SPA:SPA + 512],
                                       AF.Exp), s_ac)
            sca(lambda s: s.activation(y2b[:, SPB:SPB + 512], y2b[:, SPB:SPB + 512],
                                       AF.Ln, bias=1.0), s_ac)
            w8("vector", s_ac)
            vec(lambda v: v.tensor_relu(y2b[:, SPA:SPA + 512], dtp[:]))
            vec(lambda v: v.tensor_tensor(dtp[:], y2b[:, SPA:SPA + 512],
                                          y2b[:, SPB:SPB + 512], ALU.add))
            vec(lambda v: v.tensor_tensor(
                a_tm[:], dtp[:], A(abc, 0, [[H, 128], [0, 8], [1, H]]), ALU.mult), s_dv)
            w8("tensor", s_dv)
            for k in range(8):
                bank = 4 + k % 2
                pe_take(bank)
                tr(psb[bank][0:64, 0:128], a_tm[:, k * 64:(k + 1) * 64], s_pe[bank])
                use("vector", bank)
                vec(lambda v, k=k, bank=bank: v.tensor_copy(
                    cacc[0:64, k * 128:(k + 1) * 128], psb[bank][0:64, 0:128]),
                    rel(bank))
            vec(lambda v: v.tensor_tensor_scan(
                strip[0:64, 0:NTOK], cacc[0:64, 0:NTOK],
                A(zcol, 0, [[1, 64], [0, NTOK]]), 0.0, ALU.add, ALU.add), s_dv)
            w8("tensor", s_dv)
            for k in range(8):
                bank = 4 + k % 2
                pe_take(bank)
                tr(psb[bank][:, 0:64], strip[0:64, k * 128:(k + 1) * 128],
                   s_pe[bank], ident=eye[0:64, 0:64])
                use("vector", bank)
                vec(lambda v, k=k, bank=bank: v.tensor_copy(
                    l_tm[:, k * 64:(k + 1) * 64], psb[bank][:, 0:64]), rel(bank))
            vec(lambda v: v.engine_nop(), s_dv)

        def chunk_decays(k):
            bank = 5
            w8("vector", s_pe[bank], s_pe[bank].v)
            vec(lambda v, k=k: v.tensor_scalar(
                emsk[:, 0:64], eye[0:64, 0:64],
                l_cm_col(k * 128 + 127), None, ALU.mult))
            prevcol = zcol[:, 0:1] if k == 0 else l_cm_col(k * 128 - 1)
            vec(lambda v, prevcol=prevcol: v.tensor_scalar(
                emsk[:, 64:128], eye[0:64, 0:64], prevcol, None, ALU.mult), s_dv)
            pe_take(bank)
            w8("tensor", s_dv)
            mm(psb[bank][:, 0:H], ones64[:], emsk[:, 0:64], True, True)
            mm(psb[bank][:, H:2 * H], ones64[:], emsk[:, 64:128], True, True,
               s_pe[bank])
            use("vector", bank)
            vec(lambda v, bank=bank, k=k: v.tensor_tensor(
                y2b[:, SPA:SPA + H], psb[bank][:, 0:H], l_tm[:, k * 64:(k + 1) * 64],
                ALU.subtract))
            vec(lambda v, bank=bank, k=k: v.tensor_tensor(
                y2b[:, SPA + H:SPA + 2 * H], l_tm[:, k * 64:(k + 1) * 64],
                psb[bank][:, H:2 * H], ALU.subtract))
            vec(lambda v: v.tensor_tensor(
                y2b[:, SPA + 2 * H:SPA + 3 * H], y2b[:, SPA:SPA + H],
                y2b[:, SPA + H:SPA + 2 * H], ALU.add), rel(bank))
            vec(lambda v: v.engine_nop(), s_dv)
            w8("scalar", s_dv)
            sca(lambda s: s.activation(wexp[:], y2b[:, SPA:SPA + H], AF.Exp))
            sca(lambda s: s.activation(lamtm[:], y2b[:, SPA + H:SPA + 2 * H], AF.Exp))
            sca(lambda s: s.activation(lambc[:], y2b[:, SPA + 2 * H:SPA + 3 * H],
                                       AF.Exp), s_ac)
            w8("vector", s_ac)
            vec(lambda v, k=k: v.tensor_tensor(
                qtm[:], dtp[:, k * 64:(k + 1) * 64], wexp[:], ALU.mult), s_dv)

        def scan_phase(own):
            vec(lambda v: v.memset(S[:], 0.0), s_dv)
            dt_chain()
            ystg_rel = [s_bo.v, s_bo.v]
            xtm_rel = [s_dv.v, s_dv.v]
            lpw_rel = s_dv.v
            btm_rel = s_pe[4].v
            bc_rel = s_pe[3].v
            dma("sync", xtm[0][:], xtm_h[0], s_bi)
            xtm_ld = [s_bi.v, 0]
            for k in range(NCH):
                bx = k % 2
                if k + 1 < NCH:
                    w8("sync", s_dv, xtm_rel[(k + 1) % 2])
                    dma("sync", xtm[(k + 1) % 2][:], xtm_h[k + 1], s_bi)
                    xtm_ld[(k + 1) % 2] = s_bi.v
                chunk_decays(k)
                w8("sync", s_pe[4], btm_rel)
                if own:
                    w8("sync", s_pe[3], bc_rel)
                    dma("sync", bcmc[:],
                        A(bcm_h.tensor, k * 128,
                          [[NTOK, 128], [128 * NTOK, 8], [1, 128]]), s_bi)
                    dma("sync", ccmc[:],
                        A(ccm_h.tensor, k * 128,
                          [[NTOK, 128], [128 * NTOK, 8], [1, 128]]), s_bi)
                    w8("sync", s_bi)
                    for g8 in range(8):
                        dmat("sync", A(btmc, g8 * 128, [[GN, 128], [1, 128]]),
                             bcmc[:, g8 * 128:(g8 + 1) * 128], s_tp)
                    btm_ld = (s_tp, s_tp.v)
                    ccm_ld = s_bi.v
                else:
                    dma("sync", btmc[:], btm_h[k], s_bi)
                    btm_ld = (s_bi, s_bi.v)
                w8("vector", s_bi, xtm_ld[bx])
                if own:
                    vec(lambda v, k=k, bx=bx: v.tensor_tensor(
                        ubuf[:], xtm[bx][:],
                        A(dtp, k * 64, [[512, 128], [1, 64], [0, 64]]), ALU.mult),
                        s_dv)
                    w8("tensor", s_dv)
                    w8("tensor", s_bi, ccm_ld)
                    for g8 in range(8):
                        bank = 2 + g8 // 4
                        if g8 % 4 == 0:
                            pe_take(bank)
                        mm(psb[bank][:, (g8 % 4) * 128:(g8 % 4) * 128 + 128],
                           bcmc[:, g8 * 128:(g8 + 1) * 128],
                           ccmc[:, g8 * 128:(g8 + 1) * 128],
                           True, True, s_pe[bank] if g8 % 4 == 3 else None)
                    use("vector", 2)
                    vec(lambda v: v.tensor_tensor(
                        ynst[:, 0:512], psb[2][:],
                        A(triu, 0, [[128, 128], [0, 4], [1, 128]]), ALU.mult), rel(2))
                    use("vector", 3)
                    vec(lambda v: v.tensor_tensor(
                        ynst[:, 512:1024], psb[3][:],
                        A(triu, 0, [[128, 128], [0, 4], [1, 128]]), ALU.mult), rel(3))
                    bc_rel = s_pe[3].v
                for wv in range(8):
                    h0 = wv * 8
                    if own:
                        w8("vector", s_pe[3], s_pe[3].v)
                        vec(lambda v, wv=wv: v.tensor_copy(
                            sbfw[:], S[:, wv * 512:(wv + 1) * 512]), s_dv)
                        w8("vector", s_pe[7], s_pe[7].v)
                        vec(lambda v, k=k, h0=h0: v.tensor_tensor(
                            y2b[:, RH0:RH0 + 1024],
                            A(a_tm, k * 64 + h0, [[512, 128], [1, 8], [0, 128]]),
                            A(triu, 0, [[128, 128], [0, 8], [1, 128]]), ALU.mult),
                            s_dv)
                        w8("tensor", s_dv)
                        pe_take(6)
                        mm(psb[6][:], stri[:], y2b[:, RH0:RH0 + 512], True, True,
                           s_pe[6])
                        pe_take(7)
                        mm(psb[7][:], stri[:], y2b[:, RH0 + 512:RH0 + 1024], True,
                           True, s_pe[7])
                        use("scalar", 6)
                        w8("scalar", s_dv, lpw_rel)
                        sca(lambda s: s.activation(ynst[:, 1024:1536], psb[6][:], AF.Exp),
                            rel(6))
                        use("scalar", 7)
                        sca(lambda s: s.activation(ynst[:, 1536:2048], psb[7][:],
                                                   AF.Exp), rel(7))
                        sca(lambda s: s.nop(), s_ac)
                        w8("vector", s_ac)
                        w8("vector", s_pe[2], s_pe[2].v)
                        vec(lambda v, wv=wv: v.tensor_tensor(
                            ynst[:, 2048:3072], ynst[:, 1024:2048],
                            A(ynst, wv * 128, [[DM, 128], [0, 8], [1, 128]]),
                            ALU.mult), s_dv)
                        lpw_rel = s_dv.v
                        w8("tensor", s_dv)
                        pe_take(2)
                        pe_take(3)
                        w8("tensor", btm_ld[0], btm_ld[1])
                        for hh in range(8):
                            mm(psb[2][:, hh * 64:(hh + 1) * 64],
                               ynst[:, 2048 + hh * 128:2048 + (hh + 1) * 128],
                               ubuf[:, (h0 + hh) * 64:(h0 + hh + 1) * 64],
                               True, True, s_pe[2] if hh == 7 else None)
                        for hh in range(8):
                            mm(psb[3][:, hh * 64:(hh + 1) * 64],
                               ccmc[:, wv * 128:(wv + 1) * 128],
                               sbfw[:, hh * 64:(hh + 1) * 64],
                               True, True, s_pe[3] if hh == 7 else None)
                    w8("vector", s_pe[4], s_pe[4].v)
                    vec(lambda v, wv=wv, bx=bx, h0=h0: v.tensor_tensor(
                        ynst[:, 3072:3584], xtm[bx][:, wv * 512:(wv + 1) * 512],
                        A(qtm, h0, [[H, 128], [1, 8], [0, 64]]), ALU.mult), s_dv)
                    w8("tensor", s_dv)
                    if not own:
                        w8("tensor", btm_ld[0], btm_ld[1])
                    pe_take(4)
                    for hh in range(8):
                        mm(psb[4][:, hh * 64:(hh + 1) * 64],
                           btmc[:, wv * 128:(wv + 1) * 128],
                           ynst[:, 3072 + hh * 64:3072 + (hh + 1) * 64],
                           True, True, s_pe[4] if hh == 7 else None)
                    if wv == 7:
                        btm_rel = s_pe[4].v
                    if own:
                        vec(lambda v, wv=wv, bx=bx, h0=h0: v.tensor_tensor(
                            ynst[:, 3584:4096], xtm[bx][:, wv * 512:(wv + 1) * 512],
                            A(dbc, h0, [[H, 128], [1, 8], [0, 64]]), ALU.mult))
                        use("vector", 3)
                        vec(lambda v, h0=h0: v.tensor_tensor(
                            y2b[:, EV0:EV0 + 512], psb[3][:],
                            A(lamtm, h0, [[H, 128], [1, 8], [0, 64]]), ALU.mult),
                            rel(3))
                        use("vector", 2)
                        vec(lambda v: v.scalar_tensor_tensor(
                            y2b[:, EV0:EV0 + 512], psb[2][:], 1.0,
                            y2b[:, EV0:EV0 + 512], ALU.mult, ALU.add), rel(2))
                        yb = (k * 8 + wv) % 2
                        w8("vector", s_bo, ystg_rel[yb])
                        vec(lambda v, yb=yb: v.tensor_tensor(
                            ystg[yb][:], y2b[:, EV0:EV0 + 512],
                            ynst[:, 3584:4096], ALU.add), s_dv)
                        w8("sync", s_dv)
                        dma("sync", A(y_h.tensor, k * 128 * NI + wv * 512,
                                      [[NI, 128], [1, 512]]), ystg[yb][:], s_bo)
                        ystg_rel[yb] = s_bo.v
                    use("vector", 4)
                    vec(lambda v, wv=wv, h0=h0: v.tensor_tensor(
                        S[:, wv * 512:(wv + 1) * 512], S[:, wv * 512:(wv + 1) * 512],
                        A(lambc, h0, [[H, 128], [1, 8], [0, 64]]), ALU.mult))
                    vec(lambda v, wv=wv: v.tensor_tensor(
                        S[:, wv * 512:(wv + 1) * 512], S[:, wv * 512:(wv + 1) * 512],
                        psb[4][:], ALU.add), rel(4))
                    vec(lambda v: v.engine_nop(), s_dv)
                    if wv == 7:
                        xtm_rel[bx] = s_dv.v

        ln_phase(x_pre)
        barrier()
        plan_pre = [("x", cc) for cc in range(32)] + [("bp", cc) for cc in range(8)]
        cm_group(plan_pre)
        tm_group(own=False)
        ac_rel = s_dv.v
        for icg in range(8):
            w8("sync", s_dv, ac_rel)
            dma("sync", y2b[:, 0:2176],
                A(a_cat.tensor, icg * 4 * 128 * 544,
                  [[544, 128], [128 * 544, 4], [1, 544]]), s_x)
            w8("vector", s_x)
            for n in range(16):
                for j in range(4):
                    ic = icg * 4 + j
                    if n == 0:
                        vec(lambda v, ic=ic, j=j: v.tensor_scalar(
                            ubuf[:, ic * 64:ic * 64 + 32],
                            y2b[:, j * 544:j * 544 + 32],
                            gcol[:, 0:1], None, ALU.mult))
                    else:
                        vec(lambda v, ic=ic, j=j, n=n: v.scalar_tensor_tensor(
                            ubuf[:, ic * 64:ic * 64 + 32],
                            y2b[:, j * 544 + n * 32:j * 544 + n * 32 + 32],
                            gcol[:, n:n + 1], ubuf[:, ic * 64:ic * 64 + 32],
                            ALU.mult, ALU.add))
            for j in range(4):
                ic = icg * 4 + j
                vec(lambda v, ic=ic, j=j: v.tensor_copy(
                    ubuf[:, ic * 64 + 32:ic * 64 + 64],
                    y2b[:, j * 544 + 512:j * 544 + 544]),
                    s_dv if j == 3 else None)
            ac_rel = s_dv.v
        bg_rel = [s_dv.v, s_dv.v]
        for n in range(16):
            w8("sync", s_dv, bg_rel[0])
            dma("sync", y2b[0:32, 0:DM], b_gat[:, n * DM:(n + 1) * DM], s_x)
            w8("vector", s_x)
            if n == 0:
                vec(lambda v: v.tensor_scalar(xtm[0][0:32, 0:DM], y2b[0:32, 0:DM],
                                              gcol[0:32, 0:1], None, ALU.mult), s_dv)
            else:
                vec(lambda v, n=n: v.scalar_tensor_tensor(
                    xtm[0][0:32, 0:DM], y2b[0:32, 0:DM],
                    gcol[0:32, n:n + 1], xtm[0][0:32, 0:DM], ALU.mult, ALU.add),
                    s_dv)
            bg_rel[0] = s_dv.v
        vec(lambda v: v.engine_nop(), s_dv)
        w8("sync", s_dv)
        dma("sync", a2_h, ubuf[:, 0:2048], s_bo)
        dma("sync", be_h, xtm[0][0:32, 0:DM], s_bo)
        barrier()
        scan_phase(own=False)
        vec(lambda v: v.tensor_copy(tpre[:], S[:]), s_dv)
        w8("scalar", s_dv)
        sca(lambda s: s.activation(dprc[:], l_cm_col(1023), AF.Exp), s_ac)
        w8("gpsimd", s_dv)
        w8("gpsimd", s_ac)
        gp(lambda g: g.dma_start(
            out=A(pay_T.tensor, regs["s1T"] if me == "gpsimd" else 0,
                  [[NI, 128], [1, NI]]), in_=tpre[:]), s_pay, 16)
        gp(lambda g: g.dma_start(
            out=A(pay_D.tensor, regs["s1D"] if me == "gpsimd" else 0,
                  [[1, 64], [1, 1]]), in_=dprc[:]), s_pay, 16)
        w8("gpsimd", s_pay)
        gp(lambda g: g.dma_start(
            out=A(pay_F.tensor, regs["s1F"] if me == "gpsimd" else 0,
                  [[4, 1], [1, 4]]), in_=noncesb[:, 0:4]), s_pay, 16)
        barrier()
        if phases <= 0:
            w8("scalar", s_pay)
            return
        ln_phase(x_own)
        barrier()

        plan_own = ([("x", cc) for cc in range(32)] + [("bo", cc) for cc in range(8)]
                    + [("c", cc) for cc in range(8)])
        cm_group(plan_own)
        tm_group(own=True)
        barrier()
        scan_phase(own=True)
        w8("scalar", s_dv, s_dv.v)
        sca(lambda s: s.activation(dwnc[:], l_cm_col(1023), AF.Exp), s_ac)
        w8("vector", s_pe[2], s_pe[2].v)
        w8("vector", s_pe[3], s_pe[3].v)
        w8("vector", s_pe[4], s_pe[4].v)
        vec(lambda v: v.tensor_copy(ubuf[:], S[:]), s_dv)
        w8("gpsimd", s_dv)
        w8("gpsimd", s_ac)
        gp(lambda g: g.dma_start(
            out=A(pay_T.tensor, regs["s2T"] if me == "gpsimd" else 0,
                  [[NI, 128], [1, NI]]), in_=ubuf[:]), s_pay, 16)
        gp(lambda g: g.dma_start(
            out=A(pay_D.tensor, regs["s2D"] if me == "gpsimd" else 0,
                  [[1, 64], [1, 1]]), in_=dwnc[:]), s_pay, 16)
        w8("gpsimd", s_pay)
        gp(lambda g: g.dma_start(
            out=A(pay_F.tensor, regs["s2F"] if me == "gpsimd" else 0,
                  [[4, 1], [1, 4]]), in_=noncesb[:, 4:8]), s_pay, 16)
        w8("sync", s_pay, s_pay.v)
        w8("sync", s_dv, s_dv.v)
        dma("sync", ubuf[:, 0:2048], a2_h, s_bi)
        dma("sync", xtm[0][0:32, 0:DM], be_h, s_bi)
        w8("gpsimd", s_dv, s_dv.v)
        dma("gpsimd", xtm[1][0:32, 0:DM], b_shd, s_w)
        w8("tensor", s_bi)
        w8("tensor", s_dv)
        w8("tensor", s_w)
        for tt2 in range(2):
            for half in range(2):
                bank = 2 + half
                pe_take(bank)
                for ic in range(32):
                    mm(psb[bank][0:32, 0:512],
                       ubuf[:, ic * 64 + half * 32:ic * 64 + half * 32 + 32],
                       A(xnar, ic * NROW + HALO + tt2 * 512,
                         [[32 * NROW, 128], [1, 512]]),
                       ic == 0, ic == 31, s_pe[bank] if ic == 31 else None)
                use("vector", bank)
                doff = 2048 + half * 1024 + tt2 * 512
                vec(lambda v, doff=doff, bank=bank: v.tensor_copy(
                    ubuf[0:32, doff:doff + 512], psb[bank][0:32, 0:512]), rel(bank))
        vec(lambda v: v.engine_nop(), s_dv)
        w8("tensor", s_dv)
        slab_rel = [None, None]
        ost_rel = [s_so.v, s_so.v]
        ptl_rel = [s_dv.v, s_dv.v]
        bias_rel = None
        for ot in range(16):
            b = ot % 2
            if slab_rel[b] is not None:
                w8("gpsimd", s_pe[slab_rel[b][0]], slab_rel[b][1])
            dma("gpsimd", wsl[b][:, 0:8192],
                A(ffn_kp.tensor, ot * 128 * 8192, [[8192, 128], [1, 8192]]), s_w)
            if bias_rel is not None:
                w8("gpsimd", s_pe[bias_rel[0]], bias_rel[1])
            dma("gpsimd", biasl[:],
                A(bias_r.tensor, ot * 256, [[256, 1], [1, 256]]), s_w)
            for mi in range(8):
                bank = (ot * 8 + mi) % 2
                pe_take(bank)
                w8("tensor", s_w)
                for ic in range(32):
                    mm(psb[bank][:, 0:256],
                       A(xnar, ic * NROW + HALO + mi * 128,
                         [[32 * NROW, 128], [1, 128]]),
                       wsl[b][:, ic * 256:(ic + 1) * 256], ic == 0, False)
                mm(psb[bank][:, 0:256], ubuf[0:32, 2048 + mi * 128:2048 + mi * 128 + 128],
                   xtm[0][0:32, ot * 256:(ot + 1) * 256], False, False)
                mm(psb[bank][:, 0:256], ubuf[0:32, 3072 + mi * 128:3072 + mi * 128 + 128],
                   xtm[1][0:32, ot * 256:(ot + 1) * 256], False, False)
                mm(psb[bank][:, 0:256], ones1b[:], biasl[:], False, True,
                   s_pe[bank])
                if mi == 7:
                    slab_rel[b] = (bank, s_pe[bank].v)
                    bias_rel = (bank, s_pe[bank].v)
                pb = (ot * 8 + mi) % 2
                w8("scalar", s_dv, ptl_rel[pb])
                dma("scalar", ptl[pb],
                    A(x_own.tensor, (HALO + mi * 128) * DM + ot * 256,
                      [[DM, 128], [1, 256]]), s_gl)
                ob = (ot * 8 + mi) % 2
                use("vector", bank)
                w8("vector", s_gl)
                w8("vector", s_so, ost_rel[ob])
                vec(lambda v, bank=bank, pb=pb, ob=ob: v.scalar_tensor_tensor(
                    ost[ob], psb[bank][:, 0:256], 1.0, ptl[pb],
                    ALU.mult, ALU.add), rel(bank))
                vec(lambda v: v.engine_nop(), s_dv)
                ptl_rel[pb] = s_dv.v
                w8("scalar", s_fr[bank], s_fr[bank].v)
                dma("scalar", A(out.tensor, mi * 128 * DM + ot * 256,
                                [[DM, 128], [1, 256]]), ost[ob], s_so)
                ost_rel[ob] = s_so.v
        barrier()
        if phases <= 1:
            if debug:
                dma("sync", d_y, A(xnar, 0, [[32 * NROW, 128], [1, NCH * NI]]), s_bo)
                w8("sync", s_bo)
            w8("scalar", s_so)
            w8("vector", s_so)
            return

        if debug:
            w8("sync", s_bo, s_bo.v)
            dma("sync", d_y, A(y_h.tensor, 0, [[NI, 128], [128 * NI, 8], [1, NI]]),
                s_bo)
        barrier()
        if phases <= 2:
            w8("scalar", s_bo)
            w8("vector", s_bo)
            return

        if me == "gpsimd":
            eng.reg_mov(regs["nd"], 1)
            eng.reg_mov(regs["cnt"], s_rcv.v)
            wg = eng.While(regs["nd"])
            wg.__enter__()
            eng.reg_add(regs["cnt"], regs["cnt"], 16)
            eng.dma_start(out=flagsb[:],
                          in_=A(pay_F.tensor, regs["rF"], [[4, 1], [1, 4]])
                          ).then_inc(s_rcv.h, 16)
            eng.wait_ge(s_rcv.h, regs["cnt"])
            eng.reg_load(regs["flg"], flagsb[0:1, 0:1])
            eng.reg_sub(regs["nd"], regs["flg"], regs["non"])
            wg.__exit__(None, None, None)
        w8("gpsimd", s_dv, s_dv.v)
        w8("gpsimd", s_pe[0], s_pe[0].v)
        w8("gpsimd", s_pe[1], s_pe[1].v)
        gp(lambda g: g.dma_start(
            out=sin[:], in_=A(pay_T.tensor, regs["rT"] if me == "gpsimd" else 0,
                              [[NI, 128], [1, NI]])), s_pay, 16)
        gp(lambda g: g.dma_start(
            out=drvc[:], in_=A(pay_D.tensor, regs["rD"] if me == "gpsimd" else 0,
                               [[1, 64], [1, 1]])), s_pay, 16)
        gp(lambda g: g.dma_start(
            out=ubuf[:], in_=A(pay_T.tensor, regs["r2T"] if me == "gpsimd" else 0,
                               [[NI, 128], [1, NI]])), s_pay, 16)
        gp(lambda g: g.dma_start(
            out=drv2[:], in_=A(pay_D.tensor, regs["r2D"] if me == "gpsimd" else 0,
                               [[1, 64], [1, 1]])), s_pay, 16)
        w8("vector", s_pay)
        vec(lambda v: v.tensor_tensor(ccol[:, 0:1], drv2[:], dprc[:], ALU.mult), s_dv)
        w8("vector", s_dv)
        vec(lambda v: v.tensor_scalar(ccol[:, 0:1], ccol[:, 0:1], prmf[0:64, 1:2],
                                      prmf[0:64, 0:1], ALU.mult, ALU.add), s_dv)
        w8("vector", s_dv)
        vec(lambda v: v.tensor_scalar(ccol[:, 1:2], drvc[:], prmf[0:64, 2:3], None,
                                      ALU.mult), s_dv)
        w8("vector", s_dv)
        vec(lambda v: v.scalar_tensor_tensor(ccol[:, 1:2], drv2[:],
                                             prmf[0:64, 3:4], ccol[:, 1:2],
                                             ALU.mult, ALU.add), s_dv)
        w8("vector", s_dv)
        vec(lambda v: v.tensor_copy(ccol[:, 2:3], prmf[0:64, 4:5]), s_dv)
        w8("vector", s_dv)
        w8("vector", s_pe[5], s_pe[5].v)
        for j in range(3):
            vec(lambda v, j=j: v.tensor_scalar(
                emsk[:, j * 64:(j + 1) * 64] if j < 2 else emsk2[:, 0:64],
                eye[0:64, 0:64], ccol[:, j:j + 1], None, ALU.mult), s_dv)
        pe_take(5)
        w8("tensor", s_dv)
        mm(psb[5][:, 0:H], ones64[:], emsk[:, 0:64], True, True)
        mm(psb[5][:, H:2 * H], ones64[:], emsk[:, 64:128], True, True)
        mm(psb[5][:, 2 * H:3 * H], ones64[:], emsk2[:, 0:64], True, True, s_pe[5])
        use("vector", 5)
        vec(lambda v: v.tensor_copy(drbc[:], psb[5][:, 0:H]))
        vec(lambda v: v.tensor_copy(dobc[:], psb[5][:, H:2 * H]))
        vec(lambda v: v.tensor_copy(drb3[:], psb[5][:, 2 * H:3 * H]), rel(5))
        vec(lambda v: v.tensor_tensor(
            y2b[:], sin[:], A(drbc, 0, [[H, 128], [1, 64], [0, 64]]), ALU.mult))
        vec(lambda v: v.tensor_tensor(
            sin[:], tpre[:], A(dobc, 0, [[H, 128], [1, 64], [0, 64]]), ALU.mult))
        vec(lambda v: v.tensor_tensor(y2b[:], y2b[:], sin[:], ALU.add))
        vec(lambda v: v.tensor_tensor(
            sin[:], ubuf[:], A(drb3, 0, [[H, 128], [1, 64], [0, 64]]), ALU.mult))
        vec(lambda v: v.tensor_tensor(y2b[:], y2b[:], sin[:], ALU.add))
        vec(lambda v: v.tensor_copy(sin[:], y2b[:]), s_dv)
        if debug:
            w8("sync", s_dv, s_dv.v)
            dma("sync", d_sin, sin[:], s_bo)
        barrier()
        if phases <= 3:
            w8("scalar", s_bo)
            w8("vector", s_bo)
            return

        gch_rel = [s_dv.v, s_dv.v]
        dma("sync", xtm[0][:], gate_h[0], s_gl)
        gch_ld = [s_gl.v, 0]
        yld = [0] * NCH
        dma("sync", A(xnar, 0, [[32 * NROW, 128], [1, NI]]), y_h[0], s_bi)
        yld[0] = s_bi.v
        ccld = [0] * NCH
        pev = [None] * NCH

        def csl(k, a, b):
            return (ccmc if k % 2 == 0 else tpre)[:, a:b]

        dma("sync", csl(0, 0, 1024),
            A(ccm_h.tensor, 0, [[NTOK, 128], [128 * NTOK, 8], [1, 128]]), s_bi)
        ccld[0] = s_bi.v
        for k in range(NCH):
            gb = k % 2
            if k + 1 < NCH:
                w8("sync", s_dv, gch_rel[(k + 1) % 2])
                dma("sync", xtm[(k + 1) % 2][:], gate_h[k + 1], s_gl)
                gch_ld[(k + 1) % 2] = s_gl.v
                dma("sync", A(xnar, (k + 1) * NI, [[32 * NROW, 128], [1, NI]]),
                    y_h[k + 1], s_bi)
                yld[k + 1] = s_bi.v
                if k >= 1:
                    w8("sync", s_pe[0], pev[k - 1][0])
                    w8("sync", s_pe[1], pev[k - 1][1])
                dma("sync", csl(k + 1, 0, 1024),
                    A(ccm_h.tensor, (k + 1) * 128,
                      [[NTOK, 128], [128 * NTOK, 8], [1, 128]]), s_bi)
                ccld[k + 1] = s_bi.v
            w8("vector", s_bi, yld[k])
            w8("tensor", s_bi, yld[k])
            w8("scalar", s_dv, s_dv.v)
            sca(lambda s, k=k: s.activation(lsg[:], l_tm[:, k * 64:(k + 1) * 64],
                                            AF.Exp), s_ac)
            w8("tensor", s_ac)
            w8("tensor", s_bi, ccld[k])
            for wv in range(8):
                bank = wv % 2
                pe_take(bank)
                for hh in range(8):
                    mm(psb[bank][:, hh * 64:(hh + 1) * 64],
                       csl(k, wv * 128, (wv + 1) * 128),
                       sin[:, (wv * 8 + hh) * 64:(wv * 8 + hh + 1) * 64],
                       True, True, s_pe[bank] if hh == 7 else None)
                use("vector", bank)
                vec(lambda v, wv=wv, bank=bank: v.tensor_tensor(
                    y2b[:, EV0:EV0 + 512], psb[bank][:],
                    A(lsg, wv * 8, [[H, 128], [1, 8], [0, 64]]), ALU.mult), rel(bank))
                vec(lambda v, k=k, wv=wv: v.tensor_tensor(
                    A(xnar, k * NI + wv * 512, [[32 * NROW, 128], [1, 512]]),
                    A(xnar, k * NI + wv * 512, [[32 * NROW, 128], [1, 512]]),
                    y2b[:, EV0:EV0 + 512], ALU.add), s_dv)
            pev[k] = (s_pe[0].v, s_pe[1].v)
            w8("scalar", s_dv)
            sca(lambda s, k=k: s.activation(
                y2b[:], A(xnar, k * NI, [[32 * NROW, 128], [1, NI]]), AF.Square),
                s_ac)
            w8("vector", s_ac)
            for g8 in range(8):
                vec(lambda v, g8=g8: v.reduce_sum(
                    ssb[:, g8:g8 + 1], y2b[:, g8 * 512:(g8 + 1) * 512], AX),
                    s_dv if g8 == 7 else None)
            w8("vector", s_dv)
            vec(lambda v: v.tensor_scalar(ssb[:, 0:8], ssb[:, 0:8], 1.0 / 512.0, EPS,
                                          ALU.mult, ALU.add), s_dv)
            w8("vector", s_dv)
            vec(lambda v: v.reciprocal(ssb[:, 8:16], ssb[:, 0:8]), s_dv)
            w8("scalar", s_dv)
            sca(lambda s: s.activation(ssb[:, 8:16], ssb[:, 8:16], AF.Sqrt), s_ac)
            w8("vector", s_ac)
            w8("vector", s_gl, gch_ld[gb])
            w8("vector", s_tp, s_tp.v)
            vec(lambda v, k=k: v.tensor_tensor(
                ynst[:], A(xnar, k * NI, [[32 * NROW, 128], [1, NI]]),
                A(ssb, 8, [[16, 128], [1, 8], [0, 512]]), ALU.mult))
            vec(lambda v, gb=gb: v.tensor_tensor(ynst[:], ynst[:], xtm[gb][:],
                                                 ALU.mult), s_dv)
            gch_rel[gb] = s_dv.v
            w8("sync", s_dv)
            dmat("sync", A(xnar, k * NI, [[32 * NROW, 128], [128, 32], [1, 128]]),
                 ynst[:], s_tp)
        barrier()

        slab_rel = [None, None]
        ost_rel = [s_so.v, s_so.v]
        ptl_rel = [s_dv.v, s_dv.v]
        for ot in range(16):
            b = ot % 2
            if slab_rel[b] is not None:
                w8("gpsimd", s_pe[slab_rel[b][0]], slab_rel[b][1])
            dma("gpsimd", wsl[b][:, 0:8192],
                A(w_op.tensor, ot * 128 * 8192, [[8192, 128], [1, 8192]]), s_w)
            for mi in range(8):
                bank = (ot * 8 + mi) % 2
                pe_take(bank)
                w8("tensor", s_w)
                for ic in range(32):
                    mm(psb[bank][:, 0:256],
                       A(xnar, mi * NI + ic * 128, [[32 * NROW, 128], [1, 128]]),
                       wsl[b][:, ic * 256:(ic + 1) * 256],
                       ic == 0, ic == 31, s_pe[bank] if ic == 31 else None)
                if mi == 7:
                    slab_rel[b] = (bank, s_pe[bank].v)
                pb = (ot * 8 + mi) % 2
                w8("scalar", s_dv, ptl_rel[pb])
                dma("scalar", ptl[pb],
                    A(out.tensor, mi * 128 * DM + ot * 256, [[DM, 128], [1, 256]]),
                    s_gl)
                ob = (ot * 8 + mi) % 2
                use("vector", bank)
                w8("vector", s_gl)
                w8("vector", s_so, ost_rel[ob])
                vec(lambda v, bank=bank, pb=pb, ob=ob: v.scalar_tensor_tensor(
                    ost[ob], psb[bank][:, 0:256], 1.0, ptl[pb],
                    ALU.mult, ALU.add), rel(bank))
                vec(lambda v: v.engine_nop(), s_dv)
                ptl_rel[pb] = s_dv.v
                w8("scalar", s_fr[bank], s_fr[bank].v)
                dma("scalar", A(out.tensor, mi * 128 * DM + ot * 256,
                                [[DM, 128], [1, 256]]), ost[ob], s_so)
                ost_rel[ob] = s_so.v
        w8("scalar", s_so)
        w8("vector", s_so)

    for en in ["sync", "gpsimd", "vector", "scalar", "tensor"]:
        getattr(block, en)(lambda e, en=en: emit(en))

    for g in reversed(ctxs):
        g.__exit__(None, None, None)

    return nc, {}


def kernel(**inputs):
    x = np.asarray(inputs["L0_attn_hidden_states"], dtype=np.float32)
    w_in_t = np.ascontiguousarray(np.asarray(inputs["L0_attn_in_proj_w"],
                                             dtype=np.float32).T)
    w_cm = np.ascontiguousarray(
        w_in_t[:, 4096:10240].reshape(32, 128, 48, 128)
        .transpose(2, 1, 0, 3).reshape(48 * 128, 4096))
    w_tm = np.ascontiguousarray(
        w_in_t[:, 0:4096].reshape(32, 128, 16, 256)
        .transpose(2, 1, 0, 3).reshape(16 * 128, 8192))
    w_dt = np.ascontiguousarray(
        w_in_t[:, 10240:10304].reshape(32, 128, 64)
        .transpose(1, 0, 2).reshape(128, 2048))
    conv_w = np.asarray(inputs["L0_attn_conv_weight"], dtype=np.float32)
    convw = np.ascontiguousarray(
        conv_w.reshape(48, 128, KC).transpose(1, 0, 2).reshape(128, 48 * KC))
    convb = np.ascontiguousarray(
        np.asarray(inputs["L0_attn_conv_bias"], dtype=np.float32).reshape(48, 128).T)
    dtb = np.asarray(inputs["L0_attn_dt_bias"], dtype=np.float32).reshape(1, H)
    alog = np.asarray(inputs["L0_attn_A_log"], dtype=np.float32).reshape(1, H)
    dvec = np.asarray(inputs["L0_attn_D"], dtype=np.float32).reshape(1, H)
    nw = np.asarray(inputs["L0_attn_norm_weight"], dtype=np.float32).reshape(1, DM)
    w_out_t = np.ascontiguousarray(np.asarray(inputs["L0_attn_out_proj_w"],
                                              dtype=np.float32).T)
    w_op = np.ascontiguousarray(
        w_out_t.reshape(32, 128, 16, 256).transpose(2, 1, 0, 3)
        .reshape(16 * 128, 8192))
    gate = np.asarray(inputs["L0_ffn_gate"], dtype=np.float32)
    ffn_k = np.ascontiguousarray(np.asarray(inputs["L0_ffn_kernel"],
                                            dtype=np.float32))
    ffn_kp = np.ascontiguousarray(
        ffn_k.reshape(32, 128, 16, 256).transpose(2, 1, 0, 3)
        .reshape(16 * 128, 8192))
    bias = np.asarray(inputs["L0_ffn_bias"], dtype=np.float32).reshape(1, DM)
    lora_A = np.asarray(inputs["L0_ffn_lora_A"], dtype=np.float32)
    lora_B = np.asarray(inputs["L0_ffn_lora_B"], dtype=np.float32)
    lora_As = np.asarray(inputs["L0_ffn_lora_A_shared"], dtype=np.float32)
    lora_Bs = np.asarray(inputs["L0_ffn_lora_B_shared"], dtype=np.float32)

    a_cat = np.ascontiguousarray(np.concatenate(
        [lora_A.transpose(2, 0, 1).reshape(DM, 512), lora_As.T], axis=1))
    b_gat = np.ascontiguousarray(lora_B.transpose(2, 0, 1).reshape(32, 16 * DM))
    b_shd = np.ascontiguousarray(lora_Bs.T)

    eye = np.eye(128, dtype=np.float32)
    triu01 = np.triu(np.ones((128, 128), dtype=np.float32))
    strict = np.tril(np.ones((128, 128), dtype=np.float32), -1)

    n1, n2 = [int(v) for v in np.random.randint(1, 2 ** 30, size=2)]
    nonce = np.array([[n1] * 4 + [n2] * 4], dtype=np.int32)

    globals()['LN_SERIAL_G'] = LN_SERIAL
    globals()['WSL_TEST_G'] = WSL_TEST
    globals()['LN_NOSTATS_G'] = LN_NOSTATS
    nc, _ = build_program(phases=PHASES, debug=DEBUG)

    in_maps = []
    for c in range(8):
        bb, r = c // 4, c % 4
        q = c % 2
        own0 = r * 1024
        xo = np.zeros((NROW, DM), dtype=np.float32)
        h0 = max(own0 - HALO, 0)
        xo[HALO - (own0 - h0):HALO] = x[bb, h0:own0]
        xo[HALO:HALO + 1024] = x[bb, own0:own0 + 1024]
        xp = np.zeros((NROW, DM), dtype=np.float32)
        if r >= 2:
            pre0 = (r - 2) * 1024
            ph0 = max(pre0 - HALO, 0)
            xp[HALO - (pre0 - ph0):HALO] = x[bb, ph0:pre0]
            xp[HALO:HALO + 1024] = x[bb, pre0:pre0 + 1024]
        prmf = np.zeros((128, 8), dtype=np.float32)
        prmf[:, 0] = 1.0 if r == 2 else 0.0
        prmf[:, 1] = 0.0 if r == 0 else 1.0
        s1, s2 = q * 2 + 0, q * 2 + 1
        rkind = 0 if r in (0, 2) else 1
        rslot = (1 - q) * 2 + rkind
        prmi = np.zeros((1, 16), dtype=np.int32)
        prmi[0, 0:3] = [s1 * 128 * NI, s1 * H, s1 * 4]
        prmi[0, 3:6] = [s2 * 128 * NI, s2 * H, s2 * 4]
        prmi[0, 6:9] = [rslot * 128 * NI, rslot * H, rslot * 4]
        prmi[0, 9] = n1 if rkind == 0 else n2
        in_maps.append({
            "x_own": xo, "x_pre": xp, "w_cm": w_cm, "w_tm": w_tm, "w_dt": w_dt,
            "w_op": w_op, "ffn_kp": ffn_kp,
            "a_cat": a_cat, "b_gat": b_gat, "b_shd": b_shd,
            "bias_r": bias, "nw_r": nw, "dvec_r": dvec, "alog_r": alog,
            "dtb_r": dtb, "convw": convw, "convb": convb,
            "gate_r": np.ascontiguousarray(gate[4 * bb + r].reshape(1, 16)),
            "eye_i": eye, "triu_i": triu01, "stri_i": strict,
            "prm_f": prmf, "prm_i": prmi, "nonce_r": nonce,
        })

    res = run_bass_kernel_spmd(nc, in_maps, core_ids=list(range(8)))
    outf = np.empty((2, 4096, DM), dtype=np.float32)
    for c in range(8):
        bb, r = c // 4, c % 4
        outf[bb, r * 1024:(r + 1) * 1024] = res.results[c]["out"]
    return outf


# version 14
# speedup vs baseline: 1.7647x; 1.0011x over previous
import numpy as np
import ml_dtypes
import concourse.bass as bass
import concourse.mybir as mybir
from concourse.bass_utils import run_bass_kernel_spmd

F32 = mybir.dt.float32
BF16 = mybir.dt.bfloat16
I32 = mybir.dt.int32
AF = mybir.ActivationFunctionType
ALU = mybir.AluOpType
AX = mybir.AxisListType.X
A = bass.AP

DM = 4096
H, P, NS, G, KC = 64, 64, 128, 8, 4
NI = 4096
GN = G * NS
CONV = NI + 2 * GN
PROJ = NI + CONV + H
EPS = 1e-5
Q = 128
NCH = 8
NTOK = Q * NCH
HALO = 4
NROW = 1152
TT = (0, 512, 1024)
TW = (512, 512, HALO)
COL_GATE, COL_X, COL_B, COL_C, COL_DT = 0, NI, 2 * NI, 2 * NI + GN, PROJ - H
PHASES = 9
DEBUG = False
LN_SERIAL = False
WSL_TEST = False
LN_NOSTATS = False
RH0, EV0, SPA, SPB = 0, 1024, 1536, 2048


def build_program(phases=9, debug=False):
    LN_SERIAL = globals().get('LN_SERIAL_G', False)
    WSL_TEST = globals().get('WSL_TEST_G', False)
    LN_NOSTATS = globals().get('LN_NOSTATS_G', False)
    nc = bass.Bass("TRN2", target_bir_lowering=False,
                   detect_race_conditions=False)

    x_own = nc.dram_tensor("x_own", [NROW, DM], F32, kind="ExternalInput").ap()
    x_pre = nc.dram_tensor("x_pre", [NROW, DM], F32, kind="ExternalInput").ap()
    w_cm = nc.dram_tensor("w_cm", [48 * 128, 4096], F32, kind="ExternalInput").ap()
    w_tm = nc.dram_tensor("w_tm", [16 * 128, 8192], F32, kind="ExternalInput").ap()
    w_dt = nc.dram_tensor("w_dt", [128, 2048], F32, kind="ExternalInput").ap()
    w_op = nc.dram_tensor("w_op", [16 * 128, 8192], F32, kind="ExternalInput").ap()
    ffn_kp = nc.dram_tensor("ffn_kp", [16 * 128, 8192], F32,
                            kind="ExternalInput").ap()
    a_cat = nc.dram_tensor("a_cat", [DM, 544], F32, kind="ExternalInput").ap()
    b_gat = nc.dram_tensor("b_gat", [32, 16 * DM], F32, kind="ExternalInput").ap()
    b_shd = nc.dram_tensor("b_shd", [32, DM], F32, kind="ExternalInput").ap()
    bias_r = nc.dram_tensor("bias_r", [1, DM], F32, kind="ExternalInput").ap()
    nw_r = nc.dram_tensor("nw_r", [1, DM], F32, kind="ExternalInput").ap()
    dvec_r = nc.dram_tensor("dvec_r", [1, H], F32, kind="ExternalInput").ap()
    alog_r = nc.dram_tensor("alog_r", [1, H], F32, kind="ExternalInput").ap()
    dtb_r = nc.dram_tensor("dtb_r", [1, H], F32, kind="ExternalInput").ap()
    convw = nc.dram_tensor("convw", [128, 48 * KC], F32, kind="ExternalInput").ap()
    convb = nc.dram_tensor("convb", [128, 48], F32, kind="ExternalInput").ap()
    gate_r = nc.dram_tensor("gate_r", [1, 16], F32, kind="ExternalInput").ap()
    eye_i = nc.dram_tensor("eye_i", [128, 128], F32, kind="ExternalInput").ap()
    triu_i = nc.dram_tensor("triu_i", [128, 128], F32, kind="ExternalInput").ap()
    stri_i = nc.dram_tensor("stri_i", [128, 128], F32, kind="ExternalInput").ap()
    prm_f = nc.dram_tensor("prm_f", [128, 8], F32, kind="ExternalInput").ap()
    prm_i = nc.dram_tensor("prm_i", [1, 16], I32, kind="ExternalInput").ap()
    nonce_r = nc.dram_tensor("nonce_r", [1, 8], I32, kind="ExternalInput").ap()

    out = nc.dram_tensor("out", [NTOK, DM], F32, kind="ExternalOutput").ap()

    pay_T = nc.dram_tensor("pay_T", [4, 128, NI], BF16, kind="Internal",
                           addr_space="Shared").ap()
    pay_D = nc.dram_tensor("pay_D", [4, H, 1], F32, kind="Internal",
                           addr_space="Shared").ap()
    pay_F = nc.dram_tensor("pay_F", [4, 1, 4], I32, kind="Internal",
                           addr_space="Shared").ap()
    _ik = "ExternalOutput" if debug else "Internal"
    xtm_h = nc.dram_tensor("xtm_h", [NCH, 128, NI], BF16, kind=_ik).ap()
    btm_h = nc.dram_tensor("btm_h", [NCH, 128, GN], BF16, kind=_ik).ap()
    bcm_h = nc.dram_tensor("bcm_h", [G, 128, NTOK], BF16, kind=_ik).ap()
    ccm_h = nc.dram_tensor("ccm_h", [G, 128, NTOK], BF16, kind=_ik).ap()
    gate_h = nc.dram_tensor("gate_h", [NCH, 128, DM], BF16, kind=_ik).ap()
    y_h = nc.dram_tensor("y_h", [NCH, 128, NI], BF16, kind=_ik).ap()
    a2_h = nc.dram_tensor("a2_h", [128, 2048], BF16, kind="Internal").ap()
    be_h = nc.dram_tensor("be_h", [32, DM], BF16, kind="Internal").ap()
    d_y = (nc.dram_tensor("d_y", [128, NCH * NI], BF16, kind="ExternalOutput").ap()
           if debug else None)
    d_sin = (nc.dram_tensor("d_sin", [128, NI], BF16, kind="ExternalOutput").ap()
             if debug else None)

    ctxs = []

    def sb(name, shape, dtype):
        g = nc.sbuf_tensor(name, shape, dtype)
        t = g.__enter__()
        ctxs.append(g)
        return t

    def psalloc(name):
        g = nc.psum_tensor(name, [128, 512], F32)
        t = g.__enter__()
        ctxs.append(g)
        return t

    eye = sb("eye", [128, 128], F32)
    triu = sb("triu", [128, 128], F32)
    stri = sb("stri", [128, 128], F32)
    ones1 = sb("ones1", [1, 128], F32)
    ones64 = sb("ones64", [64, 128], F32)
    emsk = sb("emsk", [64, 128], F32)
    zrow = sb("zrow", [1, H], F32)
    zcol = sb("zcol", [64, 1], F32)
    dprc = sb("dprc", [64, 1], F32)
    dwnc = sb("dwnc", [64, 1], F32)
    drvc = sb("drvc", [64, 1], F32)
    rowb = sb("rowb", [1, 4 * H], F32)
    rowb2 = sb("rowb2", [1, 256], F32)
    biasl = sb("biasl", [1, 256], BF16)
    ones1b = sb("ones1b", [1, 128], BF16)
    abc = sb("abc", [128, H], F32)
    dtbbc = sb("dtbbc", [128, H], F32)
    dbc = sb("dbc", [128, H], F32)
    nwt = sb("nwt", [128, 256], BF16)
    cw = sb("cw", [128, 48 * KC], F32)
    cb = sb("cb", [128, 48], F32)
    gcol = sb("gcol", [128, 16], F32)
    prmf = sb("prmf", [128, 8], F32)
    prmi = sb("prmi", [1, 16], I32)
    flagsb = sb("flagsb", [1, 4], I32)
    noncesb = sb("noncesb", [1, 8], I32)
    drbc = sb("drbc", [128, H], F32)
    drb3 = sb("drb3", [128, H], F32)
    drv2 = sb("drv2", [64, 1], F32)
    ccol = sb("ccol", [64, 4], F32)
    emsk2 = sb("emsk2", [64, 64], F32)
    dobc = sb("dobc", [128, H], F32)

    xnar = sb("xnar", [128, 32 * NROW], BF16)
    wsl0 = sb("wsl0", [128, 8192], BF16)
    strip = sb("strip", [128, NTOK + HALO], F32)
    cacc = sb("cacc", [128, NTOK], F32)
    xbcbf = sb("xbcbf", [128, NTOK], BF16)
    tpstg = sb("tpstg", [128, NTOK], BF16)
    ccmc = sb("ccmc", [128, GN], BF16)
    bcmc = sb("bcmc", [128, GN], BF16)
    btmc = sb("btmc", [128, GN], BF16)
    dtp = sb("dtp", [128, 512], F32)
    a_tm = sb("a_tm", [128, 512], F32)
    l_tm = sb("l_tm", [128, 512], F32)
    wexp = sb("wexp", [128, H], F32)
    lamtm = sb("lamtm", [128, H], F32)
    lambc = sb("lambc", [128, H], F32)
    lsg = sb("lsg", [128, H], F32)
    qtm = sb("qtm", [128, H], F32)
    S = sb("S", [128, NI], F32)
    sbfw = sb("sbfw", [128, 512], BF16)
    tpre = sb("tpre", [128, NI], BF16)
    sin = sb("sin", [128, NI], BF16)
    xtm = [sb("xtm0", [128, NI], BF16), sb("xtm1", [128, NI], BF16)]
    ubuf = sb("ubuf", [128, NI], BF16)
    y2b = sb("y2b", [128, DM], F32)
    ssb = sb("ssb", [128, 16], F32)
    gld = [sb("gld0", [128, 512], BF16), sb("gld1", [128, 512], BF16)]
    ynst = sb("ynst", [128, DM], BF16)
    lnst = sb("lnst", [128, 8], F32)

    ystg = gld
    wsl = [wsl0, wsl0 if WSL_TEST else S.bitcast(BF16)]
    ptl = [y2b[:, 3072:3328], y2b[:, 3328:3584]]
    ost = [y2b[:, 3584:3840], y2b[:, 3840:4096]]
    gmsb = ynst
    lpw = ynst
    mpw = ynst
    udw = ynst
    xdw = ynst

    xin = S
    dtraw = l_tm

    psb = [psalloc(f"psb{i}") for i in range(8)]

    class Sem:
        __slots__ = ("h", "v")

        def __init__(self, h):
            self.h = h
            self.v = 0

    def mksem(name):
        g = nc.semaphore(name)
        h = g.__enter__()
        ctxs.append(g)
        return Sem(h)

    s_x = mksem("s_x")
    s_w = mksem("s_w")
    s_tp = mksem("s_tp")
    s_bi = mksem("s_bi")
    s_bo = mksem("s_bo")
    s_gl = mksem("s_gl")
    s_so = mksem("s_so")
    s_pay = mksem("s_pay")
    s_rcv = mksem("s_rcv")
    s_dv = mksem("s_dv")
    s_ac = mksem("s_ac")
    s_gp = mksem("s_gp")
    s_tb = mksem("s_tb")
    s_ab = mksem("s_ab")
    s_ph = mksem("s_ph")
    s_pe = [mksem(f"s_pe{i}") for i in range(8)]
    s_fr = [mksem(f"s_fr{i}") for i in range(8)]
    DMAS = [s_x, s_w, s_tp, s_bi, s_bo, s_gl, s_so, s_pay]
    ALLSEMS = DMAS + [s_rcv, s_dv, s_ac, s_gp, s_tb, s_ab, s_ph] + s_pe + s_fr

    blkg = nc.Block()
    block = blkg.__enter__()
    ctxs.append(blkg)

    def emit(me):
        for _s in ALLSEMS:
            _s.v = 0
        E = {"sync": nc.sync, "gpsimd": nc.gpsimd, "vector": nc.vector,
             "scalar": nc.scalar, "tensor": nc.tensor}
        eng = E[me]
        regs = {}
        if me == "gpsimd":
            for rn in ["s1T", "s1D", "s1F", "s2T", "s2D", "s2F", "rT", "rD",
                       "rF", "non", "r2T", "r2D", "nd", "cnt", "flg"]:
                g = eng.register("r_" + rn)
                regs[rn] = g.__enter__()
                ctxs.append(g)

        def w8(en, sem, v=None):
            if me == en:
                eng.wait_ge(sem.h, sem.v if v is None else v)

        def dma(en, outap, inap, sem):
            sem.v += 16
            if me == en:
                eng.dma_start(out=outap, in_=inap).then_inc(sem.h, 16)
            return sem.v

        def dmat(en, outap, inap, sem):
            sem.v += 16
            if me == en:
                eng.dma_start_transpose(outap, inap).then_inc(sem.h, 16)
            return sem.v

        def mm(outap, lhsT, rhs, start, stop, sem=None):
            if sem is not None:
                sem.v += 1
            if me == "tensor":
                i = eng.matmul(outap, lhsT, rhs, start=start, stop=stop)
                if sem is not None:
                    i.then_inc(sem.h, 1)

        def tr(outap, inap, sem=None, ident=None):
            if sem is not None:
                sem.v += 1
            if me == "tensor":
                i = eng.transpose(outap, inap, eye[:] if ident is None else ident)
                if sem is not None:
                    i.then_inc(sem.h, 1)

        def vec(fn, sem=None):
            if sem is not None:
                sem.v += 1
            if me == "vector":
                i = fn(eng)
                if sem is not None:
                    i.then_inc(sem.h, 1)

        def sca(fn, sem=None):
            if sem is not None:
                sem.v += 1
            if me == "scalar":
                i = fn(eng)
                if sem is not None:
                    i.then_inc(sem.h, 1)

        def gp(fn, sem=None, inc=1):
            if sem is not None:
                sem.v += inc
            if me == "gpsimd":
                i = fn(eng)
                if sem is not None:
                    i.then_inc(sem.h, inc)

        def barrier():
            s_tb.v += 1
            if me == "tensor":
                eng.nop().then_inc(s_tb.h, 1)
            s_ab.v += 1
            if me == "scalar":
                eng.nop().then_inc(s_ab.h, 1)
            s_gp.v += 1
            if me == "gpsimd":
                eng.nop().then_inc(s_gp.h, 1)
            if me == "vector":
                for s in DMAS + [s_tb, s_ab, s_gp]:
                    eng.wait_ge(s.h, s.v)
            s_ph.v += 1
            if me == "vector":
                eng.nop().then_inc(s_ph.h, 1)
            if me != "vector":
                eng.wait_ge(s_ph.h, s_ph.v)

        def l_cm_col(j):
            return strip[0:64, j:j + 1]

        def pe_take(bank):
            w8("tensor", s_fr[bank])

        def use(en, bank):
            w8(en, s_pe[bank])

        def rel(bank):
            return s_fr[bank]

        dma("sync", eye[:], eye_i, s_x)
        dma("sync", triu[:], triu_i, s_x)
        dma("sync", stri[:], stri_i, s_x)
        dma("sync", cw[:], convw, s_x)
        dma("sync", cb[:], convb, s_x)
        dma("sync", prmf[:], prm_f, s_x)
        dma("sync", prmi[:], prm_i, s_x)
        dma("sync", noncesb[:], nonce_r, s_x)
        dma("sync", gcol[:, 0:16], A(gate_r.tensor, 0, [[0, 128], [1, 16]]), s_x)
        dma("sync", rowb[:, 0:H], alog_r, s_x)
        dma("sync", rowb[:, H:2 * H], dtb_r, s_x)
        dma("sync", rowb[:, 2 * H:3 * H], dvec_r, s_x)
        w8("vector", s_x)
        w8("scalar", s_x)
        w8("gpsimd", s_x)
        w8("tensor", s_x)
        vec(lambda v: v.memset(ones1[:], 1.0))
        vec(lambda v: v.memset(ones1b[:], 1.0))
        vec(lambda v: v.memset(ones64[:], 1.0))
        vec(lambda v: v.memset(zcol[:], 0.0))
        vec(lambda v: v.memset(zrow[:], 0.0), s_dv)
        sca(lambda s: s.activation(rowb[:, 3 * H:4 * H], rowb[:, 0:H], AF.Exp), s_ac)
        w8("vector", s_ac)
        vec(lambda v: v.tensor_scalar_mul(rowb[:, 3 * H:4 * H], rowb[:, 3 * H:4 * H],
                                          -1.0), s_dv)
        w8("tensor", s_dv)
        pe_take(0)
        mm(psb[0][:, 0:H], ones1[:], rowb[:, 3 * H:4 * H], True, True)
        mm(psb[0][:, H:2 * H], ones1[:], rowb[:, H:2 * H], True, True)
        mm(psb[0][:, 2 * H:3 * H], ones1[:], rowb[:, 2 * H:3 * H], True, True, s_pe[0])
        use("vector", 0)
        vec(lambda v: v.tensor_copy(abc[:], psb[0][:, 0:H]))
        vec(lambda v: v.tensor_copy(dtbbc[:], psb[0][:, H:2 * H]))
        vec(lambda v: v.tensor_copy(dbc[:], psb[0][:, 2 * H:3 * H]), rel(0))
        if me == "gpsimd":
            names = ["s1T", "s1D", "s1F", "s2T", "s2D", "s2F", "rT", "rD", "rF",
                     "non", "r2T", "r2D"]
            for j, rn in enumerate(names):
                eng.reg_load(regs[rn], prmi[0:1, j:j + 1])
        barrier()

        def ln_phase(x_dram):
            xbufs = [xin, y2b]
            ybufs = [ynst[:], wsl0[:, 0:4096]]
            ubufs = [ubuf[:], wsl0[:, 4096:8192]]
            udst = [A(ubuf, 0, [[NI, 128], [128, 32], [1, 128]]),
                    A(wsl0, 4096, [[8192, 128], [128, 32], [1, 128]])]
            xrel = [s_ac.v, s_ac.v]
            yrel = [s_tp.v, s_tp.v]
            urel = [s_dv.v, s_dv.v]
            pend = None
            for i in range(9):
                b = i % 2
                xb, yb = xbufs[b], ybufs[b]
                c0 = b * 4
                w8("gpsimd", s_ac, xrel[b])
                dma("gpsimd", xb[:], x_dram[i * 128:(i + 1) * 128, :], s_w)
                w8("vector", s_w)
                w8("vector", s_ac, xrel[b])
                for j in range(8):
                    vec(lambda v, j=j, b=b, xb=xb: v.bn_stats(
                        strip[:, b * 48 + j * 6:b * 48 + (j + 1) * 6],
                        xb[:, j * 512:(j + 1) * 512]), s_dv if j == 7 else None)
                w8("vector", s_dv)
                vec(lambda v, c0=c0, b=b: v.bn_aggr(
                    lnst[:, c0:c0 + 2], strip[:, b * 48:b * 48 + 48]), s_dv)
                w8("vector", s_dv)
                vec(lambda v, c0=c0: v.tensor_scalar_add(
                    lnst[:, c0 + 1:c0 + 2], lnst[:, c0 + 1:c0 + 2], EPS), s_dv)
                w8("scalar", s_dv)
                sca(lambda s, c0=c0: s.activation(
                    lnst[:, c0 + 2:c0 + 3], lnst[:, c0 + 1:c0 + 2], AF.Sqrt),
                    s_ac)
                w8("vector", s_ac)
                vec(lambda v, c0=c0: v.reciprocal(
                    lnst[:, c0 + 2:c0 + 3], lnst[:, c0 + 2:c0 + 3]), s_dv)
                w8("vector", s_dv)
                vec(lambda v, c0=c0: v.tensor_scalar(
                    lnst[:, c0 + 3:c0 + 4], lnst[:, c0:c0 + 1],
                    lnst[:, c0 + 2:c0 + 3], -1.0, ALU.mult, ALU.mult), s_dv)
                bias_done = s_dv.v
                if pend is not None:
                    pi, pb, ptp = pend
                    w8("vector", s_tp, ptp)
                    vec(lambda v, pi=pi, pb=pb: v.tensor_copy(
                        A(xnar, pi * 128, [[32 * NROW, 128], [NROW, 32],
                                           [1, 128]]), ubufs[pb]), s_dv)
                    urel[pb] = s_dv.v
                w8("scalar", s_dv, bias_done)
                w8("scalar", s_tp, yrel[b])
                sca(lambda s, xb=xb, yb=yb, c0=c0: s.activation(
                    yb, xb[:], AF.Identity,
                    bias=lnst[:, c0 + 3:c0 + 4], scale=lnst[:, c0 + 2:c0 + 3]),
                    s_ac)
                xrel[b] = s_ac.v
                w8("sync", s_ac)
                w8("sync", s_dv, urel[b])
                dmat("sync", udst[b], yb, s_tp)
                yrel[b] = s_tp.v
                pend = (i, b, s_tp.v)
            pi, pb, ptp = pend
            w8("vector", s_tp, ptp)
            vec(lambda v, pi=pi, pb=pb: v.tensor_copy(
                A(xnar, pi * 128, [[32 * NROW, 128], [NROW, 32], [1, 128]]),
                ubufs[pb]), s_dv)

        def conv_silu(cslot):
            vec(lambda v, cslot=cslot: v.tensor_scalar(
                cacc[:], strip[:, 1:1 + NTOK], cw[:, cslot * KC:cslot * KC + 1],
                None, ALU.mult))
            for k in range(1, KC):
                vec(lambda v, k=k, cslot=cslot: v.scalar_tensor_tensor(
                    cacc[:], strip[:, 1 + k:1 + k + NTOK],
                    cw[:, cslot * KC + k:cslot * KC + k + 1], cacc[:],
                    ALU.mult, ALU.add))
            vec(lambda v: v.engine_nop(), s_dv)
            w8("scalar", s_dv)
            sca(lambda s, cslot=cslot: s.activation(
                xbcbf[:], cacc[:], AF.Silu, bias=cb[:, cslot:cslot + 1]), s_ac)

        def cm_group(plan):
            slab_rel = [None, None]
            cm_group.slab_rel = slab_rel
            strip_rel = 0
            xbc_guard = (s_dv, 0)
            for idx, (kind, cc) in enumerate(plan):
                b = idx % 2
                j = {"x": 0, "bo": 32, "bp": 32, "c": 40}[kind] + cc
                if slab_rel[b] is not None:
                    w8("gpsimd", s_pe[slab_rel[b][0]], slab_rel[b][1])
                dma("gpsimd", wsl[b][:, 0:4096],
                    A(w_cm.tensor, j * 128 * 4096, [[4096, 128], [1, 4096]]),
                    s_w)
                for ti in range(3):
                    bank = (idx * 3 + ti) % 2
                    pe_take(bank)
                    w8("tensor", s_w)
                    for ic in range(32):
                        mm(psb[bank][:, 0:TW[ti]], wsl[b][:, ic * 128:(ic + 1) * 128],
                           A(xnar, ic * NROW + TT[ti], [[32 * NROW, 128], [1, TW[ti]]]),
                           ic == 0, ic == 31, s_pe[bank] if ic == 31 else None)
                    if ti == 2:
                        slab_rel[b] = (bank, s_pe[bank].v)
                    use("scalar", bank)
                    w8("scalar", s_dv, strip_rel)
                    sca(lambda s, bank=bank, ti=ti: s.copy(
                        strip[:, TT[ti]:TT[ti] + TW[ti]], psb[bank][:, 0:TW[ti]]),
                        rel(bank))
                w8("vector", s_fr[0], s_fr[0].v)
                w8("vector", s_fr[1], s_fr[1].v)
                w8("scalar", xbc_guard[0], xbc_guard[1])
                conv_silu({"x": cc, "bo": 32 + cc, "bp": 32 + cc, "c": 40 + cc}[kind])
                strip_rel = s_dv.v
                if kind in ("x", "bp"):
                    w8("sync", s_ac)
                    dmat("sync", A(tpstg, 0, [[NTOK, 128], [128, 8], [1, 128]]),
                         xbcbf[:], s_tp)
                    xbc_guard = (s_tp, s_tp.v)
                    w8("sync", s_tp)
                    if kind == "x":
                        dma("sync", A(xtm_h.tensor, cc * 128,
                                      [[NI, 128], [128 * NI, 8], [1, 128]]), tpstg[:],
                            s_bo)
                    else:
                        dma("sync", A(btm_h.tensor, cc * 128,
                                      [[GN, 128], [128 * GN, 8], [1, 128]]), tpstg[:],
                            s_bo)
                else:
                    w8("sync", s_ac)
                    hb = bcm_h if kind == "bo" else ccm_h
                    dma("sync", hb[cc], xbcbf[:], s_bo)
                    xbc_guard = (s_bo, s_bo.v)

        def tm_group(own):
            tiles = ([("g", COL_GATE + i * 256, 256) for i in range(16)] if own
                     else [])
            tiles.append(("d", COL_DT, 64))
            slab_rel = list(getattr(cm_group, "slab_rel", [None, None]))
            gld_rel = [s_bo.v, s_bo.v]
            for tj, (kind, col, wid) in enumerate(tiles):
                b = tj % 2
                if slab_rel[b] is not None:
                    w8("gpsimd", s_pe[slab_rel[b][0]], slab_rel[b][1])
                wsrc = (A(w_tm.tensor, (col // 256) * 128 * 8192,
                          [[8192, 128], [1, 8192]]) if kind == "g"
                        else A(w_dt.tensor, 0, [[2048, 128], [1, 2048]]))
                dma("gpsimd", wsl[b][:, 0:32 * wid], wsrc, s_w)
                if kind == "g":
                    bank5 = 5
                    pe_take(bank5)
                    w8("tensor", s_dv, s_dv.v)
                    coff = col - COL_GATE
                    w8("sync", s_pe[bank5], s_pe[bank5].v)
                    dma("sync", rowb2[:], A(nw_r.tensor, coff, [[256, 1], [1, 256]]),
                        s_x)
                    w8("tensor", s_x)
                    mm(psb[bank5][:, 0:256], ones1[:], rowb2[:], True, True,
                       s_pe[bank5])
                    use("vector", bank5)
                    vec(lambda v, bank5=bank5: v.tensor_copy(
                        nwt[:], psb[bank5][:, 0:256]), rel(bank5))
                    vec(lambda v: v.engine_nop(), s_dv)
                for mi in range(8):
                    bank = (tj * 8 + mi) % 2
                    pe_take(bank)
                    w8("tensor", s_w)
                    for ic in range(32):
                        mm(psb[bank][:, 0:wid],
                           A(xnar, ic * NROW + HALO + mi * 128,
                             [[32 * NROW, 128], [1, 128]]),
                           wsl[b][:, ic * wid:(ic + 1) * wid],
                           ic == 0, ic == 31, s_pe[bank] if ic == 31 else None)
                    if mi == 7:
                        slab_rel[b] = (bank, s_pe[bank].v)
                    if kind == "g":
                        gb = (tj * 8 + mi) % 2
                        use("scalar", bank)
                        sca(lambda s, bank=bank: s.activation(
                            ynst[:, 0:256], psb[bank][:, 0:256], AF.Silu), s_ac)
                        w8("vector", s_ac)
                        w8("vector", s_bo, gld_rel[gb])
                        w8("vector", s_dv, s_dv.v)
                        vec(lambda v, gb=gb: v.tensor_tensor(
                            gld[gb][:, 0:256], ynst[:, 0:256], nwt[:], ALU.mult),
                            rel(bank))
                        vec(lambda v: v.engine_nop(), s_dv)
                        w8("scalar", s_fr[bank], s_fr[bank].v)
                        coff = col - COL_GATE
                        dma("scalar",
                            A(gate_h.tensor, mi * 128 * DM + coff,
                              [[DM, 128], [1, 256]]), gld[gb][:, 0:256], s_bo)
                        gld_rel[gb] = s_bo.v
                    else:
                        use("vector", bank)
                        vec(lambda v, bank=bank, mi=mi: v.tensor_copy(
                            dtraw[:, mi * 64:(mi + 1) * 64], psb[bank][:, 0:64]),
                            rel(bank))

        def dt_chain():
            vec(lambda v: v.tensor_tensor(
                dtp[:], dtraw[:], A(dtbbc, 0, [[H, 128], [0, 8], [1, H]]), ALU.add))
            vec(lambda v: v.tensor_scalar_mul(y2b[:, SPA:SPA + 512], dtp[:], -1.0))
            vec(lambda v: v.tensor_tensor(y2b[:, SPA:SPA + 512], y2b[:, SPA:SPA + 512],
                                          dtp[:], ALU.min), s_dv)
            w8("scalar", s_dv)
            sca(lambda s: s.activation(y2b[:, SPB:SPB + 512], y2b[:, SPA:SPA + 512],
                                       AF.Exp), s_ac)
            sca(lambda s: s.activation(y2b[:, SPB:SPB + 512], y2b[:, SPB:SPB + 512],
                                       AF.Ln, bias=1.0), s_ac)
            w8("vector", s_ac)
            vec(lambda v: v.tensor_relu(y2b[:, SPA:SPA + 512], dtp[:]))
            vec(lambda v: v.tensor_tensor(dtp[:], y2b[:, SPA:SPA + 512],
                                          y2b[:, SPB:SPB + 512], ALU.add))
            vec(lambda v: v.tensor_tensor(
                a_tm[:], dtp[:], A(abc, 0, [[H, 128], [0, 8], [1, H]]), ALU.mult), s_dv)
            w8("tensor", s_dv)
            for k in range(8):
                bank = 4 + k % 2
                pe_take(bank)
                tr(psb[bank][0:64, 0:128], a_tm[:, k * 64:(k + 1) * 64], s_pe[bank])
                use("vector", bank)
                vec(lambda v, k=k, bank=bank: v.tensor_copy(
                    cacc[0:64, k * 128:(k + 1) * 128], psb[bank][0:64, 0:128]),
                    rel(bank))
            vec(lambda v: v.tensor_tensor_scan(
                strip[0:64, 0:NTOK], cacc[0:64, 0:NTOK],
                A(zcol, 0, [[1, 64], [0, NTOK]]), 0.0, ALU.add, ALU.add), s_dv)
            w8("tensor", s_dv)
            for k in range(8):
                bank = 4 + k % 2
                pe_take(bank)
                tr(psb[bank][:, 0:64], strip[0:64, k * 128:(k + 1) * 128],
                   s_pe[bank], ident=eye[0:64, 0:64])
                use("vector", bank)
                vec(lambda v, k=k, bank=bank: v.tensor_copy(
                    l_tm[:, k * 64:(k + 1) * 64], psb[bank][:, 0:64]), rel(bank))
            vec(lambda v: v.engine_nop(), s_dv)

        def chunk_decays(k):
            bank = 5
            w8("vector", s_pe[bank], s_pe[bank].v)
            vec(lambda v, k=k: v.tensor_scalar(
                emsk[:, 0:64], eye[0:64, 0:64],
                l_cm_col(k * 128 + 127), None, ALU.mult))
            prevcol = zcol[:, 0:1] if k == 0 else l_cm_col(k * 128 - 1)
            vec(lambda v, prevcol=prevcol: v.tensor_scalar(
                emsk[:, 64:128], eye[0:64, 0:64], prevcol, None, ALU.mult), s_dv)
            pe_take(bank)
            w8("tensor", s_dv)
            mm(psb[bank][:, 0:H], ones64[:], emsk[:, 0:64], True, True)
            mm(psb[bank][:, H:2 * H], ones64[:], emsk[:, 64:128], True, True,
               s_pe[bank])
            use("vector", bank)
            vec(lambda v, bank=bank, k=k: v.tensor_tensor(
                y2b[:, SPA:SPA + H], psb[bank][:, 0:H], l_tm[:, k * 64:(k + 1) * 64],
                ALU.subtract))
            vec(lambda v, bank=bank, k=k: v.tensor_tensor(
                y2b[:, SPA + H:SPA + 2 * H], l_tm[:, k * 64:(k + 1) * 64],
                psb[bank][:, H:2 * H], ALU.subtract))
            vec(lambda v: v.tensor_tensor(
                y2b[:, SPA + 2 * H:SPA + 3 * H], y2b[:, SPA:SPA + H],
                y2b[:, SPA + H:SPA + 2 * H], ALU.add), rel(bank))
            vec(lambda v: v.engine_nop(), s_dv)
            w8("scalar", s_dv)
            sca(lambda s: s.activation(wexp[:], y2b[:, SPA:SPA + H], AF.Exp))
            sca(lambda s: s.activation(lamtm[:], y2b[:, SPA + H:SPA + 2 * H], AF.Exp))
            sca(lambda s: s.activation(lambc[:], y2b[:, SPA + 2 * H:SPA + 3 * H],
                                       AF.Exp), s_ac)
            w8("vector", s_ac)
            vec(lambda v, k=k: v.tensor_tensor(
                qtm[:], dtp[:, k * 64:(k + 1) * 64], wexp[:], ALU.mult), s_dv)

        def scan_phase(own):
            vec(lambda v: v.memset(S[:], 0.0), s_dv)
            dt_chain()
            ystg_rel = [s_bo.v, s_bo.v]
            xtm_rel = [s_dv.v, s_dv.v]
            lpw_rel = s_dv.v
            btm_rel = s_pe[4].v
            bc_rel = s_pe[3].v
            dma("sync", xtm[0][:], xtm_h[0], s_bi)
            xtm_ld = [s_bi.v, 0]
            for k in range(NCH):
                bx = k % 2
                if k + 1 < NCH:
                    w8("sync", s_dv, xtm_rel[(k + 1) % 2])
                    dma("sync", xtm[(k + 1) % 2][:], xtm_h[k + 1], s_bi)
                    xtm_ld[(k + 1) % 2] = s_bi.v
                chunk_decays(k)
                w8("sync", s_pe[4], btm_rel)
                if own:
                    w8("sync", s_pe[3], bc_rel)
                    dma("sync", bcmc[:],
                        A(bcm_h.tensor, k * 128,
                          [[NTOK, 128], [128 * NTOK, 8], [1, 128]]), s_bi)
                    dma("sync", ccmc[:],
                        A(ccm_h.tensor, k * 128,
                          [[NTOK, 128], [128 * NTOK, 8], [1, 128]]), s_bi)
                    w8("sync", s_bi)
                    for g8 in range(8):
                        dmat("sync", A(btmc, g8 * 128, [[GN, 128], [1, 128]]),
                             bcmc[:, g8 * 128:(g8 + 1) * 128], s_tp)
                    btm_ld = (s_tp, s_tp.v)
                    ccm_ld = s_bi.v
                else:
                    dma("sync", btmc[:], btm_h[k], s_bi)
                    btm_ld = (s_bi, s_bi.v)
                w8("vector", s_bi, xtm_ld[bx])
                if own:
                    vec(lambda v, k=k, bx=bx: v.tensor_tensor(
                        ubuf[:], xtm[bx][:],
                        A(dtp, k * 64, [[512, 128], [1, 64], [0, 64]]), ALU.mult),
                        s_dv)
                    w8("tensor", s_dv)
                    w8("tensor", s_bi, ccm_ld)
                    for g8 in range(8):
                        bank = 2 + g8 // 4
                        if g8 % 4 == 0:
                            pe_take(bank)
                        mm(psb[bank][:, (g8 % 4) * 128:(g8 % 4) * 128 + 128],
                           bcmc[:, g8 * 128:(g8 + 1) * 128],
                           ccmc[:, g8 * 128:(g8 + 1) * 128],
                           True, True, s_pe[bank] if g8 % 4 == 3 else None)
                    use("vector", 2)
                    vec(lambda v: v.tensor_tensor(
                        ynst[:, 0:512], psb[2][:],
                        A(triu, 0, [[128, 128], [0, 4], [1, 128]]), ALU.mult), rel(2))
                    use("vector", 3)
                    vec(lambda v: v.tensor_tensor(
                        ynst[:, 512:1024], psb[3][:],
                        A(triu, 0, [[128, 128], [0, 4], [1, 128]]), ALU.mult), rel(3))
                    bc_rel = s_pe[3].v
                for wv in range(8):
                    h0 = wv * 8
                    if own:
                        w8("vector", s_pe[3], s_pe[3].v)
                        vec(lambda v, wv=wv: v.tensor_copy(
                            sbfw[:], S[:, wv * 512:(wv + 1) * 512]), s_dv)
                        w8("vector", s_pe[7], s_pe[7].v)
                        vec(lambda v, k=k, h0=h0: v.tensor_tensor(
                            y2b[:, RH0:RH0 + 1024],
                            A(a_tm, k * 64 + h0, [[512, 128], [1, 8], [0, 128]]),
                            A(triu, 0, [[128, 128], [0, 8], [1, 128]]), ALU.mult),
                            s_dv)
                        w8("tensor", s_dv)
                        pe_take(6)
                        mm(psb[6][:], stri[:], y2b[:, RH0:RH0 + 512], True, True,
                           s_pe[6])
                        pe_take(7)
                        mm(psb[7][:], stri[:], y2b[:, RH0 + 512:RH0 + 1024], True,
                           True, s_pe[7])
                        use("scalar", 6)
                        w8("scalar", s_dv, lpw_rel)
                        sca(lambda s: s.activation(ynst[:, 1024:1536], psb[6][:], AF.Exp),
                            rel(6))
                        use("scalar", 7)
                        sca(lambda s: s.activation(ynst[:, 1536:2048], psb[7][:],
                                                   AF.Exp), rel(7))
                        sca(lambda s: s.nop(), s_ac)
                        w8("vector", s_ac)
                        w8("vector", s_pe[2], s_pe[2].v)
                        vec(lambda v, wv=wv: v.tensor_tensor(
                            ynst[:, 2048:3072], ynst[:, 1024:2048],
                            A(ynst, wv * 128, [[DM, 128], [0, 8], [1, 128]]),
                            ALU.mult), s_dv)
                        lpw_rel = s_dv.v
                        w8("tensor", s_dv)
                        pe_take(2)
                        pe_take(3)
                        w8("tensor", btm_ld[0], btm_ld[1])
                        for hh in range(8):
                            mm(psb[2][:, hh * 64:(hh + 1) * 64],
                               ynst[:, 2048 + hh * 128:2048 + (hh + 1) * 128],
                               ubuf[:, (h0 + hh) * 64:(h0 + hh + 1) * 64],
                               True, True, s_pe[2] if hh == 7 else None)
                        for hh in range(8):
                            mm(psb[3][:, hh * 64:(hh + 1) * 64],
                               ccmc[:, wv * 128:(wv + 1) * 128],
                               sbfw[:, hh * 64:(hh + 1) * 64],
                               True, True, s_pe[3] if hh == 7 else None)
                    w8("vector", s_pe[4], s_pe[4].v)
                    vec(lambda v, wv=wv, bx=bx, h0=h0: v.tensor_tensor(
                        ynst[:, 3072:3584], xtm[bx][:, wv * 512:(wv + 1) * 512],
                        A(qtm, h0, [[H, 128], [1, 8], [0, 64]]), ALU.mult), s_dv)
                    w8("tensor", s_dv)
                    if not own:
                        w8("tensor", btm_ld[0], btm_ld[1])
                    pe_take(4)
                    for hh in range(8):
                        mm(psb[4][:, hh * 64:(hh + 1) * 64],
                           btmc[:, wv * 128:(wv + 1) * 128],
                           ynst[:, 3072 + hh * 64:3072 + (hh + 1) * 64],
                           True, True, s_pe[4] if hh == 7 else None)
                    if wv == 7:
                        btm_rel = s_pe[4].v
                    if own:
                        vec(lambda v, wv=wv, bx=bx, h0=h0: v.tensor_tensor(
                            ynst[:, 3584:4096], xtm[bx][:, wv * 512:(wv + 1) * 512],
                            A(dbc, h0, [[H, 128], [1, 8], [0, 64]]), ALU.mult))
                        use("vector", 3)
                        vec(lambda v, h0=h0: v.tensor_tensor(
                            y2b[:, EV0:EV0 + 512], psb[3][:],
                            A(lamtm, h0, [[H, 128], [1, 8], [0, 64]]), ALU.mult),
                            rel(3))
                        use("vector", 2)
                        vec(lambda v: v.scalar_tensor_tensor(
                            y2b[:, EV0:EV0 + 512], psb[2][:], 1.0,
                            y2b[:, EV0:EV0 + 512], ALU.mult, ALU.add), rel(2))
                        yb = (k * 8 + wv) % 2
                        w8("vector", s_bo, ystg_rel[yb])
                        vec(lambda v, yb=yb: v.tensor_tensor(
                            ystg[yb][:], y2b[:, EV0:EV0 + 512],
                            ynst[:, 3584:4096], ALU.add), s_dv)
                        w8("sync", s_dv)
                        dma("sync", A(y_h.tensor, k * 128 * NI + wv * 512,
                                      [[NI, 128], [1, 512]]), ystg[yb][:], s_bo)
                        ystg_rel[yb] = s_bo.v
                    use("vector", 4)
                    vec(lambda v, wv=wv, h0=h0: v.tensor_tensor(
                        S[:, wv * 512:(wv + 1) * 512], S[:, wv * 512:(wv + 1) * 512],
                        A(lambc, h0, [[H, 128], [1, 8], [0, 64]]), ALU.mult))
                    vec(lambda v, wv=wv: v.tensor_tensor(
                        S[:, wv * 512:(wv + 1) * 512], S[:, wv * 512:(wv + 1) * 512],
                        psb[4][:], ALU.add), rel(4))
                    vec(lambda v: v.engine_nop(), s_dv)
                    if wv == 7:
                        xtm_rel[bx] = s_dv.v

        ln_phase(x_pre)
        barrier()
        plan_pre = [("x", cc) for cc in range(32)] + [("bp", cc) for cc in range(8)]
        cm_group(plan_pre)
        tm_group(own=False)
        ac_rel = s_dv.v
        for icg in range(8):
            w8("sync", s_dv, ac_rel)
            dma("sync", y2b[:, 0:2176],
                A(a_cat.tensor, icg * 4 * 128 * 544,
                  [[544, 128], [128 * 544, 4], [1, 544]]), s_x)
            w8("vector", s_x)
            for n in range(16):
                for j in range(4):
                    ic = icg * 4 + j
                    if n == 0:
                        vec(lambda v, ic=ic, j=j: v.tensor_scalar(
                            ubuf[:, ic * 64:ic * 64 + 32],
                            y2b[:, j * 544:j * 544 + 32],
                            gcol[:, 0:1], None, ALU.mult))
                    else:
                        vec(lambda v, ic=ic, j=j, n=n: v.scalar_tensor_tensor(
                            ubuf[:, ic * 64:ic * 64 + 32],
                            y2b[:, j * 544 + n * 32:j * 544 + n * 32 + 32],
                            gcol[:, n:n + 1], ubuf[:, ic * 64:ic * 64 + 32],
                            ALU.mult, ALU.add))
            for j in range(4):
                ic = icg * 4 + j
                vec(lambda v, ic=ic, j=j: v.tensor_copy(
                    ubuf[:, ic * 64 + 32:ic * 64 + 64],
                    y2b[:, j * 544 + 512:j * 544 + 544]),
                    s_dv if j == 3 else None)
            ac_rel = s_dv.v
        bg_rel = [s_dv.v, s_dv.v]
        for n in range(16):
            w8("sync", s_dv, bg_rel[0])
            dma("sync", y2b[0:32, 0:DM], b_gat[:, n * DM:(n + 1) * DM], s_x)
            w8("vector", s_x)
            if n == 0:
                vec(lambda v: v.tensor_scalar(xtm[0][0:32, 0:DM], y2b[0:32, 0:DM],
                                              gcol[0:32, 0:1], None, ALU.mult), s_dv)
            else:
                vec(lambda v, n=n: v.scalar_tensor_tensor(
                    xtm[0][0:32, 0:DM], y2b[0:32, 0:DM],
                    gcol[0:32, n:n + 1], xtm[0][0:32, 0:DM], ALU.mult, ALU.add),
                    s_dv)
            bg_rel[0] = s_dv.v
        vec(lambda v: v.engine_nop(), s_dv)
        w8("sync", s_dv)
        dma("sync", a2_h, ubuf[:, 0:2048], s_bo)
        dma("sync", be_h, xtm[0][0:32, 0:DM], s_bo)
        barrier()
        scan_phase(own=False)
        vec(lambda v: v.tensor_copy(tpre[:], S[:]), s_dv)
        w8("scalar", s_dv)
        sca(lambda s: s.activation(dprc[:], l_cm_col(1023), AF.Exp), s_ac)
        w8("gpsimd", s_dv)
        w8("gpsimd", s_ac)
        gp(lambda g: g.dma_start(
            out=A(pay_T.tensor, regs["s1T"] if me == "gpsimd" else 0,
                  [[NI, 128], [1, NI]]), in_=tpre[:]), s_pay, 16)
        gp(lambda g: g.dma_start(
            out=A(pay_D.tensor, regs["s1D"] if me == "gpsimd" else 0,
                  [[1, 64], [1, 1]]), in_=dprc[:]), s_pay, 16)
        w8("gpsimd", s_pay)
        gp(lambda g: g.dma_start(
            out=A(pay_F.tensor, regs["s1F"] if me == "gpsimd" else 0,
                  [[4, 1], [1, 4]]), in_=noncesb[:, 0:4]), s_pay, 16)
        barrier()
        if phases <= 0:
            w8("scalar", s_pay)
            return
        ln_phase(x_own)
        barrier()

        plan_own = ([("x", cc) for cc in range(32)] + [("bo", cc) for cc in range(8)]
                    + [("c", cc) for cc in range(8)])
        cm_group(plan_own)
        tm_group(own=True)
        barrier()
        scan_phase(own=True)
        w8("scalar", s_dv, s_dv.v)
        sca(lambda s: s.activation(dwnc[:], l_cm_col(1023), AF.Exp), s_ac)
        w8("vector", s_pe[2], s_pe[2].v)
        w8("vector", s_pe[3], s_pe[3].v)
        w8("vector", s_pe[4], s_pe[4].v)
        vec(lambda v: v.tensor_copy(ubuf[:], S[:]), s_dv)
        w8("gpsimd", s_dv)
        w8("gpsimd", s_ac)
        gp(lambda g: g.dma_start(
            out=A(pay_T.tensor, regs["s2T"] if me == "gpsimd" else 0,
                  [[NI, 128], [1, NI]]), in_=ubuf[:]), s_pay, 16)
        gp(lambda g: g.dma_start(
            out=A(pay_D.tensor, regs["s2D"] if me == "gpsimd" else 0,
                  [[1, 64], [1, 1]]), in_=dwnc[:]), s_pay, 16)
        w8("gpsimd", s_pay)
        gp(lambda g: g.dma_start(
            out=A(pay_F.tensor, regs["s2F"] if me == "gpsimd" else 0,
                  [[4, 1], [1, 4]]), in_=noncesb[:, 4:8]), s_pay, 16)
        w8("sync", s_pay, s_pay.v)
        w8("sync", s_dv, s_dv.v)
        dma("sync", ubuf[:, 0:2048], a2_h, s_bi)
        dma("sync", xtm[0][0:32, 0:DM], be_h, s_bi)
        w8("gpsimd", s_dv, s_dv.v)
        dma("gpsimd", xtm[1][0:32, 0:DM], b_shd, s_w)
        w8("tensor", s_bi)
        w8("tensor", s_dv)
        w8("tensor", s_w)
        for tt2 in range(2):
            for half in range(2):
                bank = 2 + half
                pe_take(bank)
                for ic in range(32):
                    mm(psb[bank][0:32, 0:512],
                       ubuf[:, ic * 64 + half * 32:ic * 64 + half * 32 + 32],
                       A(xnar, ic * NROW + HALO + tt2 * 512,
                         [[32 * NROW, 128], [1, 512]]),
                       ic == 0, ic == 31, s_pe[bank] if ic == 31 else None)
                use("vector", bank)
                doff = 2048 + half * 1024 + tt2 * 512
                vec(lambda v, doff=doff, bank=bank: v.tensor_copy(
                    ubuf[0:32, doff:doff + 512], psb[bank][0:32, 0:512]), rel(bank))
        vec(lambda v: v.engine_nop(), s_dv)
        w8("tensor", s_dv)
        slab_rel = [None, None]
        ost_rel = [s_so.v, s_so.v]
        ptl_rel = [s_dv.v, s_dv.v]
        bias_rel = None
        for ot in range(16):
            b = ot % 2
            if slab_rel[b] is not None:
                w8("gpsimd", s_pe[slab_rel[b][0]], slab_rel[b][1])
            dma("gpsimd", wsl[b][:, 0:8192],
                A(ffn_kp.tensor, ot * 128 * 8192, [[8192, 128], [1, 8192]]), s_w)
            if bias_rel is not None:
                w8("gpsimd", s_pe[bias_rel[0]], bias_rel[1])
            dma("gpsimd", biasl[:],
                A(bias_r.tensor, ot * 256, [[256, 1], [1, 256]]), s_w)
            for mi in range(8):
                bank = (ot * 8 + mi) % 2
                pe_take(bank)
                w8("tensor", s_w)
                for ic in range(32):
                    mm(psb[bank][:, 0:256],
                       A(xnar, ic * NROW + HALO + mi * 128,
                         [[32 * NROW, 128], [1, 128]]),
                       wsl[b][:, ic * 256:(ic + 1) * 256], ic == 0, False)
                mm(psb[bank][:, 0:256], ubuf[0:32, 2048 + mi * 128:2048 + mi * 128 + 128],
                   xtm[0][0:32, ot * 256:(ot + 1) * 256], False, False)
                mm(psb[bank][:, 0:256], ubuf[0:32, 3072 + mi * 128:3072 + mi * 128 + 128],
                   xtm[1][0:32, ot * 256:(ot + 1) * 256], False, False)
                mm(psb[bank][:, 0:256], ones1b[:], biasl[:], False, True,
                   s_pe[bank])
                if mi == 7:
                    slab_rel[b] = (bank, s_pe[bank].v)
                    bias_rel = (bank, s_pe[bank].v)
                pb = (ot * 8 + mi) % 2
                w8("scalar", s_dv, ptl_rel[pb])
                dma("scalar", ptl[pb],
                    A(x_own.tensor, (HALO + mi * 128) * DM + ot * 256,
                      [[DM, 128], [1, 256]]), s_gl)
                ob = (ot * 8 + mi) % 2
                use("vector", bank)
                w8("vector", s_gl)
                w8("vector", s_so, ost_rel[ob])
                vec(lambda v, bank=bank, pb=pb, ob=ob: v.scalar_tensor_tensor(
                    ost[ob], psb[bank][:, 0:256], 1.0, ptl[pb],
                    ALU.mult, ALU.add), rel(bank))
                vec(lambda v: v.engine_nop(), s_dv)
                ptl_rel[pb] = s_dv.v
                w8("scalar", s_fr[bank], s_fr[bank].v)
                dma("scalar", A(out.tensor, mi * 128 * DM + ot * 256,
                                [[DM, 128], [1, 256]]), ost[ob], s_so)
                ost_rel[ob] = s_so.v
        barrier()
        if phases <= 1:
            if debug:
                dma("sync", d_y, A(xnar, 0, [[32 * NROW, 128], [1, NCH * NI]]), s_bo)
                w8("sync", s_bo)
            w8("scalar", s_so)
            w8("vector", s_so)
            return

        if debug:
            w8("sync", s_bo, s_bo.v)
            dma("sync", d_y, A(y_h.tensor, 0, [[NI, 128], [128 * NI, 8], [1, NI]]),
                s_bo)
        barrier()
        if phases <= 2:
            w8("scalar", s_bo)
            w8("vector", s_bo)
            return

        if me == "gpsimd":
            eng.reg_mov(regs["nd"], 1)
            eng.reg_mov(regs["cnt"], s_rcv.v)
            wg = eng.While(regs["nd"])
            wg.__enter__()
            eng.reg_add(regs["cnt"], regs["cnt"], 16)
            eng.dma_start(out=flagsb[:],
                          in_=A(pay_F.tensor, regs["rF"], [[4, 1], [1, 4]])
                          ).then_inc(s_rcv.h, 16)
            eng.wait_ge(s_rcv.h, regs["cnt"])
            eng.reg_load(regs["flg"], flagsb[0:1, 0:1])
            eng.reg_sub(regs["nd"], regs["flg"], regs["non"])
            wg.__exit__(None, None, None)
        w8("gpsimd", s_dv, s_dv.v)
        w8("gpsimd", s_pe[0], s_pe[0].v)
        w8("gpsimd", s_pe[1], s_pe[1].v)
        gp(lambda g: g.dma_start(
            out=sin[:], in_=A(pay_T.tensor, regs["rT"] if me == "gpsimd" else 0,
                              [[NI, 128], [1, NI]])), s_pay, 16)
        gp(lambda g: g.dma_start(
            out=drvc[:], in_=A(pay_D.tensor, regs["rD"] if me == "gpsimd" else 0,
                               [[1, 64], [1, 1]])), s_pay, 16)
        gp(lambda g: g.dma_start(
            out=ubuf[:], in_=A(pay_T.tensor, regs["r2T"] if me == "gpsimd" else 0,
                               [[NI, 128], [1, NI]])), s_pay, 16)
        gp(lambda g: g.dma_start(
            out=drv2[:], in_=A(pay_D.tensor, regs["r2D"] if me == "gpsimd" else 0,
                               [[1, 64], [1, 1]])), s_pay, 16)
        w8("vector", s_pay)
        vec(lambda v: v.tensor_tensor(ccol[:, 0:1], drv2[:], dprc[:], ALU.mult), s_dv)
        w8("vector", s_dv)
        vec(lambda v: v.tensor_scalar(ccol[:, 0:1], ccol[:, 0:1], prmf[0:64, 1:2],
                                      prmf[0:64, 0:1], ALU.mult, ALU.add), s_dv)
        w8("vector", s_dv)
        vec(lambda v: v.tensor_scalar(ccol[:, 1:2], drvc[:], prmf[0:64, 2:3], None,
                                      ALU.mult), s_dv)
        w8("vector", s_dv)
        vec(lambda v: v.scalar_tensor_tensor(ccol[:, 1:2], drv2[:],
                                             prmf[0:64, 3:4], ccol[:, 1:2],
                                             ALU.mult, ALU.add), s_dv)
        w8("vector", s_dv)
        vec(lambda v: v.tensor_copy(ccol[:, 2:3], prmf[0:64, 4:5]), s_dv)
        w8("vector", s_dv)
        w8("vector", s_pe[5], s_pe[5].v)
        for j in range(3):
            vec(lambda v, j=j: v.tensor_scalar(
                emsk[:, j * 64:(j + 1) * 64] if j < 2 else emsk2[:, 0:64],
                eye[0:64, 0:64], ccol[:, j:j + 1], None, ALU.mult), s_dv)
        pe_take(5)
        w8("tensor", s_dv)
        mm(psb[5][:, 0:H], ones64[:], emsk[:, 0:64], True, True)
        mm(psb[5][:, H:2 * H], ones64[:], emsk[:, 64:128], True, True)
        mm(psb[5][:, 2 * H:3 * H], ones64[:], emsk2[:, 0:64], True, True, s_pe[5])
        use("vector", 5)
        vec(lambda v: v.tensor_copy(drbc[:], psb[5][:, 0:H]))
        vec(lambda v: v.tensor_copy(dobc[:], psb[5][:, H:2 * H]))
        vec(lambda v: v.tensor_copy(drb3[:], psb[5][:, 2 * H:3 * H]), rel(5))
        vec(lambda v: v.tensor_tensor(
            y2b[:], sin[:], A(drbc, 0, [[H, 128], [1, 64], [0, 64]]), ALU.mult))
        vec(lambda v: v.tensor_tensor(
            sin[:], tpre[:], A(dobc, 0, [[H, 128], [1, 64], [0, 64]]), ALU.mult))
        vec(lambda v: v.tensor_tensor(y2b[:], y2b[:], sin[:], ALU.add))
        vec(lambda v: v.tensor_tensor(
            sin[:], ubuf[:], A(drb3, 0, [[H, 128], [1, 64], [0, 64]]), ALU.mult))
        vec(lambda v: v.tensor_tensor(y2b[:], y2b[:], sin[:], ALU.add))
        vec(lambda v: v.tensor_copy(sin[:], y2b[:]), s_dv)
        if debug:
            w8("sync", s_dv, s_dv.v)
            dma("sync", d_sin, sin[:], s_bo)
        barrier()
        if phases <= 3:
            w8("scalar", s_bo)
            w8("vector", s_bo)
            return

        gch_rel = [s_dv.v, s_dv.v]
        dma("sync", xtm[0][:], gate_h[0], s_gl)
        gch_ld = [s_gl.v, 0]
        yld = [0] * NCH
        dma("sync", A(xnar, 0, [[32 * NROW, 128], [1, NI]]), y_h[0], s_bi)
        yld[0] = s_bi.v
        ccld = [0] * NCH
        pev = [None] * NCH

        def csl(k, a, b):
            return (ccmc if k % 2 == 0 else tpre)[:, a:b]

        dma("sync", csl(0, 0, 1024),
            A(ccm_h.tensor, 0, [[NTOK, 128], [128 * NTOK, 8], [1, 128]]), s_bi)
        ccld[0] = s_bi.v
        lsgb = [lsg[:], strip[:, 0:64]]
        expv = [0] * NCH
        sca(lambda s: s.activation(lsgb[0], l_tm[:, 0:64], AF.Exp), s_ac)
        expv[0] = s_ac.v
        for k in range(NCH):
            gb = k % 2
            if k + 1 < NCH:
                w8("sync", s_dv, gch_rel[(k + 1) % 2])
                dma("sync", xtm[(k + 1) % 2][:], gate_h[k + 1], s_gl)
                gch_ld[(k + 1) % 2] = s_gl.v
                dma("sync", A(xnar, (k + 1) * NI, [[32 * NROW, 128], [1, NI]]),
                    y_h[k + 1], s_bi)
                yld[k + 1] = s_bi.v
                if k >= 1:
                    w8("sync", s_pe[0], pev[k - 1][0])
                    w8("sync", s_pe[1], pev[k - 1][1])
                dma("sync", csl(k + 1, 0, 1024),
                    A(ccm_h.tensor, (k + 1) * 128,
                      [[NTOK, 128], [128 * NTOK, 8], [1, 128]]), s_bi)
                ccld[k + 1] = s_bi.v
            w8("vector", s_bi, yld[k])
            w8("tensor", s_bi, yld[k])
            w8("tensor", s_ac, expv[k])
            w8("tensor", s_bi, ccld[k])
            for wv in range(8):
                bank = wv % 2
                pe_take(bank)
                for hh in range(8):
                    mm(psb[bank][:, hh * 64:(hh + 1) * 64],
                       csl(k, wv * 128, (wv + 1) * 128),
                       sin[:, (wv * 8 + hh) * 64:(wv * 8 + hh + 1) * 64],
                       True, True, s_pe[bank] if hh == 7 else None)
                use("vector", bank)
                lap = (A(lsg, wv * 8, [[H, 128], [1, 8], [0, 64]]) if k % 2 == 0
                       else A(strip, wv * 8, [[NTOK + HALO, 128], [1, 8], [0, 64]]))
                vec(lambda v, bank=bank, lap=lap: v.tensor_tensor(
                    y2b[:, EV0:EV0 + 512], psb[bank][:], lap, ALU.mult), rel(bank))
                vec(lambda v, k=k, wv=wv: v.tensor_tensor(
                    A(xnar, k * NI + wv * 512, [[32 * NROW, 128], [1, 512]]),
                    A(xnar, k * NI + wv * 512, [[32 * NROW, 128], [1, 512]]),
                    y2b[:, EV0:EV0 + 512], ALU.add), s_dv)
            pev[k] = (s_pe[0].v, s_pe[1].v)
            if k + 1 < NCH:
                w8("scalar", s_dv, s_dv.v)
                sca(lambda s, k=k: s.activation(
                    lsgb[(k + 1) % 2], l_tm[:, (k + 1) * 64:(k + 2) * 64],
                    AF.Exp), s_ac)
                expv[k + 1] = s_ac.v
            w8("scalar", s_dv)
            sca(lambda s, k=k: s.activation(
                y2b[:], A(xnar, k * NI, [[32 * NROW, 128], [1, NI]]), AF.Square),
                s_ac)
            w8("vector", s_ac)
            for g8 in range(8):
                vec(lambda v, g8=g8: v.reduce_sum(
                    ssb[:, g8:g8 + 1], y2b[:, g8 * 512:(g8 + 1) * 512], AX),
                    s_dv if g8 == 7 else None)
            w8("vector", s_dv)
            vec(lambda v: v.tensor_scalar(ssb[:, 0:8], ssb[:, 0:8], 1.0 / 512.0, EPS,
                                          ALU.mult, ALU.add), s_dv)
            w8("vector", s_dv)
            vec(lambda v: v.reciprocal(ssb[:, 8:16], ssb[:, 0:8]), s_dv)
            w8("scalar", s_dv)
            sca(lambda s: s.activation(ssb[:, 8:16], ssb[:, 8:16], AF.Sqrt), s_ac)
            w8("vector", s_ac)
            w8("vector", s_gl, gch_ld[gb])
            w8("vector", s_tp, s_tp.v)
            vec(lambda v, k=k: v.tensor_tensor(
                ynst[:], A(xnar, k * NI, [[32 * NROW, 128], [1, NI]]),
                A(ssb, 8, [[16, 128], [1, 8], [0, 512]]), ALU.mult))
            vec(lambda v, gb=gb: v.tensor_tensor(ynst[:], ynst[:], xtm[gb][:],
                                                 ALU.mult), s_dv)
            gch_rel[gb] = s_dv.v
            w8("sync", s_dv)
            dmat("sync", A(xnar, k * NI, [[32 * NROW, 128], [128, 32], [1, 128]]),
                 ynst[:], s_tp)
        barrier()

        slab_rel = [None, None]
        ost_rel = [s_so.v, s_so.v]
        ptl_rel = [s_dv.v, s_dv.v]
        for ot in range(16):
            b = ot % 2
            if slab_rel[b] is not None:
                w8("gpsimd", s_pe[slab_rel[b][0]], slab_rel[b][1])
            dma("gpsimd", wsl[b][:, 0:8192],
                A(w_op.tensor, ot * 128 * 8192, [[8192, 128], [1, 8192]]), s_w)
            for mi in range(8):
                bank = (ot * 8 + mi) % 2
                pe_take(bank)
                w8("tensor", s_w)
                for ic in range(32):
                    mm(psb[bank][:, 0:256],
                       A(xnar, mi * NI + ic * 128, [[32 * NROW, 128], [1, 128]]),
                       wsl[b][:, ic * 256:(ic + 1) * 256],
                       ic == 0, ic == 31, s_pe[bank] if ic == 31 else None)
                if mi == 7:
                    slab_rel[b] = (bank, s_pe[bank].v)
                pb = (ot * 8 + mi) % 2
                w8("scalar", s_dv, ptl_rel[pb])
                dma("scalar", ptl[pb],
                    A(out.tensor, mi * 128 * DM + ot * 256, [[DM, 128], [1, 256]]),
                    s_gl)
                ob = (ot * 8 + mi) % 2
                use("vector", bank)
                w8("vector", s_gl)
                w8("vector", s_so, ost_rel[ob])
                vec(lambda v, bank=bank, pb=pb, ob=ob: v.scalar_tensor_tensor(
                    ost[ob], psb[bank][:, 0:256], 1.0, ptl[pb],
                    ALU.mult, ALU.add), rel(bank))
                vec(lambda v: v.engine_nop(), s_dv)
                ptl_rel[pb] = s_dv.v
                w8("scalar", s_fr[bank], s_fr[bank].v)
                dma("scalar", A(out.tensor, mi * 128 * DM + ot * 256,
                                [[DM, 128], [1, 256]]), ost[ob], s_so)
                ost_rel[ob] = s_so.v
        w8("scalar", s_so)
        w8("vector", s_so)

    for en in ["sync", "gpsimd", "vector", "scalar", "tensor"]:
        getattr(block, en)(lambda e, en=en: emit(en))

    for g in reversed(ctxs):
        g.__exit__(None, None, None)

    return nc, {}


def kernel(**inputs):
    x = np.asarray(inputs["L0_attn_hidden_states"], dtype=np.float32)
    w_in_t = np.ascontiguousarray(np.asarray(inputs["L0_attn_in_proj_w"],
                                             dtype=np.float32).T)
    w_cm = np.ascontiguousarray(
        w_in_t[:, 4096:10240].reshape(32, 128, 48, 128)
        .transpose(2, 1, 0, 3).reshape(48 * 128, 4096))
    w_tm = np.ascontiguousarray(
        w_in_t[:, 0:4096].reshape(32, 128, 16, 256)
        .transpose(2, 1, 0, 3).reshape(16 * 128, 8192))
    w_dt = np.ascontiguousarray(
        w_in_t[:, 10240:10304].reshape(32, 128, 64)
        .transpose(1, 0, 2).reshape(128, 2048))
    conv_w = np.asarray(inputs["L0_attn_conv_weight"], dtype=np.float32)
    convw = np.ascontiguousarray(
        conv_w.reshape(48, 128, KC).transpose(1, 0, 2).reshape(128, 48 * KC))
    convb = np.ascontiguousarray(
        np.asarray(inputs["L0_attn_conv_bias"], dtype=np.float32).reshape(48, 128).T)
    dtb = np.asarray(inputs["L0_attn_dt_bias"], dtype=np.float32).reshape(1, H)
    alog = np.asarray(inputs["L0_attn_A_log"], dtype=np.float32).reshape(1, H)
    dvec = np.asarray(inputs["L0_attn_D"], dtype=np.float32).reshape(1, H)
    nw = np.asarray(inputs["L0_attn_norm_weight"], dtype=np.float32).reshape(1, DM)
    w_out_t = np.ascontiguousarray(np.asarray(inputs["L0_attn_out_proj_w"],
                                              dtype=np.float32).T)
    w_op = np.ascontiguousarray(
        w_out_t.reshape(32, 128, 16, 256).transpose(2, 1, 0, 3)
        .reshape(16 * 128, 8192))
    gate = np.asarray(inputs["L0_ffn_gate"], dtype=np.float32)
    ffn_k = np.ascontiguousarray(np.asarray(inputs["L0_ffn_kernel"],
                                            dtype=np.float32))
    ffn_kp = np.ascontiguousarray(
        ffn_k.reshape(32, 128, 16, 256).transpose(2, 1, 0, 3)
        .reshape(16 * 128, 8192))
    bias = np.asarray(inputs["L0_ffn_bias"], dtype=np.float32).reshape(1, DM)
    lora_A = np.asarray(inputs["L0_ffn_lora_A"], dtype=np.float32)
    lora_B = np.asarray(inputs["L0_ffn_lora_B"], dtype=np.float32)
    lora_As = np.asarray(inputs["L0_ffn_lora_A_shared"], dtype=np.float32)
    lora_Bs = np.asarray(inputs["L0_ffn_lora_B_shared"], dtype=np.float32)

    a_cat = np.ascontiguousarray(np.concatenate(
        [lora_A.transpose(2, 0, 1).reshape(DM, 512), lora_As.T], axis=1))
    b_gat = np.ascontiguousarray(lora_B.transpose(2, 0, 1).reshape(32, 16 * DM))
    b_shd = np.ascontiguousarray(lora_Bs.T)

    eye = np.eye(128, dtype=np.float32)
    triu01 = np.triu(np.ones((128, 128), dtype=np.float32))
    strict = np.tril(np.ones((128, 128), dtype=np.float32), -1)

    n1, n2 = [int(v) for v in np.random.randint(1, 2 ** 30, size=2)]
    nonce = np.array([[n1] * 4 + [n2] * 4], dtype=np.int32)

    globals()['LN_SERIAL_G'] = LN_SERIAL
    globals()['WSL_TEST_G'] = WSL_TEST
    globals()['LN_NOSTATS_G'] = LN_NOSTATS
    nc, _ = build_program(phases=PHASES, debug=DEBUG)

    in_maps = []
    for c in range(8):
        bb, r = c // 4, c % 4
        q = c % 2
        own0 = r * 1024
        xo = np.zeros((NROW, DM), dtype=np.float32)
        h0 = max(own0 - HALO, 0)
        xo[HALO - (own0 - h0):HALO] = x[bb, h0:own0]
        xo[HALO:HALO + 1024] = x[bb, own0:own0 + 1024]
        xp = np.zeros((NROW, DM), dtype=np.float32)
        if r >= 2:
            pre0 = (r - 2) * 1024
            ph0 = max(pre0 - HALO, 0)
            xp[HALO - (pre0 - ph0):HALO] = x[bb, ph0:pre0]
            xp[HALO:HALO + 1024] = x[bb, pre0:pre0 + 1024]
        prmf = np.zeros((128, 8), dtype=np.float32)
        prmf[:, 0] = 1.0 if r == 2 else 0.0
        prmf[:, 1] = 0.0 if r == 0 else 1.0
        s1, s2 = q * 2 + 0, q * 2 + 1
        rkind = 0 if r in (0, 2) else 1
        rslot = (1 - q) * 2 + rkind
        prmi = np.zeros((1, 16), dtype=np.int32)
        prmi[0, 0:3] = [s1 * 128 * NI, s1 * H, s1 * 4]
        prmi[0, 3:6] = [s2 * 128 * NI, s2 * H, s2 * 4]
        prmi[0, 6:9] = [rslot * 128 * NI, rslot * H, rslot * 4]
        prmi[0, 9] = n1 if rkind == 0 else n2
        in_maps.append({
            "x_own": xo, "x_pre": xp, "w_cm": w_cm, "w_tm": w_tm, "w_dt": w_dt,
            "w_op": w_op, "ffn_kp": ffn_kp,
            "a_cat": a_cat, "b_gat": b_gat, "b_shd": b_shd,
            "bias_r": bias, "nw_r": nw, "dvec_r": dvec, "alog_r": alog,
            "dtb_r": dtb, "convw": convw, "convb": convb,
            "gate_r": np.ascontiguousarray(gate[4 * bb + r].reshape(1, 16)),
            "eye_i": eye, "triu_i": triu01, "stri_i": strict,
            "prm_f": prmf, "prm_i": prmi, "nonce_r": nonce,
        })

    res = run_bass_kernel_spmd(nc, in_maps, core_ids=list(range(8)))
    outf = np.empty((2, 4096, DM), dtype=np.float32)
    for c in range(8):
        bb, r = c // 4, c % 4
        outf[bb, r * 1024:(r + 1) * 1024] = res.results[c]["out"]
    return outf
